# version 09
# speedup vs baseline: 1.0629x; 1.0017x over previous
import numpy as np

import concourse.bass as bass
import concourse.mybir as mybir
from concourse.bass_utils import run_bass_kernel_spmd

F32 = mybir.dt.float32
AF = mybir.ActivationFunctionType
OP = mybir.AluOpType
AX = mybir.AxisListType

D = 2048
NCORES = 8
TILE = 256
CH = 64
NCH_T = TILE // CH
H = 32
DH = 64
PC = D // 128
DFF = 5461
DFFP = 5504
NF = DFFP // 128
E8 = 8
LN_EPS = 1e-5
REPS = 1e-10
BIS_IT = 64

ENGS = ("tensor", "vector", "scalar", "sync", "gpsimd")


def _tf_round(v0, v1, rot):
    v0 = (v0 + v1).astype(np.uint32)
    v1 = ((v1 << np.uint32(rot)) | (v1 >> np.uint32(32 - rot))).astype(np.uint32)
    return v0, (v0 ^ v1).astype(np.uint32)


def _threefry2x32(k1, k2, x1, x2):
    rots = ([13, 15, 26, 6], [17, 29, 16, 24])
    ks = [np.uint32(k1), np.uint32(k2),
          np.uint32(np.uint32(k1) ^ np.uint32(k2) ^ np.uint32(0x1BD11BDA))]
    x0 = (x1 + ks[0]).astype(np.uint32)
    y0 = (x2 + ks[1]).astype(np.uint32)
    for i in range(5):
        for r in rots[i % 2]:
            x0, y0 = _tf_round(x0, y0, r)
        x0 = (x0 + ks[(i + 1) % 3]).astype(np.uint32)
        y0 = (y0 + ks[(i + 2) % 3] + np.uint32(i + 1)).astype(np.uint32)
    return x0, y0


def _bits(seed, n):
    k1 = np.uint32(np.uint64(seed) >> np.uint64(32))
    k2 = np.uint32(np.uint64(seed) & np.uint64(0xFFFFFFFF))
    idx = np.arange(n, dtype=np.uint64)
    return _threefry2x32(k1, k2, (idx >> np.uint64(32)).astype(np.uint32),
                         (idx & np.uint64(0xFFFFFFFF)).astype(np.uint32))


def uniform_f32(seed, n):
    b1, b2 = _bits(seed, n)
    fb = ((b1 ^ b2).astype(np.uint32) >> np.uint32(9)) | np.uint32(0x3F800000)
    return fb.view(np.float32) - np.float32(1.0)


def uniform_f64(seed, n):
    b1, b2 = _bits(seed, n)
    bits64 = (b1.astype(np.uint64) << np.uint64(32)) | b2.astype(np.uint64)
    fb = (bits64 >> np.uint64(12)) | np.uint64(0x3FF0000000000000)
    return fb.view(np.float64) - np.float64(1.0)


class Buf:
    __slots__ = ("name", "writer", "readers")

    def __init__(self, name):
        self.name = name
        self.writer = None
        self.readers = []


class Prog:
    def __init__(self):
        self.items = {e: [] for e in ENGS}
        self.val = {}
        self.dma_chain = {}

    def op(self, eng, fn, reads=(), writes=(), writes_acc=(), sem=None, amt=1):
        s = sem if sem else "P_" + eng
        if s not in self.val:
            self.val[s] = 0
        waits = {}

        def addw(tok):
            if tok is not None and waits.get(tok[0], 0) < tok[1]:
                waits[tok[0]] = tok[1]

        for b in reads:
            addw(b.writer)
        for b in writes:
            addw(b.writer)
            for t in b.readers:
                addw(t)
        self.val[s] += amt
        tok = (s, self.val[s])
        self.items[eng].append((waits, fn, (s, amt)))
        for b in reads:
            b.readers.append(tok)
        for b in writes:
            b.writer = tok
            b.readers = []
        for b in writes_acc:
            b.writer = tok
        return tok

    def dma(self, eng, out_ap, in_ap, reads=(), writes=(), writes_acc=(),
            sem="d_gen", accum=None, slow=False):
        if accum is None:
            fn = (lambda e, o=out_ap, i=in_ap, sl=slow:
                  e.dma_start(out=o, in_=i, allow_slow_non_contiguous=sl))
        else:
            fn = (lambda e, o=out_ap, i=in_ap, a=accum, sl=slow:
                  e.dma_start(out=o, in_=i, accum_op=a,
                              allow_slow_non_contiguous=sl))
        ch = self.dma_chain.get(sem)
        if ch is None:
            ch = Buf("chain_" + sem)
            self.dma_chain[sem] = ch
        tok = self.op(eng, fn, list(reads) + [ch], writes, writes_acc,
                      sem=sem, amt=16)
        ch.writer = tok
        ch.readers = []
        return tok

    def wait_only(self, eng, bufs):
        waits = {}
        for b in bufs:
            if b.writer is not None:
                sm, v = b.writer
                if waits.get(sm, 0) < v:
                    waits[sm] = v
                if sm not in self.val:
                    self.val[sm] = v
        self.items[eng].append((waits, None, None))


def _emit(eng_obj, items, handles, waited):
    for waits, fn, inc in items:
        for sm in sorted(waits):
            v = waits[sm]
            if waited.get(sm, 0) >= v:
                continue
            eng_obj.wait_ge(handles[sm], v)
            waited[sm] = v
        if fn is None:
            continue
        inst = fn(eng_obj)
        if inc is not None:
            inst.then_inc(handles[inc[0]], inc[1])


def build_nc(ncores, tpc, dbg=False, trunc=None):
    attn_lvl = 9
    if trunc is not None and trunc.startswith("a"):
        attn_lvl = int(trunc[1:])
        trunc = "t7"
    NT = tpc // TILE
    NCHUNK = NT * NCH_T
    TB = TILE // 128
    SMW = (tpc // 128) * E8
    GW = ncores * SMW
    NIDX = GW // E8
    KSEL = ncores * tpc // E8

    nc = bass.Bass("TRN2", target_bir_lowering=False)
    p = Prog()

    class _Done(Exception):
        pass

    def ck(stage):
        if trunc == stage:
            raise _Done()

    if trunc == "noacc":
        pass

    def din(name, shape):
        return nc.dram_tensor(name, list(shape), F32, kind="ExternalInput").ap()

    x_tm = din("x_tm", (tpc, D))
    x_fm = din("x_fm", (128, PC * tpc))
    x_bnd = din("x_bnd", (1, D))
    qw4 = din("qw4", (PC, 128, PC, 128))
    kw4 = din("kw4", (PC, 128, PC, 128))
    ow4 = din("ow4", (PC, 128, PC, 128))
    vw = din("vw", (D, D))
    gw4 = din("gw4", (NF, 128, PC, 128))
    uw4 = din("uw4", (NF, 128, PC, 128))
    dw4 = din("dw4", (PC, 128, NF, 128))
    m2w4 = din("m2w4", (4, PC, 128, 128))
    maw1 = din("maw1", (128, PC * 128))
    tdw1 = din("tdw1", (128, PC * 64))
    tdw2 = din("tdw2", (128, 1024))
    rw = din("rw", (128, PC * E8))
    cols_d = din("cols", (128, 16 * PC))
    bgu = din("bgu", (128, 2 * NF))
    vbrow = din("vbrow", (1, D))
    rbrow = din("rbrow", (1, E8))
    ident = din("ident", (128, 128))
    mask8d = din("mask8", (64, 512))
    onescd = din("onesc", (128, 1))
    onesrd = din("onesr", (1, 128))
    incl2 = din("incl2", (128, 16))
    u3 = din("u3", (128, 3 * SMW))

    y_fm = nc.dram_tensor("y_fm", [128, PC * tpc], F32, kind="ExternalOutput").ap()

    qt_scr = nc.dram_tensor("qt_scr", [128, PC * tpc], F32).ap()
    o_scr = nc.dram_tensor("o_scr", [128, PC * tpc], F32).ap()
    ffn_scr = nc.dram_tensor("ffn_scr", [128, PC * tpc], F32).ap()
    bnd_scr = nc.dram_tensor("bnd_scr", [1, D], F32).ap()
    if dbg:
        dbg_wlog = nc.dram_tensor("dbg_wlog", [128, PC * tpc], F32).ap()
        dbg_xw = nc.dram_tensor("dbg_xw", [128, PC * tpc], F32).ap()
        dbg_oc = nc.dram_tensor("dbg_oc", [128, PC * tpc], F32).ap()
        dbg_y = nc.dram_tensor("dbg_y", [128, PC * tpc], F32).ap()
        d_dbg = Buf("d_dbg")
    ccsm_i = nc.dram_tensor("ccsm_i", [128, SMW], F32).ap()
    ccsm_o = nc.dram_tensor("ccsm_o", [128 * ncores, SMW], F32).ap()
    ccsu_i = nc.dram_tensor("ccsu_i", [128, 1040], F32).ap()
    ccsu_o = nc.dram_tensor("ccsu_o", [128 * ncores, 1040], F32).ap()

    d_qt = Buf("d_qt"); d_o = Buf("d_o"); d_ffn = Buf("d_ffn")
    d_bnd = Buf("d_bnd"); d_y = Buf("d_y")
    d_ci = Buf("d_ci"); d_co = Buf("d_co")
    d_si = Buf("d_si"); d_so = Buf("d_so")

    from contextlib import ExitStack
    with ExitStack() as _st:
        BB0 = _st.enter_context(nc.sbuf_tensor("BB0", [128, PC * TILE], F32))
        BB1 = _st.enter_context(nc.sbuf_tensor("BB1", [128, PC * TILE], F32))
        BB2 = _st.enter_context(nc.sbuf_tensor("BB2", [128, PC * TILE], F32))
        BB3 = _st.enter_context(nc.sbuf_tensor("BB3", [128, PC * TILE], F32))
        BB4 = _st.enter_context(nc.sbuf_tensor("BB4", [128, PC * TILE], F32))
        BB5 = _st.enter_context(nc.sbuf_tensor("BB5", [128, PC * TILE], F32))
        BB6 = _st.enter_context(nc.sbuf_tensor("BB6", [128, PC * TILE], F32))
        RG0 = _st.enter_context(nc.sbuf_tensor("RG0", [128, 2048], F32))
        RG1 = _st.enter_context(nc.sbuf_tensor("RG1", [128, 2048], F32))
        RG2 = _st.enter_context(nc.sbuf_tensor("RG2", [128, 2048], F32))
        RG3 = _st.enter_context(nc.sbuf_tensor("RG3", [128, 2048], F32))
        S_sb = _st.enter_context(nc.sbuf_tensor("S_sb", [128, H * 32], F32))
        Sin_sb = _st.enter_context(nc.sbuf_tensor("Sin_sb", [128, H * 32], F32))
        SC_sb = _st.enter_context(nc.sbuf_tensor("SC_sb", [128, H * 32], F32))
        rw_sb = _st.enter_context(nc.sbuf_tensor("rw_sb", [128, PC * E8], F32))
        vbbc_sb = _st.enter_context(nc.sbuf_tensor("vbbc_sb", [128, D], F32))
        rbbc_sb = _st.enter_context(nc.sbuf_tensor("rbbc_sb", [128, E8], F32))
        cols_sb = _st.enter_context(nc.sbuf_tensor("cols_sb", [128, 16 * PC], F32))
        bgu_sb = _st.enter_context(nc.sbuf_tensor("bgu_sb", [128, 2 * NF], F32))
        id_sb = _st.enter_context(nc.sbuf_tensor("id_sb", [128, 128], F32))
        mask8_sb = _st.enter_context(nc.sbuf_tensor("mask8_sb", [64, 512], F32))
        ones_sb = _st.enter_context(nc.sbuf_tensor("ones_sb", [128, 1], F32))
        onesr_sb = _st.enter_context(nc.sbuf_tensor("onesr_sb", [1, 128], F32))
        incl_sb = _st.enter_context(nc.sbuf_tensor("incl_sb", [128, 16], F32))
        gum_sb = _st.enter_context(nc.sbuf_tensor("gum_sb", [128, SMW], F32))
        smOwn = _st.enter_context(nc.sbuf_tensor("smOwn", [128, SMW], F32))
        flag_sb = _st.enter_context(nc.sbuf_tensor("flag_sb", [128, 1], F32))
        qb8_sb = _st.enter_context(nc.sbuf_tensor("qb8_sb", [128, PC], F32))
        Bbase = _st.enter_context(nc.sbuf_tensor("Bbase", [128, (NCHUNK + 1) * PC], F32))
        Bcarry = _st.enter_context(nc.sbuf_tensor("Bcarry", [128, PC], F32))
        Dcols = _st.enter_context(nc.sbuf_tensor("Dcols", [128, NCH_T * PC], F32))
        eb_sb = _st.enter_context(nc.sbuf_tensor("eb_sb", [128, PC], F32))
        xnlast = _st.enter_context(nc.sbuf_tensor("xnlast", [128, PC], F32))
        tanh_sb = _st.enter_context(nc.sbuf_tensor("tanh_sb", [128, TILE], F32))
        wtd_sb = _st.enter_context(nc.sbuf_tensor("wtd_sb", [128, TILE], F32))
        statc = _st.enter_context(nc.sbuf_tensor("statc", [128, 24], F32))
        wgt_sb = _st.enter_context(nc.sbuf_tensor("wgt_sb", [128, tpc // 128], F32))
        wgtbc_sb = _st.enter_context(nc.sbuf_tensor("wgtbc_sb", [128, 2 * TILE], F32))
        ones128 = _st.enter_context(nc.sbuf_tensor("ones128", [128, 128], F32))
        bis8 = _st.enter_context(nc.sbuf_tensor("bis8", [128, 48], F32))
        dsl_sb = _st.enter_context(nc.sbuf_tensor("dsl_sb", [128, PC], F32))
        cstc = _st.enter_context(nc.sbuf_tensor("cstc", [128, 2], F32))
        ROWA = _st.enter_context(nc.sbuf_tensor("ROWA", [1, 2048], F32))
        ROWB = _st.enter_context(nc.sbuf_tensor("ROWB", [1, 2048], F32))
        PB0 = _st.enter_context(nc.psum_tensor("PB0", [128, 512], F32))
        PB1 = _st.enter_context(nc.psum_tensor("PB1", [128, 512], F32))
        PB2 = _st.enter_context(nc.psum_tensor("PB2", [128, 512], F32))
        PB3 = _st.enter_context(nc.psum_tensor("PB3", [128, 512], F32))
        PB4 = _st.enter_context(nc.psum_tensor("PB4", [128, 512], F32))
        PB5 = _st.enter_context(nc.psum_tensor("PB5", [128, 512], F32))
        PB6 = _st.enter_context(nc.psum_tensor("PB6", [128, 512], F32))
        PB7 = _st.enter_context(nc.psum_tensor("PB7", [128, 512], F32))
        block = _st.enter_context(nc.Block())
        smAll = BB0[:, 0:GW]
        bisc = BB0[:, 1024:1024 + GW]
        midbc = BB0[:, 2048:2048 + GW]
        u3_sb = BB2[:, 0:3 * SMW]
        Ur_sb = BB6[:, 0:1040]
        wgtrow = ROWB
        vrow_sb = ROWA
        bndrow = ROWB
        abrow = ROWA
        PB = [PB0, PB1, PB2, PB3, PB4, PB5, PB6, PB7]
        pb = [Buf("PB%d" % i) for i in range(8)]
        BBT = [BB0, BB1, BB2, BB3, BB4, BB5, BB6]
        bb = [Buf("BB%d" % i) for i in range(7)]
        RGT = [RG0, RG1, RG2, RG3]
        rg = [Buf("RG%d" % i) for i in range(4)]

        b_S = Buf("S"); b_Sin = Buf("Sin"); b_SC = Buf("SC")
        b_smAll = bb[0]; b_bisc = bb[0]; b_midbc = bb[0]
        b_Ur = bb[6]
        b_rw = Buf("rw"); b_vbbc = Buf("vbbc")
        b_rbbc = Buf("rbbc"); b_cols = Buf("cols"); b_bgu = Buf("bgu")
        b_id = Buf("id"); b_mask8 = Buf("mask8"); b_ones = Buf("ones")
        b_onesr = Buf("onesr"); b_incl = Buf("incl"); b_u3 = bb[2]
        b_gum = Buf("gum"); b_smOwn = Buf("smOwn"); b_flag = Buf("flag")
        b_qb8 = Buf("qb8"); b_Bb = Buf("Bbase"); b_Bc = Buf("Bcarry")
        b_Dc = Buf("Dcols"); b_eb = Buf("eb"); b_xnl = Buf("xnlast")
        b_tanh = Buf("tanh"); b_wtd = Buf("wtd"); b_st = Buf("statc")
        b_wgt = Buf("wgt"); b_wgb = Buf("wgtbc")
        b_bis = Buf("bis8"); b_vr = Buf("rowa"); b_bnd = Buf("rowb")
        b_ab = b_vr; b_wgr = b_bnd; b_dsl = Buf("dsl")

        CL = {"maa_x": 0, "maa_r": 1, "maa_k": 2, "maa_v": 3, "maa_w": 4,
              "tdec": 5, "q_b": 6, "k_b": 7, "ln_g": 8, "ln_b": 9, "b_dn": 10}

        def col(nm, pcc):
            i = CL[nm] * PC + pcc
            return cols_sb[:, i:i + 1]

        wring = [0]
        held = set()

        def rn():
            for _ in range(8):
                i = wring[0] % 4
                wring[0] += 1
                if i not in held:
                    return i
            raise RuntimeError("ring exhausted")

        def v3(i, b=TILE):
            return BBT[i][:, :].rearrange("p (a c) -> p a c", c=b)

        Bb3 = Bbase[:, :].rearrange("p (m c) -> p m c", c=PC)

        try:
            LD = "d_c"
            p.dma("sync", cols_sb[:, :], cols_d[:, :], writes=[b_cols], sem=LD)
            p.dma("sync", bgu_sb[:, :], bgu[:, :], writes=[b_bgu], sem=LD)
            p.dma("sync", id_sb[:, :], ident[:, :], writes=[b_id], sem=LD)
            p.dma("sync", mask8_sb[:, :], mask8d[:, :], writes=[b_mask8], sem=LD)
            p.dma("sync", ones_sb[:, :], onescd[:, :], writes=[b_ones], sem=LD)
            p.dma("sync", onesr_sb[:, :], onesrd[:, :], writes=[b_onesr], sem=LD)
            p.dma("sync", incl_sb[:, :], incl2[:, :], writes=[b_incl], sem=LD)
            p.dma("sync", u3_sb[:, :], u3[:, :], writes=[b_u3], sem=LD)
            p.dma("sync", rw_sb[:, :], rw[:, :], writes=[b_rw], sem=LD)
            p.dma("sync", bndrow[:, :], x_bnd[:, :], writes=[b_bnd], sem=LD)
            p.dma("sync", vrow_sb[:, :], vbrow[:, :], writes=[b_vr], sem=LD)
            p.dma("sync", statc[0:1, 0:E8], rbrow[:, :], writes=[b_st], sem=LD)

            b_cst = Buf("cst")
            p.op("vector", lambda e: e.memset(cstc[:, 0:1], REPS), writes=[b_cst])
            p.op("vector", lambda e: e.memset(cstc[:, 1:2], LN_EPS), writes=[b_cst])
            b_o128 = Buf("o128")
            p.op("vector", lambda e: e.memset(ones128[:, :], 1.0), writes=[b_o128])

            p.op("vector", lambda e: e.memset(flag_sb[:, :], 1.0), writes=[b_flag])
            p.op("vector", lambda e: e.tensor_scalar(
                flag_sb[:, :], flag_sb[:, :], 2.0 ** -30, None, OP.add),
                writes=[b_flag])
            p.op("vector", lambda e: e.tensor_scalar(
                flag_sb[:, :], flag_sb[:, :], -1.0, 2.0 ** 30, OP.add, OP.mult),
                writes=[b_flag])
            p.op("vector", lambda e: e.tensor_scalar(
                qb8_sb[:, :], cols_sb[:, CL["q_b"] * PC:(CL["q_b"] + 1) * PC],
                0.125, None, OP.mult), reads=[b_cols], writes=[b_qb8])
            p.op("vector", lambda e: e.scalar_tensor_tensor(
                u3_sb[:, 0:SMW], u3_sb[:, SMW:2 * SMW], flag_sb[:, 0:1],
                u3_sb[:, 0:SMW], OP.mult, OP.add), reads=[b_flag], writes=[b_u3])
            p.op("vector", lambda e: e.scalar_tensor_tensor(
                u3_sb[:, 0:SMW], u3_sb[:, 2 * SMW:3 * SMW], flag_sb[:, 0:1],
                u3_sb[:, 0:SMW], OP.mult, OP.add), reads=[b_flag], writes=[b_u3])
            p.op("scalar", lambda e: e.activation(
                gum_sb[:, :], u3_sb[:, 0:SMW], AF.Ln, bias=cstc[:, 0:1]),
                reads=[b_u3], writes=[b_gum])
            p.op("scalar", lambda e: e.activation(
                gum_sb[:, :], gum_sb[:, :], AF.Ln, bias=cstc[:, 0:1], scale=-1.0),
                writes=[b_gum])
            for i in range(4):
                p.op("tensor", lambda e, i=i: e.matmul(
                    PB0[:, 0:512], onesr_sb[0:1, :], vrow_sb[0:1, i * 512:(i + 1) * 512],
                    start=True, stop=True), reads=[b_vr, b_onesr], writes=[pb[0]])
                p.op("vector", lambda e, i=i: e.tensor_copy(
                    vbbc_sb[:, i * 512:(i + 1) * 512], PB0[:, 0:512]),
                    reads=[pb[0]], writes=[b_vbbc])
            p.op("tensor", lambda e: e.matmul(
                PB1[:, 0:E8], onesr_sb[0:1, :], statc[0:1, 0:E8],
                start=True, stop=True), reads=[b_st, b_onesr], writes=[pb[1]])
            p.op("vector", lambda e: e.tensor_copy(rbbc_sb[:, :], PB1[:, 0:E8]),
                 reads=[pb[1]], writes=[b_rbbc])

            p.op("vector", lambda e: e.reduce_sum(
                statc[0:1, 16:17], bndrow[0:1, :], axis=AX.X),
                reads=[b_bnd], writes=[b_st])
            p.op("scalar", lambda e: e.activation(
                BBT[1][0:1, 0:D], bndrow[0:1, 0:D],
                AF.Square, accum_out=statc[0:1, 17:18]),
                reads=[b_bnd], writes=[bb[1], b_st])
            p.op("vector", lambda e: e.tensor_scalar(
                statc[0:1, 16:18], statc[0:1, 16:18], 1.0 / D, None, OP.mult),
                writes=[b_st])
            p.op("scalar", lambda e: e.activation(
                statc[0:1, 18:19], statc[0:1, 16:17], AF.Square), reads=[b_st],
                writes=[b_st])
            p.op("vector", lambda e: e.tensor_sub(
                statc[0:1, 19:20], statc[0:1, 17:18], statc[0:1, 18:19]),
                writes=[b_st])
            p.op("scalar", lambda e: e.activation(
                statc[0:1, 20:21], statc[0:1, 19:20], AF.Sqrt, bias=cstc[0:1, 1:2]),
                writes=[b_st])
            p.op("vector", lambda e: e.reciprocal(
                statc[0:1, 21:22], statc[0:1, 20:21]), writes=[b_st])
            p.op("vector", lambda e: e.tensor_scalar(
                bndrow[0:1, :], bndrow[0:1, :], statc[0:1, 16:17],
                statc[0:1, 21:22], OP.subtract, OP.mult),
                reads=[b_st], writes=[b_bnd])
            for pcc in range(PC):
                p.op("tensor", lambda e, pcc=pcc: e.matmul(
                    PB2[:, pcc:pcc + 1], bndrow[0:1, pcc * 128:(pcc + 1) * 128],
                    ones_sb[0:1, 0:1], start=True, stop=True),
                    reads=[b_bnd, b_ones], writes=[pb[2]])
            p.op("vector", lambda e: e.tensor_copy(
                xnlast[:, :], PB2[:, 0:PC]), reads=[pb[2]], writes=[b_xnl])

            p.op("vector", lambda e: e.memset(S_sb[:, :], 0.0), writes=[b_S])
            p.op("vector", lambda e: e.memset(Bcarry[:, :], 0.0), writes=[b_Bc])
            p.op("vector", lambda e: e.memset(Bbase[:, 0:PC], 0.0), writes=[b_Bb])

            ck("pre")
            def mix_into_bb2(mname, r, dst_i=2):
                for pcc in range(PC):
                    ri = rn()
                    bank = pcc % 2
                    p.dma("sync", RGT[ri][:, 0:128], m2w4[r, pcc],
                          writes=[rg[ri]], sem="d_w%d" % ri)
                    p.op("tensor", lambda e, ri=ri, bank=bank: e.matmul(
                        PB[bank][:, 0:TILE], RGT[ri][:, 0:128], tanh_sb[:, :],
                        start=True, stop=True),
                        reads=[rg[ri], b_tanh], writes=[pb[bank]])
                    p.op("vector", lambda e, pcc=pcc, bank=bank, mname=mname,
                         dst_i=dst_i: e.scalar_tensor_tensor(
                             BBT[dst_i][:, pcc * TILE:(pcc + 1) * TILE],
                             PB[bank][:, 0:TILE], col(mname, pcc),
                             BBT[1][:, pcc * TILE:(pcc + 1) * TILE],
                             OP.add, OP.mult),
                         reads=[pb[bank], b_cols, bb[1]], writes=[bb[dst_i]])
                    p.op("vector", lambda e, pcc=pcc, dst_i=dst_i: e.tensor_add(
                        BBT[dst_i][:, pcc * TILE:(pcc + 1) * TILE],
                        BBT[dst_i][:, pcc * TILE:(pcc + 1) * TILE],
                        BBT[0][:, pcc * TILE:(pcc + 1) * TILE]),
                        reads=[bb[0]], writes=[bb[dst_i]])

            def proj_2048(w4, dst_i, evac):
                for do in range(PC):
                    ri = rn()
                    bank = do % 2
                    p.dma("sync", RGT[ri][:, :],
                          w4[do].rearrange("p a b -> p (a b)"),
                          writes=[rg[ri]], sem="d_w%d" % ri)

                    def mmg(e, ri=ri, bank=bank):
                        last = None
                        for k in range(PC):
                            last = e.matmul(PB[bank][:, 0:TILE],
                                            RGT[ri][:, k * 128:(k + 1) * 128],
                                            BBT[2][:, k * TILE:(k + 1) * TILE],
                                            start=(k == 0), stop=(k == PC - 1))
                        return last
                    p.op("tensor", mmg, reads=[rg[ri], bb[2]], writes=[pb[bank]])
                    evac(do, bank)

            def bis_iter():
                p.op("vector", lambda e: e.tensor_tensor(
                    bisc[:, :], smAll[:, :], midbc[:, :], OP.is_ge),
                    reads=[b_smAll, b_midbc], writes=[b_bisc])
                p.op("vector", lambda e: e.tensor_reduce(
                    bis8[:, 24:32].rearrange("p (a c) -> p a c", c=1),
                    bisc[:, :].rearrange("p (i e) -> p i e", e=E8).transpose([0, 2, 1]),
                    AX.X, OP.add), reads=[b_bisc], writes=[b_bis])
                p.op("tensor", lambda e: e.matmul(
                    PB3[:, 0:E8], ones128[:, :], bis8[:, 24:32],
                    start=True, stop=True), reads=[b_bis, b_o128], writes=[pb[3]])
                p.op("vector", lambda e: e.tensor_copy(
                    bis8[:, 32:40], PB3[:, 0:E8]), reads=[pb[3]], writes=[b_bis])
                p.op("vector", lambda e: e.tensor_scalar(
                    bis8[:, 32:40], bis8[:, 32:40], float(KSEL), None, OP.is_ge),
                    writes=[b_bis])
                p.op("vector", lambda e: e.tensor_sub(
                    bis8[:, 40:48], bis8[:, 16:24], bis8[:, 0:8]), writes=[b_bis])
                p.op("vector", lambda e: e.tensor_mul(
                    bis8[:, 40:48], bis8[:, 40:48], bis8[:, 32:40]), writes=[b_bis])
                p.op("vector", lambda e: e.tensor_add(
                    bis8[:, 0:8], bis8[:, 0:8], bis8[:, 40:48]), writes=[b_bis])
                p.op("vector", lambda e: e.tensor_sub(
                    bis8[:, 40:48], bis8[:, 16:24], bis8[:, 8:16]), writes=[b_bis])
                p.op("vector", lambda e: e.tensor_scalar(
                    bis8[:, 32:40], bis8[:, 32:40], -1.0, 1.0, OP.mult, OP.add),
                    writes=[b_bis])
                p.op("vector", lambda e: e.tensor_mul(
                    bis8[:, 40:48], bis8[:, 40:48], bis8[:, 32:40]), writes=[b_bis])
                p.op("vector", lambda e: e.tensor_add(
                    bis8[:, 8:16], bis8[:, 8:16], bis8[:, 40:48]), writes=[b_bis])
                p.op("vector", lambda e: e.tensor_add(
                    bis8[:, 16:24], bis8[:, 0:8], bis8[:, 8:16]), writes=[b_bis])
                p.op("vector", lambda e: e.tensor_scalar(
                    bis8[:, 16:24], bis8[:, 16:24], 0.5, None, OP.mult),
                    writes=[b_bis])
                p.op("vector", lambda e: e.tensor_copy(
                    midbc[:, :].rearrange("p (i e) -> p i e", e=E8),
                    bis8[:, 16:24].unsqueeze(1).broadcast_to([128, GW // E8, E8])),
                    reads=[b_bis], writes=[b_midbc])

            def stage_T12(t):
                tok0 = t * TILE

                for tb in range(TB):
                    dst = BBT[1][:, tb * D:(tb + 1) * D]
                    p.dma("sync", dst,
                          x_tm[tok0 + tb * 128: tok0 + (tb + 1) * 128, :],
                          writes=[bb[1]], sem="d_x%d" % tb)
                    sc = 8 * tb
                    p.op("vector", lambda e, dst=dst, sc=sc: e.reduce_sum(
                        statc[:, sc:sc + 1], dst, axis=AX.X),
                        reads=[bb[1]], writes=[b_st])
                    p.op("scalar", lambda e, dst=dst, sc=sc, tb=tb: e.activation(
                        BBT[2][:, tb * D:(tb + 1) * D], dst, AF.Square,
                        accum_out=statc[:, sc + 1:sc + 2]),
                        reads=[bb[1]], writes=[bb[2], b_st])
                    p.op("vector", lambda e, sc=sc: e.tensor_scalar(
                        statc[:, sc + 2:sc + 4], statc[:, sc:sc + 2], 1.0 / D,
                        None, OP.mult), writes=[b_st])
                    p.op("scalar", lambda e, sc=sc: e.activation(
                        statc[:, sc + 4:sc + 5], statc[:, sc + 2:sc + 3],
                        AF.Square), writes=[b_st])
                    p.op("vector", lambda e, sc=sc: e.tensor_sub(
                        statc[:, sc + 5:sc + 6], statc[:, sc + 3:sc + 4],
                        statc[:, sc + 4:sc + 5]), writes=[b_st])
                    p.op("scalar", lambda e, sc=sc: e.activation(
                        statc[:, sc + 6:sc + 7], statc[:, sc + 5:sc + 6], AF.Sqrt,
                        bias=cstc[:, 1:2]), writes=[b_st])
                    p.op("vector", lambda e, sc=sc: e.reciprocal(
                        statc[:, sc + 7:sc + 8], statc[:, sc + 6:sc + 7]),
                        writes=[b_st])
                    p.op("vector", lambda e, dst=dst, sc=sc: e.tensor_scalar(
                        dst, dst, statc[:, sc + 2:sc + 3], statc[:, sc + 7:sc + 8],
                        OP.subtract, OP.mult), reads=[b_st], writes=[bb[1]])


            def stage_T34(t):
                tok0 = t * TILE
                for tb in range(TB):
                    for g in range(4):
                        def tg(e, tb=tb, g=g):
                            last = None
                            for j in range(4):
                                pcx = g * 4 + j
                                last = e.transpose(
                                    PB6[:, j * 128:(j + 1) * 128],
                                    BBT[1][:, tb * D + pcx * 128: tb * D + (pcx + 1) * 128],
                                    id_sb[:, :])
                            return last
                        p.op("tensor", tg, reads=[bb[1], b_id], writes=[pb[6]])
                        p.op("vector", lambda e, tb=tb, g=g: e.tensor_copy(
                            v3(0)[:, g * 4:(g + 1) * 4, tb * 128:(tb + 1) * 128],
                            PB6[:, 0:512].rearrange("p (a c) -> p a c", c=128)),
                            reads=[pb[6]], writes=[bb[0]])

                p.op("vector", lambda e: e.tensor_copy(
                    statc[:, 16:16 + PC] if False else dsl_sb[:, :],
                    v3(0)[:, :, TILE - 1:TILE].rearrange("p a c -> p (a c)")),
                    reads=[bb[0]], writes=[b_dsl])
                p.op("vector", lambda e: e.tensor_sub(
                    v3(1)[:, :, 1:TILE], v3(0)[:, :, 0:TILE - 1],
                    v3(0)[:, :, 1:TILE]), reads=[bb[0]], writes=[bb[1]])
                p.op("vector", lambda e: e.tensor_sub(
                    v3(1)[:, :, 0:1], xnlast[:, :].unsqueeze(2),
                    v3(0)[:, :, 0:1]), reads=[bb[0], b_xnl], writes=[bb[1]])
                p.op("vector", lambda e: e.tensor_copy(xnlast[:, :], dsl_sb[:, :]),
                     reads=[b_dsl], writes=[b_xnl])

                ck("t3")
                for tb in range(TB):
                    blk = t * TB + tb

                    def rmm(e, tb=tb):
                        last = None
                        for k in range(PC):
                            last = e.matmul(
                                PB7[:, 0:E8],
                                BBT[0][:, k * TILE + tb * 128: k * TILE + (tb + 1) * 128],
                                rw_sb[:, k * E8:(k + 1) * E8],
                                start=(k == 0), stop=(k == PC - 1))
                        return last
                    p.op("tensor", rmm, reads=[bb[0], b_rw], writes=[pb[7]])
                    sm = smOwn[:, blk * E8:(blk + 1) * E8]
                    p.op("vector", lambda e, sm=sm: e.tensor_add(
                        sm, PB7[:, 0:E8], rbbc_sb[:, :]),
                        reads=[pb[7], b_rbbc], writes=[b_smOwn])
                    p.op("vector", lambda e, sm=sm, blk=blk: e.tensor_sub(
                        sm, sm, gum_sb[:, blk * E8:(blk + 1) * E8]),
                        reads=[b_gum], writes=[b_smOwn])
                    p.op("vector", lambda e, sm=sm: e.reduce_max(
                        statc[:, 16:17], sm, axis=AX.X, negate=True),
                        reads=[b_smOwn], writes=[b_st])
                    p.op("scalar", lambda e, sm=sm: e.activation(
                        sm, sm, AF.Exp, bias=statc[:, 16:17],
                        accum_out=statc[:, 17:18]),
                        reads=[b_st], writes=[b_smOwn, b_st])
                    p.op("vector", lambda e: e.reciprocal(
                        statc[:, 18:19], statc[:, 17:18]), writes=[b_st])
                    p.op("vector", lambda e, sm=sm: e.tensor_scalar(
                        sm, sm, statc[:, 18:19], None, OP.mult),
                        reads=[b_st], writes=[b_smOwn])


            stage_T12(0)
            stage_T34(0)

            for t in range(NT):
                tok0 = t * TILE
                ck("t4")
                ri_m = rn()
                p.dma("sync", RGT[ri_m][:, :], maw1[:, :], writes=[rg[ri_m]],
                      sem="d_w%d" % ri_m)
                ri_t = rn()
                held.add(ri_t)
                p.dma("sync", RGT[ri_t][:, 0:1024], tdw1[:, :], writes=[rg[ri_t]],
                      sem="d_w%d" % ri_t)
                p.dma("sync", RGT[ri_t][:, 1024:2048], tdw2[:, :],
                      writes_acc=[rg[ri_t]], sem="d_w%d" % ri_t)
                for pcc in range(PC):
                    p.op("vector", lambda e, pcc=pcc: e.scalar_tensor_tensor(
                        BBT[2][:, pcc * TILE:(pcc + 1) * TILE],
                        BBT[1][:, pcc * TILE:(pcc + 1) * TILE],
                        col("maa_x", pcc),
                        BBT[0][:, pcc * TILE:(pcc + 1) * TILE],
                        OP.mult, OP.add),
                        reads=[bb[1], bb[0], b_cols], writes=[bb[2]])

                def mm_tanh(e, ri=ri_m):
                    last = None
                    for k in range(PC):
                        last = e.matmul(PB0[:, 0:TILE],
                                        RGT[ri][:, k * 128:(k + 1) * 128],
                                        BBT[2][:, k * TILE:(k + 1) * TILE],
                                        start=(k == 0), stop=(k == PC - 1))
                    return last
                p.op("tensor", mm_tanh, reads=[bb[2], rg[ri_m]], writes=[pb[0]])
                p.op("scalar", lambda e: e.activation(
                    tanh_sb[:, :], PB0[:, 0:TILE], AF.Tanh),
                    reads=[pb[0]], writes=[b_tanh])

                mix_into_bb2("maa_w", 3)
                if dbg:
                    p.dma("scalar", dbg_xw[:, tok0 * PC:(tok0 + TILE) * PC],
                        BBT[2][:, :],
                        reads=[bb[2]], writes_acc=[d_dbg], sem="d_dbg")

                def mm_wtd(e, ri=ri_t):
                    last = None
                    for k in range(PC):
                        last = e.matmul(PB2[0:64, 0:TILE],
                                        RGT[ri][:, k * 64:(k + 1) * 64],
                                        BBT[2][:, k * TILE:(k + 1) * TILE],
                                        start=(k == 0), stop=(k == PC - 1))
                    return last
                p.op("tensor", mm_wtd, reads=[bb[2], rg[ri_t]], writes=[pb[2]])
                p.op("scalar", lambda e: e.activation(
                    wtd_sb[0:64, :], PB2[0:64, 0:TILE], AF.Tanh),
                    reads=[pb[2]], writes=[b_wtd])
                p.op("vector", lambda e: e.tensor_copy(
                    wtd_sb[64:128, :], wtd_sb[0:64, :]), writes=[b_wtd])
                for pcc in range(PC):
                    bank = pcc % 2
                    rh = (pcc // 8) * 64
                    cc0 = 1024 + (pcc % 8) * 128
                    p.op("tensor", lambda e, bank=bank, rh=rh, cc0=cc0, ri=ri_t:
                         e.matmul(
                             PB[bank][:, 0:TILE],
                             RGT[ri][rh:rh + 64, cc0:cc0 + 128],
                             wtd_sb[rh:rh + 64, :], start=True, stop=True),
                         reads=[rg[ri_t], b_wtd], writes=[pb[bank]])
                    p.op("scalar", lambda e, pcc=pcc, bank=bank: e.activation(
                        BBT[3][:, pcc * TILE:(pcc + 1) * TILE],
                        PB[bank][:, 0:TILE], AF.Exp, bias=col("tdec", pcc)),
                        reads=[pb[bank], b_cols], writes=[bb[3]])
                    p.op("vector", lambda e, pcc=pcc: e.tensor_scalar(
                        BBT[3][:, pcc * TILE:(pcc + 1) * TILE],
                        BBT[3][:, pcc * TILE:(pcc + 1) * TILE],
                        5.0, -1.0, OP.min, OP.mult), writes=[bb[3]])

                if dbg:
                    p.dma("scalar", dbg_wlog[:, tok0 * PC:(tok0 + TILE) * PC],
                        BBT[3][:, :],
                        reads=[bb[3]], writes_acc=[d_dbg], sem="d_dbg")
                held.discard(ri_t)
                mix_into_bb2("maa_v", 2, dst_i=6)
                for k in range(PC):
                    ri = rn()
                    p.dma("sync", RGT[ri][:, :], vw[k * 128:(k + 1) * 128, :],
                          writes=[rg[ri]], sem="d_w%d" % ri)

                    def vmm(e, ri=ri, k=k):
                        last = None
                        for tb in range(TB):
                            for do in range(4):
                                last = e.matmul(
                                    PB[tb * 4 + do][:, 0:512],
                                    BBT[6][:, k * TILE + tb * 128: k * TILE + (tb + 1) * 128],
                                    RGT[ri][:, do * 512:(do + 1) * 512],
                                    start=(k == 0), stop=(k == PC - 1))
                        return last
                    p.op("tensor", vmm, reads=[rg[ri], bb[6]], writes=pb[0:8])
                for tb in range(TB):
                    for do in range(4):
                        p.op("vector", lambda e, tb=tb, do=do: e.tensor_add(
                            BBT[6][:, tb * D + do * 512: tb * D + (do + 1) * 512],
                            PB[tb * 4 + do][:, 0:512],
                            vbbc_sb[:, do * 512:(do + 1) * 512]),
                            reads=[pb[tb * 4 + do], b_vbbc], writes=[bb[6]])

                mix_into_bb2("maa_r", 0)

                def ev_q(do, bank):
                    p.op("scalar", lambda e, do=do, bank=bank: e.activation(
                        BBT[4][:, do * TILE:(do + 1) * TILE], PB[bank][:, 0:TILE],
                        AF.Identity, bias=qb8_sb[:, do:do + 1], scale=0.125),
                        reads=[pb[bank], b_qb8], writes=[bb[4]])
                proj_2048(qw4, 4, ev_q)

                mix_into_bb2("maa_k", 1)

                CH0 = t * NCH_T
                for pcc in range(PC):
                    p.op("vector", lambda e, pcc=pcc: e.tensor_tensor_scan(
                        BBT[1][:, pcc * TILE:(pcc + 1) * TILE],
                        BBT[3][:, pcc * TILE:(pcc + 1) * TILE],
                        BBT[3][:, pcc * TILE:(pcc + 1) * TILE],
                        Bcarry[:, pcc:pcc + 1], OP.add, OP.bypass),
                        reads=[bb[3], b_Bc], writes=[bb[1]])
                p.op("vector", lambda e, CH0=CH0: e.tensor_copy(
                    Bbase[:, CH0 * PC:(CH0 + 1) * PC], Bcarry[:, :]),
                    reads=[b_Bc], writes=[b_Bb])
                for m in range(1, NCH_T):
                    p.op("vector", lambda e, m=m, CH0=CH0: e.tensor_copy(
                        Bbase[:, (CH0 + m) * PC:(CH0 + m + 1) * PC],
                        v3(1)[:, :, m * CH - 1:m * CH].rearrange(
                            "p a c -> p (a c)")),
                        reads=[bb[1]], writes=[b_Bb])
                p.op("vector", lambda e, CH0=CH0: e.tensor_copy(
                    Bbase[:, (CH0 + NCH_T) * PC:(CH0 + NCH_T + 1) * PC],
                    v3(1)[:, :, TILE - 1:TILE].rearrange("p a c -> p (a c)")),
                    reads=[bb[1]], writes=[b_Bb])
                p.op("vector", lambda e, CH0=CH0: e.tensor_copy(
                    Bcarry[:, :],
                    Bbase[:, (CH0 + NCH_T) * PC:(CH0 + NCH_T + 1) * PC]),
                    reads=[b_Bb], writes=[b_Bc])
                p.op("vector", lambda e, CH0=CH0: e.tensor_sub(
                    Dcols[:, :],
                    Bbase[:, (CH0 + 1) * PC:(CH0 + 1 + NCH_T) * PC],
                    Bbase[:, CH0 * PC:(CH0 + NCH_T) * PC]),
                    reads=[b_Bb], writes=[b_Dc])
                p.op("scalar", lambda e: e.activation(
                    Dcols[:, :], Dcols[:, :], AF.Exp), writes=[b_Dc])
                X4 = lambda i: BBT[i][:, :].rearrange(
                    "p (q m c) -> p q m c", m=NCH_T, c=CH)
                for m in range(NCH_T):
                    p.op("scalar", lambda e, m=m, CH0=CH0: e.copy(
                        v3(3)[:, :, m * CH:(m + 1) * CH],
                        Bbase[:, (CH0 + m) * PC:(CH0 + m + 1) * PC]
                        .unsqueeze(2).broadcast_to([128, PC, CH])),
                        reads=[b_Bb], writes=[bb[3]])
                p.op("vector", lambda e: e.tensor_sub(
                    BBT[3][:, :], BBT[1][:, :], BBT[3][:, :]),
                    reads=[bb[1]], writes=[bb[3]])
                p.op("scalar", lambda e: e.activation(
                    BBT[1][:, :], BBT[3][:, :], AF.Exp, scale=-1.0),
                    reads=[bb[3]], writes=[bb[1]])
                p.op("scalar", lambda e: e.activation(
                    BBT[3][:, :], BBT[3][:, :], AF.Exp), writes=[bb[3]])
                p.op("vector", lambda e: e.tensor_mul(
                    BBT[4][:, :], BBT[4][:, :], BBT[3][:, :]),
                    reads=[bb[3]], writes=[bb[4]])
                p.op("scalar", lambda e: e.copy(
                    X4(3)[:, :, :, 1:CH], X4(1)[:, :, :, 0:CH - 1]),
                    reads=[bb[1]], writes=[bb[3]])
                p.op("vector", lambda e: e.memset(
                    X4(3)[:, :, :, 0:1], 1.0), writes=[bb[3]])
                p.op("vector", lambda e: e.tensor_sub(
                    BBT[3][:, :], BBT[1][:, :], BBT[3][:, :]),
                    reads=[bb[1]], writes=[bb[3]])
                p.dma("scalar", qt_scr[:, tok0 * PC:(tok0 + TILE) * PC],
                      BBT[4][:, :], reads=[bb[4]], writes_acc=[d_qt], sem="d_qt")

                def ev_k(do, bank):
                    p.op("scalar", lambda e, do=do, bank=bank: e.activation(
                        BBT[5][:, do * TILE:(do + 1) * TILE], PB[bank][:, 0:TILE],
                        AF.Identity, bias=col("k_b", do)),
                        reads=[pb[bank], b_cols], writes=[bb[5]])
                proj_2048(kw4, 5, ev_k)

                ck("t5")
                p.op("vector", lambda e: e.tensor_mul(
                    BBT[5][:, :], BBT[5][:, :], BBT[3][:, :]),
                    reads=[bb[3]], writes=[bb[5]])
                p.op("vector", lambda e: e.tensor_mul(
                    X4(2)[:, :, :, :], X4(5)[:, :, :, :],
                    Dcols[:, :].rearrange("p (m q) -> p m q", q=PC)
                    .transpose([0, 2, 1]).unsqueeze(3)
                    .broadcast_to([128, PC, NCH_T, CH])),
                    reads=[bb[5], b_Dc], writes=[bb[2]])

                ck("t6")
                b_kh = bb[1]
                b_Pt = bb[3]
                b_ost = bb[1]
                b_vsw = bb[3]
                for m in range(NCH_T):
                    mg = CH0 + m
                    par = (m % 2) * 64
                    opar = 64 - par
                    tb = m // 2
                    if m % 2 == 0:
                        p.op("vector", lambda e, tb=tb: e.tensor_copy(
                            BBT[3][0:64, 2048:4096],
                            BBT[6][64:128, tb * D:tb * D + 2048]),
                            reads=[bb[6]], writes=[b_vsw])
                        p.op("vector", lambda e, tb=tb: e.tensor_copy(
                            BBT[3][64:128, 2048:4096],
                            BBT[6][0:64, tb * D:tb * D + 2048]),
                            reads=[bb[6]], writes=[b_vsw])

                    def vsl(h, par=par, tb=tb):
                        hp = (h % 2) * 64
                        if hp == par:
                            return BBT[6][par:par + 64,
                                          tb * D + h * DH: tb * D + (h + 1) * DH]
                        return BBT[3][hp:hp + 64, 2048 + h * DH:2048 + (h + 1) * DH]

                    def vrd(h, par=par):
                        return bb[6] if (h % 2) * 64 == par else b_vsw

                    for g in range(4):
                        def ttg(e, m=m, g=g):
                            last = None
                            for j in range(4):
                                pcx = g * 4 + j
                                last = e.transpose(
                                    PB6[0:64, j * 128:(j + 1) * 128],
                                    BBT[2][:, pcx * TILE + m * CH: pcx * TILE + (m + 1) * CH],
                                    id_sb[:, :])
                            return last
                        p.op("tensor", ttg, reads=[bb[2], b_id], writes=[pb[6]])
                        p.op("vector", lambda e, g=g: e.tensor_copy(
                            BBT[1][0:64, g * 512:(g + 1) * 512],
                            PB6[0:64, 0:512]), reads=[pb[6]], writes=[b_kh])
                        p.op("scalar", lambda e, g=g: e.activation(
                            BBT[1][64:128, g * 512:(g + 1) * 512],
                            PB6[0:64, 0:512], AF.Identity),
                            reads=[pb[6]], writes=[b_kh])

                    def qsl(h, m=m):
                        return BBT[4][(h % 2) * 64:(h % 2) * 64 + 64,
                                      (h // 2) * TILE + m * CH:
                                      (h // 2) * TILE + (m + 1) * CH]

                    def ksl(h, m=m):
                        return BBT[5][(h % 2) * 64:(h % 2) * 64 + 64,
                                      (h // 2) * TILE + m * CH:
                                      (h // 2) * TILE + (m + 1) * CH]

                    for half in range(2 if attn_lvl >= 2 else 0):
                        pbank = 4 + half

                        def pmm(e, half=half, pbank=pbank, qsl=qsl, ksl=ksl):
                            last = None
                            for i in range(16):
                                h = half * 16 + i
                                hp = (h % 2) * 64
                                sl = ((h // 2) % 8) * 64
                                last = e.matmul(
                                    PB[pbank][hp:hp + 64, sl:sl + 64],
                                    ksl(h), qsl(h), start=True, stop=True)
                            return last
                        p.op("tensor", pmm, reads=[bb[4], bb[5]],
                             writes=[pb[pbank]])
                        for hp in (0, 64):
                            p.op("vector", lambda e, half=half, pbank=pbank, hp=hp:
                                 e.tensor_mul(
                                     BBT[3][hp:hp + 64, :]
                                     .rearrange("p (a c) -> p a c", c=CH)
                                     [:, half * 16 + (hp // 64):half * 16 + 16:2, :],
                                     PB[pbank][hp:hp + 64, 0:512]
                                     .rearrange("p (a c) -> p a c", c=CH),
                                     mask8_sb[:, :]
                                     .rearrange("p (a c) -> p a c", c=CH)),
                                 reads=[pb[pbank], b_mask8], writes=[b_Pt])

                    for half in range(2 if attn_lvl >= 3 else 0):
                        obank = half

                        def omm(e, half=half, obank=obank, vsl=vsl, qsl=qsl):
                            last = None
                            for i in range(16):
                                h = half * 16 + i
                                hp = (h % 2) * 64
                                sl = ((h // 2) % 8) * 64
                                last = e.matmul(
                                    PB[obank][hp:hp + 64, sl:sl + 64],
                                    vsl(h),
                                    BBT[3][hp:hp + 64, h * 64:(h + 1) * 64],
                                    start=True, stop=False)
                                last = e.matmul(
                                    PB[obank][hp:hp + 64, sl:sl + 64],
                                    S_sb[hp:hp + 64,
                                         (h // 2) * 64:(h // 2) * 64 + 64],
                                    qsl(h), start=False, stop=True)
                            return last
                        p.op("tensor", omm,
                             reads=[bb[6], b_vsw, b_Pt, b_S, bb[4]],
                             writes=[pb[obank]])

                    for half in range(2 if attn_lvl >= 4 else 0):
                        sbank = 2 + half

                        def smm(e, half=half, sbank=sbank, par=par, vsl=vsl):
                            last = None
                            for i in range(16):
                                h = half * 16 + i
                                hp = (h % 2) * 64
                                pcc = h // 2
                                sl = (pcc % 8) * 64
                                last = e.matmul(
                                    PB[sbank][hp:hp + 64, sl:sl + 64],
                                    BBT[1][hp:hp + 64,
                                           pcc * 128 + hp: pcc * 128 + hp + 64],
                                    vsl(h), start=True, stop=True)
                            return last
                        p.op("tensor", smm, reads=[b_kh, bb[6], b_vsw],
                             writes=[pb[sbank]])

                    if attn_lvl >= 5:
                        for half in range(2):
                            p.op("scalar", lambda e, half=half: e.activation(
                                BBT[1][:, 2048 + half * 512:2048 + half * 512 + 512],
                                PB[half][:, 0:512], AF.Identity),
                                reads=[pb[half]], writes=[b_ost])
                        p.dma("scalar",
                              o_scr[:, (tok0 + m * CH) * PC:(tok0 + (m + 1) * CH) * PC],
                              BBT[1][:, 2048:3072], reads=[b_ost],
                              writes_acc=[d_o], sem="d_o")

                    if attn_lvl >= 4:
                        p.op("vector", lambda e, m=m: e.tensor_mul(
                            S_sb[:, :].rearrange("p (q c) -> p q c", c=64),
                            S_sb[:, :].rearrange("p (q c) -> p q c", c=64),
                            Dcols[:, m * PC:(m + 1) * PC]
                            .unsqueeze(2).broadcast_to([128, PC, 64])),
                            reads=[b_Dc], writes=[b_S])
                        for half in range(2):
                            p.op("vector", lambda e, half=half: e.tensor_add(
                                S_sb[:, half * 512:(half + 1) * 512],
                                S_sb[:, half * 512:(half + 1) * 512],
                                PB[2 + half][:, 0:512]),
                                reads=[pb[2 + half]], writes=[b_S])

                if t == NT - 1:
                    p.dma("gpsimd", ccsm_i[:, :], smOwn[:, :], reads=[b_smOwn],
                          writes=[d_ci], sem="d_cci")
                    if ncores > 1:
                        p.op("gpsimd", lambda e: e.collective_compute(
                            "AllGather", OP.bypass,
                            replica_groups=[list(range(ncores))],
                            ins=[ccsm_i.opt()], outs=[ccsm_o.opt()]),
                            reads=[d_ci], writes=[d_co], sem="cc1")
                    else:
                        p.dma("gpsimd", ccsm_o[0:128, :], ccsm_i[:, :],
                              reads=[d_ci], writes=[d_co], sem="cc1")
                    p.op("scalar", lambda e: e.activation(
                        dsl_sb[:, :], Bcarry[:, :], AF.Exp),
                        reads=[b_Bc], writes=[b_dsl])
                    p.dma("gpsimd", ccsu_i[:, 0:1024], S_sb[:, :], reads=[b_S],
                          writes=[d_si], sem="d_ccs")
                    p.dma("gpsimd", ccsu_i[:, 1024:1040], dsl_sb[:, :],
                          reads=[b_dsl], writes_acc=[d_si], sem="d_ccs")
                    if ncores > 1:
                        p.op("gpsimd", lambda e: e.collective_compute(
                            "AllGather", OP.bypass,
                            replica_groups=[list(range(ncores))],
                            ins=[ccsu_i.opt()], outs=[ccsu_o.opt()]),
                            reads=[d_si], writes=[d_so], sem="cc2")
                    else:
                        p.dma("gpsimd", ccsu_o[0:128, :], ccsu_i[:, :],
                              reads=[d_si], writes=[d_so], sem="cc2")

                if t + 1 < NT:
                    stage_T12(t + 1)
                ck("t7b")
                hhome = [2, 4, 5]

                def hT(c):
                    return BBT[hhome[c // 16]], bb[hhome[c // 16]], (c % 16) * TILE

                for c in range(NF):
                    rg_g = rn()
                    rg_u = rn()
                    p.dma("sync", RGT[rg_g][:, :],
                          gw4[c].rearrange("p a b -> p (a b)"),
                          writes=[rg[rg_g]], sem="d_w%d" % rg_g)
                    p.dma("sync", RGT[rg_u][:, :],
                          uw4[c].rearrange("p a b -> p (a b)"),
                          writes=[rg[rg_u]], sem="d_w%d" % rg_u)

                    def gmm(e, ri=rg_g):
                        last = None
                        for k in range(PC):
                            last = e.matmul(PB2[:, 0:TILE],
                                            RGT[ri][:, k * 128:(k + 1) * 128],
                                            BBT[0][:, k * TILE:(k + 1) * TILE],
                                            start=(k == 0), stop=(k == PC - 1))
                        return last
                    p.op("tensor", gmm, reads=[rg[rg_g], bb[0]], writes=[pb[2]])

                    def umm(e, ri=rg_u):
                        last = None
                        for k in range(PC):
                            last = e.matmul(PB3[:, 0:TILE],
                                            RGT[ri][:, k * 128:(k + 1) * 128],
                                            BBT[0][:, k * TILE:(k + 1) * TILE],
                                            start=(k == 0), stop=(k == PC - 1))
                        return last
                    p.op("tensor", umm, reads=[rg[rg_u], bb[0]], writes=[pb[3]])
                    hten, hbuf, hoff = hT(c)
                    p.op("scalar", lambda e, c=c: e.activation(
                        tanh_sb[:, :], PB2[:, 0:TILE], AF.Silu,
                        bias=bgu_sb[:, c:c + 1]),
                        reads=[pb[2], b_bgu], writes=[b_tanh])
                    p.op("vector", lambda e, c=c, hten=hten, hoff=hoff:
                         e.scalar_tensor_tensor(
                             hten[:, hoff:hoff + TILE], PB3[:, 0:TILE],
                             bgu_sb[:, NF + c:NF + c + 1], tanh_sb[:, :],
                             OP.add, OP.mult),
                         reads=[pb[3], b_bgu, b_tanh], writes=[hbuf])

                parts = [(0, 15), (15, 30), (30, NF)]
                for do in range(PC):
                    bank = do % 2
                    for (c0, c1) in parts:
                        ri = rn()
                        p.dma("sync", RGT[ri][:, 0:(c1 - c0) * 128],
                              dw4[do][:, c0:c1, :].rearrange("p a b -> p (a b)"),
                              writes=[rg[ri]], sem="d_w%d" % ri)

                        def dmm(e, ri=ri, c0=c0, c1=c1, bank=bank):
                            last = None
                            for c in range(c0, c1):
                                hten, _, hoff = hT(c)
                                last = e.matmul(
                                    PB[bank][:, 0:TILE],
                                    RGT[ri][:, (c - c0) * 128:(c - c0 + 1) * 128],
                                    hten[:, hoff:hoff + TILE],
                                    start=(c == 0), stop=(c == NF - 1))
                            return last
                        p.op("tensor", dmm,
                             reads=[rg[ri]] + [bb[i] for i in hhome],
                             writes=[pb[bank]])
                    p.op("vector", lambda e, do=do, bank=bank: e.tensor_scalar(
                        BBT[6][:, do * TILE:(do + 1) * TILE], PB[bank][:, 0:TILE],
                        col("b_dn", do), None, OP.add),
                        reads=[pb[bank], b_cols], writes=[bb[6]])
                p.dma("scalar", ffn_scr[:, tok0 * PC:(tok0 + TILE) * PC],
                      BBT[6][:, :], reads=[bb[6]], writes_acc=[d_ffn], sem="d_ff")

                if t + 1 < NT:
                    stage_T34(t + 1)
            ck("t8")
            p.dma("scalar", BBT[4][:, :], qt_scr[:, 0:TILE * PC],
                  reads=[d_qt], writes=[bb[4]], sem="d_rq")
            p.dma("scalar", BBT[1][:, :], o_scr[:, 0:TILE * PC],
                  reads=[d_o], writes=[bb[1]], sem="d_ro")
            p.dma("scalar", BBT[2][:, :], x_fm[:, 0:TILE * PC],
                  writes=[bb[2]], sem="d_rx")
            for r in range(ncores):
                p.dma("gpsimd", smAll[:, r * SMW:(r + 1) * SMW],
                      ccsm_o[r * 128:(r + 1) * 128, :],
                      reads=[d_co], writes=[b_smAll], sem="d_sml")
            p.op("vector", lambda e: e.memset(Sin_sb[:, :], 0.0), writes=[b_Sin])
            for r in range(ncores - 1):
                if r % 2 == 0:
                    src, bsrc = Ur_sb, b_Ur
                else:
                    src, bsrc = BBT[3], bb[3]
                p.dma("gpsimd", src[:, 0:1040], ccsu_o[r * 128:(r + 1) * 128, :],
                      reads=[d_so], writes=[bsrc], sem="d_url")
                p.op("vector", lambda e, r=r, src=src: e.scalar_tensor_tensor(
                    dsl_sb[:, :], src[:, 1024:1040], incl_sb[:, r:r + 1],
                    incl_sb[:, 8 + r:8 + r + 1].broadcast_to([128, PC]),
                    OP.mult, OP.add), reads=[bsrc, b_incl], writes=[b_dsl])
                p.op("vector", lambda e: e.tensor_mul(
                    Sin_sb[:, :].rearrange("p (q c) -> p q c", c=64),
                    Sin_sb[:, :].rearrange("p (q c) -> p q c", c=64),
                    dsl_sb[:, :].unsqueeze(2).broadcast_to([128, PC, 64])),
                    reads=[b_dsl], writes=[b_Sin])
                p.op("vector", lambda e, r=r, src=src: e.scalar_tensor_tensor(
                    Sin_sb[:, :], src[:, 0:1024], incl_sb[:, r:r + 1],
                    Sin_sb[:, :], OP.mult, OP.add),
                    reads=[bsrc, b_incl], writes=[b_Sin])
            ck("c0")
            p.op("vector", lambda e: e.memset(bis8[:, 0:8], 0.0), writes=[b_bis])
            p.op("vector", lambda e: e.memset(bis8[:, 8:16], 1.0), writes=[b_bis])
            p.op("vector", lambda e: e.memset(bis8[:, 16:24], 0.5), writes=[b_bis])
            p.op("vector", lambda e: e.memset(midbc[:, :], 0.5), writes=[b_midbc])
            bis_done = [0]

            ck("bisinit")

            FOLD0 = max(1, (NT + 1) // 2)
            BIS_PER = -(-BIS_IT // FOLD0)
            pending = list(range(FOLD0))

            def wgt_bc(tok, half):
                for tb in range(TB):
                    p.op("tensor", lambda e, tok=tok, tb=tb, half=half: e.matmul(
                        PB2[:, half * TILE + tb * 128:
                            half * TILE + (tb + 1) * 128],
                        onesr_sb[0:1, :],
                        wgtrow[0:1, tok + tb * 128:tok + (tb + 1) * 128],
                        start=True, stop=True),
                        reads=[b_wgr, b_onesr], writes=[pb[2]])
                p.op("vector", lambda e, half=half: e.tensor_copy(
                    wgtbc_sb[:, half * TILE:(half + 1) * TILE],
                    PB2[:, half * TILE:(half + 1) * TILE]),
                    reads=[pb[2]], writes=[b_wgb])

            def mul_wgt(i, half):
                p.op("vector", lambda e, i=i, half=half: e.tensor_mul(
                    BBT[i][:, :].rearrange("p (a c) -> p a c", c=TILE),
                    BBT[i][:, :].rearrange("p (a c) -> p a c", c=TILE),
                    wgtbc_sb[:, half * TILE:(half + 1) * TILE]
                    .unsqueeze(1).broadcast_to([128, PC, TILE])),
                    reads=[b_wgb], writes=[bb[i]])

            def emit_wgt():
                for blk in range(tpc // 128):
                    p.op("vector", lambda e, blk=blk: e.tensor_tensor(
                        bisc[:, 0:E8], smOwn[:, blk * E8:(blk + 1) * E8],
                        bis8[:, 0:8], OP.is_ge),
                        reads=[b_smOwn, b_bis], writes=[b_bisc])
                    p.op("vector", lambda e, blk=blk: e.tensor_mul(
                        bisc[:, 0:E8], bisc[:, 0:E8],
                        smOwn[:, blk * E8:(blk + 1) * E8]),
                        reads=[b_smOwn], writes=[b_bisc])
                    p.op("vector", lambda e, blk=blk: e.reduce_sum(
                        wgt_sb[:, blk:blk + 1], bisc[:, 0:E8], axis=AX.X),
                        reads=[b_bisc], writes=[b_wgt])
                NBLK = tpc // 128
                p.op("tensor", lambda e: e.transpose(
                    PB7[0:NBLK, 0:128], wgt_sb[:, :], id_sb[:, :]),
                    reads=[b_wgt, b_id], writes=[pb[7]])
                p.op("vector", lambda e: e.tensor_copy(
                    tanh_sb[0:NBLK, 0:128], PB7[0:NBLK, 0:128]),
                    reads=[pb[7]], writes=[b_tanh])
                for blk in range(NBLK):
                    p.op("tensor", lambda e, blk=blk: e.matmul(
                        PB7[0:1, 128:256], id_sb[0:NBLK, blk:blk + 1],
                        tanh_sb[0:NBLK, 0:128], start=True, stop=True),
                        reads=[b_tanh, b_id], writes=[pb[7]])
                    p.op("vector", lambda e, blk=blk: e.tensor_copy(
                        wgtrow[0:1, blk * 128:(blk + 1) * 128],
                        PB7[0:1, 128:256]), reads=[pb[7]], writes=[b_wgr])

            def acc_tile(j):
                tj = j * TILE
                p.dma("scalar", BBT[4][:, :],
                      ffn_scr[:, tj * PC:(tj + TILE) * PC],
                      reads=[d_ffn], writes=[bb[4]], sem="d_rf")
                wgt_bc(tj, 1)
                mul_wgt(4, 1)
                if trunc == "noacc":
                    p.dma("gpsimd", ffn_scr[:, tj * PC:(tj + TILE) * PC],
                          BBT[4][:, :], reads=[bb[4], d_ffn],
                          writes_acc=[d_y], sem="d_ya")
                else:
                    for q in range(4):
                        p.dma("gpsimd",
                              y_fm[:, tj * PC + q * 1024:
                                   tj * PC + (q + 1) * 1024],
                              BBT[4][:, q * 1024:(q + 1) * 1024],
                              reads=[bb[4], d_y], writes_acc=[d_y],
                              sem="d_ya", accum=OP.add)

            for t in range(NT):
                tok0 = t * TILE
                CH0 = t * NCH_T
                if t > 0:
                    p.dma("scalar", BBT[4][:, :],
                          qt_scr[:, tok0 * PC:(tok0 + TILE) * PC],
                          reads=[d_qt], writes=[bb[4]], sem="d_rq")
                    p.dma("scalar", BBT[1][:, :],
                          o_scr[:, tok0 * PC:(tok0 + TILE) * PC],
                          reads=[d_o], writes=[bb[1]], sem="d_ro")
                    p.dma("scalar", BBT[2][:, :],
                          x_fm[:, tok0 * PC:(tok0 + TILE) * PC],
                          writes=[bb[2]], sem="d_rx")
                if t >= FOLD0:
                    p.dma("scalar", BBT[3][:, :],
                          ffn_scr[:, tok0 * PC:(tok0 + TILE) * PC],
                          reads=[d_ffn], writes=[bb[3]], sem="d_rf")
                    wgt_bc(tok0, 0)

                ck("creload")
                for m in range(NCH_T):
                    mg = CH0 + m
                    p.op("scalar", lambda e, mg=mg: e.activation(
                        eb_sb[:, :], Bbase[:, mg * PC:(mg + 1) * PC], AF.Exp),
                        reads=[b_Bb], writes=[b_eb])
                    p.op("vector", lambda e: e.tensor_mul(
                        SC_sb[:, :].rearrange("p (q c) -> p q c", c=64),
                        Sin_sb[:, :].rearrange("p (q c) -> p q c", c=64),
                        eb_sb[:, :].unsqueeze(2).broadcast_to([128, PC, 64])),
                        reads=[b_Sin, b_eb], writes=[b_SC])

                    for half in range(2):
                        def cmm(e, half=half, m=m):
                            last = None
                            for i in range(16):
                                h = half * 16 + i
                                hp = (h % 2) * 64
                                sl = ((h // 2) % 8) * 64
                                last = e.matmul(
                                    PB[half][hp:hp + 64, sl:sl + 64],
                                    SC_sb[hp:hp + 64,
                                          (h // 2) * 64:(h // 2) * 64 + 64],
                                    BBT[4][hp:hp + 64,
                                           (h // 2) * TILE + m * CH:
                                           (h // 2) * TILE + (m + 1) * CH],
                                    start=True, stop=True)
                            return last
                        p.op("tensor", cmm, reads=[b_SC, bb[4]],
                             writes=[pb[half]])
                    for half in range(2):
                        p.op("vector", lambda e, half=half, m=m: e.tensor_add(
                            BBT[1][:, m * 1024 + half * 512:
                                   m * 1024 + half * 512 + 512],
                            BBT[1][:, m * 1024 + half * 512:
                                   m * 1024 + half * 512 + 512],
                            PB[half][:, 0:512]),
                            reads=[pb[half]], writes=[bb[1]])

                ck("corr")
                if t >= FOLD0 and pending:
                    acc_tile(pending.pop(0))
                if t >= FOLD0:
                    mul_wgt(3, 0)
                p.op("scalar", lambda e: e.activation(
                    BBT[6][:, :], BBT[1][:, :], AF.Square),
                    reads=[bb[1]], writes=[bb[6]])

                def smm1(e):
                    last = None
                    v1 = BBT[1][:, :].rearrange("p (m q c) -> p m q c", q=PC, c=CH)
                    for k in range(PC):
                        last = e.matmul(PB4[0:1, 0:TILE], ones_sb[:, 0:1],
                                        v1[:, :, k, :],
                                        start=(k == 0), stop=(k == PC - 1))
                    return last
                p.op("tensor", smm1, reads=[bb[1], b_ones], writes=[pb[4]])
                p.op("vector", lambda e: e.tensor_scalar(
                    abrow[0:1, 0:TILE], PB4[0:1, 0:TILE], 1.0 / D, None, OP.mult),
                    reads=[pb[4]], writes=[b_ab])

                def smm2(e):
                    last = None
                    v6 = BBT[6][:, :].rearrange("p (m q c) -> p m q c", q=PC, c=CH)
                    for k in range(PC):
                        last = e.matmul(PB4[0:1, 256:256 + TILE], ones_sb[:, 0:1],
                                        v6[:, :, k, :],
                                        start=(k == 0), stop=(k == PC - 1))
                    return last
                p.op("tensor", smm2, reads=[bb[6], b_ones], writes=[pb[4]])
                p.op("vector", lambda e: e.tensor_scalar(
                    abrow[0:1, TILE:2 * TILE], PB4[0:1, 256:256 + TILE], 1.0 / D,
                    None, OP.mult), reads=[pb[4]], writes=[b_ab])
                p.op("scalar", lambda e: e.activation(
                    abrow[0:1, 2 * TILE:3 * TILE], abrow[0:1, 0:TILE], AF.Square),
                    writes=[b_ab])
                p.op("vector", lambda e: e.tensor_sub(
                    abrow[0:1, TILE:2 * TILE], abrow[0:1, TILE:2 * TILE],
                    abrow[0:1, 2 * TILE:3 * TILE]), writes=[b_ab])
                p.op("scalar", lambda e: e.activation(
                    abrow[0:1, TILE:2 * TILE], abrow[0:1, TILE:2 * TILE], AF.Sqrt,
                    bias=cstc[0:1, 1:2]), writes=[b_ab])
                p.op("vector", lambda e: e.reciprocal(
                    abrow[0:1, TILE:2 * TILE], abrow[0:1, TILE:2 * TILE]),
                    writes=[b_ab])
                p.op("vector", lambda e: e.tensor_mul(
                    abrow[0:1, 2 * TILE:3 * TILE], abrow[0:1, 0:TILE],
                    abrow[0:1, TILE:2 * TILE]), writes=[b_ab])
                p.op("vector", lambda e: e.tensor_scalar(
                    abrow[0:1, 2 * TILE:3 * TILE], abrow[0:1, 2 * TILE:3 * TILE],
                    -1.0, None, OP.mult), writes=[b_ab])
                p.op("tensor", lambda e: e.matmul(
                    PB5[:, 0:TILE], onesr_sb[0:1, :], abrow[0:1, TILE:2 * TILE],
                    start=True, stop=True), reads=[b_ab, b_onesr], writes=[pb[5]])
                p.op("tensor", lambda e: e.matmul(
                    PB5[:, 256:256 + TILE], onesr_sb[0:1, :],
                    abrow[0:1, 2 * TILE:3 * TILE],
                    start=True, stop=True), reads=[b_ab, b_onesr], writes=[pb[5]])
                v1c = BBT[1][:, :].rearrange("p (m q c) -> p m q c", q=PC, c=CH)
                v5c = BBT[5][:, :].rearrange("p (m q c) -> p m q c", q=PC, c=CH)
                abc = PB5[:, 0:TILE].rearrange("p (m c) -> p m c", c=CH)
                bbc = PB5[:, 256:256 + TILE].rearrange("p (m c) -> p m c", c=CH)
                p.op("vector", lambda e, v5c=v5c, v1c=v1c, abc=abc: e.tensor_mul(
                    v5c, v1c,
                    abc.unsqueeze(2).broadcast_to([128, NCH_T, PC, CH])),
                    reads=[bb[1], pb[5]], writes=[bb[5]])
                p.op("vector", lambda e, v5c=v5c, bbc=bbc: e.tensor_add(
                    v5c, v5c,
                    bbc.unsqueeze(2).broadcast_to([128, NCH_T, PC, CH])),
                    reads=[pb[5]], writes=[bb[5]])
                for pcc in range(PC):
                    p.op("vector", lambda e, pcc=pcc, v5c=v5c: e.tensor_scalar(
                        v5c[:, :, pcc, :], v5c[:, :, pcc, :],
                        col("ln_g", pcc), col("ln_b", pcc), OP.mult, OP.add),
                        reads=[b_cols], writes=[bb[5]])

                if dbg:
                    p.dma("scalar", dbg_y[:, tok0 * PC:(tok0 + TILE) * PC],
                        BBT[5][:, :],
                        reads=[bb[5]], writes_acc=[d_dbg], sem="d_dbg")
                ck("ln2")
                for do in range(PC):
                    ri = rn()
                    bank = 6 + do % 2
                    p.dma("sync", RGT[ri][:, :],
                          ow4[do].rearrange("p a b -> p (a b)"),
                          writes=[rg[ri]], sem="d_w%d" % ri)

                    def wmm(e, ri=ri, bank=bank):
                        last = None
                        v5 = BBT[5][:, :].rearrange("p (m q c) -> p m q c",
                                                    q=PC, c=CH)
                        for k in range(PC):
                            last = e.matmul(PB[bank][:, 0:TILE],
                                            RGT[ri][:, k * 128:(k + 1) * 128],
                                            v5[:, :, k, :],
                                            start=(k == 0), stop=(k == PC - 1))
                        return last
                    p.op("tensor", wmm, reads=[rg[ri], bb[5]], writes=[pb[bank]])
                    p.op("vector", lambda e, do=do, bank=bank: e.tensor_add(
                        BBT[6][:, do * TILE:(do + 1) * TILE], PB[bank][:, 0:TILE],
                        BBT[2][:, do * TILE:(do + 1) * TILE]),
                        reads=[pb[bank], bb[2]], writes=[bb[6]])
                    if t < FOLD0:
                        tgt = min(BIS_IT, t * BIS_PER
                                  + (BIS_PER * (do + 1) + PC - 1) // PC)
                        while bis_done[0] < tgt:
                            bis_iter()
                            bis_done[0] += 1
                if t == FOLD0 - 1:
                    while bis_done[0] < BIS_IT:
                        bis_iter()
                        bis_done[0] += 1
                    emit_wgt()
                if t >= FOLD0:
                    p.op("vector", lambda e: e.tensor_add(
                        BBT[6][:, :], BBT[6][:, :], BBT[3][:, :]),
                        reads=[bb[3]], writes=[bb[6]])
                p.dma("scalar", y_fm[:, tok0 * PC:(tok0 + TILE) * PC],
                      BBT[6][:, :], reads=[bb[6]], writes_acc=[d_y], sem="d_y")

            ck("ow")
            for j in list(pending):
                pending.remove(j)
                acc_tile(j)
            ck("wgt")

        except _Done:
            pass
        if trunc is not None:
            p.dma("scalar", y_fm[:, 0:PC * TILE], BBT[0][:, :],
                  writes_acc=[d_y], sem="d_y")
        p.wait_only("sync", [d_y])

        handles = {}
        cms = []
        for sname in sorted(p.val):
            cm = nc.semaphore(sname)
            handles[sname] = cm.__enter__()
            cms.append(cm)
        for ename, deco in (("tensor", block.tensor), ("vector", block.vector),
                            ("scalar", block.scalar), ("sync", block.sync),
                            ("gpsimd", block.gpsimd)):
            items = p.items[ename]
            waited = {}
            deco(lambda eng, items=items, waited=waited:
                 _emit(eng, items, handles, waited))
        for cm in reversed(cms):
            cm.__exit__(None, None, None)

    return nc


def stage_inputs(inputs, ncores, tpc):
    f32 = np.float32
    x = np.ascontiguousarray(np.asarray(inputs["L0_attn_x"], dtype=f32)[0])
    T = x.shape[0]
    assert T == ncores * tpc

    def c4(w):
        return np.ascontiguousarray(
            w.reshape(PC, 128, PC, 128).transpose(2, 1, 0, 3))

    qw4 = c4(np.asarray(inputs["L0_attn_q_w"], dtype=f32))
    kw4 = c4(np.asarray(inputs["L0_attn_k_w"], dtype=f32))
    ow4 = c4(np.asarray(inputs["L0_attn_o_w"], dtype=f32))
    vw = np.ascontiguousarray(np.asarray(inputs["L0_attn_v_w"], dtype=f32))
    wg = np.zeros((D, DFFP), dtype=f32)
    wg[:, :DFF] = np.asarray(inputs["L0_ffn_w_gate"], dtype=f32)
    wu = np.zeros((D, DFFP), dtype=f32)
    wu[:, :DFF] = np.asarray(inputs["L0_ffn_w_up"], dtype=f32)
    wd = np.zeros((DFFP, D), dtype=f32)
    wd[:DFF, :] = np.asarray(inputs["L0_ffn_w_down"], dtype=f32)
    gw4 = np.ascontiguousarray(
        wg.reshape(PC, 128, NF, 128).transpose(2, 1, 0, 3))
    uw4 = np.ascontiguousarray(
        wu.reshape(PC, 128, NF, 128).transpose(2, 1, 0, 3))
    dw4 = np.ascontiguousarray(
        wd.reshape(NF, 128, PC, 128).transpose(2, 1, 0, 3))
    m2 = np.asarray(inputs["L0_attn_maa_w2"], dtype=f32)
    m2w4 = np.zeros((4, PC, 128, 128), dtype=f32)
    for r in range(4):
        m2w4[r, :, 32 * r:32 * r + 32, :] = \
            m2[r].reshape(32, PC, 128).transpose(1, 0, 2)
    maw1 = np.ascontiguousarray(
        np.asarray(inputs["L0_attn_maa_w1"], dtype=f32)
        .reshape(PC, 128, 128).transpose(1, 0, 2).reshape(128, PC * 128))
    tdw1 = np.ascontiguousarray(
        np.asarray(inputs["L0_attn_td_w1"], dtype=f32)
        .reshape(PC, 128, 64).transpose(1, 0, 2).reshape(128, PC * 64))
    _t2 = np.asarray(inputs["L0_attn_td_w2"], dtype=f32)
    tdw2 = np.ascontiguousarray(np.concatenate([_t2[:, 0:1024], _t2[:, 1024:2048]], axis=0))
    rwm = np.ascontiguousarray(
        np.asarray(inputs["L0_ffn_router_w"], dtype=f32)
        .reshape(PC, 128, E8).transpose(1, 0, 2).reshape(128, PC * E8))

    def vcol(name):
        return np.asarray(inputs[name], dtype=f32).reshape(PC, 128).T

    cols = np.zeros((128, 16 * PC), dtype=f32)
    for i, nm in enumerate(["L0_attn_maa_x", "L0_attn_maa_r", "L0_attn_maa_k",
                            "L0_attn_maa_v", "L0_attn_maa_w",
                            "L0_attn_time_decay", "L0_attn_q_b",
                            "L0_attn_k_b", "L0_attn_ln_g", "L0_attn_ln_b",
                            "L0_ffn_b_down"]):
        cols[:, i * PC:(i + 1) * PC] = vcol(nm)
    bg = np.zeros(DFFP, dtype=f32)
    bg[:DFF] = np.asarray(inputs["L0_ffn_b_gate"], dtype=f32)
    bu = np.zeros(DFFP, dtype=f32)
    bu[:DFF] = np.asarray(inputs["L0_ffn_b_up"], dtype=f32)
    bgu = np.concatenate([bg.reshape(NF, 128).T, bu.reshape(NF, 128).T],
                         axis=1).astype(f32)
    bgu = np.ascontiguousarray(bgu)
    vbrow = np.asarray(inputs["L0_attn_v_b"], dtype=f32).reshape(1, D)
    rbrow = np.asarray(inputs["L0_ffn_router_b"], dtype=f32).reshape(1, E8)
    ident = np.eye(128, dtype=f32)
    m1 = np.triu(np.ones((CH, CH), dtype=f32))
    mask8 = np.tile(m1, (1, 8))
    onesc = np.ones((128, 1), dtype=f32)
    onesr = np.ones((1, 128), dtype=f32)

    n = T * E8
    u32 = uniform_f32(42, n).reshape(T, E8)
    u64 = uniform_f64(42, n).reshape(T, E8)
    du = u64 - u32.astype(np.float64)
    duhi = du.astype(f32)
    dulo = (du - duhi.astype(np.float64)).astype(f32)

    SMW = (tpc // 128) * E8
    in_maps = []
    for c in range(ncores):
        sl = slice(c * tpc, (c + 1) * tpc)
        x_sl = np.ascontiguousarray(x[sl])
        x_fmc = np.ascontiguousarray(
            x_sl.reshape(tpc // TILE, TILE, PC, 128)
            .transpose(3, 0, 2, 1).reshape(128, PC * tpc))
        x_bnd = (np.zeros((1, D), dtype=f32) if c == 0
                 else np.ascontiguousarray(x[c * tpc - 1:c * tpc, :]))
        incl2 = np.zeros((128, 16), dtype=f32)
        for r in range(E8):
            incl2[:, r] = 1.0 if r < c else 0.0
            incl2[:, 8 + r] = 0.0 if r < c else 1.0

        def ublk(a):
            return np.ascontiguousarray(
                a[sl].reshape(tpc // 128, 128, E8).transpose(1, 0, 2)
                .reshape(128, SMW))

        u3 = np.concatenate([ublk(u32), ublk(duhi), ublk(dulo)], axis=1)
        in_maps.append({
            "x_tm": x_sl, "x_fm": x_fmc, "x_bnd": x_bnd,
            "qw4": qw4, "kw4": kw4, "ow4": ow4, "vw": vw,
            "gw4": gw4, "uw4": uw4, "dw4": dw4, "m2w4": m2w4,
            "maw1": maw1, "tdw1": tdw1, "tdw2": tdw2, "rw": rwm,
            "cols": cols, "bgu": bgu, "vbrow": vbrow, "rbrow": rbrow,
            "ident": ident, "mask8": mask8, "onesc": onesc, "onesr": onesr,
            "incl2": incl2, "u3": np.ascontiguousarray(u3),
        })
    return in_maps


def kernel(**inputs):
    ncores = NCORES
    T = np.asarray(inputs["L0_attn_x"]).shape[1]
    tpc = T // ncores
    in_maps = stage_inputs(inputs, ncores, tpc)
    nc = build_nc(ncores, tpc)
    res = run_bass_kernel_spmd(nc, in_maps, core_ids=list(range(ncores)))
    outs = []
    for c in range(ncores):
        yf = np.asarray(res.results[c]["y_fm"]).reshape(
            128, tpc // TILE, PC, TILE)
        outs.append(yf.transpose(1, 3, 2, 0).reshape(tpc, D))
    return np.concatenate(outs, axis=0).reshape(1, T, D).astype(np.float32)


# version 10
# speedup vs baseline: 1.0721x; 1.0087x over previous
import numpy as np

import concourse.bass as bass
import concourse.mybir as mybir
from concourse.bass_utils import run_bass_kernel_spmd

F32 = mybir.dt.float32
AF = mybir.ActivationFunctionType
OP = mybir.AluOpType
AX = mybir.AxisListType

D = 2048
NCORES = 8
TILE = 256
CH = 64
NCH_T = TILE // CH
H = 32
DH = 64
PC = D // 128
DFF = 5461
DFFP = 5504
NF = DFFP // 128
E8 = 8
LN_EPS = 1e-5
REPS = 1e-10
BIS_IT = 64

ENGS = ("tensor", "vector", "scalar", "sync", "gpsimd")


def _tf_round(v0, v1, rot):
    v0 = (v0 + v1).astype(np.uint32)
    v1 = ((v1 << np.uint32(rot)) | (v1 >> np.uint32(32 - rot))).astype(np.uint32)
    return v0, (v0 ^ v1).astype(np.uint32)


def _threefry2x32(k1, k2, x1, x2):
    rots = ([13, 15, 26, 6], [17, 29, 16, 24])
    ks = [np.uint32(k1), np.uint32(k2),
          np.uint32(np.uint32(k1) ^ np.uint32(k2) ^ np.uint32(0x1BD11BDA))]
    x0 = (x1 + ks[0]).astype(np.uint32)
    y0 = (x2 + ks[1]).astype(np.uint32)
    for i in range(5):
        for r in rots[i % 2]:
            x0, y0 = _tf_round(x0, y0, r)
        x0 = (x0 + ks[(i + 1) % 3]).astype(np.uint32)
        y0 = (y0 + ks[(i + 2) % 3] + np.uint32(i + 1)).astype(np.uint32)
    return x0, y0


def _bits(seed, n):
    k1 = np.uint32(np.uint64(seed) >> np.uint64(32))
    k2 = np.uint32(np.uint64(seed) & np.uint64(0xFFFFFFFF))
    idx = np.arange(n, dtype=np.uint64)
    return _threefry2x32(k1, k2, (idx >> np.uint64(32)).astype(np.uint32),
                         (idx & np.uint64(0xFFFFFFFF)).astype(np.uint32))


def uniform_f32(seed, n):
    b1, b2 = _bits(seed, n)
    fb = ((b1 ^ b2).astype(np.uint32) >> np.uint32(9)) | np.uint32(0x3F800000)
    return fb.view(np.float32) - np.float32(1.0)


def uniform_f64(seed, n):
    b1, b2 = _bits(seed, n)
    bits64 = (b1.astype(np.uint64) << np.uint64(32)) | b2.astype(np.uint64)
    fb = (bits64 >> np.uint64(12)) | np.uint64(0x3FF0000000000000)
    return fb.view(np.float64) - np.float64(1.0)


class Buf:
    __slots__ = ("name", "writer", "readers")

    def __init__(self, name):
        self.name = name
        self.writer = None
        self.readers = []


class Prog:
    def __init__(self):
        self.items = {e: [] for e in ENGS}
        self.val = {}
        self.dma_chain = {}

    def op(self, eng, fn, reads=(), writes=(), writes_acc=(), sem=None, amt=1):
        s = sem if sem else "P_" + eng
        if s not in self.val:
            self.val[s] = 0
        waits = {}

        def addw(tok):
            if tok is not None and waits.get(tok[0], 0) < tok[1]:
                waits[tok[0]] = tok[1]

        for b in reads:
            addw(b.writer)
        for b in writes:
            addw(b.writer)
            for t in b.readers:
                addw(t)
        self.val[s] += amt
        tok = (s, self.val[s])
        self.items[eng].append((waits, fn, (s, amt)))
        for b in reads:
            b.readers.append(tok)
        for b in writes:
            b.writer = tok
            b.readers = []
        for b in writes_acc:
            b.writer = tok
        return tok

    def dma(self, eng, out_ap, in_ap, reads=(), writes=(), writes_acc=(),
            sem="d_gen", accum=None, slow=False):
        if accum is None:
            fn = (lambda e, o=out_ap, i=in_ap, sl=slow:
                  e.dma_start(out=o, in_=i, allow_slow_non_contiguous=sl))
        else:
            fn = (lambda e, o=out_ap, i=in_ap, a=accum, sl=slow:
                  e.dma_start(out=o, in_=i, accum_op=a,
                              allow_slow_non_contiguous=sl))
        ch = self.dma_chain.get(sem)
        if ch is None:
            ch = Buf("chain_" + sem)
            self.dma_chain[sem] = ch
        tok = self.op(eng, fn, list(reads) + [ch], writes, writes_acc,
                      sem=sem, amt=16)
        ch.writer = tok
        ch.readers = []
        return tok

    def wait_only(self, eng, bufs):
        waits = {}
        for b in bufs:
            if b.writer is not None:
                sm, v = b.writer
                if waits.get(sm, 0) < v:
                    waits[sm] = v
                if sm not in self.val:
                    self.val[sm] = v
        self.items[eng].append((waits, None, None))


def _emit(eng_obj, items, handles, waited):
    for waits, fn, inc in items:
        for sm in sorted(waits):
            v = waits[sm]
            if waited.get(sm, 0) >= v:
                continue
            eng_obj.wait_ge(handles[sm], v)
            waited[sm] = v
        if fn is None:
            continue
        inst = fn(eng_obj)
        if inc is not None:
            inst.then_inc(handles[inc[0]], inc[1])


def build_nc(ncores, tpc, dbg=False, trunc=None):
    attn_lvl = 9
    if trunc is not None and trunc.startswith("a"):
        attn_lvl = int(trunc[1:])
        trunc = "t7"
    NT = tpc // TILE
    NCHUNK = NT * NCH_T
    TB = TILE // 128
    SMW = (tpc // 128) * E8
    GW = ncores * SMW
    NIDX = GW // E8
    KSEL = ncores * tpc // E8

    nc = bass.Bass("TRN2", target_bir_lowering=False)
    p = Prog()

    class _Done(Exception):
        pass

    def ck(stage):
        if trunc == stage:
            raise _Done()

    if trunc == "noacc":
        pass

    def din(name, shape):
        return nc.dram_tensor(name, list(shape), F32, kind="ExternalInput").ap()

    x_tm = din("x_tm", (tpc, D))
    x_fm = din("x_fm", (128, PC * tpc))
    x_bnd = din("x_bnd", (1, D))
    qw4 = din("qw4", (PC, 128, PC, 128))
    kw4 = din("kw4", (PC, 128, PC, 128))
    ow4 = din("ow4", (PC, 128, PC, 128))
    vw = din("vw", (D, D))
    gw4 = din("gw4", (NF, 128, PC, 128))
    uw4 = din("uw4", (NF, 128, PC, 128))
    dw4 = din("dw4", (PC, 128, NF, 128))
    m2w4 = din("m2w4", (4, PC, 128, 128))
    maw1 = din("maw1", (128, PC * 128))
    tdw1 = din("tdw1", (128, PC * 64))
    tdw2 = din("tdw2", (128, 1024))
    rw = din("rw", (128, PC * E8))
    cols_d = din("cols", (128, 16 * PC))
    bgu = din("bgu", (128, 2 * NF))
    vbrow = din("vbrow", (1, D))
    rbrow = din("rbrow", (1, E8))
    ident = din("ident", (128, 128))
    mask8d = din("mask8", (64, 512))
    onescd = din("onesc", (128, 1))
    onesrd = din("onesr", (1, 128))
    incl2 = din("incl2", (128, 16))
    u3 = din("u3", (128, 3 * SMW))

    y_fm = nc.dram_tensor("y_fm", [128, PC * tpc], F32, kind="ExternalOutput").ap()

    qt_scr = nc.dram_tensor("qt_scr", [128, PC * tpc], F32).ap()
    o_scr = nc.dram_tensor("o_scr", [128, PC * tpc], F32).ap()
    ffn_scr = nc.dram_tensor("ffn_scr", [128, PC * tpc], F32).ap()
    bnd_scr = nc.dram_tensor("bnd_scr", [1, D], F32).ap()
    if dbg:
        dbg_wlog = nc.dram_tensor("dbg_wlog", [128, PC * tpc], F32).ap()
        dbg_xw = nc.dram_tensor("dbg_xw", [128, PC * tpc], F32).ap()
        dbg_oc = nc.dram_tensor("dbg_oc", [128, PC * tpc], F32).ap()
        dbg_y = nc.dram_tensor("dbg_y", [128, PC * tpc], F32).ap()
        d_dbg = Buf("d_dbg")
    ccsm_i = nc.dram_tensor("ccsm_i", [128, SMW], F32).ap()
    ccsm_o = nc.dram_tensor("ccsm_o", [128 * ncores, SMW], F32).ap()
    ccsu_i = nc.dram_tensor("ccsu_i", [128, 1040], F32).ap()
    ccsu_o = nc.dram_tensor("ccsu_o", [128 * ncores, 1040], F32).ap()

    d_qt = Buf("d_qt"); d_o = Buf("d_o"); d_ffn = Buf("d_ffn")
    d_bnd = Buf("d_bnd"); d_y = Buf("d_y")
    d_ci = Buf("d_ci"); d_co = Buf("d_co")
    d_si = Buf("d_si"); d_so = Buf("d_so")

    from contextlib import ExitStack
    with ExitStack() as _st:
        BB0 = _st.enter_context(nc.sbuf_tensor("BB0", [128, PC * TILE], F32))
        BB1 = _st.enter_context(nc.sbuf_tensor("BB1", [128, PC * TILE], F32))
        BB2 = _st.enter_context(nc.sbuf_tensor("BB2", [128, PC * TILE], F32))
        BB3 = _st.enter_context(nc.sbuf_tensor("BB3", [128, PC * TILE], F32))
        BB4 = _st.enter_context(nc.sbuf_tensor("BB4", [128, PC * TILE], F32))
        BB5 = _st.enter_context(nc.sbuf_tensor("BB5", [128, PC * TILE], F32))
        BB6 = _st.enter_context(nc.sbuf_tensor("BB6", [128, PC * TILE], F32))
        RG0 = _st.enter_context(nc.sbuf_tensor("RG0", [128, 2048], F32))
        RG1 = _st.enter_context(nc.sbuf_tensor("RG1", [128, 2048], F32))
        RG2 = _st.enter_context(nc.sbuf_tensor("RG2", [128, 2048], F32))
        RG3 = _st.enter_context(nc.sbuf_tensor("RG3", [128, 2048], F32))
        S_sb = _st.enter_context(nc.sbuf_tensor("S_sb", [128, H * 32], F32))
        Sin_sb = _st.enter_context(nc.sbuf_tensor("Sin_sb", [128, H * 32], F32))
        SC_sb = _st.enter_context(nc.sbuf_tensor("SC_sb", [128, H * 32], F32))
        rw_sb = _st.enter_context(nc.sbuf_tensor("rw_sb", [128, PC * E8], F32))
        vbbc_sb = _st.enter_context(nc.sbuf_tensor("vbbc_sb", [128, D], F32))
        rbbc_sb = _st.enter_context(nc.sbuf_tensor("rbbc_sb", [128, E8], F32))
        cols_sb = _st.enter_context(nc.sbuf_tensor("cols_sb", [128, 16 * PC], F32))
        bgu_sb = _st.enter_context(nc.sbuf_tensor("bgu_sb", [128, 2 * NF], F32))
        id_sb = _st.enter_context(nc.sbuf_tensor("id_sb", [128, 128], F32))
        mask8_sb = _st.enter_context(nc.sbuf_tensor("mask8_sb", [64, 512], F32))
        ones_sb = _st.enter_context(nc.sbuf_tensor("ones_sb", [128, 1], F32))
        onesr_sb = _st.enter_context(nc.sbuf_tensor("onesr_sb", [1, 128], F32))
        incl_sb = _st.enter_context(nc.sbuf_tensor("incl_sb", [128, 16], F32))
        gum_sb = _st.enter_context(nc.sbuf_tensor("gum_sb", [128, SMW], F32))
        smOwn = _st.enter_context(nc.sbuf_tensor("smOwn", [128, SMW], F32))
        flag_sb = _st.enter_context(nc.sbuf_tensor("flag_sb", [128, 1], F32))
        qb8_sb = _st.enter_context(nc.sbuf_tensor("qb8_sb", [128, PC], F32))
        Bbase = _st.enter_context(nc.sbuf_tensor("Bbase", [128, (NCHUNK + 1) * PC], F32))
        Bcarry = _st.enter_context(nc.sbuf_tensor("Bcarry", [128, PC], F32))
        Dcols = _st.enter_context(nc.sbuf_tensor("Dcols", [128, NCH_T * PC], F32))
        eb_sb = _st.enter_context(nc.sbuf_tensor("eb_sb", [128, PC], F32))
        xnlast = _st.enter_context(nc.sbuf_tensor("xnlast", [128, PC], F32))
        tanh_sb = _st.enter_context(nc.sbuf_tensor("tanh_sb", [128, TILE], F32))
        wtd_sb = _st.enter_context(nc.sbuf_tensor("wtd_sb", [128, TILE], F32))
        statc = _st.enter_context(nc.sbuf_tensor("statc", [128, 24], F32))
        wgt_sb = _st.enter_context(nc.sbuf_tensor("wgt_sb", [128, tpc // 128], F32))
        wgtbc_sb = _st.enter_context(nc.sbuf_tensor("wgtbc_sb", [128, 2 * TILE], F32))
        ones128 = _st.enter_context(nc.sbuf_tensor("ones128", [128, 128], F32))
        bis8 = _st.enter_context(nc.sbuf_tensor("bis8", [128, 48], F32))
        dsl_sb = _st.enter_context(nc.sbuf_tensor("dsl_sb", [128, PC], F32))
        cstc = _st.enter_context(nc.sbuf_tensor("cstc", [128, 2], F32))
        ROWA = _st.enter_context(nc.sbuf_tensor("ROWA", [1, 2048], F32))
        ROWB = _st.enter_context(nc.sbuf_tensor("ROWB", [1, 2048], F32))
        PB0 = _st.enter_context(nc.psum_tensor("PB0", [128, 512], F32))
        PB1 = _st.enter_context(nc.psum_tensor("PB1", [128, 512], F32))
        PB2 = _st.enter_context(nc.psum_tensor("PB2", [128, 512], F32))
        PB3 = _st.enter_context(nc.psum_tensor("PB3", [128, 512], F32))
        PB4 = _st.enter_context(nc.psum_tensor("PB4", [128, 512], F32))
        PB5 = _st.enter_context(nc.psum_tensor("PB5", [128, 512], F32))
        PB6 = _st.enter_context(nc.psum_tensor("PB6", [128, 512], F32))
        PB7 = _st.enter_context(nc.psum_tensor("PB7", [128, 512], F32))
        block = _st.enter_context(nc.Block())
        smAll = BB0[:, 0:GW]
        bisc = BB0[:, 1024:1024 + GW]
        midbc = BB0[:, 2048:2048 + GW]
        u3_sb = BB2[:, 0:3 * SMW]
        Ur_sb = BB6[:, 0:1040]
        wgtrow = ROWB
        vrow_sb = ROWA
        bndrow = ROWB
        abrow = ROWA
        PB = [PB0, PB1, PB2, PB3, PB4, PB5, PB6, PB7]
        pb = [Buf("PB%d" % i) for i in range(8)]
        BBT = [BB0, BB1, BB2, BB3, BB4, BB5, BB6]
        bb = [Buf("BB%d" % i) for i in range(7)]
        RGT = [RG0, RG1, RG2, RG3]
        rg = [Buf("RG%d" % i) for i in range(4)]

        b_S = Buf("S"); b_Sin = Buf("Sin"); b_SC = Buf("SC")
        b_smAll = bb[0]; b_bisc = bb[0]; b_midbc = bb[0]
        b_Ur = bb[6]
        b_rw = Buf("rw"); b_vbbc = Buf("vbbc")
        b_rbbc = Buf("rbbc"); b_cols = Buf("cols"); b_bgu = Buf("bgu")
        b_id = Buf("id"); b_mask8 = Buf("mask8"); b_ones = Buf("ones")
        b_onesr = Buf("onesr"); b_incl = Buf("incl"); b_u3 = bb[2]
        b_gum = Buf("gum"); b_smOwn = Buf("smOwn"); b_flag = Buf("flag")
        b_qb8 = Buf("qb8"); b_Bb = Buf("Bbase"); b_Bc = Buf("Bcarry")
        b_Dc = Buf("Dcols"); b_eb = Buf("eb"); b_xnl = Buf("xnlast")
        b_tanh = Buf("tanh"); b_wtd = Buf("wtd"); b_st = Buf("statc")
        b_wgt = Buf("wgt"); b_wgb = Buf("wgtbc")
        b_bis = Buf("bis8"); b_vr = Buf("rowa"); b_bnd = Buf("rowb")
        b_ab = b_vr; b_wgr = b_bnd; b_dsl = Buf("dsl")

        CL = {"maa_x": 0, "maa_r": 1, "maa_k": 2, "maa_v": 3, "maa_w": 4,
              "tdec": 5, "q_b": 6, "k_b": 7, "ln_g": 8, "ln_b": 9, "b_dn": 10}

        def col(nm, pcc):
            i = CL[nm] * PC + pcc
            return cols_sb[:, i:i + 1]

        wring = [0]
        held = set()

        def rn():
            for _ in range(8):
                i = wring[0] % 4
                wring[0] += 1
                if i not in held:
                    return i
            raise RuntimeError("ring exhausted")

        def v3(i, b=TILE):
            return BBT[i][:, :].rearrange("p (a c) -> p a c", c=b)

        Bb3 = Bbase[:, :].rearrange("p (m c) -> p m c", c=PC)

        try:
            LD = "d_c"
            p.dma("sync", cols_sb[:, :], cols_d[:, :], writes=[b_cols], sem=LD)
            p.dma("sync", bgu_sb[:, :], bgu[:, :], writes=[b_bgu], sem=LD)
            p.dma("sync", id_sb[:, :], ident[:, :], writes=[b_id], sem=LD)
            p.dma("sync", mask8_sb[:, :], mask8d[:, :], writes=[b_mask8], sem=LD)
            p.dma("sync", ones_sb[:, :], onescd[:, :], writes=[b_ones], sem=LD)
            p.dma("sync", onesr_sb[:, :], onesrd[:, :], writes=[b_onesr], sem=LD)
            p.dma("sync", incl_sb[:, :], incl2[:, :], writes=[b_incl], sem=LD)
            p.dma("sync", u3_sb[:, :], u3[:, :], writes=[b_u3], sem=LD)
            p.dma("sync", rw_sb[:, :], rw[:, :], writes=[b_rw], sem=LD)
            p.dma("sync", bndrow[:, :], x_bnd[:, :], writes=[b_bnd], sem=LD)
            p.dma("sync", vrow_sb[:, :], vbrow[:, :], writes=[b_vr], sem=LD)
            p.dma("sync", statc[0:1, 0:E8], rbrow[:, :], writes=[b_st], sem=LD)

            b_cst = Buf("cst")
            p.op("vector", lambda e: e.memset(cstc[:, 0:1], REPS), writes=[b_cst])
            p.op("vector", lambda e: e.memset(cstc[:, 1:2], LN_EPS), writes=[b_cst])
            b_o128 = Buf("o128")
            p.op("vector", lambda e: e.memset(ones128[:, :], 1.0), writes=[b_o128])

            p.op("vector", lambda e: e.memset(flag_sb[:, :], 1.0), writes=[b_flag])
            p.op("vector", lambda e: e.tensor_scalar(
                flag_sb[:, :], flag_sb[:, :], 2.0 ** -30, None, OP.add),
                writes=[b_flag])
            p.op("vector", lambda e: e.tensor_scalar(
                flag_sb[:, :], flag_sb[:, :], -1.0, 2.0 ** 30, OP.add, OP.mult),
                writes=[b_flag])
            p.op("vector", lambda e: e.tensor_scalar(
                qb8_sb[:, :], cols_sb[:, CL["q_b"] * PC:(CL["q_b"] + 1) * PC],
                0.125, None, OP.mult), reads=[b_cols], writes=[b_qb8])
            p.op("vector", lambda e: e.scalar_tensor_tensor(
                u3_sb[:, 0:SMW], u3_sb[:, SMW:2 * SMW], flag_sb[:, 0:1],
                u3_sb[:, 0:SMW], OP.mult, OP.add), reads=[b_flag], writes=[b_u3])
            p.op("vector", lambda e: e.scalar_tensor_tensor(
                u3_sb[:, 0:SMW], u3_sb[:, 2 * SMW:3 * SMW], flag_sb[:, 0:1],
                u3_sb[:, 0:SMW], OP.mult, OP.add), reads=[b_flag], writes=[b_u3])
            p.op("scalar", lambda e: e.activation(
                gum_sb[:, :], u3_sb[:, 0:SMW], AF.Ln, bias=cstc[:, 0:1]),
                reads=[b_u3], writes=[b_gum])
            p.op("scalar", lambda e: e.activation(
                gum_sb[:, :], gum_sb[:, :], AF.Ln, bias=cstc[:, 0:1], scale=-1.0),
                writes=[b_gum])
            for i in range(4):
                p.op("tensor", lambda e, i=i: e.matmul(
                    PB0[:, 0:512], onesr_sb[0:1, :], vrow_sb[0:1, i * 512:(i + 1) * 512],
                    start=True, stop=True), reads=[b_vr, b_onesr], writes=[pb[0]])
                p.op("vector", lambda e, i=i: e.tensor_copy(
                    vbbc_sb[:, i * 512:(i + 1) * 512], PB0[:, 0:512]),
                    reads=[pb[0]], writes=[b_vbbc])
            p.op("tensor", lambda e: e.matmul(
                PB1[:, 0:E8], onesr_sb[0:1, :], statc[0:1, 0:E8],
                start=True, stop=True), reads=[b_st, b_onesr], writes=[pb[1]])
            p.op("vector", lambda e: e.tensor_copy(rbbc_sb[:, :], PB1[:, 0:E8]),
                 reads=[pb[1]], writes=[b_rbbc])

            p.op("vector", lambda e: e.reduce_sum(
                statc[0:1, 16:17], bndrow[0:1, :], axis=AX.X),
                reads=[b_bnd], writes=[b_st])
            p.op("scalar", lambda e: e.activation(
                BBT[1][0:1, 0:D], bndrow[0:1, 0:D],
                AF.Square, accum_out=statc[0:1, 17:18]),
                reads=[b_bnd], writes=[bb[1], b_st])
            p.op("vector", lambda e: e.tensor_scalar(
                statc[0:1, 16:18], statc[0:1, 16:18], 1.0 / D, None, OP.mult),
                writes=[b_st])
            p.op("scalar", lambda e: e.activation(
                statc[0:1, 18:19], statc[0:1, 16:17], AF.Square), reads=[b_st],
                writes=[b_st])
            p.op("vector", lambda e: e.tensor_sub(
                statc[0:1, 19:20], statc[0:1, 17:18], statc[0:1, 18:19]),
                writes=[b_st])
            p.op("scalar", lambda e: e.activation(
                statc[0:1, 20:21], statc[0:1, 19:20], AF.Sqrt, bias=cstc[0:1, 1:2]),
                writes=[b_st])
            p.op("vector", lambda e: e.reciprocal(
                statc[0:1, 21:22], statc[0:1, 20:21]), writes=[b_st])
            p.op("vector", lambda e: e.tensor_scalar(
                bndrow[0:1, :], bndrow[0:1, :], statc[0:1, 16:17],
                statc[0:1, 21:22], OP.subtract, OP.mult),
                reads=[b_st], writes=[b_bnd])
            for pcc in range(PC):
                p.op("tensor", lambda e, pcc=pcc: e.matmul(
                    PB2[:, pcc:pcc + 1], bndrow[0:1, pcc * 128:(pcc + 1) * 128],
                    ones_sb[0:1, 0:1], start=True, stop=True),
                    reads=[b_bnd, b_ones], writes=[pb[2]])
            p.op("vector", lambda e: e.tensor_copy(
                xnlast[:, :], PB2[:, 0:PC]), reads=[pb[2]], writes=[b_xnl])

            p.op("vector", lambda e: e.memset(S_sb[:, :], 0.0), writes=[b_S])
            p.op("vector", lambda e: e.memset(Bcarry[:, :], 0.0), writes=[b_Bc])
            p.op("vector", lambda e: e.memset(Bbase[:, 0:PC], 0.0), writes=[b_Bb])

            ck("pre")
            def mix_into_bb2(mname, r, dst_i=2):
                for pcc in range(PC):
                    ri = rn()
                    bank = pcc % 2
                    p.dma("sync", RGT[ri][:, 0:128], m2w4[r, pcc],
                          writes=[rg[ri]], sem="d_w%d" % ri)
                    p.op("tensor", lambda e, ri=ri, bank=bank: e.matmul(
                        PB[bank][:, 0:TILE], RGT[ri][:, 0:128], tanh_sb[:, :],
                        start=True, stop=True),
                        reads=[rg[ri], b_tanh], writes=[pb[bank]])
                    p.op("vector", lambda e, pcc=pcc, bank=bank, mname=mname,
                         dst_i=dst_i: e.scalar_tensor_tensor(
                             BBT[dst_i][:, pcc * TILE:(pcc + 1) * TILE],
                             PB[bank][:, 0:TILE], col(mname, pcc),
                             BBT[1][:, pcc * TILE:(pcc + 1) * TILE],
                             OP.add, OP.mult),
                         reads=[pb[bank], b_cols, bb[1]], writes=[bb[dst_i]])
                    p.op("vector", lambda e, pcc=pcc, dst_i=dst_i: e.tensor_add(
                        BBT[dst_i][:, pcc * TILE:(pcc + 1) * TILE],
                        BBT[dst_i][:, pcc * TILE:(pcc + 1) * TILE],
                        BBT[0][:, pcc * TILE:(pcc + 1) * TILE]),
                        reads=[bb[0]], writes=[bb[dst_i]])

            def proj_2048(w4, dst_i, evac):
                for do in range(PC):
                    ri = rn()
                    bank = do % 2
                    p.dma("sync", RGT[ri][:, :],
                          w4[do].rearrange("p a b -> p (a b)"),
                          writes=[rg[ri]], sem="d_w%d" % ri)

                    def mmg(e, ri=ri, bank=bank):
                        last = None
                        for k in range(PC):
                            last = e.matmul(PB[bank][:, 0:TILE],
                                            RGT[ri][:, k * 128:(k + 1) * 128],
                                            BBT[2][:, k * TILE:(k + 1) * TILE],
                                            start=(k == 0), stop=(k == PC - 1))
                        return last
                    p.op("tensor", mmg, reads=[rg[ri], bb[2]], writes=[pb[bank]])
                    evac(do, bank)

            def bis_iter():
                p.op("vector", lambda e: e.tensor_tensor(
                    bisc[:, :], smAll[:, :], midbc[:, :], OP.is_ge),
                    reads=[b_smAll, b_midbc], writes=[b_bisc])
                p.op("vector", lambda e: e.tensor_reduce(
                    bis8[:, 24:32].rearrange("p (a c) -> p a c", c=1),
                    bisc[:, :].rearrange("p (i e) -> p i e", e=E8).transpose([0, 2, 1]),
                    AX.X, OP.add), reads=[b_bisc], writes=[b_bis])
                p.op("tensor", lambda e: e.matmul(
                    PB3[:, 0:E8], ones128[:, :], bis8[:, 24:32],
                    start=True, stop=True), reads=[b_bis, b_o128], writes=[pb[3]])
                p.op("vector", lambda e: e.tensor_copy(
                    bis8[:, 32:40], PB3[:, 0:E8]), reads=[pb[3]], writes=[b_bis])
                p.op("vector", lambda e: e.tensor_scalar(
                    bis8[:, 32:40], bis8[:, 32:40], float(KSEL), None, OP.is_ge),
                    writes=[b_bis])
                p.op("vector", lambda e: e.tensor_sub(
                    bis8[:, 40:48], bis8[:, 16:24], bis8[:, 0:8]), writes=[b_bis])
                p.op("vector", lambda e: e.tensor_mul(
                    bis8[:, 40:48], bis8[:, 40:48], bis8[:, 32:40]), writes=[b_bis])
                p.op("vector", lambda e: e.tensor_add(
                    bis8[:, 0:8], bis8[:, 0:8], bis8[:, 40:48]), writes=[b_bis])
                p.op("vector", lambda e: e.tensor_sub(
                    bis8[:, 40:48], bis8[:, 16:24], bis8[:, 8:16]), writes=[b_bis])
                p.op("vector", lambda e: e.tensor_scalar(
                    bis8[:, 32:40], bis8[:, 32:40], -1.0, 1.0, OP.mult, OP.add),
                    writes=[b_bis])
                p.op("vector", lambda e: e.tensor_mul(
                    bis8[:, 40:48], bis8[:, 40:48], bis8[:, 32:40]), writes=[b_bis])
                p.op("vector", lambda e: e.tensor_add(
                    bis8[:, 8:16], bis8[:, 8:16], bis8[:, 40:48]), writes=[b_bis])
                p.op("vector", lambda e: e.tensor_add(
                    bis8[:, 16:24], bis8[:, 0:8], bis8[:, 8:16]), writes=[b_bis])
                p.op("vector", lambda e: e.tensor_scalar(
                    bis8[:, 16:24], bis8[:, 16:24], 0.5, None, OP.mult),
                    writes=[b_bis])
                p.op("vector", lambda e: e.tensor_copy(
                    midbc[:, :].rearrange("p (i e) -> p i e", e=E8),
                    bis8[:, 16:24].unsqueeze(1).broadcast_to([128, GW // E8, E8])),
                    reads=[b_bis], writes=[b_midbc])

            def stage_T12(t):
                tok0 = t * TILE

                for tb in range(TB):
                    dst = BBT[2][:, tb * D:(tb + 1) * D]
                    p.dma("gpsimd", dst,
                          x_tm[tok0 + tb * 128: tok0 + (tb + 1) * 128, :],
                          writes=[bb[2]], sem="d_x%d" % tb)
                    sc = 8 * tb
                    p.op("vector", lambda e, dst=dst, sc=sc: e.reduce_sum(
                        statc[:, sc:sc + 1], dst, axis=AX.X),
                        reads=[bb[2]], writes=[b_st])
                    p.op("scalar", lambda e, dst=dst, sc=sc, tb=tb: e.activation(
                        BBT[1][:, tb * D:(tb + 1) * D], dst, AF.Square,
                        accum_out=statc[:, sc + 1:sc + 2]),
                        reads=[bb[2]], writes=[bb[1], b_st])
                    p.op("vector", lambda e, sc=sc: e.tensor_scalar(
                        statc[:, sc + 2:sc + 4], statc[:, sc:sc + 2], 1.0 / D,
                        None, OP.mult), writes=[b_st])
                    p.op("scalar", lambda e, sc=sc: e.activation(
                        statc[:, sc + 4:sc + 5], statc[:, sc + 2:sc + 3],
                        AF.Square), writes=[b_st])
                    p.op("vector", lambda e, sc=sc: e.tensor_sub(
                        statc[:, sc + 5:sc + 6], statc[:, sc + 3:sc + 4],
                        statc[:, sc + 4:sc + 5]), writes=[b_st])
                    p.op("scalar", lambda e, sc=sc: e.activation(
                        statc[:, sc + 6:sc + 7], statc[:, sc + 5:sc + 6], AF.Sqrt,
                        bias=cstc[:, 1:2]), writes=[b_st])
                    p.op("vector", lambda e, sc=sc: e.reciprocal(
                        statc[:, sc + 7:sc + 8], statc[:, sc + 6:sc + 7]),
                        writes=[b_st])
                    p.op("vector", lambda e, dst=dst, sc=sc: e.tensor_scalar(
                        dst, dst, statc[:, sc + 2:sc + 3], statc[:, sc + 7:sc + 8],
                        OP.subtract, OP.mult), reads=[b_st], writes=[bb[2]])


            def stage_T34(t):
                tok0 = t * TILE
                for tb in range(TB):
                    for g in range(4):
                        def tg(e, tb=tb, g=g):
                            last = None
                            for j in range(4):
                                pcx = g * 4 + j
                                last = e.transpose(
                                    PB6[:, j * 128:(j + 1) * 128],
                                    BBT[2][:, tb * D + pcx * 128: tb * D + (pcx + 1) * 128],
                                    id_sb[:, :])
                            return last
                        p.op("tensor", tg, reads=[bb[2], b_id], writes=[pb[6]])
                        p.op("vector", lambda e, tb=tb, g=g: e.tensor_copy(
                            v3(0)[:, g * 4:(g + 1) * 4, tb * 128:(tb + 1) * 128],
                            PB6[:, 0:512].rearrange("p (a c) -> p a c", c=128)),
                            reads=[pb[6]], writes=[bb[0]])

                p.op("vector", lambda e: e.tensor_copy(
                    statc[:, 16:16 + PC] if False else dsl_sb[:, :],
                    v3(0)[:, :, TILE - 1:TILE].rearrange("p a c -> p (a c)")),
                    reads=[bb[0]], writes=[b_dsl])
                p.op("vector", lambda e: e.tensor_sub(
                    v3(1)[:, :, 1:TILE], v3(0)[:, :, 0:TILE - 1],
                    v3(0)[:, :, 1:TILE]), reads=[bb[0]], writes=[bb[1]])
                p.op("vector", lambda e: e.tensor_sub(
                    v3(1)[:, :, 0:1], xnlast[:, :].unsqueeze(2),
                    v3(0)[:, :, 0:1]), reads=[bb[0], b_xnl], writes=[bb[1]])
                p.op("vector", lambda e: e.tensor_copy(xnlast[:, :], dsl_sb[:, :]),
                     reads=[b_dsl], writes=[b_xnl])

                ck("t3")
                for tb in range(TB):
                    blk = t * TB + tb

                    def rmm(e, tb=tb):
                        last = None
                        for k in range(PC):
                            last = e.matmul(
                                PB7[:, 0:E8],
                                BBT[0][:, k * TILE + tb * 128: k * TILE + (tb + 1) * 128],
                                rw_sb[:, k * E8:(k + 1) * E8],
                                start=(k == 0), stop=(k == PC - 1))
                        return last
                    p.op("tensor", rmm, reads=[bb[0], b_rw], writes=[pb[7]])
                    sm = smOwn[:, blk * E8:(blk + 1) * E8]
                    p.op("vector", lambda e, sm=sm: e.tensor_add(
                        sm, PB7[:, 0:E8], rbbc_sb[:, :]),
                        reads=[pb[7], b_rbbc], writes=[b_smOwn])
                    p.op("vector", lambda e, sm=sm, blk=blk: e.tensor_sub(
                        sm, sm, gum_sb[:, blk * E8:(blk + 1) * E8]),
                        reads=[b_gum], writes=[b_smOwn])
                    p.op("vector", lambda e, sm=sm: e.reduce_max(
                        statc[:, 16:17], sm, axis=AX.X, negate=True),
                        reads=[b_smOwn], writes=[b_st])
                    p.op("scalar", lambda e, sm=sm: e.activation(
                        sm, sm, AF.Exp, bias=statc[:, 16:17],
                        accum_out=statc[:, 17:18]),
                        reads=[b_st], writes=[b_smOwn, b_st])
                    p.op("vector", lambda e: e.reciprocal(
                        statc[:, 18:19], statc[:, 17:18]), writes=[b_st])
                    p.op("vector", lambda e, sm=sm: e.tensor_scalar(
                        sm, sm, statc[:, 18:19], None, OP.mult),
                        reads=[b_st], writes=[b_smOwn])


            stage_T12(0)
            stage_T34(0)

            for t in range(NT):
                tok0 = t * TILE
                ck("t4")
                ri_m = rn()
                p.dma("sync", RGT[ri_m][:, :], maw1[:, :], writes=[rg[ri_m]],
                      sem="d_w%d" % ri_m)
                ri_t = rn()
                held.add(ri_t)
                p.dma("sync", RGT[ri_t][:, 0:1024], tdw1[:, :], writes=[rg[ri_t]],
                      sem="d_w%d" % ri_t)
                p.dma("sync", RGT[ri_t][:, 1024:2048], tdw2[:, :],
                      writes_acc=[rg[ri_t]], sem="d_w%d" % ri_t)
                for pcc in range(PC):
                    p.op("vector", lambda e, pcc=pcc: e.scalar_tensor_tensor(
                        BBT[2][:, pcc * TILE:(pcc + 1) * TILE],
                        BBT[1][:, pcc * TILE:(pcc + 1) * TILE],
                        col("maa_x", pcc),
                        BBT[0][:, pcc * TILE:(pcc + 1) * TILE],
                        OP.mult, OP.add),
                        reads=[bb[1], bb[0], b_cols], writes=[bb[2]])

                def mm_tanh(e, ri=ri_m):
                    last = None
                    for k in range(PC):
                        last = e.matmul(PB0[:, 0:TILE],
                                        RGT[ri][:, k * 128:(k + 1) * 128],
                                        BBT[2][:, k * TILE:(k + 1) * TILE],
                                        start=(k == 0), stop=(k == PC - 1))
                    return last
                p.op("tensor", mm_tanh, reads=[bb[2], rg[ri_m]], writes=[pb[0]])
                p.op("scalar", lambda e: e.activation(
                    tanh_sb[:, :], PB0[:, 0:TILE], AF.Tanh),
                    reads=[pb[0]], writes=[b_tanh])

                mix_into_bb2("maa_w", 3)
                if dbg:
                    p.dma("scalar", dbg_xw[:, tok0 * PC:(tok0 + TILE) * PC],
                        BBT[2][:, :],
                        reads=[bb[2]], writes_acc=[d_dbg], sem="d_dbg")

                def mm_wtd(e, ri=ri_t):
                    last = None
                    for k in range(PC):
                        last = e.matmul(PB2[0:64, 0:TILE],
                                        RGT[ri][:, k * 64:(k + 1) * 64],
                                        BBT[2][:, k * TILE:(k + 1) * TILE],
                                        start=(k == 0), stop=(k == PC - 1))
                    return last
                p.op("tensor", mm_wtd, reads=[bb[2], rg[ri_t]], writes=[pb[2]])
                p.op("scalar", lambda e: e.activation(
                    wtd_sb[0:64, :], PB2[0:64, 0:TILE], AF.Tanh),
                    reads=[pb[2]], writes=[b_wtd])
                p.op("vector", lambda e: e.tensor_copy(
                    wtd_sb[64:128, :], wtd_sb[0:64, :]), writes=[b_wtd])
                for pcc in range(PC):
                    bank = pcc % 2
                    rh = (pcc // 8) * 64
                    cc0 = 1024 + (pcc % 8) * 128
                    p.op("tensor", lambda e, bank=bank, rh=rh, cc0=cc0, ri=ri_t:
                         e.matmul(
                             PB[bank][:, 0:TILE],
                             RGT[ri][rh:rh + 64, cc0:cc0 + 128],
                             wtd_sb[rh:rh + 64, :], start=True, stop=True),
                         reads=[rg[ri_t], b_wtd], writes=[pb[bank]])
                    p.op("scalar", lambda e, pcc=pcc, bank=bank: e.activation(
                        BBT[3][:, pcc * TILE:(pcc + 1) * TILE],
                        PB[bank][:, 0:TILE], AF.Exp, bias=col("tdec", pcc)),
                        reads=[pb[bank], b_cols], writes=[bb[3]])
                    p.op("vector", lambda e, pcc=pcc: e.tensor_scalar(
                        BBT[3][:, pcc * TILE:(pcc + 1) * TILE],
                        BBT[3][:, pcc * TILE:(pcc + 1) * TILE],
                        5.0, -1.0, OP.min, OP.mult), writes=[bb[3]])

                if dbg:
                    p.dma("scalar", dbg_wlog[:, tok0 * PC:(tok0 + TILE) * PC],
                        BBT[3][:, :],
                        reads=[bb[3]], writes_acc=[d_dbg], sem="d_dbg")
                held.discard(ri_t)
                mix_into_bb2("maa_v", 2, dst_i=6)
                for k in range(PC):
                    ri = rn()
                    p.dma("sync", RGT[ri][:, :], vw[k * 128:(k + 1) * 128, :],
                          writes=[rg[ri]], sem="d_w%d" % ri)

                    def vmm(e, ri=ri, k=k):
                        last = None
                        for tb in range(TB):
                            for do in range(4):
                                last = e.matmul(
                                    PB[tb * 4 + do][:, 0:512],
                                    BBT[6][:, k * TILE + tb * 128: k * TILE + (tb + 1) * 128],
                                    RGT[ri][:, do * 512:(do + 1) * 512],
                                    start=(k == 0), stop=(k == PC - 1))
                        return last
                    p.op("tensor", vmm, reads=[rg[ri], bb[6]], writes=pb[0:8])
                for tb in range(TB):
                    for do in range(4):
                        p.op("vector", lambda e, tb=tb, do=do: e.tensor_add(
                            BBT[6][:, tb * D + do * 512: tb * D + (do + 1) * 512],
                            PB[tb * 4 + do][:, 0:512],
                            vbbc_sb[:, do * 512:(do + 1) * 512]),
                            reads=[pb[tb * 4 + do], b_vbbc], writes=[bb[6]])

                mix_into_bb2("maa_r", 0)

                def ev_q(do, bank):
                    p.op("scalar", lambda e, do=do, bank=bank: e.activation(
                        BBT[4][:, do * TILE:(do + 1) * TILE], PB[bank][:, 0:TILE],
                        AF.Identity, bias=qb8_sb[:, do:do + 1], scale=0.125),
                        reads=[pb[bank], b_qb8], writes=[bb[4]])
                proj_2048(qw4, 4, ev_q)

                mix_into_bb2("maa_k", 1)

                CH0 = t * NCH_T
                for pcc in range(PC):
                    p.op("vector", lambda e, pcc=pcc: e.tensor_tensor_scan(
                        BBT[1][:, pcc * TILE:(pcc + 1) * TILE],
                        BBT[3][:, pcc * TILE:(pcc + 1) * TILE],
                        BBT[3][:, pcc * TILE:(pcc + 1) * TILE],
                        Bcarry[:, pcc:pcc + 1], OP.add, OP.bypass),
                        reads=[bb[3], b_Bc], writes=[bb[1]])
                p.op("vector", lambda e, CH0=CH0: e.tensor_copy(
                    Bbase[:, CH0 * PC:(CH0 + 1) * PC], Bcarry[:, :]),
                    reads=[b_Bc], writes=[b_Bb])
                for m in range(1, NCH_T):
                    p.op("vector", lambda e, m=m, CH0=CH0: e.tensor_copy(
                        Bbase[:, (CH0 + m) * PC:(CH0 + m + 1) * PC],
                        v3(1)[:, :, m * CH - 1:m * CH].rearrange(
                            "p a c -> p (a c)")),
                        reads=[bb[1]], writes=[b_Bb])
                p.op("vector", lambda e, CH0=CH0: e.tensor_copy(
                    Bbase[:, (CH0 + NCH_T) * PC:(CH0 + NCH_T + 1) * PC],
                    v3(1)[:, :, TILE - 1:TILE].rearrange("p a c -> p (a c)")),
                    reads=[bb[1]], writes=[b_Bb])
                p.op("vector", lambda e, CH0=CH0: e.tensor_copy(
                    Bcarry[:, :],
                    Bbase[:, (CH0 + NCH_T) * PC:(CH0 + NCH_T + 1) * PC]),
                    reads=[b_Bb], writes=[b_Bc])
                p.op("vector", lambda e, CH0=CH0: e.tensor_sub(
                    Dcols[:, :],
                    Bbase[:, (CH0 + 1) * PC:(CH0 + 1 + NCH_T) * PC],
                    Bbase[:, CH0 * PC:(CH0 + NCH_T) * PC]),
                    reads=[b_Bb], writes=[b_Dc])
                p.op("scalar", lambda e: e.activation(
                    Dcols[:, :], Dcols[:, :], AF.Exp), writes=[b_Dc])
                X4 = lambda i: BBT[i][:, :].rearrange(
                    "p (q m c) -> p q m c", m=NCH_T, c=CH)
                for m in range(NCH_T):
                    p.op("scalar", lambda e, m=m, CH0=CH0: e.copy(
                        v3(3)[:, :, m * CH:(m + 1) * CH],
                        Bbase[:, (CH0 + m) * PC:(CH0 + m + 1) * PC]
                        .unsqueeze(2).broadcast_to([128, PC, CH])),
                        reads=[b_Bb], writes=[bb[3]])
                p.op("vector", lambda e: e.tensor_sub(
                    BBT[3][:, :], BBT[1][:, :], BBT[3][:, :]),
                    reads=[bb[1]], writes=[bb[3]])
                p.op("scalar", lambda e: e.activation(
                    BBT[1][:, :], BBT[3][:, :], AF.Exp, scale=-1.0),
                    reads=[bb[3]], writes=[bb[1]])
                p.op("scalar", lambda e: e.activation(
                    BBT[3][:, :], BBT[3][:, :], AF.Exp), writes=[bb[3]])
                p.op("vector", lambda e: e.tensor_mul(
                    BBT[4][:, :], BBT[4][:, :], BBT[3][:, :]),
                    reads=[bb[3]], writes=[bb[4]])
                p.op("scalar", lambda e: e.copy(
                    X4(3)[:, :, :, 1:CH], X4(1)[:, :, :, 0:CH - 1]),
                    reads=[bb[1]], writes=[bb[3]])
                p.op("vector", lambda e: e.memset(
                    X4(3)[:, :, :, 0:1], 1.0), writes=[bb[3]])
                p.op("vector", lambda e: e.tensor_sub(
                    BBT[3][:, :], BBT[1][:, :], BBT[3][:, :]),
                    reads=[bb[1]], writes=[bb[3]])
                p.dma("scalar", qt_scr[:, tok0 * PC:(tok0 + TILE) * PC],
                      BBT[4][:, :], reads=[bb[4]], writes_acc=[d_qt], sem="d_qt")

                def ev_k(do, bank):
                    p.op("scalar", lambda e, do=do, bank=bank: e.activation(
                        BBT[5][:, do * TILE:(do + 1) * TILE], PB[bank][:, 0:TILE],
                        AF.Identity, bias=col("k_b", do)),
                        reads=[pb[bank], b_cols], writes=[bb[5]])
                    p.op("vector", lambda e, do=do: e.tensor_mul(
                        BBT[5][:, do * TILE:(do + 1) * TILE],
                        BBT[5][:, do * TILE:(do + 1) * TILE],
                        BBT[3][:, do * TILE:(do + 1) * TILE]),
                        reads=[bb[3]], writes=[bb[5]])
                proj_2048(kw4, 5, ev_k)

                ck("t5")
                ck("t6")
                b_kh = bb[1]
                b_Pt = bb[3]
                b_ost = bb[1]
                b_vsw = bb[3]
                for m in range(NCH_T):
                    mg = CH0 + m
                    par = (m % 2) * 64
                    opar = 64 - par
                    tb = m // 2
                    if m % 2 == 0:
                        p.op("vector", lambda e, tb=tb: e.tensor_copy(
                            BBT[3][0:64, 2048:4096],
                            BBT[6][64:128, tb * D:tb * D + 2048]),
                            reads=[bb[6]], writes=[b_vsw])
                        p.op("vector", lambda e, tb=tb: e.tensor_copy(
                            BBT[3][64:128, 2048:4096],
                            BBT[6][0:64, tb * D:tb * D + 2048]),
                            reads=[bb[6]], writes=[b_vsw])

                    def vsl(h, par=par, tb=tb):
                        hp = (h % 2) * 64
                        if hp == par:
                            return BBT[6][par:par + 64,
                                          tb * D + h * DH: tb * D + (h + 1) * DH]
                        return BBT[3][hp:hp + 64, 2048 + h * DH:2048 + (h + 1) * DH]

                    def vrd(h, par=par):
                        return bb[6] if (h % 2) * 64 == par else b_vsw

                    for g in range(4):
                        def ttg(e, m=m, g=g):
                            last = None
                            for j in range(4):
                                pcx = g * 4 + j
                                last = e.transpose(
                                    PB6[0:64, j * 128:(j + 1) * 128],
                                    BBT[5][:, pcx * TILE + m * CH: pcx * TILE + (m + 1) * CH],
                                    id_sb[:, :])
                            return last
                        p.op("tensor", ttg, reads=[bb[5], b_id], writes=[pb[6]])
                        p.op("vector", lambda e, g=g: e.tensor_copy(
                            BBT[1][0:64, g * 512:(g + 1) * 512],
                            PB6[0:64, 0:512]), reads=[pb[6]], writes=[b_kh])
                        p.op("scalar", lambda e, g=g: e.activation(
                            BBT[1][64:128, g * 512:(g + 1) * 512],
                            PB6[0:64, 0:512], AF.Identity),
                            reads=[pb[6]], writes=[b_kh])

                    def qsl(h, m=m):
                        return BBT[4][(h % 2) * 64:(h % 2) * 64 + 64,
                                      (h // 2) * TILE + m * CH:
                                      (h // 2) * TILE + (m + 1) * CH]

                    def ksl(h, m=m):
                        return BBT[5][(h % 2) * 64:(h % 2) * 64 + 64,
                                      (h // 2) * TILE + m * CH:
                                      (h // 2) * TILE + (m + 1) * CH]

                    for half in range(2 if attn_lvl >= 2 else 0):
                        pbank = 4 + half

                        def pmm(e, half=half, pbank=pbank, qsl=qsl, ksl=ksl):
                            last = None
                            for i in range(16):
                                h = half * 16 + i
                                hp = (h % 2) * 64
                                sl = ((h // 2) % 8) * 64
                                last = e.matmul(
                                    PB[pbank][hp:hp + 64, sl:sl + 64],
                                    ksl(h), qsl(h), start=True, stop=True)
                            return last
                        p.op("tensor", pmm, reads=[bb[4], bb[5]],
                             writes=[pb[pbank]])
                        for hp in (0, 64):
                            p.op("vector", lambda e, half=half, pbank=pbank, hp=hp:
                                 e.tensor_mul(
                                     BBT[3][hp:hp + 64, :]
                                     .rearrange("p (a c) -> p a c", c=CH)
                                     [:, half * 16 + (hp // 64):half * 16 + 16:2, :],
                                     PB[pbank][hp:hp + 64, 0:512]
                                     .rearrange("p (a c) -> p a c", c=CH),
                                     mask8_sb[:, :]
                                     .rearrange("p (a c) -> p a c", c=CH)),
                                 reads=[pb[pbank], b_mask8], writes=[b_Pt])

                    for half in range(2 if attn_lvl >= 3 else 0):
                        obank = half

                        def omm(e, half=half, obank=obank, vsl=vsl, qsl=qsl):
                            last = None
                            for i in range(16):
                                h = half * 16 + i
                                hp = (h % 2) * 64
                                sl = ((h // 2) % 8) * 64
                                last = e.matmul(
                                    PB[obank][hp:hp + 64, sl:sl + 64],
                                    vsl(h),
                                    BBT[3][hp:hp + 64, h * 64:(h + 1) * 64],
                                    start=True, stop=False)
                                last = e.matmul(
                                    PB[obank][hp:hp + 64, sl:sl + 64],
                                    S_sb[hp:hp + 64,
                                         (h // 2) * 64:(h // 2) * 64 + 64],
                                    qsl(h), start=False, stop=True)
                            return last
                        p.op("tensor", omm,
                             reads=[bb[6], b_vsw, b_Pt, b_S, bb[4]],
                             writes=[pb[obank]])

                    for half in range(2 if attn_lvl >= 4 else 0):
                        sbank = 2 + half

                        def smm(e, half=half, sbank=sbank, par=par, vsl=vsl):
                            last = None
                            for i in range(16):
                                h = half * 16 + i
                                hp = (h % 2) * 64
                                pcc = h // 2
                                sl = (pcc % 8) * 64
                                last = e.matmul(
                                    PB[sbank][hp:hp + 64, sl:sl + 64],
                                    BBT[1][hp:hp + 64,
                                           pcc * 128 + hp: pcc * 128 + hp + 64],
                                    vsl(h), start=True, stop=True)
                            return last
                        p.op("tensor", smm, reads=[b_kh, bb[6], b_vsw],
                             writes=[pb[sbank]])

                    if attn_lvl >= 5:
                        for half in range(2):
                            p.op("scalar", lambda e, half=half: e.activation(
                                BBT[1][:, 2048 + half * 512:2048 + half * 512 + 512],
                                PB[half][:, 0:512], AF.Identity),
                                reads=[pb[half]], writes=[b_ost])
                        p.dma("scalar",
                              o_scr[:, (tok0 + m * CH) * PC:(tok0 + (m + 1) * CH) * PC],
                              BBT[1][:, 2048:3072], reads=[b_ost],
                              writes_acc=[d_o], sem="d_o")

                    if attn_lvl >= 4:
                        for half in range(2):
                            p.op("vector", lambda e, half=half: e.tensor_add(
                                S_sb[:, half * 512:(half + 1) * 512],
                                S_sb[:, half * 512:(half + 1) * 512],
                                PB[2 + half][:, 0:512]),
                                reads=[pb[2 + half]], writes=[b_S])
                        p.op("vector", lambda e, m=m: e.tensor_mul(
                            S_sb[:, :].rearrange("p (q c) -> p q c", c=64),
                            S_sb[:, :].rearrange("p (q c) -> p q c", c=64),
                            Dcols[:, m * PC:(m + 1) * PC]
                            .unsqueeze(2).broadcast_to([128, PC, 64])),
                            reads=[b_Dc], writes=[b_S])

                if t == NT - 1:
                    p.dma("gpsimd", ccsm_i[:, :], smOwn[:, :], reads=[b_smOwn],
                          writes=[d_ci], sem="d_cci")
                    if ncores > 1:
                        p.op("gpsimd", lambda e: e.collective_compute(
                            "AllGather", OP.bypass,
                            replica_groups=[list(range(ncores))],
                            ins=[ccsm_i.opt()], outs=[ccsm_o.opt()]),
                            reads=[d_ci], writes=[d_co], sem="cc1")
                    else:
                        p.dma("gpsimd", ccsm_o[0:128, :], ccsm_i[:, :],
                              reads=[d_ci], writes=[d_co], sem="cc1")
                    p.op("scalar", lambda e: e.activation(
                        dsl_sb[:, :], Bcarry[:, :], AF.Exp),
                        reads=[b_Bc], writes=[b_dsl])
                    p.dma("gpsimd", ccsu_i[:, 0:1024], S_sb[:, :], reads=[b_S],
                          writes=[d_si], sem="d_ccs")
                    p.dma("gpsimd", ccsu_i[:, 1024:1040], dsl_sb[:, :],
                          reads=[b_dsl], writes_acc=[d_si], sem="d_ccs")
                    if ncores > 1:
                        p.op("gpsimd", lambda e: e.collective_compute(
                            "AllGather", OP.bypass,
                            replica_groups=[list(range(ncores))],
                            ins=[ccsu_i.opt()], outs=[ccsu_o.opt()]),
                            reads=[d_si], writes=[d_so], sem="cc2")
                    else:
                        p.dma("gpsimd", ccsu_o[0:128, :], ccsu_i[:, :],
                              reads=[d_si], writes=[d_so], sem="cc2")

                if t + 1 < NT:
                    stage_T12(t + 1)
                ck("t7b")
                hhome = [1, 4, 5]

                def hT(c):
                    return BBT[hhome[c // 16]], bb[hhome[c // 16]], (c % 16) * TILE

                for c in range(NF):
                    rg_g = rn()
                    rg_u = rn()
                    p.dma("sync", RGT[rg_g][:, :],
                          gw4[c].rearrange("p a b -> p (a b)"),
                          writes=[rg[rg_g]], sem="d_w%d" % rg_g)
                    p.dma("sync", RGT[rg_u][:, :],
                          uw4[c].rearrange("p a b -> p (a b)"),
                          writes=[rg[rg_u]], sem="d_w%d" % rg_u)

                    def gmm(e, ri=rg_g):
                        last = None
                        for k in range(PC):
                            last = e.matmul(PB2[:, 0:TILE],
                                            RGT[ri][:, k * 128:(k + 1) * 128],
                                            BBT[0][:, k * TILE:(k + 1) * TILE],
                                            start=(k == 0), stop=(k == PC - 1))
                        return last
                    p.op("tensor", gmm, reads=[rg[rg_g], bb[0]], writes=[pb[2]])

                    def umm(e, ri=rg_u):
                        last = None
                        for k in range(PC):
                            last = e.matmul(PB3[:, 0:TILE],
                                            RGT[ri][:, k * 128:(k + 1) * 128],
                                            BBT[0][:, k * TILE:(k + 1) * TILE],
                                            start=(k == 0), stop=(k == PC - 1))
                        return last
                    p.op("tensor", umm, reads=[rg[rg_u], bb[0]], writes=[pb[3]])
                    hten, hbuf, hoff = hT(c)
                    p.op("scalar", lambda e, c=c: e.activation(
                        tanh_sb[:, :], PB2[:, 0:TILE], AF.Silu,
                        bias=bgu_sb[:, c:c + 1]),
                        reads=[pb[2], b_bgu], writes=[b_tanh])
                    p.op("vector", lambda e, c=c, hten=hten, hoff=hoff:
                         e.scalar_tensor_tensor(
                             hten[:, hoff:hoff + TILE], PB3[:, 0:TILE],
                             bgu_sb[:, NF + c:NF + c + 1], tanh_sb[:, :],
                             OP.add, OP.mult),
                         reads=[pb[3], b_bgu, b_tanh], writes=[hbuf])

                parts = [(0, 15), (15, 30), (30, NF)]
                for do in range(PC):
                    bank = do % 2
                    for (c0, c1) in parts:
                        ri = rn()
                        p.dma("sync", RGT[ri][:, 0:(c1 - c0) * 128],
                              dw4[do][:, c0:c1, :].rearrange("p a b -> p (a b)"),
                              writes=[rg[ri]], sem="d_w%d" % ri)

                        def dmm(e, ri=ri, c0=c0, c1=c1, bank=bank):
                            last = None
                            for c in range(c0, c1):
                                hten, _, hoff = hT(c)
                                last = e.matmul(
                                    PB[bank][:, 0:TILE],
                                    RGT[ri][:, (c - c0) * 128:(c - c0 + 1) * 128],
                                    hten[:, hoff:hoff + TILE],
                                    start=(c == 0), stop=(c == NF - 1))
                            return last
                        p.op("tensor", dmm,
                             reads=[rg[ri]] + [bb[i] for i in hhome],
                             writes=[pb[bank]])
                    p.op("vector", lambda e, do=do, bank=bank: e.tensor_scalar(
                        BBT[6][:, do * TILE:(do + 1) * TILE], PB[bank][:, 0:TILE],
                        col("b_dn", do), None, OP.add),
                        reads=[pb[bank], b_cols], writes=[bb[6]])
                p.dma("scalar", ffn_scr[:, tok0 * PC:(tok0 + TILE) * PC],
                      BBT[6][:, :], reads=[bb[6]], writes_acc=[d_ffn], sem="d_ff")

                if t + 1 < NT:
                    stage_T34(t + 1)
            ck("t8")
            p.dma("scalar", BBT[4][:, :], qt_scr[:, 0:TILE * PC],
                  reads=[d_qt], writes=[bb[4]], sem="d_rq")
            p.dma("scalar", BBT[1][:, :], o_scr[:, 0:TILE * PC],
                  reads=[d_o], writes=[bb[1]], sem="d_ro")
            p.dma("scalar", BBT[2][:, :], x_fm[:, 0:TILE * PC],
                  writes=[bb[2]], sem="d_rx")
            for r in range(ncores):
                p.dma("gpsimd", smAll[:, r * SMW:(r + 1) * SMW],
                      ccsm_o[r * 128:(r + 1) * 128, :],
                      reads=[d_co], writes=[b_smAll], sem="d_sml")
            p.op("vector", lambda e: e.memset(Sin_sb[:, :], 0.0), writes=[b_Sin])
            for r in range(ncores - 1):
                if r % 2 == 0:
                    src, bsrc = Ur_sb, b_Ur
                else:
                    src, bsrc = BBT[3], bb[3]
                p.dma("gpsimd", src[:, 0:1040], ccsu_o[r * 128:(r + 1) * 128, :],
                      reads=[d_so], writes=[bsrc], sem="d_url")
                p.op("vector", lambda e, r=r, src=src: e.scalar_tensor_tensor(
                    dsl_sb[:, :], src[:, 1024:1040], incl_sb[:, r:r + 1],
                    incl_sb[:, 8 + r:8 + r + 1].broadcast_to([128, PC]),
                    OP.mult, OP.add), reads=[bsrc, b_incl], writes=[b_dsl])
                p.op("vector", lambda e: e.tensor_mul(
                    Sin_sb[:, :].rearrange("p (q c) -> p q c", c=64),
                    Sin_sb[:, :].rearrange("p (q c) -> p q c", c=64),
                    dsl_sb[:, :].unsqueeze(2).broadcast_to([128, PC, 64])),
                    reads=[b_dsl], writes=[b_Sin])
                p.op("vector", lambda e, r=r, src=src: e.scalar_tensor_tensor(
                    Sin_sb[:, :], src[:, 0:1024], incl_sb[:, r:r + 1],
                    Sin_sb[:, :], OP.mult, OP.add),
                    reads=[bsrc, b_incl], writes=[b_Sin])
            ck("c0")
            p.op("vector", lambda e: e.memset(bis8[:, 0:8], 0.0), writes=[b_bis])
            p.op("vector", lambda e: e.memset(bis8[:, 8:16], 1.0), writes=[b_bis])
            p.op("vector", lambda e: e.memset(bis8[:, 16:24], 0.5), writes=[b_bis])
            p.op("vector", lambda e: e.memset(midbc[:, :], 0.5), writes=[b_midbc])
            bis_done = [0]

            ck("bisinit")

            FOLD0 = max(1, (NT + 1) // 2)
            BIS_PER = -(-BIS_IT // FOLD0)
            pending = list(range(FOLD0))

            def wgt_bc(tok, half):
                for tb in range(TB):
                    p.op("tensor", lambda e, tok=tok, tb=tb, half=half: e.matmul(
                        PB2[:, half * TILE + tb * 128:
                            half * TILE + (tb + 1) * 128],
                        onesr_sb[0:1, :],
                        wgtrow[0:1, tok + tb * 128:tok + (tb + 1) * 128],
                        start=True, stop=True),
                        reads=[b_wgr, b_onesr], writes=[pb[2]])
                p.op("vector", lambda e, half=half: e.tensor_copy(
                    wgtbc_sb[:, half * TILE:(half + 1) * TILE],
                    PB2[:, half * TILE:(half + 1) * TILE]),
                    reads=[pb[2]], writes=[b_wgb])

            def mul_wgt(i, half):
                p.op("vector", lambda e, i=i, half=half: e.tensor_mul(
                    BBT[i][:, :].rearrange("p (a c) -> p a c", c=TILE),
                    BBT[i][:, :].rearrange("p (a c) -> p a c", c=TILE),
                    wgtbc_sb[:, half * TILE:(half + 1) * TILE]
                    .unsqueeze(1).broadcast_to([128, PC, TILE])),
                    reads=[b_wgb], writes=[bb[i]])

            def emit_wgt():
                for blk in range(tpc // 128):
                    p.op("vector", lambda e, blk=blk: e.tensor_tensor(
                        bisc[:, 0:E8], smOwn[:, blk * E8:(blk + 1) * E8],
                        bis8[:, 0:8], OP.is_ge),
                        reads=[b_smOwn, b_bis], writes=[b_bisc])
                    p.op("vector", lambda e, blk=blk: e.tensor_mul(
                        bisc[:, 0:E8], bisc[:, 0:E8],
                        smOwn[:, blk * E8:(blk + 1) * E8]),
                        reads=[b_smOwn], writes=[b_bisc])
                    p.op("vector", lambda e, blk=blk: e.reduce_sum(
                        wgt_sb[:, blk:blk + 1], bisc[:, 0:E8], axis=AX.X),
                        reads=[b_bisc], writes=[b_wgt])
                NBLK = tpc // 128
                p.op("tensor", lambda e: e.transpose(
                    PB7[0:NBLK, 0:128], wgt_sb[:, :], id_sb[:, :]),
                    reads=[b_wgt, b_id], writes=[pb[7]])
                p.op("vector", lambda e: e.tensor_copy(
                    tanh_sb[0:NBLK, 0:128], PB7[0:NBLK, 0:128]),
                    reads=[pb[7]], writes=[b_tanh])
                for blk in range(NBLK):
                    p.op("tensor", lambda e, blk=blk: e.matmul(
                        PB7[0:1, 128:256], id_sb[0:NBLK, blk:blk + 1],
                        tanh_sb[0:NBLK, 0:128], start=True, stop=True),
                        reads=[b_tanh, b_id], writes=[pb[7]])
                    p.op("vector", lambda e, blk=blk: e.tensor_copy(
                        wgtrow[0:1, blk * 128:(blk + 1) * 128],
                        PB7[0:1, 128:256]), reads=[pb[7]], writes=[b_wgr])

            def acc_tile(j):
                tj = j * TILE
                p.dma("scalar", BBT[4][:, :],
                      ffn_scr[:, tj * PC:(tj + TILE) * PC],
                      reads=[d_ffn], writes=[bb[4]], sem="d_rf")
                wgt_bc(tj, 1)
                mul_wgt(4, 1)
                if trunc == "noacc":
                    p.dma("gpsimd", ffn_scr[:, tj * PC:(tj + TILE) * PC],
                          BBT[4][:, :], reads=[bb[4], d_ffn],
                          writes_acc=[d_y], sem="d_ya")
                else:
                    for q in range(4):
                        p.dma("gpsimd",
                              y_fm[:, tj * PC + q * 1024:
                                   tj * PC + (q + 1) * 1024],
                              BBT[4][:, q * 1024:(q + 1) * 1024],
                              reads=[bb[4], d_y], writes_acc=[d_y],
                              sem="d_ya", accum=OP.add)

            for t in range(NT):
                tok0 = t * TILE
                CH0 = t * NCH_T
                if t > 0:
                    p.dma("scalar", BBT[4][:, :],
                          qt_scr[:, tok0 * PC:(tok0 + TILE) * PC],
                          reads=[d_qt], writes=[bb[4]], sem="d_rq")
                    p.dma("scalar", BBT[1][:, :],
                          o_scr[:, tok0 * PC:(tok0 + TILE) * PC],
                          reads=[d_o], writes=[bb[1]], sem="d_ro")
                    p.dma("scalar", BBT[2][:, :],
                          x_fm[:, tok0 * PC:(tok0 + TILE) * PC],
                          writes=[bb[2]], sem="d_rx")
                if t >= FOLD0:
                    p.dma("scalar", BBT[3][:, :],
                          ffn_scr[:, tok0 * PC:(tok0 + TILE) * PC],
                          reads=[d_ffn], writes=[bb[3]], sem="d_rf")
                    wgt_bc(tok0, 0)

                ck("creload")
                for m in range(NCH_T):
                    mg = CH0 + m
                    p.op("scalar", lambda e, mg=mg: e.activation(
                        eb_sb[:, :], Bbase[:, mg * PC:(mg + 1) * PC], AF.Exp),
                        reads=[b_Bb], writes=[b_eb])
                    p.op("vector", lambda e: e.tensor_mul(
                        SC_sb[:, :].rearrange("p (q c) -> p q c", c=64),
                        Sin_sb[:, :].rearrange("p (q c) -> p q c", c=64),
                        eb_sb[:, :].unsqueeze(2).broadcast_to([128, PC, 64])),
                        reads=[b_Sin, b_eb], writes=[b_SC])

                    for half in range(2):
                        def cmm(e, half=half, m=m):
                            last = None
                            for i in range(16):
                                h = half * 16 + i
                                hp = (h % 2) * 64
                                sl = ((h // 2) % 8) * 64
                                last = e.matmul(
                                    PB[half][hp:hp + 64, sl:sl + 64],
                                    SC_sb[hp:hp + 64,
                                          (h // 2) * 64:(h // 2) * 64 + 64],
                                    BBT[4][hp:hp + 64,
                                           (h // 2) * TILE + m * CH:
                                           (h // 2) * TILE + (m + 1) * CH],
                                    start=True, stop=True)
                            return last
                        p.op("tensor", cmm, reads=[b_SC, bb[4]],
                             writes=[pb[half]])
                    for half in range(2):
                        p.op("vector", lambda e, half=half, m=m: e.tensor_add(
                            BBT[1][:, m * 1024 + half * 512:
                                   m * 1024 + half * 512 + 512],
                            BBT[1][:, m * 1024 + half * 512:
                                   m * 1024 + half * 512 + 512],
                            PB[half][:, 0:512]),
                            reads=[pb[half]], writes=[bb[1]])

                ck("corr")
                if t >= FOLD0 and pending:
                    acc_tile(pending.pop(0))
                if t >= FOLD0:
                    mul_wgt(3, 0)
                p.op("scalar", lambda e: e.activation(
                    BBT[6][:, :], BBT[1][:, :], AF.Square),
                    reads=[bb[1]], writes=[bb[6]])

                def smm1(e):
                    last = None
                    v1 = BBT[1][:, :].rearrange("p (m q c) -> p m q c", q=PC, c=CH)
                    for k in range(PC):
                        last = e.matmul(PB4[0:1, 0:TILE], ones_sb[:, 0:1],
                                        v1[:, :, k, :],
                                        start=(k == 0), stop=(k == PC - 1))
                    return last
                p.op("tensor", smm1, reads=[bb[1], b_ones], writes=[pb[4]])
                p.op("vector", lambda e: e.tensor_scalar(
                    abrow[0:1, 0:TILE], PB4[0:1, 0:TILE], 1.0 / D, None, OP.mult),
                    reads=[pb[4]], writes=[b_ab])

                def smm2(e):
                    last = None
                    v6 = BBT[6][:, :].rearrange("p (m q c) -> p m q c", q=PC, c=CH)
                    for k in range(PC):
                        last = e.matmul(PB4[0:1, 256:256 + TILE], ones_sb[:, 0:1],
                                        v6[:, :, k, :],
                                        start=(k == 0), stop=(k == PC - 1))
                    return last
                p.op("tensor", smm2, reads=[bb[6], b_ones], writes=[pb[4]])
                p.op("vector", lambda e: e.tensor_scalar(
                    abrow[0:1, TILE:2 * TILE], PB4[0:1, 256:256 + TILE], 1.0 / D,
                    None, OP.mult), reads=[pb[4]], writes=[b_ab])
                p.op("scalar", lambda e: e.activation(
                    abrow[0:1, 2 * TILE:3 * TILE], abrow[0:1, 0:TILE], AF.Square),
                    writes=[b_ab])
                p.op("vector", lambda e: e.tensor_sub(
                    abrow[0:1, TILE:2 * TILE], abrow[0:1, TILE:2 * TILE],
                    abrow[0:1, 2 * TILE:3 * TILE]), writes=[b_ab])
                p.op("scalar", lambda e: e.activation(
                    abrow[0:1, TILE:2 * TILE], abrow[0:1, TILE:2 * TILE], AF.Sqrt,
                    bias=cstc[0:1, 1:2]), writes=[b_ab])
                p.op("vector", lambda e: e.reciprocal(
                    abrow[0:1, TILE:2 * TILE], abrow[0:1, TILE:2 * TILE]),
                    writes=[b_ab])
                p.op("vector", lambda e: e.tensor_mul(
                    abrow[0:1, 2 * TILE:3 * TILE], abrow[0:1, 0:TILE],
                    abrow[0:1, TILE:2 * TILE]), writes=[b_ab])
                p.op("vector", lambda e: e.tensor_scalar(
                    abrow[0:1, 2 * TILE:3 * TILE], abrow[0:1, 2 * TILE:3 * TILE],
                    -1.0, None, OP.mult), writes=[b_ab])
                p.op("tensor", lambda e: e.matmul(
                    PB5[:, 0:TILE], onesr_sb[0:1, :], abrow[0:1, TILE:2 * TILE],
                    start=True, stop=True), reads=[b_ab, b_onesr], writes=[pb[5]])
                p.op("tensor", lambda e: e.matmul(
                    PB5[:, 256:256 + TILE], onesr_sb[0:1, :],
                    abrow[0:1, 2 * TILE:3 * TILE],
                    start=True, stop=True), reads=[b_ab, b_onesr], writes=[pb[5]])
                v1c = BBT[1][:, :].rearrange("p (m q c) -> p m q c", q=PC, c=CH)
                v5c = BBT[5][:, :].rearrange("p (m q c) -> p m q c", q=PC, c=CH)
                abc = PB5[:, 0:TILE].rearrange("p (m c) -> p m c", c=CH)
                bbc = PB5[:, 256:256 + TILE].rearrange("p (m c) -> p m c", c=CH)
                p.op("vector", lambda e, v5c=v5c, v1c=v1c, abc=abc: e.tensor_mul(
                    v5c, v1c,
                    abc.unsqueeze(2).broadcast_to([128, NCH_T, PC, CH])),
                    reads=[bb[1], pb[5]], writes=[bb[5]])
                p.op("vector", lambda e, v5c=v5c, bbc=bbc: e.tensor_add(
                    v5c, v5c,
                    bbc.unsqueeze(2).broadcast_to([128, NCH_T, PC, CH])),
                    reads=[pb[5]], writes=[bb[5]])
                for pcc in range(PC):
                    p.op("vector", lambda e, pcc=pcc, v5c=v5c: e.tensor_scalar(
                        v5c[:, :, pcc, :], v5c[:, :, pcc, :],
                        col("ln_g", pcc), col("ln_b", pcc), OP.mult, OP.add),
                        reads=[b_cols], writes=[bb[5]])

                if dbg:
                    p.dma("scalar", dbg_y[:, tok0 * PC:(tok0 + TILE) * PC],
                        BBT[5][:, :],
                        reads=[bb[5]], writes_acc=[d_dbg], sem="d_dbg")
                ck("ln2")
                for do in range(PC):
                    ri = rn()
                    bank = 6 + do % 2
                    p.dma("sync", RGT[ri][:, :],
                          ow4[do].rearrange("p a b -> p (a b)"),
                          writes=[rg[ri]], sem="d_w%d" % ri)

                    def wmm(e, ri=ri, bank=bank):
                        last = None
                        v5 = BBT[5][:, :].rearrange("p (m q c) -> p m q c",
                                                    q=PC, c=CH)
                        for k in range(PC):
                            last = e.matmul(PB[bank][:, 0:TILE],
                                            RGT[ri][:, k * 128:(k + 1) * 128],
                                            v5[:, :, k, :],
                                            start=(k == 0), stop=(k == PC - 1))
                        return last
                    p.op("tensor", wmm, reads=[rg[ri], bb[5]], writes=[pb[bank]])
                    p.op("vector", lambda e, do=do, bank=bank: e.tensor_add(
                        BBT[6][:, do * TILE:(do + 1) * TILE], PB[bank][:, 0:TILE],
                        BBT[2][:, do * TILE:(do + 1) * TILE]),
                        reads=[pb[bank], bb[2]], writes=[bb[6]])
                    if t < FOLD0:
                        tgt = min(BIS_IT, t * BIS_PER
                                  + (BIS_PER * (do + 1) + PC - 1) // PC)
                        while bis_done[0] < tgt:
                            bis_iter()
                            bis_done[0] += 1
                if t == FOLD0 - 1:
                    while bis_done[0] < BIS_IT:
                        bis_iter()
                        bis_done[0] += 1
                    emit_wgt()
                if t >= FOLD0:
                    p.op("vector", lambda e: e.tensor_add(
                        BBT[6][:, :], BBT[6][:, :], BBT[3][:, :]),
                        reads=[bb[3]], writes=[bb[6]])
                p.dma("scalar", y_fm[:, tok0 * PC:(tok0 + TILE) * PC],
                      BBT[6][:, :], reads=[bb[6]], writes_acc=[d_y], sem="d_y")

            ck("ow")
            for j in list(pending):
                pending.remove(j)
                acc_tile(j)
            ck("wgt")

        except _Done:
            pass
        if trunc is not None:
            p.dma("scalar", y_fm[:, 0:PC * TILE], BBT[0][:, :],
                  writes_acc=[d_y], sem="d_y")
        p.wait_only("sync", [d_y])

        handles = {}
        cms = []
        for sname in sorted(p.val):
            cm = nc.semaphore(sname)
            handles[sname] = cm.__enter__()
            cms.append(cm)
        for ename, deco in (("tensor", block.tensor), ("vector", block.vector),
                            ("scalar", block.scalar), ("sync", block.sync),
                            ("gpsimd", block.gpsimd)):
            items = p.items[ename]
            waited = {}
            deco(lambda eng, items=items, waited=waited:
                 _emit(eng, items, handles, waited))
        for cm in reversed(cms):
            cm.__exit__(None, None, None)

    return nc


def stage_inputs(inputs, ncores, tpc):
    f32 = np.float32
    x = np.ascontiguousarray(np.asarray(inputs["L0_attn_x"], dtype=f32)[0])
    T = x.shape[0]
    assert T == ncores * tpc

    def c4(w):
        return np.ascontiguousarray(
            w.reshape(PC, 128, PC, 128).transpose(2, 1, 0, 3))

    qw4 = c4(np.asarray(inputs["L0_attn_q_w"], dtype=f32))
    kw4 = c4(np.asarray(inputs["L0_attn_k_w"], dtype=f32))
    ow4 = c4(np.asarray(inputs["L0_attn_o_w"], dtype=f32))
    vw = np.ascontiguousarray(np.asarray(inputs["L0_attn_v_w"], dtype=f32))
    wg = np.zeros((D, DFFP), dtype=f32)
    wg[:, :DFF] = np.asarray(inputs["L0_ffn_w_gate"], dtype=f32)
    wu = np.zeros((D, DFFP), dtype=f32)
    wu[:, :DFF] = np.asarray(inputs["L0_ffn_w_up"], dtype=f32)
    wd = np.zeros((DFFP, D), dtype=f32)
    wd[:DFF, :] = np.asarray(inputs["L0_ffn_w_down"], dtype=f32)
    gw4 = np.ascontiguousarray(
        wg.reshape(PC, 128, NF, 128).transpose(2, 1, 0, 3))
    uw4 = np.ascontiguousarray(
        wu.reshape(PC, 128, NF, 128).transpose(2, 1, 0, 3))
    dw4 = np.ascontiguousarray(
        wd.reshape(NF, 128, PC, 128).transpose(2, 1, 0, 3))
    m2 = np.asarray(inputs["L0_attn_maa_w2"], dtype=f32)
    m2w4 = np.zeros((4, PC, 128, 128), dtype=f32)
    for r in range(4):
        m2w4[r, :, 32 * r:32 * r + 32, :] = \
            m2[r].reshape(32, PC, 128).transpose(1, 0, 2)
    maw1 = np.ascontiguousarray(
        np.asarray(inputs["L0_attn_maa_w1"], dtype=f32)
        .reshape(PC, 128, 128).transpose(1, 0, 2).reshape(128, PC * 128))
    tdw1 = np.ascontiguousarray(
        np.asarray(inputs["L0_attn_td_w1"], dtype=f32)
        .reshape(PC, 128, 64).transpose(1, 0, 2).reshape(128, PC * 64))
    _t2 = np.asarray(inputs["L0_attn_td_w2"], dtype=f32)
    tdw2 = np.ascontiguousarray(np.concatenate([_t2[:, 0:1024], _t2[:, 1024:2048]], axis=0))
    rwm = np.ascontiguousarray(
        np.asarray(inputs["L0_ffn_router_w"], dtype=f32)
        .reshape(PC, 128, E8).transpose(1, 0, 2).reshape(128, PC * E8))

    def vcol(name):
        return np.asarray(inputs[name], dtype=f32).reshape(PC, 128).T

    cols = np.zeros((128, 16 * PC), dtype=f32)
    for i, nm in enumerate(["L0_attn_maa_x", "L0_attn_maa_r", "L0_attn_maa_k",
                            "L0_attn_maa_v", "L0_attn_maa_w",
                            "L0_attn_time_decay", "L0_attn_q_b",
                            "L0_attn_k_b", "L0_attn_ln_g", "L0_attn_ln_b",
                            "L0_ffn_b_down"]):
        cols[:, i * PC:(i + 1) * PC] = vcol(nm)
    bg = np.zeros(DFFP, dtype=f32)
    bg[:DFF] = np.asarray(inputs["L0_ffn_b_gate"], dtype=f32)
    bu = np.zeros(DFFP, dtype=f32)
    bu[:DFF] = np.asarray(inputs["L0_ffn_b_up"], dtype=f32)
    bgu = np.concatenate([bg.reshape(NF, 128).T, bu.reshape(NF, 128).T],
                         axis=1).astype(f32)
    bgu = np.ascontiguousarray(bgu)
    vbrow = np.asarray(inputs["L0_attn_v_b"], dtype=f32).reshape(1, D)
    rbrow = np.asarray(inputs["L0_ffn_router_b"], dtype=f32).reshape(1, E8)
    ident = np.eye(128, dtype=f32)
    m1 = np.triu(np.ones((CH, CH), dtype=f32))
    mask8 = np.tile(m1, (1, 8))
    onesc = np.ones((128, 1), dtype=f32)
    onesr = np.ones((1, 128), dtype=f32)

    n = T * E8
    u32 = uniform_f32(42, n).reshape(T, E8)
    u64 = uniform_f64(42, n).reshape(T, E8)
    du = u64 - u32.astype(np.float64)
    duhi = du.astype(f32)
    dulo = (du - duhi.astype(np.float64)).astype(f32)

    SMW = (tpc // 128) * E8
    in_maps = []
    for c in range(ncores):
        sl = slice(c * tpc, (c + 1) * tpc)
        x_sl = np.ascontiguousarray(x[sl])
        x_fmc = np.ascontiguousarray(
            x_sl.reshape(tpc // TILE, TILE, PC, 128)
            .transpose(3, 0, 2, 1).reshape(128, PC * tpc))
        x_bnd = (np.zeros((1, D), dtype=f32) if c == 0
                 else np.ascontiguousarray(x[c * tpc - 1:c * tpc, :]))
        incl2 = np.zeros((128, 16), dtype=f32)
        for r in range(E8):
            incl2[:, r] = 1.0 if r < c else 0.0
            incl2[:, 8 + r] = 0.0 if r < c else 1.0

        def ublk(a):
            return np.ascontiguousarray(
                a[sl].reshape(tpc // 128, 128, E8).transpose(1, 0, 2)
                .reshape(128, SMW))

        u3 = np.concatenate([ublk(u32), ublk(duhi), ublk(dulo)], axis=1)
        in_maps.append({
            "x_tm": x_sl, "x_fm": x_fmc, "x_bnd": x_bnd,
            "qw4": qw4, "kw4": kw4, "ow4": ow4, "vw": vw,
            "gw4": gw4, "uw4": uw4, "dw4": dw4, "m2w4": m2w4,
            "maw1": maw1, "tdw1": tdw1, "tdw2": tdw2, "rw": rwm,
            "cols": cols, "bgu": bgu, "vbrow": vbrow, "rbrow": rbrow,
            "ident": ident, "mask8": mask8, "onesc": onesc, "onesr": onesr,
            "incl2": incl2, "u3": np.ascontiguousarray(u3),
        })
    return in_maps


def kernel(**inputs):
    ncores = NCORES
    T = np.asarray(inputs["L0_attn_x"]).shape[1]
    tpc = T // ncores
    in_maps = stage_inputs(inputs, ncores, tpc)
    nc = build_nc(ncores, tpc)
    res = run_bass_kernel_spmd(nc, in_maps, core_ids=list(range(ncores)))
    outs = []
    for c in range(ncores):
        yf = np.asarray(res.results[c]["y_fm"]).reshape(
            128, tpc // TILE, PC, TILE)
        outs.append(yf.transpose(1, 3, 2, 0).reshape(tpc, D))
    return np.concatenate(outs, axis=0).reshape(1, T, D).astype(np.float32)


# version 11
# speedup vs baseline: 1.0742x; 1.0019x over previous
import numpy as np

import concourse.bass as bass
import concourse.mybir as mybir
from concourse.bass_utils import run_bass_kernel_spmd

F32 = mybir.dt.float32
AF = mybir.ActivationFunctionType
OP = mybir.AluOpType
AX = mybir.AxisListType

D = 2048
NCORES = 8
TILE = 256
CH = 64
NCH_T = TILE // CH
H = 32
DH = 64
PC = D // 128
DFF = 5461
DFFP = 5504
NF = DFFP // 128
E8 = 8
LN_EPS = 1e-5
REPS = 1e-10
BIS_IT = 64

ENGS = ("tensor", "vector", "scalar", "sync", "gpsimd")


def _tf_round(v0, v1, rot):
    v0 = (v0 + v1).astype(np.uint32)
    v1 = ((v1 << np.uint32(rot)) | (v1 >> np.uint32(32 - rot))).astype(np.uint32)
    return v0, (v0 ^ v1).astype(np.uint32)


def _threefry2x32(k1, k2, x1, x2):
    rots = ([13, 15, 26, 6], [17, 29, 16, 24])
    ks = [np.uint32(k1), np.uint32(k2),
          np.uint32(np.uint32(k1) ^ np.uint32(k2) ^ np.uint32(0x1BD11BDA))]
    x0 = (x1 + ks[0]).astype(np.uint32)
    y0 = (x2 + ks[1]).astype(np.uint32)
    for i in range(5):
        for r in rots[i % 2]:
            x0, y0 = _tf_round(x0, y0, r)
        x0 = (x0 + ks[(i + 1) % 3]).astype(np.uint32)
        y0 = (y0 + ks[(i + 2) % 3] + np.uint32(i + 1)).astype(np.uint32)
    return x0, y0


def _bits(seed, n):
    k1 = np.uint32(np.uint64(seed) >> np.uint64(32))
    k2 = np.uint32(np.uint64(seed) & np.uint64(0xFFFFFFFF))
    idx = np.arange(n, dtype=np.uint64)
    return _threefry2x32(k1, k2, (idx >> np.uint64(32)).astype(np.uint32),
                         (idx & np.uint64(0xFFFFFFFF)).astype(np.uint32))


def uniform_f32(seed, n):
    b1, b2 = _bits(seed, n)
    fb = ((b1 ^ b2).astype(np.uint32) >> np.uint32(9)) | np.uint32(0x3F800000)
    return fb.view(np.float32) - np.float32(1.0)


def uniform_f64(seed, n):
    b1, b2 = _bits(seed, n)
    bits64 = (b1.astype(np.uint64) << np.uint64(32)) | b2.astype(np.uint64)
    fb = (bits64 >> np.uint64(12)) | np.uint64(0x3FF0000000000000)
    return fb.view(np.float64) - np.float64(1.0)


class Buf:
    __slots__ = ("name", "writer", "readers")

    def __init__(self, name):
        self.name = name
        self.writer = None
        self.readers = []


class Prog:
    def __init__(self):
        self.items = {e: [] for e in ENGS}
        self.val = {}
        self.dma_chain = {}

    def op(self, eng, fn, reads=(), writes=(), writes_acc=(), sem=None, amt=1):
        s = sem if sem else "P_" + eng
        if s not in self.val:
            self.val[s] = 0
        waits = {}

        def addw(tok):
            if tok is not None and waits.get(tok[0], 0) < tok[1]:
                waits[tok[0]] = tok[1]

        for b in reads:
            addw(b.writer)
        for b in writes:
            addw(b.writer)
            for t in b.readers:
                addw(t)
        self.val[s] += amt
        tok = (s, self.val[s])
        self.items[eng].append((waits, fn, (s, amt)))
        for b in reads:
            b.readers.append(tok)
        for b in writes:
            b.writer = tok
            b.readers = []
        for b in writes_acc:
            b.writer = tok
        return tok

    def dma(self, eng, out_ap, in_ap, reads=(), writes=(), writes_acc=(),
            sem="d_gen", accum=None, slow=False):
        if accum is None:
            fn = (lambda e, o=out_ap, i=in_ap, sl=slow:
                  e.dma_start(out=o, in_=i, allow_slow_non_contiguous=sl))
        else:
            fn = (lambda e, o=out_ap, i=in_ap, a=accum, sl=slow:
                  e.dma_start(out=o, in_=i, accum_op=a,
                              allow_slow_non_contiguous=sl))
        ch = self.dma_chain.get(sem)
        if ch is None:
            ch = Buf("chain_" + sem)
            self.dma_chain[sem] = ch
        tok = self.op(eng, fn, list(reads) + [ch], writes, writes_acc,
                      sem=sem, amt=16)
        ch.writer = tok
        ch.readers = []
        return tok

    def wait_only(self, eng, bufs):
        waits = {}
        for b in bufs:
            if b.writer is not None:
                sm, v = b.writer
                if waits.get(sm, 0) < v:
                    waits[sm] = v
                if sm not in self.val:
                    self.val[sm] = v
        self.items[eng].append((waits, None, None))


def _emit(eng_obj, items, handles, waited):
    for waits, fn, inc in items:
        for sm in sorted(waits):
            v = waits[sm]
            if waited.get(sm, 0) >= v:
                continue
            eng_obj.wait_ge(handles[sm], v)
            waited[sm] = v
        if fn is None:
            continue
        inst = fn(eng_obj)
        if inc is not None:
            inst.then_inc(handles[inc[0]], inc[1])


def build_nc(ncores, tpc, dbg=False, trunc=None):
    attn_lvl = 9
    if trunc is not None and trunc.startswith("a"):
        attn_lvl = int(trunc[1:])
        trunc = "t7"
    NT = tpc // TILE
    NCHUNK = NT * NCH_T
    TB = TILE // 128
    SMW = (tpc // 128) * E8
    GW = ncores * SMW
    NIDX = GW // E8
    KSEL = ncores * tpc // E8

    nc = bass.Bass("TRN2", target_bir_lowering=False)
    p = Prog()

    class _Done(Exception):
        pass

    def ck(stage):
        if trunc == stage:
            raise _Done()

    if trunc == "noacc":
        pass

    def din(name, shape):
        return nc.dram_tensor(name, list(shape), F32, kind="ExternalInput").ap()

    x_tm = din("x_tm", (tpc, D))
    x_fm = din("x_fm", (128, PC * tpc))
    x_bnd = din("x_bnd", (1, D))
    qw4 = din("qw4", (PC, 128, PC, 128))
    kw4 = din("kw4", (PC, 128, PC, 128))
    ow4 = din("ow4", (PC, 128, PC, 128))
    vw = din("vw", (D, D))
    gw4 = din("gw4", (NF, 128, PC, 128))
    uw4 = din("uw4", (NF, 128, PC, 128))
    dw4 = din("dw4", (PC, 128, NF, 128))
    m2w4 = din("m2w4", (4, PC, 128, 128))
    maw1 = din("maw1", (128, PC * 128))
    tdw1 = din("tdw1", (128, PC * 64))
    tdw2 = din("tdw2", (128, 1024))
    rw = din("rw", (128, PC * E8))
    cols_d = din("cols", (128, 16 * PC))
    bgu = din("bgu", (128, 2 * NF))
    vbrow = din("vbrow", (1, D))
    rbrow = din("rbrow", (1, E8))
    ident = din("ident", (128, 128))
    mask8d = din("mask8", (64, 512))
    onescd = din("onesc", (128, 1))
    onesrd = din("onesr", (1, 128))
    incl2 = din("incl2", (128, 16))
    u3 = din("u3", (128, 3 * SMW))

    y_fm = nc.dram_tensor("y_fm", [128, PC * tpc], F32, kind="ExternalOutput").ap()

    qt_scr = nc.dram_tensor("qt_scr", [128, PC * tpc], F32).ap()
    o_scr = nc.dram_tensor("o_scr", [128, PC * tpc], F32).ap()
    ffn_scr = nc.dram_tensor("ffn_scr", [128, PC * tpc], F32).ap()
    bnd_scr = nc.dram_tensor("bnd_scr", [1, D], F32).ap()
    if dbg:
        dbg_wlog = nc.dram_tensor("dbg_wlog", [128, PC * tpc], F32).ap()
        dbg_xw = nc.dram_tensor("dbg_xw", [128, PC * tpc], F32).ap()
        dbg_oc = nc.dram_tensor("dbg_oc", [128, PC * tpc], F32).ap()
        dbg_y = nc.dram_tensor("dbg_y", [128, PC * tpc], F32).ap()
        d_dbg = Buf("d_dbg")
    ccsm_i = nc.dram_tensor("ccsm_i", [128, SMW], F32).ap()
    ccsm_o = nc.dram_tensor("ccsm_o", [128 * ncores, SMW], F32).ap()
    ccsu_i = nc.dram_tensor("ccsu_i", [128, 1040], F32).ap()
    ccsu_o = nc.dram_tensor("ccsu_o", [128 * ncores, 1040], F32).ap()

    d_qt = Buf("d_qt"); d_o = Buf("d_o"); d_ffn = Buf("d_ffn")
    d_bnd = Buf("d_bnd"); d_y = Buf("d_y")
    d_ci = Buf("d_ci"); d_co = Buf("d_co")
    d_si = Buf("d_si"); d_so = Buf("d_so")

    from contextlib import ExitStack
    with ExitStack() as _st:
        BB0 = _st.enter_context(nc.sbuf_tensor("BB0", [128, PC * TILE], F32))
        BB1 = _st.enter_context(nc.sbuf_tensor("BB1", [128, PC * TILE], F32))
        BB2 = _st.enter_context(nc.sbuf_tensor("BB2", [128, PC * TILE], F32))
        BB3 = _st.enter_context(nc.sbuf_tensor("BB3", [128, PC * TILE], F32))
        BB4 = _st.enter_context(nc.sbuf_tensor("BB4", [128, PC * TILE], F32))
        BB5 = _st.enter_context(nc.sbuf_tensor("BB5", [128, PC * TILE], F32))
        BB6 = _st.enter_context(nc.sbuf_tensor("BB6", [128, PC * TILE], F32))
        RG0 = _st.enter_context(nc.sbuf_tensor("RG0", [128, 2048], F32))
        RG1 = _st.enter_context(nc.sbuf_tensor("RG1", [128, 2048], F32))
        RG2 = _st.enter_context(nc.sbuf_tensor("RG2", [128, 2048], F32))
        RG3 = _st.enter_context(nc.sbuf_tensor("RG3", [128, 2048], F32))
        S_sb = _st.enter_context(nc.sbuf_tensor("S_sb", [128, H * 32], F32))
        Sin_sb = _st.enter_context(nc.sbuf_tensor("Sin_sb", [128, H * 32], F32))
        SC_sb = _st.enter_context(nc.sbuf_tensor("SC_sb", [128, H * 32], F32))
        rw_sb = _st.enter_context(nc.sbuf_tensor("rw_sb", [128, PC * E8], F32))
        vbbc_sb = _st.enter_context(nc.sbuf_tensor("vbbc_sb", [128, D], F32))
        rbbc_sb = _st.enter_context(nc.sbuf_tensor("rbbc_sb", [128, E8], F32))
        cols_sb = _st.enter_context(nc.sbuf_tensor("cols_sb", [128, 16 * PC], F32))
        bgu_sb = _st.enter_context(nc.sbuf_tensor("bgu_sb", [128, 2 * NF], F32))
        id_sb = _st.enter_context(nc.sbuf_tensor("id_sb", [128, 128], F32))
        mask8_sb = _st.enter_context(nc.sbuf_tensor("mask8_sb", [64, 512], F32))
        ones_sb = _st.enter_context(nc.sbuf_tensor("ones_sb", [128, 1], F32))
        onesr_sb = _st.enter_context(nc.sbuf_tensor("onesr_sb", [1, 128], F32))
        incl_sb = _st.enter_context(nc.sbuf_tensor("incl_sb", [128, 16], F32))
        gum_sb = _st.enter_context(nc.sbuf_tensor("gum_sb", [128, SMW], F32))
        smOwn = _st.enter_context(nc.sbuf_tensor("smOwn", [128, SMW], F32))
        flag_sb = _st.enter_context(nc.sbuf_tensor("flag_sb", [128, 1], F32))
        qb8_sb = _st.enter_context(nc.sbuf_tensor("qb8_sb", [128, PC], F32))
        Bbase = _st.enter_context(nc.sbuf_tensor("Bbase", [128, (NCHUNK + 1) * PC], F32))
        Bcarry = _st.enter_context(nc.sbuf_tensor("Bcarry", [128, PC], F32))
        Dcols = _st.enter_context(nc.sbuf_tensor("Dcols", [128, NCH_T * PC], F32))
        eb_sb = _st.enter_context(nc.sbuf_tensor("eb_sb", [128, PC], F32))
        xnlast = _st.enter_context(nc.sbuf_tensor("xnlast", [128, PC], F32))
        tanh_sb = _st.enter_context(nc.sbuf_tensor("tanh_sb", [128, TILE], F32))
        wtd_sb = _st.enter_context(nc.sbuf_tensor("wtd_sb", [128, TILE], F32))
        statc = _st.enter_context(nc.sbuf_tensor("statc", [128, 24], F32))
        wgt_sb = _st.enter_context(nc.sbuf_tensor("wgt_sb", [128, tpc // 128], F32))
        wgtbc_sb = _st.enter_context(nc.sbuf_tensor("wgtbc_sb", [128, 2 * TILE], F32))
        ones128 = _st.enter_context(nc.sbuf_tensor("ones128", [128, 128], F32))
        bis8 = _st.enter_context(nc.sbuf_tensor("bis8", [128, 48], F32))
        dsl_sb = _st.enter_context(nc.sbuf_tensor("dsl_sb", [128, PC], F32))
        cstc = _st.enter_context(nc.sbuf_tensor("cstc", [128, 2], F32))
        ROWA = _st.enter_context(nc.sbuf_tensor("ROWA", [1, 2048], F32))
        ROWB = _st.enter_context(nc.sbuf_tensor("ROWB", [1, 2048], F32))
        PB0 = _st.enter_context(nc.psum_tensor("PB0", [128, 512], F32))
        PB1 = _st.enter_context(nc.psum_tensor("PB1", [128, 512], F32))
        PB2 = _st.enter_context(nc.psum_tensor("PB2", [128, 512], F32))
        PB3 = _st.enter_context(nc.psum_tensor("PB3", [128, 512], F32))
        PB4 = _st.enter_context(nc.psum_tensor("PB4", [128, 512], F32))
        PB5 = _st.enter_context(nc.psum_tensor("PB5", [128, 512], F32))
        PB6 = _st.enter_context(nc.psum_tensor("PB6", [128, 512], F32))
        PB7 = _st.enter_context(nc.psum_tensor("PB7", [128, 512], F32))
        block = _st.enter_context(nc.Block())
        smAll = BB0[:, 0:GW]
        bisc = BB0[:, 1024:1024 + GW]
        midbc = BB0[:, 2048:2048 + GW]
        u3_sb = BB2[:, 0:3 * SMW]
        Ur_sb = BB6[:, 0:1040]
        wgtrow = ROWB
        vrow_sb = ROWA
        bndrow = ROWB
        abrow = ROWA
        PB = [PB0, PB1, PB2, PB3, PB4, PB5, PB6, PB7]
        pb = [Buf("PB%d" % i) for i in range(8)]
        BBT = [BB0, BB1, BB2, BB3, BB4, BB5, BB6]
        bb = [Buf("BB%d" % i) for i in range(7)]
        RGT = [RG0, RG1, RG2, RG3]
        rg = [Buf("RG%d" % i) for i in range(4)]

        b_S = Buf("S"); b_Sin = Buf("Sin"); b_SC = Buf("SC")
        b_smAll = bb[0]; b_bisc = bb[0]; b_midbc = bb[0]
        b_Ur = bb[6]
        b_rw = Buf("rw"); b_vbbc = Buf("vbbc")
        b_rbbc = Buf("rbbc"); b_cols = Buf("cols"); b_bgu = Buf("bgu")
        b_id = Buf("id"); b_mask8 = Buf("mask8"); b_ones = Buf("ones")
        b_onesr = Buf("onesr"); b_incl = Buf("incl"); b_u3 = bb[2]
        b_gum = Buf("gum"); b_smOwn = Buf("smOwn"); b_flag = Buf("flag")
        b_qb8 = Buf("qb8"); b_Bb = Buf("Bbase"); b_Bc = Buf("Bcarry")
        b_Dc = Buf("Dcols"); b_eb = Buf("eb"); b_xnl = Buf("xnlast")
        b_tanh = Buf("tanh"); b_wtd = Buf("wtd"); b_st = Buf("statc")
        b_wgt = Buf("wgt"); b_wgb = Buf("wgtbc")
        b_bis = Buf("bis8"); b_vr = Buf("rowa"); b_bnd = Buf("rowb")
        b_ab = b_vr; b_wgr = b_bnd; b_dsl = Buf("dsl")

        CL = {"maa_x": 0, "maa_r": 1, "maa_k": 2, "maa_v": 3, "maa_w": 4,
              "tdec": 5, "q_b": 6, "k_b": 7, "ln_g": 8, "ln_b": 9, "b_dn": 10}

        def col(nm, pcc):
            i = CL[nm] * PC + pcc
            return cols_sb[:, i:i + 1]

        wring = [0]
        held = set()

        def rn():
            for _ in range(8):
                i = wring[0] % 4
                wring[0] += 1
                if i not in held:
                    return i
            raise RuntimeError("ring exhausted")

        def v3(i, b=TILE):
            return BBT[i][:, :].rearrange("p (a c) -> p a c", c=b)

        Bb3 = Bbase[:, :].rearrange("p (m c) -> p m c", c=PC)

        try:
            LD = "d_c"
            p.dma("sync", cols_sb[:, :], cols_d[:, :], writes=[b_cols], sem=LD)
            p.dma("sync", bgu_sb[:, :], bgu[:, :], writes=[b_bgu], sem=LD)
            p.dma("sync", id_sb[:, :], ident[:, :], writes=[b_id], sem=LD)
            p.dma("sync", mask8_sb[:, :], mask8d[:, :], writes=[b_mask8], sem=LD)
            p.dma("sync", ones_sb[:, :], onescd[:, :], writes=[b_ones], sem=LD)
            p.dma("sync", onesr_sb[:, :], onesrd[:, :], writes=[b_onesr], sem=LD)
            p.dma("sync", incl_sb[:, :], incl2[:, :], writes=[b_incl], sem=LD)
            p.dma("sync", u3_sb[:, :], u3[:, :], writes=[b_u3], sem=LD)
            p.dma("sync", rw_sb[:, :], rw[:, :], writes=[b_rw], sem=LD)
            p.dma("sync", bndrow[:, :], x_bnd[:, :], writes=[b_bnd], sem=LD)
            p.dma("sync", vrow_sb[:, :], vbrow[:, :], writes=[b_vr], sem=LD)
            p.dma("sync", statc[0:1, 0:E8], rbrow[:, :], writes=[b_st], sem=LD)

            b_cst = Buf("cst")
            p.op("vector", lambda e: e.memset(cstc[:, 0:1], REPS), writes=[b_cst])
            p.op("vector", lambda e: e.memset(cstc[:, 1:2], LN_EPS), writes=[b_cst])
            b_o128 = Buf("o128")
            p.op("vector", lambda e: e.memset(ones128[:, :], 1.0), writes=[b_o128])

            p.op("vector", lambda e: e.memset(flag_sb[:, :], 1.0), writes=[b_flag])
            p.op("vector", lambda e: e.tensor_scalar(
                flag_sb[:, :], flag_sb[:, :], 2.0 ** -30, None, OP.add),
                writes=[b_flag])
            p.op("vector", lambda e: e.tensor_scalar(
                flag_sb[:, :], flag_sb[:, :], -1.0, 2.0 ** 30, OP.add, OP.mult),
                writes=[b_flag])
            p.op("vector", lambda e: e.tensor_scalar(
                qb8_sb[:, :], cols_sb[:, CL["q_b"] * PC:(CL["q_b"] + 1) * PC],
                0.125, None, OP.mult), reads=[b_cols], writes=[b_qb8])
            p.op("vector", lambda e: e.scalar_tensor_tensor(
                u3_sb[:, 0:SMW], u3_sb[:, SMW:2 * SMW], flag_sb[:, 0:1],
                u3_sb[:, 0:SMW], OP.mult, OP.add), reads=[b_flag], writes=[b_u3])
            p.op("vector", lambda e: e.scalar_tensor_tensor(
                u3_sb[:, 0:SMW], u3_sb[:, 2 * SMW:3 * SMW], flag_sb[:, 0:1],
                u3_sb[:, 0:SMW], OP.mult, OP.add), reads=[b_flag], writes=[b_u3])
            p.op("scalar", lambda e: e.activation(
                gum_sb[:, :], u3_sb[:, 0:SMW], AF.Ln, bias=cstc[:, 0:1]),
                reads=[b_u3], writes=[b_gum])
            p.op("scalar", lambda e: e.activation(
                gum_sb[:, :], gum_sb[:, :], AF.Ln, bias=cstc[:, 0:1], scale=-1.0),
                writes=[b_gum])
            for i in range(4):
                p.op("tensor", lambda e, i=i: e.matmul(
                    PB0[:, 0:512], onesr_sb[0:1, :], vrow_sb[0:1, i * 512:(i + 1) * 512],
                    start=True, stop=True), reads=[b_vr, b_onesr], writes=[pb[0]])
                p.op("vector", lambda e, i=i: e.tensor_copy(
                    vbbc_sb[:, i * 512:(i + 1) * 512], PB0[:, 0:512]),
                    reads=[pb[0]], writes=[b_vbbc])
            p.op("tensor", lambda e: e.matmul(
                PB1[:, 0:E8], onesr_sb[0:1, :], statc[0:1, 0:E8],
                start=True, stop=True), reads=[b_st, b_onesr], writes=[pb[1]])
            p.op("vector", lambda e: e.tensor_copy(rbbc_sb[:, :], PB1[:, 0:E8]),
                 reads=[pb[1]], writes=[b_rbbc])

            p.op("vector", lambda e: e.reduce_sum(
                statc[0:1, 16:17], bndrow[0:1, :], axis=AX.X),
                reads=[b_bnd], writes=[b_st])
            p.op("scalar", lambda e: e.activation(
                BBT[1][0:1, 0:D], bndrow[0:1, 0:D],
                AF.Square, accum_out=statc[0:1, 17:18]),
                reads=[b_bnd], writes=[bb[1], b_st])
            p.op("vector", lambda e: e.tensor_scalar(
                statc[0:1, 16:18], statc[0:1, 16:18], 1.0 / D, None, OP.mult),
                writes=[b_st])
            p.op("scalar", lambda e: e.activation(
                statc[0:1, 18:19], statc[0:1, 16:17], AF.Square), reads=[b_st],
                writes=[b_st])
            p.op("vector", lambda e: e.tensor_sub(
                statc[0:1, 19:20], statc[0:1, 17:18], statc[0:1, 18:19]),
                writes=[b_st])
            p.op("scalar", lambda e: e.activation(
                statc[0:1, 20:21], statc[0:1, 19:20], AF.Sqrt, bias=cstc[0:1, 1:2]),
                writes=[b_st])
            p.op("vector", lambda e: e.reciprocal(
                statc[0:1, 21:22], statc[0:1, 20:21]), writes=[b_st])
            p.op("vector", lambda e: e.tensor_scalar(
                bndrow[0:1, :], bndrow[0:1, :], statc[0:1, 16:17],
                statc[0:1, 21:22], OP.subtract, OP.mult),
                reads=[b_st], writes=[b_bnd])
            for pcc in range(PC):
                p.op("tensor", lambda e, pcc=pcc: e.matmul(
                    PB2[:, pcc:pcc + 1], bndrow[0:1, pcc * 128:(pcc + 1) * 128],
                    ones_sb[0:1, 0:1], start=True, stop=True),
                    reads=[b_bnd, b_ones], writes=[pb[2]])
            p.op("vector", lambda e: e.tensor_copy(
                xnlast[:, :], PB2[:, 0:PC]), reads=[pb[2]], writes=[b_xnl])

            p.op("vector", lambda e: e.memset(S_sb[:, :], 0.0), writes=[b_S])
            p.op("vector", lambda e: e.memset(Bcarry[:, :], 0.0), writes=[b_Bc])
            p.op("vector", lambda e: e.memset(Bbase[:, 0:PC], 0.0), writes=[b_Bb])

            ck("pre")
            def mix_into_bb2(mname, r, dst_i=2):
                for pcc in range(PC):
                    ri = rn()
                    bank = pcc % 2
                    p.dma("sync", RGT[ri][:, 0:128], m2w4[r, pcc],
                          writes=[rg[ri]], sem="d_w%d" % ri)
                    p.op("tensor", lambda e, ri=ri, bank=bank: e.matmul(
                        PB[bank][:, 0:TILE], RGT[ri][:, 0:128], tanh_sb[:, :],
                        start=True, stop=True),
                        reads=[rg[ri], b_tanh], writes=[pb[bank]])
                    p.op("vector", lambda e, pcc=pcc, bank=bank, mname=mname,
                         dst_i=dst_i: e.scalar_tensor_tensor(
                             BBT[dst_i][:, pcc * TILE:(pcc + 1) * TILE],
                             PB[bank][:, 0:TILE], col(mname, pcc),
                             BBT[1][:, pcc * TILE:(pcc + 1) * TILE],
                             OP.add, OP.mult),
                         reads=[pb[bank], b_cols, bb[1]], writes=[bb[dst_i]])
                    p.op("vector", lambda e, pcc=pcc, dst_i=dst_i: e.tensor_add(
                        BBT[dst_i][:, pcc * TILE:(pcc + 1) * TILE],
                        BBT[dst_i][:, pcc * TILE:(pcc + 1) * TILE],
                        BBT[0][:, pcc * TILE:(pcc + 1) * TILE]),
                        reads=[bb[0]], writes=[bb[dst_i]])

            def proj_2048(w4, dst_i, evac):
                for do in range(PC):
                    ri = rn()
                    bank = do % 2
                    p.dma("sync", RGT[ri][:, :],
                          w4[do].rearrange("p a b -> p (a b)"),
                          writes=[rg[ri]], sem="d_w%d" % ri)

                    def mmg(e, ri=ri, bank=bank):
                        last = None
                        for k in range(PC):
                            last = e.matmul(PB[bank][:, 0:TILE],
                                            RGT[ri][:, k * 128:(k + 1) * 128],
                                            BBT[2][:, k * TILE:(k + 1) * TILE],
                                            start=(k == 0), stop=(k == PC - 1))
                        return last
                    p.op("tensor", mmg, reads=[rg[ri], bb[2]], writes=[pb[bank]])
                    evac(do, bank)

            def bis_iter():
                p.op("vector", lambda e: e.tensor_tensor(
                    bisc[:, :], smAll[:, :], midbc[:, :], OP.is_ge),
                    reads=[b_smAll, b_midbc], writes=[b_bisc])
                p.op("vector", lambda e: e.tensor_reduce(
                    bis8[:, 24:32].rearrange("p (a c) -> p a c", c=1),
                    bisc[:, :].rearrange("p (i e) -> p i e", e=E8).transpose([0, 2, 1]),
                    AX.X, OP.add), reads=[b_bisc], writes=[b_bis])
                p.op("tensor", lambda e: e.matmul(
                    PB3[:, 0:E8], ones128[:, :], bis8[:, 24:32],
                    start=True, stop=True), reads=[b_bis, b_o128], writes=[pb[3]])
                p.op("vector", lambda e: e.tensor_copy(
                    bis8[:, 32:40], PB3[:, 0:E8]), reads=[pb[3]], writes=[b_bis])
                p.op("vector", lambda e: e.tensor_scalar(
                    bis8[:, 32:40], bis8[:, 32:40], float(KSEL), None, OP.is_ge),
                    writes=[b_bis])
                p.op("vector", lambda e: e.tensor_sub(
                    bis8[:, 40:48], bis8[:, 16:24], bis8[:, 0:8]), writes=[b_bis])
                p.op("vector", lambda e: e.tensor_mul(
                    bis8[:, 40:48], bis8[:, 40:48], bis8[:, 32:40]), writes=[b_bis])
                p.op("vector", lambda e: e.tensor_add(
                    bis8[:, 0:8], bis8[:, 0:8], bis8[:, 40:48]), writes=[b_bis])
                p.op("vector", lambda e: e.tensor_sub(
                    bis8[:, 40:48], bis8[:, 16:24], bis8[:, 8:16]), writes=[b_bis])
                p.op("vector", lambda e: e.tensor_scalar(
                    bis8[:, 32:40], bis8[:, 32:40], -1.0, 1.0, OP.mult, OP.add),
                    writes=[b_bis])
                p.op("vector", lambda e: e.tensor_mul(
                    bis8[:, 40:48], bis8[:, 40:48], bis8[:, 32:40]), writes=[b_bis])
                p.op("vector", lambda e: e.tensor_add(
                    bis8[:, 8:16], bis8[:, 8:16], bis8[:, 40:48]), writes=[b_bis])
                p.op("vector", lambda e: e.tensor_add(
                    bis8[:, 16:24], bis8[:, 0:8], bis8[:, 8:16]), writes=[b_bis])
                p.op("vector", lambda e: e.tensor_scalar(
                    bis8[:, 16:24], bis8[:, 16:24], 0.5, None, OP.mult),
                    writes=[b_bis])
                p.op("vector", lambda e: e.tensor_copy(
                    midbc[:, :].rearrange("p (i e) -> p i e", e=E8),
                    bis8[:, 16:24].unsqueeze(1).broadcast_to([128, GW // E8, E8])),
                    reads=[b_bis], writes=[b_midbc])

            def stage_T12(t):
                tok0 = t * TILE

                for tb in range(TB):
                    dst = BBT[2][:, tb * D:(tb + 1) * D]
                    p.dma("gpsimd", dst,
                          x_tm[tok0 + tb * 128: tok0 + (tb + 1) * 128, :],
                          writes=[bb[2]], sem="d_x%d" % tb)
                    sc = 8 * tb
                    p.op("vector", lambda e, dst=dst, sc=sc: e.reduce_sum(
                        statc[:, sc:sc + 1], dst, axis=AX.X),
                        reads=[bb[2]], writes=[b_st])
                    p.op("scalar", lambda e, dst=dst, sc=sc, tb=tb: e.activation(
                        BBT[1][:, tb * D:(tb + 1) * D], dst, AF.Square,
                        accum_out=statc[:, sc + 1:sc + 2]),
                        reads=[bb[2]], writes=[bb[1], b_st])
                    p.op("vector", lambda e, sc=sc: e.tensor_scalar(
                        statc[:, sc + 2:sc + 4], statc[:, sc:sc + 2], 1.0 / D,
                        None, OP.mult), writes=[b_st])
                    p.op("scalar", lambda e, sc=sc: e.activation(
                        statc[:, sc + 4:sc + 5], statc[:, sc + 2:sc + 3],
                        AF.Square), writes=[b_st])
                    p.op("vector", lambda e, sc=sc: e.tensor_sub(
                        statc[:, sc + 5:sc + 6], statc[:, sc + 3:sc + 4],
                        statc[:, sc + 4:sc + 5]), writes=[b_st])
                    p.op("scalar", lambda e, sc=sc: e.activation(
                        statc[:, sc + 6:sc + 7], statc[:, sc + 5:sc + 6], AF.Sqrt,
                        bias=cstc[:, 1:2]), writes=[b_st])
                    p.op("vector", lambda e, sc=sc: e.reciprocal(
                        statc[:, sc + 7:sc + 8], statc[:, sc + 6:sc + 7]),
                        writes=[b_st])
                    p.op("vector", lambda e, dst=dst, sc=sc: e.tensor_scalar(
                        dst, dst, statc[:, sc + 2:sc + 3], statc[:, sc + 7:sc + 8],
                        OP.subtract, OP.mult), reads=[b_st], writes=[bb[2]])


            def stage_T34(t):
                tok0 = t * TILE
                for tb in range(TB):
                    for g in range(4):
                        def tg(e, tb=tb, g=g):
                            last = None
                            for j in range(4):
                                pcx = g * 4 + j
                                last = e.transpose(
                                    PB6[:, j * 128:(j + 1) * 128],
                                    BBT[2][:, tb * D + pcx * 128: tb * D + (pcx + 1) * 128],
                                    id_sb[:, :])
                            return last
                        p.op("tensor", tg, reads=[bb[2], b_id], writes=[pb[6]])
                        p.op("vector", lambda e, tb=tb, g=g: e.tensor_copy(
                            v3(0)[:, g * 4:(g + 1) * 4, tb * 128:(tb + 1) * 128],
                            PB6[:, 0:512].rearrange("p (a c) -> p a c", c=128)),
                            reads=[pb[6]], writes=[bb[0]])

                p.op("vector", lambda e: e.tensor_copy(
                    statc[:, 16:16 + PC] if False else dsl_sb[:, :],
                    v3(0)[:, :, TILE - 1:TILE].rearrange("p a c -> p (a c)")),
                    reads=[bb[0]], writes=[b_dsl])
                p.op("vector", lambda e: e.tensor_sub(
                    v3(1)[:, :, 1:TILE], v3(0)[:, :, 0:TILE - 1],
                    v3(0)[:, :, 1:TILE]), reads=[bb[0]], writes=[bb[1]])
                p.op("vector", lambda e: e.tensor_sub(
                    v3(1)[:, :, 0:1], xnlast[:, :].unsqueeze(2),
                    v3(0)[:, :, 0:1]), reads=[bb[0], b_xnl], writes=[bb[1]])
                p.op("vector", lambda e: e.tensor_copy(xnlast[:, :], dsl_sb[:, :]),
                     reads=[b_dsl], writes=[b_xnl])

                ck("t3")
                for tb in range(TB):
                    blk = t * TB + tb

                    def rmm(e, tb=tb):
                        last = None
                        for k in range(PC):
                            last = e.matmul(
                                PB7[:, 0:E8],
                                BBT[0][:, k * TILE + tb * 128: k * TILE + (tb + 1) * 128],
                                rw_sb[:, k * E8:(k + 1) * E8],
                                start=(k == 0), stop=(k == PC - 1))
                        return last
                    p.op("tensor", rmm, reads=[bb[0], b_rw], writes=[pb[7]])
                    sm = smOwn[:, blk * E8:(blk + 1) * E8]
                    p.op("vector", lambda e, sm=sm: e.tensor_add(
                        sm, PB7[:, 0:E8], rbbc_sb[:, :]),
                        reads=[pb[7], b_rbbc], writes=[b_smOwn])
                    p.op("vector", lambda e, sm=sm, blk=blk: e.tensor_sub(
                        sm, sm, gum_sb[:, blk * E8:(blk + 1) * E8]),
                        reads=[b_gum], writes=[b_smOwn])
                    p.op("vector", lambda e, sm=sm: e.reduce_max(
                        statc[:, 16:17], sm, axis=AX.X, negate=True),
                        reads=[b_smOwn], writes=[b_st])
                    p.op("scalar", lambda e, sm=sm: e.activation(
                        sm, sm, AF.Exp, bias=statc[:, 16:17],
                        accum_out=statc[:, 17:18]),
                        reads=[b_st], writes=[b_smOwn, b_st])
                    p.op("vector", lambda e: e.reciprocal(
                        statc[:, 18:19], statc[:, 17:18]), writes=[b_st])
                    p.op("vector", lambda e, sm=sm: e.tensor_scalar(
                        sm, sm, statc[:, 18:19], None, OP.mult),
                        reads=[b_st], writes=[b_smOwn])


            stage_T12(0)
            stage_T34(0)

            for t in range(NT):
                tok0 = t * TILE
                ck("t4")
                ri_m = rn()
                p.dma("sync", RGT[ri_m][:, :], maw1[:, :], writes=[rg[ri_m]],
                      sem="d_w%d" % ri_m)
                ri_t = rn()
                held.add(ri_t)
                p.dma("sync", RGT[ri_t][:, 0:1024], tdw1[:, :], writes=[rg[ri_t]],
                      sem="d_w%d" % ri_t)
                p.dma("sync", RGT[ri_t][:, 1024:2048], tdw2[:, :],
                      writes_acc=[rg[ri_t]], sem="d_w%d" % ri_t)
                for pcc in range(PC):
                    p.op("vector", lambda e, pcc=pcc: e.scalar_tensor_tensor(
                        BBT[2][:, pcc * TILE:(pcc + 1) * TILE],
                        BBT[1][:, pcc * TILE:(pcc + 1) * TILE],
                        col("maa_x", pcc),
                        BBT[0][:, pcc * TILE:(pcc + 1) * TILE],
                        OP.mult, OP.add),
                        reads=[bb[1], bb[0], b_cols], writes=[bb[2]])

                def mm_tanh(e, ri=ri_m):
                    last = None
                    for k in range(PC):
                        last = e.matmul(PB0[:, 0:TILE],
                                        RGT[ri][:, k * 128:(k + 1) * 128],
                                        BBT[2][:, k * TILE:(k + 1) * TILE],
                                        start=(k == 0), stop=(k == PC - 1))
                    return last
                p.op("tensor", mm_tanh, reads=[bb[2], rg[ri_m]], writes=[pb[0]])
                p.op("scalar", lambda e: e.activation(
                    tanh_sb[:, :], PB0[:, 0:TILE], AF.Tanh),
                    reads=[pb[0]], writes=[b_tanh])

                mix_into_bb2("maa_w", 3)
                if dbg:
                    p.dma("scalar", dbg_xw[:, tok0 * PC:(tok0 + TILE) * PC],
                        BBT[2][:, :],
                        reads=[bb[2]], writes_acc=[d_dbg], sem="d_dbg")

                def mm_wtd(e, ri=ri_t):
                    last = None
                    for k in range(PC):
                        last = e.matmul(PB2[0:64, 0:TILE],
                                        RGT[ri][:, k * 64:(k + 1) * 64],
                                        BBT[2][:, k * TILE:(k + 1) * TILE],
                                        start=(k == 0), stop=(k == PC - 1))
                    return last
                p.op("tensor", mm_wtd, reads=[bb[2], rg[ri_t]], writes=[pb[2]])
                p.op("scalar", lambda e: e.activation(
                    wtd_sb[0:64, :], PB2[0:64, 0:TILE], AF.Tanh),
                    reads=[pb[2]], writes=[b_wtd])
                p.op("vector", lambda e: e.tensor_copy(
                    wtd_sb[64:128, :], wtd_sb[0:64, :]), writes=[b_wtd])
                for pcc in range(PC):
                    bank = pcc % 2
                    rh = (pcc // 8) * 64
                    cc0 = 1024 + (pcc % 8) * 128
                    p.op("tensor", lambda e, bank=bank, rh=rh, cc0=cc0, ri=ri_t:
                         e.matmul(
                             PB[bank][:, 0:TILE],
                             RGT[ri][rh:rh + 64, cc0:cc0 + 128],
                             wtd_sb[rh:rh + 64, :], start=True, stop=True),
                         reads=[rg[ri_t], b_wtd], writes=[pb[bank]])
                    p.op("scalar", lambda e, pcc=pcc, bank=bank: e.activation(
                        BBT[3][:, pcc * TILE:(pcc + 1) * TILE],
                        PB[bank][:, 0:TILE], AF.Exp, bias=col("tdec", pcc)),
                        reads=[pb[bank], b_cols], writes=[bb[3]])
                    p.op("vector", lambda e, pcc=pcc: e.tensor_scalar(
                        BBT[3][:, pcc * TILE:(pcc + 1) * TILE],
                        BBT[3][:, pcc * TILE:(pcc + 1) * TILE],
                        5.0, -1.0, OP.min, OP.mult), writes=[bb[3]])

                if dbg:
                    p.dma("scalar", dbg_wlog[:, tok0 * PC:(tok0 + TILE) * PC],
                        BBT[3][:, :],
                        reads=[bb[3]], writes_acc=[d_dbg], sem="d_dbg")
                held.discard(ri_t)
                mix_into_bb2("maa_v", 2, dst_i=6)
                for k in range(PC):
                    ri = rn()
                    p.dma("sync", RGT[ri][:, :], vw[k * 128:(k + 1) * 128, :],
                          writes=[rg[ri]], sem="d_w%d" % ri)

                    def vmm(e, ri=ri, k=k):
                        last = None
                        for tb in range(TB):
                            for do in range(4):
                                last = e.matmul(
                                    PB[tb * 4 + do][:, 0:512],
                                    BBT[6][:, k * TILE + tb * 128: k * TILE + (tb + 1) * 128],
                                    RGT[ri][:, do * 512:(do + 1) * 512],
                                    start=(k == 0), stop=(k == PC - 1))
                        return last
                    p.op("tensor", vmm, reads=[rg[ri], bb[6]], writes=pb[0:8])
                for tb in range(TB):
                    for do in range(4):
                        p.op("vector", lambda e, tb=tb, do=do: e.tensor_add(
                            BBT[6][:, tb * D + do * 512: tb * D + (do + 1) * 512],
                            PB[tb * 4 + do][:, 0:512],
                            vbbc_sb[:, do * 512:(do + 1) * 512]),
                            reads=[pb[tb * 4 + do], b_vbbc], writes=[bb[6]])

                mix_into_bb2("maa_r", 0)

                def ev_q(do, bank):
                    p.op("scalar", lambda e, do=do, bank=bank: e.activation(
                        BBT[4][:, do * TILE:(do + 1) * TILE], PB[bank][:, 0:TILE],
                        AF.Identity, bias=qb8_sb[:, do:do + 1], scale=0.125),
                        reads=[pb[bank], b_qb8], writes=[bb[4]])
                proj_2048(qw4, 4, ev_q)

                mix_into_bb2("maa_k", 1)

                CH0 = t * NCH_T
                for pcc in range(PC):
                    p.op("vector", lambda e, pcc=pcc: e.tensor_tensor_scan(
                        BBT[1][:, pcc * TILE:(pcc + 1) * TILE],
                        BBT[3][:, pcc * TILE:(pcc + 1) * TILE],
                        BBT[3][:, pcc * TILE:(pcc + 1) * TILE],
                        Bcarry[:, pcc:pcc + 1], OP.add, OP.bypass),
                        reads=[bb[3], b_Bc], writes=[bb[1]])
                p.op("vector", lambda e, CH0=CH0: e.tensor_copy(
                    Bbase[:, CH0 * PC:(CH0 + 1) * PC], Bcarry[:, :]),
                    reads=[b_Bc], writes=[b_Bb])
                for m in range(1, NCH_T):
                    p.op("vector", lambda e, m=m, CH0=CH0: e.tensor_copy(
                        Bbase[:, (CH0 + m) * PC:(CH0 + m + 1) * PC],
                        v3(1)[:, :, m * CH - 1:m * CH].rearrange(
                            "p a c -> p (a c)")),
                        reads=[bb[1]], writes=[b_Bb])
                p.op("vector", lambda e, CH0=CH0: e.tensor_copy(
                    Bbase[:, (CH0 + NCH_T) * PC:(CH0 + NCH_T + 1) * PC],
                    v3(1)[:, :, TILE - 1:TILE].rearrange("p a c -> p (a c)")),
                    reads=[bb[1]], writes=[b_Bb])
                p.op("vector", lambda e, CH0=CH0: e.tensor_copy(
                    Bcarry[:, :],
                    Bbase[:, (CH0 + NCH_T) * PC:(CH0 + NCH_T + 1) * PC]),
                    reads=[b_Bb], writes=[b_Bc])
                p.op("vector", lambda e, CH0=CH0: e.tensor_sub(
                    Dcols[:, :],
                    Bbase[:, (CH0 + 1) * PC:(CH0 + 1 + NCH_T) * PC],
                    Bbase[:, CH0 * PC:(CH0 + NCH_T) * PC]),
                    reads=[b_Bb], writes=[b_Dc])
                p.op("scalar", lambda e: e.activation(
                    Dcols[:, :], Dcols[:, :], AF.Exp), writes=[b_Dc])
                X4 = lambda i: BBT[i][:, :].rearrange(
                    "p (q m c) -> p q m c", m=NCH_T, c=CH)
                for m in range(NCH_T):
                    p.op("scalar", lambda e, m=m, CH0=CH0: e.copy(
                        v3(3)[:, :, m * CH:(m + 1) * CH],
                        Bbase[:, (CH0 + m) * PC:(CH0 + m + 1) * PC]
                        .unsqueeze(2).broadcast_to([128, PC, CH])),
                        reads=[b_Bb], writes=[bb[3]])
                p.op("vector", lambda e: e.tensor_sub(
                    BBT[3][:, :], BBT[1][:, :], BBT[3][:, :]),
                    reads=[bb[1]], writes=[bb[3]])
                p.op("scalar", lambda e: e.activation(
                    BBT[1][:, :], BBT[3][:, :], AF.Exp, scale=-1.0),
                    reads=[bb[3]], writes=[bb[1]])
                p.op("scalar", lambda e: e.activation(
                    BBT[3][:, :], BBT[3][:, :], AF.Exp), writes=[bb[3]])
                p.op("vector", lambda e: e.tensor_mul(
                    BBT[4][:, :], BBT[4][:, :], BBT[3][:, :]),
                    reads=[bb[3]], writes=[bb[4]])
                p.op("scalar", lambda e: e.copy(
                    X4(3)[:, :, :, 1:CH], X4(1)[:, :, :, 0:CH - 1]),
                    reads=[bb[1]], writes=[bb[3]])
                p.op("vector", lambda e: e.memset(
                    X4(3)[:, :, :, 0:1], 1.0), writes=[bb[3]])
                p.op("vector", lambda e: e.tensor_sub(
                    BBT[3][:, :], BBT[1][:, :], BBT[3][:, :]),
                    reads=[bb[1]], writes=[bb[3]])
                p.dma("scalar", qt_scr[:, tok0 * PC:(tok0 + TILE) * PC],
                      BBT[4][:, :], reads=[bb[4]], writes_acc=[d_qt], sem="d_qt")

                def ev_k(do, bank):
                    p.op("scalar", lambda e, do=do, bank=bank: e.activation(
                        BBT[5][:, do * TILE:(do + 1) * TILE], PB[bank][:, 0:TILE],
                        AF.Identity, bias=col("k_b", do)),
                        reads=[pb[bank], b_cols], writes=[bb[5]])
                    p.op("vector", lambda e, do=do: e.tensor_mul(
                        BBT[5][:, do * TILE:(do + 1) * TILE],
                        BBT[5][:, do * TILE:(do + 1) * TILE],
                        BBT[3][:, do * TILE:(do + 1) * TILE]),
                        reads=[bb[3]], writes=[bb[5]])
                proj_2048(kw4, 5, ev_k)

                ck("t5")
                ck("t6")
                b_kh = bb[1]
                b_Pt = bb[3]
                b_ost = bb[1]
                b_vsw = bb[3]
                for m in range(NCH_T):
                    mg = CH0 + m
                    par = (m % 2) * 64
                    opar = 64 - par
                    tb = m // 2
                    if m % 2 == 0:
                        p.op("vector", lambda e, tb=tb: e.tensor_copy(
                            BBT[3][0:64, 2048:4096],
                            BBT[6][64:128, tb * D:tb * D + 2048]),
                            reads=[bb[6]], writes=[b_vsw])
                        p.op("vector", lambda e, tb=tb: e.tensor_copy(
                            BBT[3][64:128, 2048:4096],
                            BBT[6][0:64, tb * D:tb * D + 2048]),
                            reads=[bb[6]], writes=[b_vsw])

                    def vsl(h, par=par, tb=tb):
                        hp = (h % 2) * 64
                        if hp == par:
                            return BBT[6][par:par + 64,
                                          tb * D + h * DH: tb * D + (h + 1) * DH]
                        return BBT[3][hp:hp + 64, 2048 + h * DH:2048 + (h + 1) * DH]

                    def vrd(h, par=par):
                        return bb[6] if (h % 2) * 64 == par else b_vsw

                    for g in range(4):
                        def ttg(e, m=m, g=g):
                            last = None
                            for j in range(4):
                                pcx = g * 4 + j
                                last = e.transpose(
                                    PB6[0:64, j * 128:(j + 1) * 128],
                                    BBT[5][:, pcx * TILE + m * CH: pcx * TILE + (m + 1) * CH],
                                    id_sb[:, :])
                            return last
                        p.op("tensor", ttg, reads=[bb[5], b_id], writes=[pb[6]])
                        p.op("vector", lambda e, g=g: e.tensor_copy(
                            BBT[1][0:64, g * 512:(g + 1) * 512],
                            PB6[0:64, 0:512]), reads=[pb[6]], writes=[b_kh])
                        p.op("scalar", lambda e, g=g: e.activation(
                            BBT[1][64:128, g * 512:(g + 1) * 512],
                            PB6[0:64, 0:512], AF.Identity),
                            reads=[pb[6]], writes=[b_kh])

                    def qsl(h, m=m):
                        return BBT[4][(h % 2) * 64:(h % 2) * 64 + 64,
                                      (h // 2) * TILE + m * CH:
                                      (h // 2) * TILE + (m + 1) * CH]

                    def ksl(h, m=m):
                        return BBT[5][(h % 2) * 64:(h % 2) * 64 + 64,
                                      (h // 2) * TILE + m * CH:
                                      (h // 2) * TILE + (m + 1) * CH]

                    for half in range(2 if attn_lvl >= 2 else 0):
                        pbank = 4 + half

                        def pmm(e, half=half, pbank=pbank, qsl=qsl, ksl=ksl):
                            last = None
                            for i in range(16):
                                h = half * 16 + i
                                hp = (h % 2) * 64
                                sl = ((h // 2) % 8) * 64
                                last = e.matmul(
                                    PB[pbank][hp:hp + 64, sl:sl + 64],
                                    ksl(h), qsl(h), start=True, stop=True)
                            return last
                        p.op("tensor", pmm, reads=[bb[4], bb[5]],
                             writes=[pb[pbank]])
                        for hp in (0, 64):
                            p.op("vector", lambda e, half=half, pbank=pbank, hp=hp:
                                 e.tensor_mul(
                                     BBT[3][hp:hp + 64, :]
                                     .rearrange("p (a c) -> p a c", c=CH)
                                     [:, half * 16 + (hp // 64):half * 16 + 16:2, :],
                                     PB[pbank][hp:hp + 64, 0:512]
                                     .rearrange("p (a c) -> p a c", c=CH),
                                     mask8_sb[:, :]
                                     .rearrange("p (a c) -> p a c", c=CH)),
                                 reads=[pb[pbank], b_mask8], writes=[b_Pt])

                    for half in range(2 if attn_lvl >= 3 else 0):
                        obank = half

                        def omm(e, half=half, obank=obank, vsl=vsl, qsl=qsl):
                            last = None
                            for i in range(16):
                                h = half * 16 + i
                                hp = (h % 2) * 64
                                sl = ((h // 2) % 8) * 64
                                last = e.matmul(
                                    PB[obank][hp:hp + 64, sl:sl + 64],
                                    vsl(h),
                                    BBT[3][hp:hp + 64, h * 64:(h + 1) * 64],
                                    start=True, stop=False)
                                last = e.matmul(
                                    PB[obank][hp:hp + 64, sl:sl + 64],
                                    S_sb[hp:hp + 64,
                                         (h // 2) * 64:(h // 2) * 64 + 64],
                                    qsl(h), start=False, stop=True)
                            return last
                        p.op("tensor", omm,
                             reads=[bb[6], b_vsw, b_Pt, b_S, bb[4]],
                             writes=[pb[obank]])

                    for half in range(2 if attn_lvl >= 4 else 0):
                        sbank = 2 + half

                        def smm(e, half=half, sbank=sbank, par=par, vsl=vsl):
                            last = None
                            for i in range(16):
                                h = half * 16 + i
                                hp = (h % 2) * 64
                                pcc = h // 2
                                sl = (pcc % 8) * 64
                                last = e.matmul(
                                    PB[sbank][hp:hp + 64, sl:sl + 64],
                                    BBT[1][hp:hp + 64,
                                           pcc * 128 + hp: pcc * 128 + hp + 64],
                                    vsl(h), start=True, stop=True)
                            return last
                        p.op("tensor", smm, reads=[b_kh, bb[6], b_vsw],
                             writes=[pb[sbank]])

                    if attn_lvl >= 5:
                        for half in range(2):
                            p.op("scalar", lambda e, half=half: e.activation(
                                BBT[1][:, 2048 + half * 512:2048 + half * 512 + 512],
                                PB[half][:, 0:512], AF.Identity),
                                reads=[pb[half]], writes=[b_ost])
                        p.dma("scalar",
                              o_scr[:, (tok0 + m * CH) * PC:(tok0 + (m + 1) * CH) * PC],
                              BBT[1][:, 2048:3072], reads=[b_ost],
                              writes_acc=[d_o], sem="d_o")

                    if attn_lvl >= 4:
                        for half in range(2):
                            p.op("vector", lambda e, half=half: e.tensor_add(
                                S_sb[:, half * 512:(half + 1) * 512],
                                S_sb[:, half * 512:(half + 1) * 512],
                                PB[2 + half][:, 0:512]),
                                reads=[pb[2 + half]], writes=[b_S])
                        p.op("vector", lambda e, m=m: e.tensor_mul(
                            S_sb[:, :].rearrange("p (q c) -> p q c", c=64),
                            S_sb[:, :].rearrange("p (q c) -> p q c", c=64),
                            Dcols[:, m * PC:(m + 1) * PC]
                            .unsqueeze(2).broadcast_to([128, PC, 64])),
                            reads=[b_Dc], writes=[b_S])

                if t == NT - 1:
                    p.dma("gpsimd", ccsm_i[:, :], smOwn[:, :], reads=[b_smOwn],
                          writes=[d_ci], sem="d_cci")
                    if ncores > 1:
                        p.op("gpsimd", lambda e: e.collective_compute(
                            "AllGather", OP.bypass,
                            replica_groups=[list(range(ncores))],
                            ins=[ccsm_i.opt()], outs=[ccsm_o.opt()]),
                            reads=[d_ci], writes=[d_co], sem="cc1")
                    else:
                        p.dma("gpsimd", ccsm_o[0:128, :], ccsm_i[:, :],
                              reads=[d_ci], writes=[d_co], sem="cc1")
                    p.op("scalar", lambda e: e.activation(
                        dsl_sb[:, :], Bcarry[:, :], AF.Exp),
                        reads=[b_Bc], writes=[b_dsl])
                    p.dma("gpsimd", ccsu_i[:, 0:1024], S_sb[:, :], reads=[b_S],
                          writes=[d_si], sem="d_ccs")
                    p.dma("gpsimd", ccsu_i[:, 1024:1040], dsl_sb[:, :],
                          reads=[b_dsl], writes_acc=[d_si], sem="d_ccs")
                    if ncores > 1:
                        p.op("gpsimd", lambda e: e.collective_compute(
                            "AllGather", OP.bypass,
                            replica_groups=[list(range(ncores))],
                            ins=[ccsu_i.opt()], outs=[ccsu_o.opt()]),
                            reads=[d_si], writes=[d_so], sem="cc2")
                    else:
                        p.dma("gpsimd", ccsu_o[0:128, :], ccsu_i[:, :],
                              reads=[d_si], writes=[d_so], sem="cc2")

                if t + 1 < NT:
                    stage_T12(t + 1)
                ck("t7b")
                hhome = [1, 4, 5]

                def hT(c):
                    return BBT[hhome[c // 16]], bb[hhome[c // 16]], (c % 16) * TILE

                for c in range(NF):
                    rg_g = rn()
                    rg_u = rn()
                    p.dma("sync", RGT[rg_g][:, :],
                          gw4[c].rearrange("p a b -> p (a b)"),
                          writes=[rg[rg_g]], sem="d_w%d" % rg_g)
                    p.dma("sync", RGT[rg_u][:, :],
                          uw4[c].rearrange("p a b -> p (a b)"),
                          writes=[rg[rg_u]], sem="d_w%d" % rg_u)

                    def gmm(e, ri=rg_g):
                        last = None
                        for k in range(PC):
                            last = e.matmul(PB2[:, 0:TILE],
                                            RGT[ri][:, k * 128:(k + 1) * 128],
                                            BBT[0][:, k * TILE:(k + 1) * TILE],
                                            start=(k == 0), stop=(k == PC - 1))
                        return last
                    p.op("tensor", gmm, reads=[rg[rg_g], bb[0]], writes=[pb[2]])

                    def umm(e, ri=rg_u):
                        last = None
                        for k in range(PC):
                            last = e.matmul(PB3[:, 0:TILE],
                                            RGT[ri][:, k * 128:(k + 1) * 128],
                                            BBT[0][:, k * TILE:(k + 1) * TILE],
                                            start=(k == 0), stop=(k == PC - 1))
                        return last
                    p.op("tensor", umm, reads=[rg[rg_u], bb[0]], writes=[pb[3]])
                    hten, hbuf, hoff = hT(c)
                    p.op("scalar", lambda e, c=c: e.activation(
                        tanh_sb[:, :], PB2[:, 0:TILE], AF.Silu,
                        bias=bgu_sb[:, c:c + 1]),
                        reads=[pb[2], b_bgu], writes=[b_tanh])
                    p.op("vector", lambda e, c=c, hten=hten, hoff=hoff:
                         e.scalar_tensor_tensor(
                             hten[:, hoff:hoff + TILE], PB3[:, 0:TILE],
                             bgu_sb[:, NF + c:NF + c + 1], tanh_sb[:, :],
                             OP.add, OP.mult),
                         reads=[pb[3], b_bgu, b_tanh], writes=[hbuf])

                parts = [(0, 15), (15, 30), (30, NF)]
                for do in range(PC):
                    bank = do % 2
                    for (c0, c1) in parts:
                        ri = rn()
                        p.dma("sync", RGT[ri][:, 0:(c1 - c0) * 128],
                              dw4[do][:, c0:c1, :].rearrange("p a b -> p (a b)"),
                              writes=[rg[ri]], sem="d_w%d" % ri)

                        def dmm(e, ri=ri, c0=c0, c1=c1, bank=bank):
                            last = None
                            for c in range(c0, c1):
                                hten, _, hoff = hT(c)
                                last = e.matmul(
                                    PB[bank][:, 0:TILE],
                                    RGT[ri][:, (c - c0) * 128:(c - c0 + 1) * 128],
                                    hten[:, hoff:hoff + TILE],
                                    start=(c == 0), stop=(c == NF - 1))
                            return last
                        p.op("tensor", dmm,
                             reads=[rg[ri]] + [bb[i] for i in hhome],
                             writes=[pb[bank]])
                    p.op("vector", lambda e, do=do, bank=bank: e.tensor_scalar(
                        BBT[6][:, do * TILE:(do + 1) * TILE], PB[bank][:, 0:TILE],
                        col("b_dn", do), None, OP.add),
                        reads=[pb[bank], b_cols], writes=[bb[6]])
                p.dma("scalar", ffn_scr[:, tok0 * PC:(tok0 + TILE) * PC],
                      BBT[6][:, :], reads=[bb[6]], writes_acc=[d_ffn], sem="d_ff")

                if t + 1 < NT:
                    stage_T34(t + 1)
            ck("t8")
            p.dma("scalar", BBT[4][:, :], qt_scr[:, 0:TILE * PC],
                  reads=[d_qt], writes=[bb[4]], sem="d_rq")
            p.dma("scalar", BBT[1][:, :], o_scr[:, 0:TILE * PC],
                  reads=[d_o], writes=[bb[1]], sem="d_ro")
            p.dma("scalar", BBT[2][:, :], x_fm[:, 0:TILE * PC],
                  writes=[bb[2]], sem="d_rx")
            for r in range(ncores):
                p.dma("gpsimd", smAll[:, r * SMW:(r + 1) * SMW],
                      ccsm_o[r * 128:(r + 1) * 128, :],
                      reads=[d_co], writes=[b_smAll], sem="d_sml")
            p.op("vector", lambda e: e.memset(Sin_sb[:, :], 0.0), writes=[b_Sin])
            for r in range(ncores - 1):
                if r % 2 == 0:
                    src, bsrc = Ur_sb, b_Ur
                else:
                    src, bsrc = BBT[3], bb[3]
                p.dma("gpsimd", src[:, 0:1040], ccsu_o[r * 128:(r + 1) * 128, :],
                      reads=[d_so], writes=[bsrc], sem="d_url")
                p.op("vector", lambda e, r=r, src=src: e.scalar_tensor_tensor(
                    dsl_sb[:, :], src[:, 1024:1040], incl_sb[:, r:r + 1],
                    incl_sb[:, 8 + r:8 + r + 1].broadcast_to([128, PC]),
                    OP.mult, OP.add), reads=[bsrc, b_incl], writes=[b_dsl])
                p.op("vector", lambda e: e.tensor_mul(
                    Sin_sb[:, :].rearrange("p (q c) -> p q c", c=64),
                    Sin_sb[:, :].rearrange("p (q c) -> p q c", c=64),
                    dsl_sb[:, :].unsqueeze(2).broadcast_to([128, PC, 64])),
                    reads=[b_dsl], writes=[b_Sin])
                p.op("vector", lambda e, r=r, src=src: e.scalar_tensor_tensor(
                    Sin_sb[:, :], src[:, 0:1024], incl_sb[:, r:r + 1],
                    Sin_sb[:, :], OP.mult, OP.add),
                    reads=[bsrc, b_incl], writes=[b_Sin])
            ck("c0")
            p.op("vector", lambda e: e.memset(bis8[:, 0:8], 0.0), writes=[b_bis])
            p.op("vector", lambda e: e.memset(bis8[:, 8:16], 1.0), writes=[b_bis])
            p.op("vector", lambda e: e.memset(bis8[:, 16:24], 0.5), writes=[b_bis])
            p.op("vector", lambda e: e.memset(midbc[:, :], 0.5), writes=[b_midbc])
            bis_done = [0]

            ck("bisinit")

            FOLD0 = max(1, (NT + 1) // 2)
            BIS_PER = -(-BIS_IT // FOLD0)
            pending = list(range(FOLD0))

            def wgt_bc(tok, half):
                for tb in range(TB):
                    p.op("tensor", lambda e, tok=tok, tb=tb, half=half: e.matmul(
                        PB2[:, half * TILE + tb * 128:
                            half * TILE + (tb + 1) * 128],
                        onesr_sb[0:1, :],
                        wgtrow[0:1, tok + tb * 128:tok + (tb + 1) * 128],
                        start=True, stop=True),
                        reads=[b_wgr, b_onesr], writes=[pb[2]])
                p.op("vector", lambda e, half=half: e.tensor_copy(
                    wgtbc_sb[:, half * TILE:(half + 1) * TILE],
                    PB2[:, half * TILE:(half + 1) * TILE]),
                    reads=[pb[2]], writes=[b_wgb])

            def mul_wgt(i, half):
                p.op("vector", lambda e, i=i, half=half: e.tensor_mul(
                    BBT[i][:, :].rearrange("p (a c) -> p a c", c=TILE),
                    BBT[i][:, :].rearrange("p (a c) -> p a c", c=TILE),
                    wgtbc_sb[:, half * TILE:(half + 1) * TILE]
                    .unsqueeze(1).broadcast_to([128, PC, TILE])),
                    reads=[b_wgb], writes=[bb[i]])

            def emit_wgt():
                for blk in range(tpc // 128):
                    p.op("vector", lambda e, blk=blk: e.tensor_tensor(
                        bisc[:, 0:E8], smOwn[:, blk * E8:(blk + 1) * E8],
                        bis8[:, 0:8], OP.is_ge),
                        reads=[b_smOwn, b_bis], writes=[b_bisc])
                    p.op("vector", lambda e, blk=blk: e.tensor_mul(
                        bisc[:, 0:E8], bisc[:, 0:E8],
                        smOwn[:, blk * E8:(blk + 1) * E8]),
                        reads=[b_smOwn], writes=[b_bisc])
                    p.op("vector", lambda e, blk=blk: e.reduce_sum(
                        wgt_sb[:, blk:blk + 1], bisc[:, 0:E8], axis=AX.X),
                        reads=[b_bisc], writes=[b_wgt])
                NBLK = tpc // 128
                p.op("tensor", lambda e: e.transpose(
                    PB7[0:NBLK, 0:128], wgt_sb[:, :], id_sb[:, :]),
                    reads=[b_wgt, b_id], writes=[pb[7]])
                p.op("vector", lambda e: e.tensor_copy(
                    tanh_sb[0:NBLK, 0:128], PB7[0:NBLK, 0:128]),
                    reads=[pb[7]], writes=[b_tanh])
                for blk in range(NBLK):
                    p.op("tensor", lambda e, blk=blk: e.matmul(
                        PB7[0:1, 128:256], id_sb[0:NBLK, blk:blk + 1],
                        tanh_sb[0:NBLK, 0:128], start=True, stop=True),
                        reads=[b_tanh, b_id], writes=[pb[7]])
                    p.op("vector", lambda e, blk=blk: e.tensor_copy(
                        wgtrow[0:1, blk * 128:(blk + 1) * 128],
                        PB7[0:1, 128:256]), reads=[pb[7]], writes=[b_wgr])

            def acc_tile(j):
                tj = j * TILE
                p.dma("scalar", BBT[4][:, :],
                      ffn_scr[:, tj * PC:(tj + TILE) * PC],
                      reads=[d_ffn], writes=[bb[4]], sem="d_rf")
                wgt_bc(tj, 1)
                mul_wgt(4, 1)
                if trunc == "noacc":
                    p.dma("gpsimd", ffn_scr[:, tj * PC:(tj + TILE) * PC],
                          BBT[4][:, :], reads=[bb[4], d_ffn],
                          writes_acc=[d_y], sem="d_ya")
                else:
                    for q in range(4):
                        p.dma("gpsimd",
                              y_fm[:, tj * PC + q * 1024:
                                   tj * PC + (q + 1) * 1024],
                              BBT[4][:, q * 1024:(q + 1) * 1024],
                              reads=[bb[4], d_y], writes_acc=[d_y],
                              sem="d_ya", accum=OP.add)

            def reload_qo(tt):
                t0 = tt * TILE
                p.dma("scalar", BBT[4][:, :],
                      qt_scr[:, t0 * PC:(t0 + TILE) * PC],
                      reads=[d_qt], writes=[bb[4]], sem="d_rq")
                p.dma("scalar", BBT[1][:, :],
                      o_scr[:, t0 * PC:(t0 + TILE) * PC],
                      reads=[d_o], writes=[bb[1]], sem="d_ro")

            def corr_m(tt, m):
                mg = tt * NCH_T + m
                p.op("scalar", lambda e, mg=mg: e.activation(
                    eb_sb[:, :], Bbase[:, mg * PC:(mg + 1) * PC], AF.Exp),
                    reads=[b_Bb], writes=[b_eb])
                p.op("vector", lambda e: e.tensor_mul(
                    SC_sb[:, :].rearrange("p (q c) -> p q c", c=64),
                    Sin_sb[:, :].rearrange("p (q c) -> p q c", c=64),
                    eb_sb[:, :].unsqueeze(2).broadcast_to([128, PC, 64])),
                    reads=[b_Sin, b_eb], writes=[b_SC])
                for half in range(2):
                    def cmm(e, half=half, m=m):
                        last = None
                        for i in range(16):
                            h = half * 16 + i
                            hp = (h % 2) * 64
                            sl = ((h // 2) % 8) * 64
                            last = e.matmul(
                                PB[half][hp:hp + 64, sl:sl + 64],
                                SC_sb[hp:hp + 64,
                                      (h // 2) * 64:(h // 2) * 64 + 64],
                                BBT[4][hp:hp + 64,
                                       (h // 2) * TILE + m * CH:
                                       (h // 2) * TILE + (m + 1) * CH],
                                start=True, stop=True)
                        return last
                    p.op("tensor", cmm, reads=[b_SC, bb[4]],
                         writes=[pb[half]])
                for half in range(2):
                    p.op("vector", lambda e, half=half, m=m: e.tensor_add(
                        BBT[1][:, m * 1024 + half * 512:
                               m * 1024 + half * 512 + 512],
                        BBT[1][:, m * 1024 + half * 512:
                               m * 1024 + half * 512 + 512],
                        PB[half][:, 0:512]),
                        reads=[pb[half]], writes=[bb[1]])

            for t in range(NT):
                tok0 = t * TILE
                if t == 0:
                    for m in range(NCH_T):
                        corr_m(0, m)
                ck("creload")
                ck("corr")
                if t >= FOLD0 and pending:
                    acc_tile(pending.pop(0))
                if t >= FOLD0:
                    p.dma("scalar", BBT[3][:, :],
                          ffn_scr[:, tok0 * PC:(tok0 + TILE) * PC],
                          reads=[d_ffn], writes=[bb[3]], sem="d_rf")
                    wgt_bc(tok0, 0)
                    mul_wgt(3, 0)
                p.op("scalar", lambda e: e.activation(
                    BBT[6][:, :], BBT[1][:, :], AF.Square),
                    reads=[bb[1]], writes=[bb[6]])

                def smm1(e):
                    last = None
                    v1 = BBT[1][:, :].rearrange("p (m q c) -> p m q c", q=PC, c=CH)
                    for k in range(PC):
                        last = e.matmul(PB4[0:1, 0:TILE], ones_sb[:, 0:1],
                                        v1[:, :, k, :],
                                        start=(k == 0), stop=(k == PC - 1))
                    return last
                p.op("tensor", smm1, reads=[bb[1], b_ones], writes=[pb[4]])
                p.op("vector", lambda e: e.tensor_scalar(
                    abrow[0:1, 0:TILE], PB4[0:1, 0:TILE], 1.0 / D, None, OP.mult),
                    reads=[pb[4]], writes=[b_ab])

                def smm2(e):
                    last = None
                    v6 = BBT[6][:, :].rearrange("p (m q c) -> p m q c", q=PC, c=CH)
                    for k in range(PC):
                        last = e.matmul(PB4[0:1, 256:256 + TILE], ones_sb[:, 0:1],
                                        v6[:, :, k, :],
                                        start=(k == 0), stop=(k == PC - 1))
                    return last
                p.op("tensor", smm2, reads=[bb[6], b_ones], writes=[pb[4]])
                p.op("vector", lambda e: e.tensor_scalar(
                    abrow[0:1, TILE:2 * TILE], PB4[0:1, 256:256 + TILE], 1.0 / D,
                    None, OP.mult), reads=[pb[4]], writes=[b_ab])
                p.op("scalar", lambda e: e.activation(
                    abrow[0:1, 2 * TILE:3 * TILE], abrow[0:1, 0:TILE], AF.Square),
                    writes=[b_ab])
                p.op("vector", lambda e: e.tensor_sub(
                    abrow[0:1, TILE:2 * TILE], abrow[0:1, TILE:2 * TILE],
                    abrow[0:1, 2 * TILE:3 * TILE]), writes=[b_ab])
                p.op("scalar", lambda e: e.activation(
                    abrow[0:1, TILE:2 * TILE], abrow[0:1, TILE:2 * TILE], AF.Sqrt,
                    bias=cstc[0:1, 1:2]), writes=[b_ab])
                p.op("vector", lambda e: e.reciprocal(
                    abrow[0:1, TILE:2 * TILE], abrow[0:1, TILE:2 * TILE]),
                    writes=[b_ab])
                p.op("vector", lambda e: e.tensor_mul(
                    abrow[0:1, 2 * TILE:3 * TILE], abrow[0:1, 0:TILE],
                    abrow[0:1, TILE:2 * TILE]), writes=[b_ab])
                p.op("vector", lambda e: e.tensor_scalar(
                    abrow[0:1, 2 * TILE:3 * TILE], abrow[0:1, 2 * TILE:3 * TILE],
                    -1.0, None, OP.mult), writes=[b_ab])
                p.op("tensor", lambda e: e.matmul(
                    PB5[:, 0:TILE], onesr_sb[0:1, :], abrow[0:1, TILE:2 * TILE],
                    start=True, stop=True), reads=[b_ab, b_onesr], writes=[pb[5]])
                p.op("tensor", lambda e: e.matmul(
                    PB5[:, 256:256 + TILE], onesr_sb[0:1, :],
                    abrow[0:1, 2 * TILE:3 * TILE],
                    start=True, stop=True), reads=[b_ab, b_onesr], writes=[pb[5]])
                v1c = BBT[1][:, :].rearrange("p (m q c) -> p m q c", q=PC, c=CH)
                v5c = BBT[5][:, :].rearrange("p (m q c) -> p m q c", q=PC, c=CH)
                abc = PB5[:, 0:TILE].rearrange("p (m c) -> p m c", c=CH)
                bbc = PB5[:, 256:256 + TILE].rearrange("p (m c) -> p m c", c=CH)
                p.op("vector", lambda e, v5c=v5c, v1c=v1c, abc=abc: e.tensor_mul(
                    v5c, v1c,
                    abc.unsqueeze(2).broadcast_to([128, NCH_T, PC, CH])),
                    reads=[bb[1], pb[5]], writes=[bb[5]])
                p.op("vector", lambda e, v5c=v5c, bbc=bbc: e.tensor_add(
                    v5c, v5c,
                    bbc.unsqueeze(2).broadcast_to([128, NCH_T, PC, CH])),
                    reads=[pb[5]], writes=[bb[5]])
                for pcc in range(PC):
                    p.op("vector", lambda e, pcc=pcc, v5c=v5c: e.tensor_scalar(
                        v5c[:, :, pcc, :], v5c[:, :, pcc, :],
                        col("ln_g", pcc), col("ln_b", pcc), OP.mult, OP.add),
                        reads=[b_cols], writes=[bb[5]])

                if dbg:
                    p.dma("scalar", dbg_y[:, tok0 * PC:(tok0 + TILE) * PC],
                        BBT[5][:, :],
                        reads=[bb[5]], writes_acc=[d_dbg], sem="d_dbg")
                ck("ln2")
                for do in range(PC):
                    ri = rn()
                    bank = 6 + do % 2
                    p.dma("sync", RGT[ri][:, :],
                          ow4[do].rearrange("p a b -> p (a b)"),
                          writes=[rg[ri]], sem="d_w%d" % ri)

                    def wmm(e, ri=ri, bank=bank):
                        last = None
                        v5 = BBT[5][:, :].rearrange("p (m q c) -> p m q c",
                                                    q=PC, c=CH)
                        for k in range(PC):
                            last = e.matmul(PB[bank][:, 0:TILE],
                                            RGT[ri][:, k * 128:(k + 1) * 128],
                                            v5[:, :, k, :],
                                            start=(k == 0), stop=(k == PC - 1))
                        return last
                    p.op("tensor", wmm, reads=[rg[ri], bb[5]], writes=[pb[bank]])
                    p.op("vector", lambda e, do=do, bank=bank: e.tensor_add(
                        BBT[6][:, do * TILE:(do + 1) * TILE], PB[bank][:, 0:TILE],
                        BBT[2][:, do * TILE:(do + 1) * TILE]),
                        reads=[pb[bank], bb[2]], writes=[bb[6]])
                    if t < FOLD0:
                        tgt = min(BIS_IT, t * BIS_PER
                                  + (BIS_PER * (do + 1) + PC - 1) // PC)
                        while bis_done[0] < tgt:
                            bis_iter()
                            bis_done[0] += 1
                    if t + 1 < NT:
                        if do == 1:
                            reload_qo(t + 1)
                        if do >= 4 and (do - 4) % 3 == 0 \
                                and (do - 4) // 3 < NCH_T:
                            corr_m(t + 1, (do - 4) // 3)
                if t == FOLD0 - 1:
                    while bis_done[0] < BIS_IT:
                        bis_iter()
                        bis_done[0] += 1
                    emit_wgt()
                if t >= FOLD0:
                    p.op("vector", lambda e: e.tensor_add(
                        BBT[6][:, :], BBT[6][:, :], BBT[3][:, :]),
                        reads=[bb[3]], writes=[bb[6]])
                p.dma("scalar", y_fm[:, tok0 * PC:(tok0 + TILE) * PC],
                      BBT[6][:, :], reads=[bb[6]], writes_acc=[d_y], sem="d_y")
                if t + 1 < NT:
                    p.dma("scalar", BBT[2][:, :],
                          x_fm[:, (tok0 + TILE) * PC:(tok0 + 2 * TILE) * PC],
                          writes=[bb[2]], sem="d_rx")

            ck("ow")
            for j in list(pending):
                pending.remove(j)
                acc_tile(j)
            ck("wgt")

        except _Done:
            pass
        if trunc is not None:
            p.dma("scalar", y_fm[:, 0:PC * TILE], BBT[0][:, :],
                  writes_acc=[d_y], sem="d_y")
        p.wait_only("sync", [d_y])

        handles = {}
        cms = []
        for sname in sorted(p.val):
            cm = nc.semaphore(sname)
            handles[sname] = cm.__enter__()
            cms.append(cm)
        for ename, deco in (("tensor", block.tensor), ("vector", block.vector),
                            ("scalar", block.scalar), ("sync", block.sync),
                            ("gpsimd", block.gpsimd)):
            items = p.items[ename]
            waited = {}
            deco(lambda eng, items=items, waited=waited:
                 _emit(eng, items, handles, waited))
        for cm in reversed(cms):
            cm.__exit__(None, None, None)

    return nc


def stage_inputs(inputs, ncores, tpc):
    f32 = np.float32
    x = np.ascontiguousarray(np.asarray(inputs["L0_attn_x"], dtype=f32)[0])
    T = x.shape[0]
    assert T == ncores * tpc

    def c4(w):
        return np.ascontiguousarray(
            w.reshape(PC, 128, PC, 128).transpose(2, 1, 0, 3))

    qw4 = c4(np.asarray(inputs["L0_attn_q_w"], dtype=f32))
    kw4 = c4(np.asarray(inputs["L0_attn_k_w"], dtype=f32))
    ow4 = c4(np.asarray(inputs["L0_attn_o_w"], dtype=f32))
    vw = np.ascontiguousarray(np.asarray(inputs["L0_attn_v_w"], dtype=f32))
    wg = np.zeros((D, DFFP), dtype=f32)
    wg[:, :DFF] = np.asarray(inputs["L0_ffn_w_gate"], dtype=f32)
    wu = np.zeros((D, DFFP), dtype=f32)
    wu[:, :DFF] = np.asarray(inputs["L0_ffn_w_up"], dtype=f32)
    wd = np.zeros((DFFP, D), dtype=f32)
    wd[:DFF, :] = np.asarray(inputs["L0_ffn_w_down"], dtype=f32)
    gw4 = np.ascontiguousarray(
        wg.reshape(PC, 128, NF, 128).transpose(2, 1, 0, 3))
    uw4 = np.ascontiguousarray(
        wu.reshape(PC, 128, NF, 128).transpose(2, 1, 0, 3))
    dw4 = np.ascontiguousarray(
        wd.reshape(NF, 128, PC, 128).transpose(2, 1, 0, 3))
    m2 = np.asarray(inputs["L0_attn_maa_w2"], dtype=f32)
    m2w4 = np.zeros((4, PC, 128, 128), dtype=f32)
    for r in range(4):
        m2w4[r, :, 32 * r:32 * r + 32, :] = \
            m2[r].reshape(32, PC, 128).transpose(1, 0, 2)
    maw1 = np.ascontiguousarray(
        np.asarray(inputs["L0_attn_maa_w1"], dtype=f32)
        .reshape(PC, 128, 128).transpose(1, 0, 2).reshape(128, PC * 128))
    tdw1 = np.ascontiguousarray(
        np.asarray(inputs["L0_attn_td_w1"], dtype=f32)
        .reshape(PC, 128, 64).transpose(1, 0, 2).reshape(128, PC * 64))
    _t2 = np.asarray(inputs["L0_attn_td_w2"], dtype=f32)
    tdw2 = np.ascontiguousarray(np.concatenate([_t2[:, 0:1024], _t2[:, 1024:2048]], axis=0))
    rwm = np.ascontiguousarray(
        np.asarray(inputs["L0_ffn_router_w"], dtype=f32)
        .reshape(PC, 128, E8).transpose(1, 0, 2).reshape(128, PC * E8))

    def vcol(name):
        return np.asarray(inputs[name], dtype=f32).reshape(PC, 128).T

    cols = np.zeros((128, 16 * PC), dtype=f32)
    for i, nm in enumerate(["L0_attn_maa_x", "L0_attn_maa_r", "L0_attn_maa_k",
                            "L0_attn_maa_v", "L0_attn_maa_w",
                            "L0_attn_time_decay", "L0_attn_q_b",
                            "L0_attn_k_b", "L0_attn_ln_g", "L0_attn_ln_b",
                            "L0_ffn_b_down"]):
        cols[:, i * PC:(i + 1) * PC] = vcol(nm)
    bg = np.zeros(DFFP, dtype=f32)
    bg[:DFF] = np.asarray(inputs["L0_ffn_b_gate"], dtype=f32)
    bu = np.zeros(DFFP, dtype=f32)
    bu[:DFF] = np.asarray(inputs["L0_ffn_b_up"], dtype=f32)
    bgu = np.concatenate([bg.reshape(NF, 128).T, bu.reshape(NF, 128).T],
                         axis=1).astype(f32)
    bgu = np.ascontiguousarray(bgu)
    vbrow = np.asarray(inputs["L0_attn_v_b"], dtype=f32).reshape(1, D)
    rbrow = np.asarray(inputs["L0_ffn_router_b"], dtype=f32).reshape(1, E8)
    ident = np.eye(128, dtype=f32)
    m1 = np.triu(np.ones((CH, CH), dtype=f32))
    mask8 = np.tile(m1, (1, 8))
    onesc = np.ones((128, 1), dtype=f32)
    onesr = np.ones((1, 128), dtype=f32)

    n = T * E8
    u32 = uniform_f32(42, n).reshape(T, E8)
    u64 = uniform_f64(42, n).reshape(T, E8)
    du = u64 - u32.astype(np.float64)
    duhi = du.astype(f32)
    dulo = (du - duhi.astype(np.float64)).astype(f32)

    SMW = (tpc // 128) * E8
    in_maps = []
    for c in range(ncores):
        sl = slice(c * tpc, (c + 1) * tpc)
        x_sl = np.ascontiguousarray(x[sl])
        x_fmc = np.ascontiguousarray(
            x_sl.reshape(tpc // TILE, TILE, PC, 128)
            .transpose(3, 0, 2, 1).reshape(128, PC * tpc))
        x_bnd = (np.zeros((1, D), dtype=f32) if c == 0
                 else np.ascontiguousarray(x[c * tpc - 1:c * tpc, :]))
        incl2 = np.zeros((128, 16), dtype=f32)
        for r in range(E8):
            incl2[:, r] = 1.0 if r < c else 0.0
            incl2[:, 8 + r] = 0.0 if r < c else 1.0

        def ublk(a):
            return np.ascontiguousarray(
                a[sl].reshape(tpc // 128, 128, E8).transpose(1, 0, 2)
                .reshape(128, SMW))

        u3 = np.concatenate([ublk(u32), ublk(duhi), ublk(dulo)], axis=1)
        in_maps.append({
            "x_tm": x_sl, "x_fm": x_fmc, "x_bnd": x_bnd,
            "qw4": qw4, "kw4": kw4, "ow4": ow4, "vw": vw,
            "gw4": gw4, "uw4": uw4, "dw4": dw4, "m2w4": m2w4,
            "maw1": maw1, "tdw1": tdw1, "tdw2": tdw2, "rw": rwm,
            "cols": cols, "bgu": bgu, "vbrow": vbrow, "rbrow": rbrow,
            "ident": ident, "mask8": mask8, "onesc": onesc, "onesr": onesr,
            "incl2": incl2, "u3": np.ascontiguousarray(u3),
        })
    return in_maps


def kernel(**inputs):
    ncores = NCORES
    T = np.asarray(inputs["L0_attn_x"]).shape[1]
    tpc = T // ncores
    in_maps = stage_inputs(inputs, ncores, tpc)
    nc = build_nc(ncores, tpc)
    res = run_bass_kernel_spmd(nc, in_maps, core_ids=list(range(ncores)))
    outs = []
    for c in range(ncores):
        yf = np.asarray(res.results[c]["y_fm"]).reshape(
            128, tpc // TILE, PC, TILE)
        outs.append(yf.transpose(1, 3, 2, 0).reshape(tpc, D))
    return np.concatenate(outs, axis=0).reshape(1, T, D).astype(np.float32)


# version 13 (final)
# speedup vs baseline: 1.0778x; 1.0034x over previous
import numpy as np

import concourse.bass as bass
import concourse.mybir as mybir
from concourse.bass_utils import run_bass_kernel_spmd

F32 = mybir.dt.float32
AF = mybir.ActivationFunctionType
OP = mybir.AluOpType
AX = mybir.AxisListType

D = 2048
NCORES = 8
TILE = 256
CH = 64
NCH_T = TILE // CH
H = 32
DH = 64
PC = D // 128
DFF = 5461
DFFP = 5504
NF = DFFP // 128
E8 = 8
LN_EPS = 1e-5
REPS = 1e-10
BIS_IT = 64

ENGS = ("tensor", "vector", "scalar", "sync", "gpsimd")


def _tf_round(v0, v1, rot):
    v0 = (v0 + v1).astype(np.uint32)
    v1 = ((v1 << np.uint32(rot)) | (v1 >> np.uint32(32 - rot))).astype(np.uint32)
    return v0, (v0 ^ v1).astype(np.uint32)


def _threefry2x32(k1, k2, x1, x2):
    rots = ([13, 15, 26, 6], [17, 29, 16, 24])
    ks = [np.uint32(k1), np.uint32(k2),
          np.uint32(np.uint32(k1) ^ np.uint32(k2) ^ np.uint32(0x1BD11BDA))]
    x0 = (x1 + ks[0]).astype(np.uint32)
    y0 = (x2 + ks[1]).astype(np.uint32)
    for i in range(5):
        for r in rots[i % 2]:
            x0, y0 = _tf_round(x0, y0, r)
        x0 = (x0 + ks[(i + 1) % 3]).astype(np.uint32)
        y0 = (y0 + ks[(i + 2) % 3] + np.uint32(i + 1)).astype(np.uint32)
    return x0, y0


def _bits(seed, n):
    k1 = np.uint32(np.uint64(seed) >> np.uint64(32))
    k2 = np.uint32(np.uint64(seed) & np.uint64(0xFFFFFFFF))
    idx = np.arange(n, dtype=np.uint64)
    return _threefry2x32(k1, k2, (idx >> np.uint64(32)).astype(np.uint32),
                         (idx & np.uint64(0xFFFFFFFF)).astype(np.uint32))


def uniform_f32(seed, n):
    b1, b2 = _bits(seed, n)
    fb = ((b1 ^ b2).astype(np.uint32) >> np.uint32(9)) | np.uint32(0x3F800000)
    return fb.view(np.float32) - np.float32(1.0)


def uniform_f64(seed, n):
    b1, b2 = _bits(seed, n)
    bits64 = (b1.astype(np.uint64) << np.uint64(32)) | b2.astype(np.uint64)
    fb = (bits64 >> np.uint64(12)) | np.uint64(0x3FF0000000000000)
    return fb.view(np.float64) - np.float64(1.0)


class Buf:
    __slots__ = ("name", "writer", "readers")

    def __init__(self, name):
        self.name = name
        self.writer = None
        self.readers = []


class Prog:
    def __init__(self):
        self.items = {e: [] for e in ENGS}
        self.val = {}
        self.dma_chain = {}

    def op(self, eng, fn, reads=(), writes=(), writes_acc=(), sem=None, amt=1):
        s = sem if sem else "P_" + eng
        if s not in self.val:
            self.val[s] = 0
        waits = {}

        def addw(tok):
            if tok is not None and waits.get(tok[0], 0) < tok[1]:
                waits[tok[0]] = tok[1]

        for b in reads:
            addw(b.writer)
        for b in writes:
            addw(b.writer)
            for t in b.readers:
                addw(t)
        self.val[s] += amt
        tok = (s, self.val[s])
        self.items[eng].append((waits, fn, (s, amt)))
        for b in reads:
            b.readers.append(tok)
        for b in writes:
            b.writer = tok
            b.readers = []
        for b in writes_acc:
            b.writer = tok
        return tok

    def dma(self, eng, out_ap, in_ap, reads=(), writes=(), writes_acc=(),
            sem="d_gen", accum=None, slow=False):
        if accum is None:
            fn = (lambda e, o=out_ap, i=in_ap, sl=slow:
                  e.dma_start(out=o, in_=i, allow_slow_non_contiguous=sl))
        else:
            fn = (lambda e, o=out_ap, i=in_ap, a=accum, sl=slow:
                  e.dma_start(out=o, in_=i, accum_op=a,
                              allow_slow_non_contiguous=sl))
        ch = self.dma_chain.get(sem)
        if ch is None:
            ch = Buf("chain_" + sem)
            self.dma_chain[sem] = ch
        tok = self.op(eng, fn, list(reads) + [ch], writes, writes_acc,
                      sem=sem, amt=16)
        ch.writer = tok
        ch.readers = []
        return tok

    def wait_only(self, eng, bufs):
        waits = {}
        for b in bufs:
            if b.writer is not None:
                sm, v = b.writer
                if waits.get(sm, 0) < v:
                    waits[sm] = v
                if sm not in self.val:
                    self.val[sm] = v
        self.items[eng].append((waits, None, None))


def _emit(eng_obj, items, handles, waited):
    for waits, fn, inc in items:
        for sm in sorted(waits):
            v = waits[sm]
            if waited.get(sm, 0) >= v:
                continue
            eng_obj.wait_ge(handles[sm], v)
            waited[sm] = v
        if fn is None:
            continue
        inst = fn(eng_obj)
        if inc is not None:
            inst.then_inc(handles[inc[0]], inc[1])


def build_nc(ncores, tpc, dbg=False, trunc=None):
    attn_lvl = 9
    if trunc is not None and trunc.startswith("a"):
        attn_lvl = int(trunc[1:])
        trunc = "t7"
    NT = tpc // TILE
    NCHUNK = NT * NCH_T
    TB = TILE // 128
    SMW = (tpc // 128) * E8
    GW = ncores * SMW
    NIDX = GW // E8
    KSEL = ncores * tpc // E8

    nc = bass.Bass("TRN2", target_bir_lowering=False)
    p = Prog()

    class _Done(Exception):
        pass

    def ck(stage):
        if trunc == stage:
            raise _Done()

    if trunc == "noacc":
        pass

    def din(name, shape):
        return nc.dram_tensor(name, list(shape), F32, kind="ExternalInput").ap()

    x_tm = din("x_tm", (tpc, D))
    x_fm = din("x_fm", (128, PC * tpc))
    x_bnd = din("x_bnd", (1, D))
    qw4 = din("qw4", (PC, 128, PC, 128))
    kw4 = din("kw4", (PC, 128, PC, 128))
    ow4 = din("ow4", (PC, 128, PC, 128))
    vw = din("vw", (D, D))
    gw4 = din("gw4", (NF, 128, PC, 128))
    uw4 = din("uw4", (NF, 128, PC, 128))
    dw4 = din("dw4", (PC, 128, NF, 128))
    m2w4 = din("m2w4", (4, PC, 128, 128))
    maw1 = din("maw1", (128, PC * 128))
    tdw1 = din("tdw1", (128, PC * 64))
    tdw2 = din("tdw2", (128, 1024))
    rw = din("rw", (128, PC * E8))
    cols_d = din("cols", (128, 16 * PC))
    bgu = din("bgu", (128, 2 * NF))
    vbrow = din("vbrow", (1, D))
    rbrow = din("rbrow", (1, E8))
    ident = din("ident", (128, 128))
    mask8d = din("mask8", (64, 512))
    onescd = din("onesc", (128, 1))
    onesrd = din("onesr", (1, 128))
    incl2 = din("incl2", (128, 16))
    u3 = din("u3", (128, 3 * SMW))

    y_fm = nc.dram_tensor("y_fm", [128, PC * tpc], F32, kind="ExternalOutput").ap()

    qt_scr = nc.dram_tensor("qt_scr", [128, PC * tpc], F32).ap()
    o_scr = nc.dram_tensor("o_scr", [128, PC * tpc], F32).ap()
    ffn_scr = nc.dram_tensor("ffn_scr", [128, PC * tpc], F32).ap()
    bnd_scr = nc.dram_tensor("bnd_scr", [1, D], F32).ap()
    if dbg:
        dbg_wlog = nc.dram_tensor("dbg_wlog", [128, PC * tpc], F32).ap()
        dbg_xw = nc.dram_tensor("dbg_xw", [128, PC * tpc], F32).ap()
        dbg_oc = nc.dram_tensor("dbg_oc", [128, PC * tpc], F32).ap()
        dbg_y = nc.dram_tensor("dbg_y", [128, PC * tpc], F32).ap()
        d_dbg = Buf("d_dbg")
    ccsm_i = nc.dram_tensor("ccsm_i", [128, SMW], F32).ap()
    ccsm_o = nc.dram_tensor("ccsm_o", [128 * ncores, SMW], F32).ap()
    ccsu_i = nc.dram_tensor("ccsu_i", [128, 1040], F32).ap()
    ccsu_o = nc.dram_tensor("ccsu_o", [128 * ncores, 1040], F32).ap()

    d_qt = Buf("d_qt"); d_o = Buf("d_o"); d_ffn = Buf("d_ffn")
    d_bnd = Buf("d_bnd"); d_y = Buf("d_y")
    d_ci = Buf("d_ci"); d_co = Buf("d_co")
    d_si = Buf("d_si"); d_so = Buf("d_so")

    from contextlib import ExitStack
    with ExitStack() as _st:
        BB0 = _st.enter_context(nc.sbuf_tensor("BB0", [128, PC * TILE], F32))
        BB1 = _st.enter_context(nc.sbuf_tensor("BB1", [128, PC * TILE], F32))
        BB2 = _st.enter_context(nc.sbuf_tensor("BB2", [128, PC * TILE], F32))
        BB3 = _st.enter_context(nc.sbuf_tensor("BB3", [128, PC * TILE], F32))
        BB4 = _st.enter_context(nc.sbuf_tensor("BB4", [128, PC * TILE], F32))
        BB5 = _st.enter_context(nc.sbuf_tensor("BB5", [128, PC * TILE], F32))
        BB6 = _st.enter_context(nc.sbuf_tensor("BB6", [128, PC * TILE], F32))
        BB7 = _st.enter_context(nc.sbuf_tensor("BB7", [128, PC * TILE], F32))
        RG0 = _st.enter_context(nc.sbuf_tensor("RG0", [128, 2048], F32))
        RG1 = _st.enter_context(nc.sbuf_tensor("RG1", [128, 2048], F32))
        RG2 = _st.enter_context(nc.sbuf_tensor("RG2", [128, 2048], F32))
        RG3 = _st.enter_context(nc.sbuf_tensor("RG3", [128, 2048], F32))
        S_sb = _st.enter_context(nc.sbuf_tensor("S_sb", [128, H * 32], F32))
        Sin_sb = _st.enter_context(nc.sbuf_tensor("Sin_sb", [128, H * 32], F32))
        SC_sb = _st.enter_context(nc.sbuf_tensor("SC_sb", [128, H * 32], F32))
        rw_sb = _st.enter_context(nc.sbuf_tensor("rw_sb", [128, PC * E8], F32))
        vbbc_sb = BB7[:, 0:D]
        rbbc_sb = _st.enter_context(nc.sbuf_tensor("rbbc_sb", [128, E8], F32))
        cols_sb = _st.enter_context(nc.sbuf_tensor("cols_sb", [128, 16 * PC], F32))
        bgu_sb = _st.enter_context(nc.sbuf_tensor("bgu_sb", [128, 2 * NF], F32))
        id_sb = _st.enter_context(nc.sbuf_tensor("id_sb", [128, 128], F32))
        mask8_sb = _st.enter_context(nc.sbuf_tensor("mask8_sb", [64, 512], F32))
        ones_sb = _st.enter_context(nc.sbuf_tensor("ones_sb", [128, 1], F32))
        onesr_sb = _st.enter_context(nc.sbuf_tensor("onesr_sb", [1, 128], F32))
        incl_sb = _st.enter_context(nc.sbuf_tensor("incl_sb", [128, 16], F32))
        gum_sb = _st.enter_context(nc.sbuf_tensor("gum_sb", [128, SMW], F32))
        smOwn = _st.enter_context(nc.sbuf_tensor("smOwn", [128, SMW], F32))
        flag_sb = _st.enter_context(nc.sbuf_tensor("flag_sb", [128, 1], F32))
        qb8_sb = _st.enter_context(nc.sbuf_tensor("qb8_sb", [128, PC], F32))
        Bbase = _st.enter_context(nc.sbuf_tensor("Bbase", [128, (NCHUNK + 1) * PC], F32))
        Bcarry = _st.enter_context(nc.sbuf_tensor("Bcarry", [128, PC], F32))
        Dcols = _st.enter_context(nc.sbuf_tensor("Dcols", [128, NCH_T * PC], F32))
        eb_sb = _st.enter_context(nc.sbuf_tensor("eb_sb", [128, PC], F32))
        xnlast = _st.enter_context(nc.sbuf_tensor("xnlast", [128, PC], F32))
        tanh_sb = _st.enter_context(nc.sbuf_tensor("tanh_sb", [128, TILE], F32))
        wtd_sb = _st.enter_context(nc.sbuf_tensor("wtd_sb", [128, TILE], F32))
        statc = _st.enter_context(nc.sbuf_tensor("statc", [128, 24], F32))
        wgt_sb = _st.enter_context(nc.sbuf_tensor("wgt_sb", [128, tpc // 128], F32))
        wgtbc_sb = _st.enter_context(nc.sbuf_tensor("wgtbc_sb", [128, 2 * TILE], F32))
        ones128 = _st.enter_context(nc.sbuf_tensor("ones128", [128, 128], F32))
        bis8 = _st.enter_context(nc.sbuf_tensor("bis8", [128, 48], F32))
        dsl_sb = _st.enter_context(nc.sbuf_tensor("dsl_sb", [128, PC], F32))
        cstc = _st.enter_context(nc.sbuf_tensor("cstc", [128, 2], F32))
        ROWA = _st.enter_context(nc.sbuf_tensor("ROWA", [1, 2048], F32))
        ROWB = _st.enter_context(nc.sbuf_tensor("ROWB", [1, 2048], F32))
        PB0 = _st.enter_context(nc.psum_tensor("PB0", [128, 512], F32))
        PB1 = _st.enter_context(nc.psum_tensor("PB1", [128, 512], F32))
        PB2 = _st.enter_context(nc.psum_tensor("PB2", [128, 512], F32))
        PB3 = _st.enter_context(nc.psum_tensor("PB3", [128, 512], F32))
        PB4 = _st.enter_context(nc.psum_tensor("PB4", [128, 512], F32))
        PB5 = _st.enter_context(nc.psum_tensor("PB5", [128, 512], F32))
        PB6 = _st.enter_context(nc.psum_tensor("PB6", [128, 512], F32))
        PB7 = _st.enter_context(nc.psum_tensor("PB7", [128, 512], F32))
        block = _st.enter_context(nc.Block())
        smAll = BB0[:, 0:GW]
        bisc = BB0[:, 1024:1024 + GW]
        midbc = BB0[:, 2048:2048 + GW]
        u3_sb = BB2[:, 0:3 * SMW]
        Ur_sb = BB6[:, 0:1040]
        wgtrow = ROWB
        vrow_sb = ROWA
        bndrow = ROWB
        abrow = ROWA
        PB = [PB0, PB1, PB2, PB3, PB4, PB5, PB6, PB7]
        pb = [Buf("PB%d" % i) for i in range(8)]
        BBT = [BB0, BB1, BB2, BB3, BB4, BB5, BB6, BB7]
        bb = [Buf("BB%d" % i) for i in range(8)]
        RGT = [RG0, RG1, RG2, RG3]
        rg = [Buf("RG%d" % i) for i in range(4)]

        b_S = Buf("S"); b_Sin = Buf("Sin"); b_SC = Buf("SC")
        b_smAll = bb[0]; b_bisc = bb[0]; b_midbc = bb[0]
        b_Ur = bb[6]
        b_rw = Buf("rw"); b_vbbc = bb[7]
        b_rbbc = Buf("rbbc"); b_cols = Buf("cols"); b_bgu = Buf("bgu")
        b_id = Buf("id"); b_mask8 = Buf("mask8"); b_ones = Buf("ones")
        b_onesr = Buf("onesr"); b_incl = Buf("incl"); b_u3 = bb[2]
        b_gum = Buf("gum"); b_smOwn = Buf("smOwn"); b_flag = Buf("flag")
        b_qb8 = Buf("qb8"); b_Bb = Buf("Bbase"); b_Bc = Buf("Bcarry")
        b_Dc = Buf("Dcols"); b_eb = Buf("eb"); b_xnl = Buf("xnlast")
        b_tanh = Buf("tanh"); b_wtd = Buf("wtd"); b_st = Buf("statc")
        b_wgt = Buf("wgt"); b_wgb = Buf("wgtbc")
        b_bis = Buf("bis8"); b_vr = Buf("rowa"); b_bnd = Buf("rowb")
        b_ab = b_vr; b_wgr = b_bnd; b_dsl = Buf("dsl")

        CL = {"maa_x": 0, "maa_r": 1, "maa_k": 2, "maa_v": 3, "maa_w": 4,
              "tdec": 5, "q_b": 6, "k_b": 7, "ln_g": 8, "ln_b": 9, "b_dn": 10}

        def col(nm, pcc):
            i = CL[nm] * PC + pcc
            return cols_sb[:, i:i + 1]

        wring = [0]
        held = set()

        def rn():
            for _ in range(8):
                i = wring[0] % 4
                wring[0] += 1
                if i not in held:
                    return i
            raise RuntimeError("ring exhausted")

        def v3(i, b=TILE):
            return BBT[i][:, :].rearrange("p (a c) -> p a c", c=b)

        Bb3 = Bbase[:, :].rearrange("p (m c) -> p m c", c=PC)

        try:
            LD = "d_c"
            p.dma("sync", cols_sb[:, :], cols_d[:, :], writes=[b_cols], sem=LD)
            p.dma("sync", bgu_sb[:, :], bgu[:, :], writes=[b_bgu], sem=LD)
            p.dma("sync", id_sb[:, :], ident[:, :], writes=[b_id], sem=LD)
            p.dma("sync", mask8_sb[:, :], mask8d[:, :], writes=[b_mask8], sem=LD)
            p.dma("sync", ones_sb[:, :], onescd[:, :], writes=[b_ones], sem=LD)
            p.dma("sync", onesr_sb[:, :], onesrd[:, :], writes=[b_onesr], sem=LD)
            p.dma("sync", incl_sb[:, :], incl2[:, :], writes=[b_incl], sem=LD)
            p.dma("sync", u3_sb[:, :], u3[:, :], writes=[b_u3], sem=LD)
            p.dma("sync", rw_sb[:, :], rw[:, :], writes=[b_rw], sem=LD)
            p.dma("sync", bndrow[:, :], x_bnd[:, :], writes=[b_bnd], sem=LD)
            p.dma("sync", vrow_sb[:, :], vbrow[:, :], writes=[b_vr], sem=LD)
            p.dma("sync", statc[0:1, 0:E8], rbrow[:, :], writes=[b_st], sem=LD)

            b_cst = Buf("cst")
            p.op("vector", lambda e: e.memset(cstc[:, 0:1], REPS), writes=[b_cst])
            p.op("vector", lambda e: e.memset(cstc[:, 1:2], LN_EPS), writes=[b_cst])
            b_o128 = Buf("o128")
            p.op("vector", lambda e: e.memset(ones128[:, :], 1.0), writes=[b_o128])

            p.op("vector", lambda e: e.memset(flag_sb[:, :], 1.0), writes=[b_flag])
            p.op("vector", lambda e: e.tensor_scalar(
                flag_sb[:, :], flag_sb[:, :], 2.0 ** -30, None, OP.add),
                writes=[b_flag])
            p.op("vector", lambda e: e.tensor_scalar(
                flag_sb[:, :], flag_sb[:, :], -1.0, 2.0 ** 30, OP.add, OP.mult),
                writes=[b_flag])
            p.op("vector", lambda e: e.tensor_scalar(
                qb8_sb[:, :], cols_sb[:, CL["q_b"] * PC:(CL["q_b"] + 1) * PC],
                0.125, None, OP.mult), reads=[b_cols], writes=[b_qb8])
            p.op("vector", lambda e: e.scalar_tensor_tensor(
                u3_sb[:, 0:SMW], u3_sb[:, SMW:2 * SMW], flag_sb[:, 0:1],
                u3_sb[:, 0:SMW], OP.mult, OP.add), reads=[b_flag], writes=[b_u3])
            p.op("vector", lambda e: e.scalar_tensor_tensor(
                u3_sb[:, 0:SMW], u3_sb[:, 2 * SMW:3 * SMW], flag_sb[:, 0:1],
                u3_sb[:, 0:SMW], OP.mult, OP.add), reads=[b_flag], writes=[b_u3])
            p.op("scalar", lambda e: e.activation(
                gum_sb[:, :], u3_sb[:, 0:SMW], AF.Ln, bias=cstc[:, 0:1]),
                reads=[b_u3], writes=[b_gum])
            p.op("scalar", lambda e: e.activation(
                gum_sb[:, :], gum_sb[:, :], AF.Ln, bias=cstc[:, 0:1], scale=-1.0),
                writes=[b_gum])
            for i in range(4):
                p.op("tensor", lambda e, i=i: e.matmul(
                    PB0[:, 0:512], onesr_sb[0:1, :], vrow_sb[0:1, i * 512:(i + 1) * 512],
                    start=True, stop=True), reads=[b_vr, b_onesr], writes=[pb[0]])
                p.op("vector", lambda e, i=i: e.tensor_copy(
                    vbbc_sb[:, i * 512:(i + 1) * 512], PB0[:, 0:512]),
                    reads=[pb[0]], writes=[b_vbbc])
            p.op("tensor", lambda e: e.matmul(
                PB1[:, 0:E8], onesr_sb[0:1, :], statc[0:1, 0:E8],
                start=True, stop=True), reads=[b_st, b_onesr], writes=[pb[1]])
            p.op("vector", lambda e: e.tensor_copy(rbbc_sb[:, :], PB1[:, 0:E8]),
                 reads=[pb[1]], writes=[b_rbbc])

            p.op("vector", lambda e: e.reduce_sum(
                statc[0:1, 16:17], bndrow[0:1, :], axis=AX.X),
                reads=[b_bnd], writes=[b_st])
            p.op("scalar", lambda e: e.activation(
                BBT[1][0:1, 0:D], bndrow[0:1, 0:D],
                AF.Square, accum_out=statc[0:1, 17:18]),
                reads=[b_bnd], writes=[bb[1], b_st])
            p.op("vector", lambda e: e.tensor_scalar(
                statc[0:1, 16:18], statc[0:1, 16:18], 1.0 / D, None, OP.mult),
                writes=[b_st])
            p.op("scalar", lambda e: e.activation(
                statc[0:1, 18:19], statc[0:1, 16:17], AF.Square), reads=[b_st],
                writes=[b_st])
            p.op("vector", lambda e: e.tensor_sub(
                statc[0:1, 19:20], statc[0:1, 17:18], statc[0:1, 18:19]),
                writes=[b_st])
            p.op("scalar", lambda e: e.activation(
                statc[0:1, 20:21], statc[0:1, 19:20], AF.Sqrt, bias=cstc[0:1, 1:2]),
                writes=[b_st])
            p.op("vector", lambda e: e.reciprocal(
                statc[0:1, 21:22], statc[0:1, 20:21]), writes=[b_st])
            p.op("vector", lambda e: e.tensor_scalar(
                bndrow[0:1, :], bndrow[0:1, :], statc[0:1, 16:17],
                statc[0:1, 21:22], OP.subtract, OP.mult),
                reads=[b_st], writes=[b_bnd])
            for pcc in range(PC):
                p.op("tensor", lambda e, pcc=pcc: e.matmul(
                    PB2[:, pcc:pcc + 1], bndrow[0:1, pcc * 128:(pcc + 1) * 128],
                    ones_sb[0:1, 0:1], start=True, stop=True),
                    reads=[b_bnd, b_ones], writes=[pb[2]])
            p.op("vector", lambda e: e.tensor_copy(
                xnlast[:, :], PB2[:, 0:PC]), reads=[pb[2]], writes=[b_xnl])

            p.op("vector", lambda e: e.memset(S_sb[:, :], 0.0), writes=[b_S])
            p.op("vector", lambda e: e.memset(Bcarry[:, :], 0.0), writes=[b_Bc])
            p.op("vector", lambda e: e.memset(Bbase[:, 0:PC], 0.0), writes=[b_Bb])

            ck("pre")
            def mix_into_bb2(mname, r, dst_i=2):
                for pcc in range(PC):
                    ri = rn()
                    bank = pcc % 2
                    p.dma("sync", RGT[ri][:, 0:128], m2w4[r, pcc],
                          writes=[rg[ri]], sem="d_w%d" % ri)
                    p.op("tensor", lambda e, ri=ri, bank=bank: e.matmul(
                        PB[bank][:, 0:TILE], RGT[ri][:, 0:128], tanh_sb[:, :],
                        start=True, stop=True),
                        reads=[rg[ri], b_tanh], writes=[pb[bank]])
                    p.op("vector", lambda e, pcc=pcc, bank=bank, mname=mname,
                         dst_i=dst_i: e.scalar_tensor_tensor(
                             BBT[dst_i][:, pcc * TILE:(pcc + 1) * TILE],
                             PB[bank][:, 0:TILE], col(mname, pcc),
                             BBT[1][:, pcc * TILE:(pcc + 1) * TILE],
                             OP.add, OP.mult),
                         reads=[pb[bank], b_cols, bb[1]], writes=[bb[dst_i]])
                    p.op("vector", lambda e, pcc=pcc, dst_i=dst_i: e.tensor_add(
                        BBT[dst_i][:, pcc * TILE:(pcc + 1) * TILE],
                        BBT[dst_i][:, pcc * TILE:(pcc + 1) * TILE],
                        BBT[0][:, pcc * TILE:(pcc + 1) * TILE]),
                        reads=[bb[0]], writes=[bb[dst_i]])

            def proj_2048(w4, dst_i, evac):
                for do in range(PC):
                    ri = rn()
                    bank = do % 2
                    p.dma("sync", RGT[ri][:, :],
                          w4[do].rearrange("p a b -> p (a b)"),
                          writes=[rg[ri]], sem="d_w%d" % ri)

                    def mmg(e, ri=ri, bank=bank):
                        last = None
                        for k in range(PC):
                            last = e.matmul(PB[bank][:, 0:TILE],
                                            RGT[ri][:, k * 128:(k + 1) * 128],
                                            BBT[2][:, k * TILE:(k + 1) * TILE],
                                            start=(k == 0), stop=(k == PC - 1))
                        return last
                    p.op("tensor", mmg, reads=[rg[ri], bb[2]], writes=[pb[bank]])
                    evac(do, bank)

            def bis_iter():
                p.op("vector", lambda e: e.tensor_tensor(
                    bisc[:, :], smAll[:, :], midbc[:, :], OP.is_ge),
                    reads=[b_smAll, b_midbc], writes=[b_bisc])
                p.op("vector", lambda e: e.tensor_reduce(
                    bis8[:, 24:32].rearrange("p (a c) -> p a c", c=1),
                    bisc[:, :].rearrange("p (i e) -> p i e", e=E8).transpose([0, 2, 1]),
                    AX.X, OP.add), reads=[b_bisc], writes=[b_bis])
                p.op("tensor", lambda e: e.matmul(
                    PB3[:, 0:E8], ones128[:, :], bis8[:, 24:32],
                    start=True, stop=True), reads=[b_bis, b_o128], writes=[pb[3]])
                p.op("vector", lambda e: e.tensor_copy(
                    bis8[:, 32:40], PB3[:, 0:E8]), reads=[pb[3]], writes=[b_bis])
                p.op("vector", lambda e: e.tensor_scalar(
                    bis8[:, 32:40], bis8[:, 32:40], float(KSEL), None, OP.is_ge),
                    writes=[b_bis])
                p.op("vector", lambda e: e.tensor_sub(
                    bis8[:, 40:48], bis8[:, 16:24], bis8[:, 0:8]), writes=[b_bis])
                p.op("vector", lambda e: e.tensor_mul(
                    bis8[:, 40:48], bis8[:, 40:48], bis8[:, 32:40]), writes=[b_bis])
                p.op("vector", lambda e: e.tensor_add(
                    bis8[:, 0:8], bis8[:, 0:8], bis8[:, 40:48]), writes=[b_bis])
                p.op("vector", lambda e: e.tensor_sub(
                    bis8[:, 40:48], bis8[:, 16:24], bis8[:, 8:16]), writes=[b_bis])
                p.op("vector", lambda e: e.tensor_scalar(
                    bis8[:, 32:40], bis8[:, 32:40], -1.0, 1.0, OP.mult, OP.add),
                    writes=[b_bis])
                p.op("vector", lambda e: e.tensor_mul(
                    bis8[:, 40:48], bis8[:, 40:48], bis8[:, 32:40]), writes=[b_bis])
                p.op("vector", lambda e: e.tensor_add(
                    bis8[:, 8:16], bis8[:, 8:16], bis8[:, 40:48]), writes=[b_bis])
                p.op("vector", lambda e: e.tensor_add(
                    bis8[:, 16:24], bis8[:, 0:8], bis8[:, 8:16]), writes=[b_bis])
                p.op("vector", lambda e: e.tensor_scalar(
                    bis8[:, 16:24], bis8[:, 16:24], 0.5, None, OP.mult),
                    writes=[b_bis])
                p.op("vector", lambda e: e.tensor_copy(
                    midbc[:, :].rearrange("p (i e) -> p i e", e=E8),
                    bis8[:, 16:24].unsqueeze(1).broadcast_to([128, GW // E8, E8])),
                    reads=[b_bis], writes=[b_midbc])

            def stage_T12(t):
                tok0 = t * TILE

                for tb in range(TB):
                    dst = BBT[2][:, tb * D:(tb + 1) * D]
                    p.dma("gpsimd", dst,
                          x_tm[tok0 + tb * 128: tok0 + (tb + 1) * 128, :],
                          writes=[bb[2]], sem="d_x%d" % tb)
                    sc = 8 * tb
                    p.op("vector", lambda e, dst=dst, sc=sc: e.reduce_sum(
                        statc[:, sc:sc + 1], dst, axis=AX.X),
                        reads=[bb[2]], writes=[b_st])
                    p.op("scalar", lambda e, dst=dst, sc=sc, tb=tb: e.activation(
                        BBT[1][:, tb * D:(tb + 1) * D], dst, AF.Square,
                        accum_out=statc[:, sc + 1:sc + 2]),
                        reads=[bb[2]], writes=[bb[1], b_st])
                    p.op("vector", lambda e, sc=sc: e.tensor_scalar(
                        statc[:, sc + 2:sc + 4], statc[:, sc:sc + 2], 1.0 / D,
                        None, OP.mult), writes=[b_st])
                    p.op("scalar", lambda e, sc=sc: e.activation(
                        statc[:, sc + 4:sc + 5], statc[:, sc + 2:sc + 3],
                        AF.Square), writes=[b_st])
                    p.op("vector", lambda e, sc=sc: e.tensor_sub(
                        statc[:, sc + 5:sc + 6], statc[:, sc + 3:sc + 4],
                        statc[:, sc + 4:sc + 5]), writes=[b_st])
                    p.op("scalar", lambda e, sc=sc: e.activation(
                        statc[:, sc + 6:sc + 7], statc[:, sc + 5:sc + 6], AF.Sqrt,
                        bias=cstc[:, 1:2]), writes=[b_st])
                    p.op("vector", lambda e, sc=sc: e.reciprocal(
                        statc[:, sc + 7:sc + 8], statc[:, sc + 6:sc + 7]),
                        writes=[b_st])
                    p.op("vector", lambda e, dst=dst, sc=sc: e.tensor_scalar(
                        dst, dst, statc[:, sc + 2:sc + 3], statc[:, sc + 7:sc + 8],
                        OP.subtract, OP.mult), reads=[b_st], writes=[bb[2]])


            def stage_T34(t):
                tok0 = t * TILE
                for tb in range(TB):
                    for g in range(4):
                        def tg(e, tb=tb, g=g):
                            last = None
                            for j in range(4):
                                pcx = g * 4 + j
                                last = e.transpose(
                                    PB6[:, j * 128:(j + 1) * 128],
                                    BBT[2][:, tb * D + pcx * 128: tb * D + (pcx + 1) * 128],
                                    id_sb[:, :])
                            return last
                        p.op("tensor", tg, reads=[bb[2], b_id], writes=[pb[6]])
                        p.op("vector", lambda e, tb=tb, g=g: e.tensor_copy(
                            v3(0)[:, g * 4:(g + 1) * 4, tb * 128:(tb + 1) * 128],
                            PB6[:, 0:512].rearrange("p (a c) -> p a c", c=128)),
                            reads=[pb[6]], writes=[bb[0]])

                p.op("vector", lambda e: e.tensor_copy(
                    statc[:, 16:16 + PC] if False else dsl_sb[:, :],
                    v3(0)[:, :, TILE - 1:TILE].rearrange("p a c -> p (a c)")),
                    reads=[bb[0]], writes=[b_dsl])
                p.op("vector", lambda e: e.tensor_sub(
                    v3(1)[:, :, 1:TILE], v3(0)[:, :, 0:TILE - 1],
                    v3(0)[:, :, 1:TILE]), reads=[bb[0]], writes=[bb[1]])
                p.op("vector", lambda e: e.tensor_sub(
                    v3(1)[:, :, 0:1], xnlast[:, :].unsqueeze(2),
                    v3(0)[:, :, 0:1]), reads=[bb[0], b_xnl], writes=[bb[1]])
                p.op("vector", lambda e: e.tensor_copy(xnlast[:, :], dsl_sb[:, :]),
                     reads=[b_dsl], writes=[b_xnl])

                ck("t3")
                for tb in range(TB):
                    blk = t * TB + tb

                    def rmm(e, tb=tb):
                        last = None
                        for k in range(PC):
                            last = e.matmul(
                                PB7[:, 0:E8],
                                BBT[0][:, k * TILE + tb * 128: k * TILE + (tb + 1) * 128],
                                rw_sb[:, k * E8:(k + 1) * E8],
                                start=(k == 0), stop=(k == PC - 1))
                        return last
                    p.op("tensor", rmm, reads=[bb[0], b_rw], writes=[pb[7]])
                    sm = smOwn[:, blk * E8:(blk + 1) * E8]
                    p.op("vector", lambda e, sm=sm: e.tensor_add(
                        sm, PB7[:, 0:E8], rbbc_sb[:, :]),
                        reads=[pb[7], b_rbbc], writes=[b_smOwn])
                    p.op("vector", lambda e, sm=sm, blk=blk: e.tensor_sub(
                        sm, sm, gum_sb[:, blk * E8:(blk + 1) * E8]),
                        reads=[b_gum], writes=[b_smOwn])
                    p.op("vector", lambda e, sm=sm: e.reduce_max(
                        statc[:, 16:17], sm, axis=AX.X, negate=True),
                        reads=[b_smOwn], writes=[b_st])
                    p.op("scalar", lambda e, sm=sm: e.activation(
                        sm, sm, AF.Exp, bias=statc[:, 16:17],
                        accum_out=statc[:, 17:18]),
                        reads=[b_st], writes=[b_smOwn, b_st])
                    p.op("vector", lambda e: e.reciprocal(
                        statc[:, 18:19], statc[:, 17:18]), writes=[b_st])
                    p.op("vector", lambda e, sm=sm: e.tensor_scalar(
                        sm, sm, statc[:, 18:19], None, OP.mult),
                        reads=[b_st], writes=[b_smOwn])


            stage_T12(0)
            stage_T34(0)

            for t in range(NT):
                tok0 = t * TILE
                ck("t4")
                ri_m = rn()
                p.dma("sync", RGT[ri_m][:, :], maw1[:, :], writes=[rg[ri_m]],
                      sem="d_w%d" % ri_m)
                ri_t = rn()
                held.add(ri_t)
                p.dma("sync", RGT[ri_t][:, 0:1024], tdw1[:, :], writes=[rg[ri_t]],
                      sem="d_w%d" % ri_t)
                p.dma("sync", RGT[ri_t][:, 1024:2048], tdw2[:, :],
                      writes_acc=[rg[ri_t]], sem="d_w%d" % ri_t)
                for pcc in range(PC):
                    p.op("vector", lambda e, pcc=pcc: e.scalar_tensor_tensor(
                        BBT[2][:, pcc * TILE:(pcc + 1) * TILE],
                        BBT[1][:, pcc * TILE:(pcc + 1) * TILE],
                        col("maa_x", pcc),
                        BBT[0][:, pcc * TILE:(pcc + 1) * TILE],
                        OP.mult, OP.add),
                        reads=[bb[1], bb[0], b_cols], writes=[bb[2]])

                def mm_tanh(e, ri=ri_m):
                    last = None
                    for k in range(PC):
                        last = e.matmul(PB0[:, 0:TILE],
                                        RGT[ri][:, k * 128:(k + 1) * 128],
                                        BBT[2][:, k * TILE:(k + 1) * TILE],
                                        start=(k == 0), stop=(k == PC - 1))
                    return last
                p.op("tensor", mm_tanh, reads=[bb[2], rg[ri_m]], writes=[pb[0]])
                p.op("scalar", lambda e: e.activation(
                    tanh_sb[:, :], PB0[:, 0:TILE], AF.Tanh),
                    reads=[pb[0]], writes=[b_tanh])

                mix_into_bb2("maa_w", 3)
                if dbg:
                    p.dma("scalar", dbg_xw[:, tok0 * PC:(tok0 + TILE) * PC],
                        BBT[2][:, :],
                        reads=[bb[2]], writes_acc=[d_dbg], sem="d_dbg")

                def mm_wtd(e, ri=ri_t):
                    last = None
                    for k in range(PC):
                        last = e.matmul(PB2[0:64, 0:TILE],
                                        RGT[ri][:, k * 64:(k + 1) * 64],
                                        BBT[2][:, k * TILE:(k + 1) * TILE],
                                        start=(k == 0), stop=(k == PC - 1))
                    return last
                p.op("tensor", mm_wtd, reads=[bb[2], rg[ri_t]], writes=[pb[2]])
                p.op("scalar", lambda e: e.activation(
                    wtd_sb[0:64, :], PB2[0:64, 0:TILE], AF.Tanh),
                    reads=[pb[2]], writes=[b_wtd])
                p.op("vector", lambda e: e.tensor_copy(
                    wtd_sb[64:128, :], wtd_sb[0:64, :]), writes=[b_wtd])
                for pcc in range(PC):
                    bank = pcc % 2
                    rh = (pcc // 8) * 64
                    cc0 = 1024 + (pcc % 8) * 128
                    p.op("tensor", lambda e, bank=bank, rh=rh, cc0=cc0, ri=ri_t:
                         e.matmul(
                             PB[bank][:, 0:TILE],
                             RGT[ri][rh:rh + 64, cc0:cc0 + 128],
                             wtd_sb[rh:rh + 64, :], start=True, stop=True),
                         reads=[rg[ri_t], b_wtd], writes=[pb[bank]])
                    p.op("scalar", lambda e, pcc=pcc, bank=bank: e.activation(
                        BBT[3][:, pcc * TILE:(pcc + 1) * TILE],
                        PB[bank][:, 0:TILE], AF.Exp, bias=col("tdec", pcc)),
                        reads=[pb[bank], b_cols], writes=[bb[3]])
                    p.op("vector", lambda e, pcc=pcc: e.tensor_scalar(
                        BBT[3][:, pcc * TILE:(pcc + 1) * TILE],
                        BBT[3][:, pcc * TILE:(pcc + 1) * TILE],
                        5.0, -1.0, OP.min, OP.mult), writes=[bb[3]])

                if dbg:
                    p.dma("scalar", dbg_wlog[:, tok0 * PC:(tok0 + TILE) * PC],
                        BBT[3][:, :],
                        reads=[bb[3]], writes_acc=[d_dbg], sem="d_dbg")
                held.discard(ri_t)
                mix_into_bb2("maa_v", 2, dst_i=6)
                for k in range(PC):
                    ri = rn()
                    p.dma("sync", RGT[ri][:, :], vw[k * 128:(k + 1) * 128, :],
                          writes=[rg[ri]], sem="d_w%d" % ri)

                    def vmm(e, ri=ri, k=k):
                        last = None
                        for tb in range(TB):
                            for do in range(4):
                                last = e.matmul(
                                    PB[tb * 4 + do][:, 0:512],
                                    BBT[6][:, k * TILE + tb * 128: k * TILE + (tb + 1) * 128],
                                    RGT[ri][:, do * 512:(do + 1) * 512],
                                    start=(k == 0), stop=(k == PC - 1))
                        return last
                    p.op("tensor", vmm, reads=[rg[ri], bb[6]], writes=pb[0:8])
                for tb in range(TB):
                    for do in range(4):
                        p.op("vector", lambda e, tb=tb, do=do: e.tensor_add(
                            BBT[6][:, tb * D + do * 512: tb * D + (do + 1) * 512],
                            PB[tb * 4 + do][:, 0:512],
                            vbbc_sb[:, do * 512:(do + 1) * 512]),
                            reads=[pb[tb * 4 + do], b_vbbc], writes=[bb[6]])

                mix_into_bb2("maa_r", 0)

                def ev_q(do, bank):
                    p.op("scalar", lambda e, do=do, bank=bank: e.activation(
                        BBT[4][:, do * TILE:(do + 1) * TILE], PB[bank][:, 0:TILE],
                        AF.Identity, bias=qb8_sb[:, do:do + 1], scale=0.125),
                        reads=[pb[bank], b_qb8], writes=[bb[4]])
                proj_2048(qw4, 4, ev_q)

                mix_into_bb2("maa_k", 1)

                CH0 = t * NCH_T
                for pcc in range(PC):
                    p.op("vector", lambda e, pcc=pcc: e.tensor_tensor_scan(
                        BBT[1][:, pcc * TILE:(pcc + 1) * TILE],
                        BBT[3][:, pcc * TILE:(pcc + 1) * TILE],
                        BBT[3][:, pcc * TILE:(pcc + 1) * TILE],
                        Bcarry[:, pcc:pcc + 1], OP.add, OP.bypass),
                        reads=[bb[3], b_Bc], writes=[bb[1]])
                p.op("vector", lambda e, CH0=CH0: e.tensor_copy(
                    Bbase[:, CH0 * PC:(CH0 + 1) * PC], Bcarry[:, :]),
                    reads=[b_Bc], writes=[b_Bb])
                for m in range(1, NCH_T):
                    p.op("vector", lambda e, m=m, CH0=CH0: e.tensor_copy(
                        Bbase[:, (CH0 + m) * PC:(CH0 + m + 1) * PC],
                        v3(1)[:, :, m * CH - 1:m * CH].rearrange(
                            "p a c -> p (a c)")),
                        reads=[bb[1]], writes=[b_Bb])
                p.op("vector", lambda e, CH0=CH0: e.tensor_copy(
                    Bbase[:, (CH0 + NCH_T) * PC:(CH0 + NCH_T + 1) * PC],
                    v3(1)[:, :, TILE - 1:TILE].rearrange("p a c -> p (a c)")),
                    reads=[bb[1]], writes=[b_Bb])
                p.op("vector", lambda e, CH0=CH0: e.tensor_copy(
                    Bcarry[:, :],
                    Bbase[:, (CH0 + NCH_T) * PC:(CH0 + NCH_T + 1) * PC]),
                    reads=[b_Bb], writes=[b_Bc])
                p.op("vector", lambda e, CH0=CH0: e.tensor_sub(
                    Dcols[:, :],
                    Bbase[:, (CH0 + 1) * PC:(CH0 + 1 + NCH_T) * PC],
                    Bbase[:, CH0 * PC:(CH0 + NCH_T) * PC]),
                    reads=[b_Bb], writes=[b_Dc])
                p.op("scalar", lambda e: e.activation(
                    Dcols[:, :], Dcols[:, :], AF.Exp), writes=[b_Dc])
                X4 = lambda i: BBT[i][:, :].rearrange(
                    "p (q m c) -> p q m c", m=NCH_T, c=CH)
                for m in range(NCH_T):
                    p.op("scalar", lambda e, m=m, CH0=CH0: e.copy(
                        v3(3)[:, :, m * CH:(m + 1) * CH],
                        Bbase[:, (CH0 + m) * PC:(CH0 + m + 1) * PC]
                        .unsqueeze(2).broadcast_to([128, PC, CH])),
                        reads=[b_Bb], writes=[bb[3]])
                p.op("vector", lambda e: e.tensor_sub(
                    BBT[3][:, :], BBT[1][:, :], BBT[3][:, :]),
                    reads=[bb[1]], writes=[bb[3]])
                p.op("scalar", lambda e: e.activation(
                    BBT[1][:, :], BBT[3][:, :], AF.Exp, scale=-1.0),
                    reads=[bb[3]], writes=[bb[1]])
                p.op("scalar", lambda e: e.activation(
                    BBT[3][:, :], BBT[3][:, :], AF.Exp), writes=[bb[3]])
                p.op("vector", lambda e: e.tensor_mul(
                    BBT[4][:, :], BBT[4][:, :], BBT[3][:, :]),
                    reads=[bb[3]], writes=[bb[4]])
                p.op("scalar", lambda e: e.copy(
                    X4(3)[:, :, :, 1:CH], X4(1)[:, :, :, 0:CH - 1]),
                    reads=[bb[1]], writes=[bb[3]])
                p.op("vector", lambda e: e.memset(
                    X4(3)[:, :, :, 0:1], 1.0), writes=[bb[3]])
                p.op("vector", lambda e: e.tensor_sub(
                    BBT[3][:, :], BBT[1][:, :], BBT[3][:, :]),
                    reads=[bb[1]], writes=[bb[3]])
                p.dma("scalar", qt_scr[:, tok0 * PC:(tok0 + TILE) * PC],
                      BBT[4][:, :], reads=[bb[4]], writes_acc=[d_qt], sem="d_qt")

                def ev_k(do, bank):
                    p.op("scalar", lambda e, do=do, bank=bank: e.activation(
                        BBT[5][:, do * TILE:(do + 1) * TILE], PB[bank][:, 0:TILE],
                        AF.Identity, bias=col("k_b", do)),
                        reads=[pb[bank], b_cols], writes=[bb[5]])
                    p.op("vector", lambda e, do=do: e.tensor_mul(
                        BBT[5][:, do * TILE:(do + 1) * TILE],
                        BBT[5][:, do * TILE:(do + 1) * TILE],
                        BBT[3][:, do * TILE:(do + 1) * TILE]),
                        reads=[bb[3]], writes=[bb[5]])
                proj_2048(kw4, 5, ev_k)

                ck("t5")
                ck("t6")
                b_kh = bb[1]
                b_Pt = bb[3]
                b_ost = bb[1]
                b_vsw = bb[3]
                for m in range(NCH_T):
                    mg = CH0 + m
                    par = (m % 2) * 64
                    opar = 64 - par
                    tb = m // 2
                    if m % 2 == 0:
                        p.op("vector", lambda e, tb=tb: e.tensor_copy(
                            BBT[3][0:64, 2048:4096],
                            BBT[6][64:128, tb * D:tb * D + 2048]),
                            reads=[bb[6]], writes=[b_vsw])
                        p.op("vector", lambda e, tb=tb: e.tensor_copy(
                            BBT[3][64:128, 2048:4096],
                            BBT[6][0:64, tb * D:tb * D + 2048]),
                            reads=[bb[6]], writes=[b_vsw])

                    def vsl(h, par=par, tb=tb):
                        hp = (h % 2) * 64
                        if hp == par:
                            return BBT[6][par:par + 64,
                                          tb * D + h * DH: tb * D + (h + 1) * DH]
                        return BBT[3][hp:hp + 64, 2048 + h * DH:2048 + (h + 1) * DH]

                    def vrd(h, par=par):
                        return bb[6] if (h % 2) * 64 == par else b_vsw

                    for g in range(4):
                        def ttg(e, m=m, g=g):
                            last = None
                            for j in range(4):
                                pcx = g * 4 + j
                                last = e.transpose(
                                    PB6[0:64, j * 128:(j + 1) * 128],
                                    BBT[5][:, pcx * TILE + m * CH: pcx * TILE + (m + 1) * CH],
                                    id_sb[:, :])
                            return last
                        p.op("tensor", ttg, reads=[bb[5], b_id], writes=[pb[6]])
                        p.op("vector", lambda e, g=g: e.tensor_copy(
                            BBT[1][0:64, g * 512:(g + 1) * 512],
                            PB6[0:64, 0:512]), reads=[pb[6]], writes=[b_kh])
                        p.op("scalar", lambda e, g=g: e.activation(
                            BBT[1][64:128, g * 512:(g + 1) * 512],
                            PB6[0:64, 0:512], AF.Identity),
                            reads=[pb[6]], writes=[b_kh])

                    def qsl(h, m=m):
                        return BBT[4][(h % 2) * 64:(h % 2) * 64 + 64,
                                      (h // 2) * TILE + m * CH:
                                      (h // 2) * TILE + (m + 1) * CH]

                    def ksl(h, m=m):
                        return BBT[5][(h % 2) * 64:(h % 2) * 64 + 64,
                                      (h // 2) * TILE + m * CH:
                                      (h // 2) * TILE + (m + 1) * CH]

                    for half in range(2 if attn_lvl >= 2 else 0):
                        pbank = 4 + half

                        def pmm(e, half=half, pbank=pbank, qsl=qsl, ksl=ksl):
                            last = None
                            for i in range(16):
                                h = half * 16 + i
                                hp = (h % 2) * 64
                                sl = ((h // 2) % 8) * 64
                                last = e.matmul(
                                    PB[pbank][hp:hp + 64, sl:sl + 64],
                                    ksl(h), qsl(h), start=True, stop=True)
                            return last
                        p.op("tensor", pmm, reads=[bb[4], bb[5]],
                             writes=[pb[pbank]])
                        for hp in (0, 64):
                            p.op("vector", lambda e, half=half, pbank=pbank, hp=hp:
                                 e.tensor_mul(
                                     BBT[3][hp:hp + 64, :]
                                     .rearrange("p (a c) -> p a c", c=CH)
                                     [:, half * 16 + (hp // 64):half * 16 + 16:2, :],
                                     PB[pbank][hp:hp + 64, 0:512]
                                     .rearrange("p (a c) -> p a c", c=CH),
                                     mask8_sb[:, :]
                                     .rearrange("p (a c) -> p a c", c=CH)),
                                 reads=[pb[pbank], b_mask8], writes=[b_Pt])

                    for half in range(2 if attn_lvl >= 3 else 0):
                        obank = half

                        def omm(e, half=half, obank=obank, vsl=vsl, qsl=qsl):
                            last = None
                            for i in range(16):
                                h = half * 16 + i
                                hp = (h % 2) * 64
                                sl = ((h // 2) % 8) * 64
                                last = e.matmul(
                                    PB[obank][hp:hp + 64, sl:sl + 64],
                                    vsl(h),
                                    BBT[3][hp:hp + 64, h * 64:(h + 1) * 64],
                                    start=True, stop=False)
                                last = e.matmul(
                                    PB[obank][hp:hp + 64, sl:sl + 64],
                                    S_sb[hp:hp + 64,
                                         (h // 2) * 64:(h // 2) * 64 + 64],
                                    qsl(h), start=False, stop=True)
                            return last
                        p.op("tensor", omm,
                             reads=[bb[6], b_vsw, b_Pt, b_S, bb[4]],
                             writes=[pb[obank]])

                    for half in range(2 if attn_lvl >= 4 else 0):
                        sbank = 2 + half

                        def smm(e, half=half, sbank=sbank, par=par, vsl=vsl):
                            last = None
                            for i in range(16):
                                h = half * 16 + i
                                hp = (h % 2) * 64
                                pcc = h // 2
                                sl = (pcc % 8) * 64
                                last = e.matmul(
                                    PB[sbank][hp:hp + 64, sl:sl + 64],
                                    BBT[1][hp:hp + 64,
                                           pcc * 128 + hp: pcc * 128 + hp + 64],
                                    vsl(h), start=True, stop=True)
                            return last
                        p.op("tensor", smm, reads=[b_kh, bb[6], b_vsw],
                             writes=[pb[sbank]])

                    if attn_lvl >= 5:
                        for half in range(2):
                            p.op("scalar", lambda e, half=half: e.activation(
                                BBT[1][:, 2048 + half * 512:2048 + half * 512 + 512],
                                PB[half][:, 0:512], AF.Identity),
                                reads=[pb[half]], writes=[b_ost])
                        p.dma("scalar",
                              o_scr[:, (tok0 + m * CH) * PC:(tok0 + (m + 1) * CH) * PC],
                              BBT[1][:, 2048:3072], reads=[b_ost],
                              writes_acc=[d_o], sem="d_o")

                    if attn_lvl >= 4:
                        for half in range(2):
                            p.op("vector", lambda e, half=half: e.tensor_add(
                                S_sb[:, half * 512:(half + 1) * 512],
                                S_sb[:, half * 512:(half + 1) * 512],
                                PB[2 + half][:, 0:512]),
                                reads=[pb[2 + half]], writes=[b_S])
                        p.op("vector", lambda e, m=m: e.tensor_mul(
                            S_sb[:, :].rearrange("p (q c) -> p q c", c=64),
                            S_sb[:, :].rearrange("p (q c) -> p q c", c=64),
                            Dcols[:, m * PC:(m + 1) * PC]
                            .unsqueeze(2).broadcast_to([128, PC, 64])),
                            reads=[b_Dc], writes=[b_S])

                if t == NT - 1:
                    p.dma("gpsimd", ccsm_i[:, :], smOwn[:, :], reads=[b_smOwn],
                          writes=[d_ci], sem="d_cci")
                    if ncores > 1:
                        p.op("gpsimd", lambda e: e.collective_compute(
                            "AllGather", OP.bypass,
                            replica_groups=[list(range(ncores))],
                            ins=[ccsm_i.opt()], outs=[ccsm_o.opt()]),
                            reads=[d_ci], writes=[d_co], sem="cc1")
                    else:
                        p.dma("gpsimd", ccsm_o[0:128, :], ccsm_i[:, :],
                              reads=[d_ci], writes=[d_co], sem="cc1")
                    p.op("scalar", lambda e: e.activation(
                        dsl_sb[:, :], Bcarry[:, :], AF.Exp),
                        reads=[b_Bc], writes=[b_dsl])
                    p.dma("gpsimd", ccsu_i[:, 0:1024], S_sb[:, :], reads=[b_S],
                          writes=[d_si], sem="d_ccs")
                    p.dma("gpsimd", ccsu_i[:, 1024:1040], dsl_sb[:, :],
                          reads=[b_dsl], writes_acc=[d_si], sem="d_ccs")
                    if ncores > 1:
                        p.op("gpsimd", lambda e: e.collective_compute(
                            "AllGather", OP.bypass,
                            replica_groups=[list(range(ncores))],
                            ins=[ccsu_i.opt()], outs=[ccsu_o.opt()]),
                            reads=[d_si], writes=[d_so], sem="cc2")
                    else:
                        p.dma("gpsimd", ccsu_o[0:128, :], ccsu_i[:, :],
                              reads=[d_si], writes=[d_so], sem="cc2")

                if t + 1 < NT:
                    stage_T12(t + 1)
                ck("t7b")
                hhome = [1, 4, 5]

                def hT(c):
                    return BBT[hhome[c // 16]], bb[hhome[c // 16]], (c % 16) * TILE

                for c in range(NF):
                    rg_g = rn()
                    rg_u = rn()
                    p.dma("sync", RGT[rg_g][:, :],
                          gw4[c].rearrange("p a b -> p (a b)"),
                          writes=[rg[rg_g]], sem="d_w%d" % rg_g)
                    p.dma("sync", RGT[rg_u][:, :],
                          uw4[c].rearrange("p a b -> p (a b)"),
                          writes=[rg[rg_u]], sem="d_w%d" % rg_u)

                    def gmm(e, ri=rg_g):
                        last = None
                        for k in range(PC):
                            last = e.matmul(PB2[:, 0:TILE],
                                            RGT[ri][:, k * 128:(k + 1) * 128],
                                            BBT[0][:, k * TILE:(k + 1) * TILE],
                                            start=(k == 0), stop=(k == PC - 1))
                        return last
                    p.op("tensor", gmm, reads=[rg[rg_g], bb[0]], writes=[pb[2]])

                    def umm(e, ri=rg_u):
                        last = None
                        for k in range(PC):
                            last = e.matmul(PB3[:, 0:TILE],
                                            RGT[ri][:, k * 128:(k + 1) * 128],
                                            BBT[0][:, k * TILE:(k + 1) * TILE],
                                            start=(k == 0), stop=(k == PC - 1))
                        return last
                    p.op("tensor", umm, reads=[rg[rg_u], bb[0]], writes=[pb[3]])
                    hten, hbuf, hoff = hT(c)
                    p.op("scalar", lambda e, c=c: e.activation(
                        tanh_sb[:, :], PB2[:, 0:TILE], AF.Silu,
                        bias=bgu_sb[:, c:c + 1]),
                        reads=[pb[2], b_bgu], writes=[b_tanh])
                    p.op("vector", lambda e, c=c, hten=hten, hoff=hoff:
                         e.scalar_tensor_tensor(
                             hten[:, hoff:hoff + TILE], PB3[:, 0:TILE],
                             bgu_sb[:, NF + c:NF + c + 1], tanh_sb[:, :],
                             OP.add, OP.mult),
                         reads=[pb[3], b_bgu, b_tanh], writes=[hbuf])

                parts = [(0, 15), (15, 30), (30, NF)]
                for do in range(PC):
                    bank = do % 2
                    for (c0, c1) in parts:
                        ri = rn()
                        p.dma("sync", RGT[ri][:, 0:(c1 - c0) * 128],
                              dw4[do][:, c0:c1, :].rearrange("p a b -> p (a b)"),
                              writes=[rg[ri]], sem="d_w%d" % ri)

                        def dmm(e, ri=ri, c0=c0, c1=c1, bank=bank):
                            last = None
                            for c in range(c0, c1):
                                hten, _, hoff = hT(c)
                                last = e.matmul(
                                    PB[bank][:, 0:TILE],
                                    RGT[ri][:, (c - c0) * 128:(c - c0 + 1) * 128],
                                    hten[:, hoff:hoff + TILE],
                                    start=(c == 0), stop=(c == NF - 1))
                            return last
                        p.op("tensor", dmm,
                             reads=[rg[ri]] + [bb[i] for i in hhome],
                             writes=[pb[bank]])
                    p.op("vector", lambda e, do=do, bank=bank: e.tensor_scalar(
                        BBT[6][:, do * TILE:(do + 1) * TILE], PB[bank][:, 0:TILE],
                        col("b_dn", do), None, OP.add),
                        reads=[pb[bank], b_cols], writes=[bb[6]])
                p.dma("scalar", ffn_scr[:, tok0 * PC:(tok0 + TILE) * PC],
                      BBT[6][:, :], reads=[bb[6]], writes_acc=[d_ffn], sem="d_ff")

                if t + 1 < NT:
                    stage_T34(t + 1)
            ck("t8")
            p.dma("scalar", BBT[4][:, :], qt_scr[:, 0:TILE * PC],
                  reads=[d_qt], writes=[bb[4]], sem="d_rq")
            p.dma("scalar", BBT[1][:, :], o_scr[:, 0:TILE * PC],
                  reads=[d_o], writes=[bb[1]], sem="d_ro")
            p.dma("scalar", BBT[2][:, :], x_fm[:, 0:TILE * PC],
                  writes=[bb[2]], sem="d_rx")
            for r in range(ncores):
                p.dma("gpsimd", smAll[:, r * SMW:(r + 1) * SMW],
                      ccsm_o[r * 128:(r + 1) * 128, :],
                      reads=[d_co], writes=[b_smAll], sem="d_sml")
            p.op("vector", lambda e: e.memset(Sin_sb[:, :], 0.0), writes=[b_Sin])
            for r in range(ncores - 1):
                if r % 2 == 0:
                    src, bsrc = Ur_sb, b_Ur
                else:
                    src, bsrc = BBT[3], bb[3]
                p.dma("gpsimd", src[:, 0:1040], ccsu_o[r * 128:(r + 1) * 128, :],
                      reads=[d_so], writes=[bsrc], sem="d_url")
                p.op("vector", lambda e, r=r, src=src: e.scalar_tensor_tensor(
                    dsl_sb[:, :], src[:, 1024:1040], incl_sb[:, r:r + 1],
                    incl_sb[:, 8 + r:8 + r + 1].broadcast_to([128, PC]),
                    OP.mult, OP.add), reads=[bsrc, b_incl], writes=[b_dsl])
                p.op("vector", lambda e: e.tensor_mul(
                    Sin_sb[:, :].rearrange("p (q c) -> p q c", c=64),
                    Sin_sb[:, :].rearrange("p (q c) -> p q c", c=64),
                    dsl_sb[:, :].unsqueeze(2).broadcast_to([128, PC, 64])),
                    reads=[b_dsl], writes=[b_Sin])
                p.op("vector", lambda e, r=r, src=src: e.scalar_tensor_tensor(
                    Sin_sb[:, :], src[:, 0:1024], incl_sb[:, r:r + 1],
                    Sin_sb[:, :], OP.mult, OP.add),
                    reads=[bsrc, b_incl], writes=[b_Sin])
            ck("c0")
            p.op("vector", lambda e: e.memset(bis8[:, 0:8], 0.0), writes=[b_bis])
            p.op("vector", lambda e: e.memset(bis8[:, 8:16], 1.0), writes=[b_bis])
            p.op("vector", lambda e: e.memset(bis8[:, 16:24], 0.5), writes=[b_bis])
            p.op("vector", lambda e: e.memset(midbc[:, :], 0.5), writes=[b_midbc])
            bis_done = [0]

            ck("bisinit")

            FOLD0 = max(1, (NT + 1) // 2)
            BIS_PER = -(-BIS_IT // FOLD0)
            pending = list(range(FOLD0))

            def wgt_bc(tok, half):
                for tb in range(TB):
                    p.op("tensor", lambda e, tok=tok, tb=tb, half=half: e.matmul(
                        PB2[:, half * TILE + tb * 128:
                            half * TILE + (tb + 1) * 128],
                        onesr_sb[0:1, :],
                        wgtrow[0:1, tok + tb * 128:tok + (tb + 1) * 128],
                        start=True, stop=True),
                        reads=[b_wgr, b_onesr], writes=[pb[2]])
                p.op("vector", lambda e, half=half: e.tensor_copy(
                    wgtbc_sb[:, half * TILE:(half + 1) * TILE],
                    PB2[:, half * TILE:(half + 1) * TILE]),
                    reads=[pb[2]], writes=[b_wgb])

            def mul_wgt(i, half):
                p.op("vector", lambda e, i=i, half=half: e.tensor_mul(
                    BBT[i][:, :].rearrange("p (a c) -> p a c", c=TILE),
                    BBT[i][:, :].rearrange("p (a c) -> p a c", c=TILE),
                    wgtbc_sb[:, half * TILE:(half + 1) * TILE]
                    .unsqueeze(1).broadcast_to([128, PC, TILE])),
                    reads=[b_wgb], writes=[bb[i]])

            def emit_wgt():
                for blk in range(tpc // 128):
                    p.op("vector", lambda e, blk=blk: e.tensor_tensor(
                        bisc[:, 0:E8], smOwn[:, blk * E8:(blk + 1) * E8],
                        bis8[:, 0:8], OP.is_ge),
                        reads=[b_smOwn, b_bis], writes=[b_bisc])
                    p.op("vector", lambda e, blk=blk: e.tensor_mul(
                        bisc[:, 0:E8], bisc[:, 0:E8],
                        smOwn[:, blk * E8:(blk + 1) * E8]),
                        reads=[b_smOwn], writes=[b_bisc])
                    p.op("vector", lambda e, blk=blk: e.reduce_sum(
                        wgt_sb[:, blk:blk + 1], bisc[:, 0:E8], axis=AX.X),
                        reads=[b_bisc], writes=[b_wgt])
                NBLK = tpc // 128
                p.op("tensor", lambda e: e.transpose(
                    PB7[0:NBLK, 0:128], wgt_sb[:, :], id_sb[:, :]),
                    reads=[b_wgt, b_id], writes=[pb[7]])
                p.op("vector", lambda e: e.tensor_copy(
                    tanh_sb[0:NBLK, 0:128], PB7[0:NBLK, 0:128]),
                    reads=[pb[7]], writes=[b_tanh])
                for blk in range(NBLK):
                    p.op("tensor", lambda e, blk=blk: e.matmul(
                        PB7[0:1, 128:256], id_sb[0:NBLK, blk:blk + 1],
                        tanh_sb[0:NBLK, 0:128], start=True, stop=True),
                        reads=[b_tanh, b_id], writes=[pb[7]])
                    p.op("vector", lambda e, blk=blk: e.tensor_copy(
                        wgtrow[0:1, blk * 128:(blk + 1) * 128],
                        PB7[0:1, 128:256]), reads=[pb[7]], writes=[b_wgr])

            def acc_tile(j):
                tj = j * TILE
                p.dma("scalar", BBT[4][:, :],
                      ffn_scr[:, tj * PC:(tj + TILE) * PC],
                      reads=[d_ffn], writes=[bb[4]], sem="d_rf")
                wgt_bc(tj, 1)
                mul_wgt(4, 1)
                if trunc == "noacc":
                    p.dma("gpsimd", ffn_scr[:, tj * PC:(tj + TILE) * PC],
                          BBT[4][:, :], reads=[bb[4], d_ffn],
                          writes_acc=[d_y], sem="d_ya")
                else:
                    for q in range(4):
                        p.dma("gpsimd",
                              y_fm[:, tj * PC + q * 1024:
                                   tj * PC + (q + 1) * 1024],
                              BBT[4][:, q * 1024:(q + 1) * 1024],
                              reads=[bb[4], d_y], writes_acc=[d_y],
                              sem="d_ya", accum=OP.add)

            def reload_qo(tt):
                t0 = tt * TILE
                p.dma("scalar", BBT[4][:, :],
                      qt_scr[:, t0 * PC:(t0 + TILE) * PC],
                      reads=[d_qt], writes=[bb[4]], sem="d_rq")
                p.dma("scalar", BBT[1][:, :],
                      o_scr[:, t0 * PC:(t0 + TILE) * PC],
                      reads=[d_o], writes=[bb[1]], sem="d_ro")

            def corr_m(tt, m):
                mg = tt * NCH_T + m
                p.op("scalar", lambda e, mg=mg: e.activation(
                    eb_sb[:, :], Bbase[:, mg * PC:(mg + 1) * PC], AF.Exp),
                    reads=[b_Bb], writes=[b_eb])
                p.op("vector", lambda e: e.tensor_mul(
                    SC_sb[:, :].rearrange("p (q c) -> p q c", c=64),
                    Sin_sb[:, :].rearrange("p (q c) -> p q c", c=64),
                    eb_sb[:, :].unsqueeze(2).broadcast_to([128, PC, 64])),
                    reads=[b_Sin, b_eb], writes=[b_SC])
                for half in range(2):
                    def cmm(e, half=half, m=m):
                        last = None
                        for i in range(16):
                            h = half * 16 + i
                            hp = (h % 2) * 64
                            sl = ((h // 2) % 8) * 64
                            last = e.matmul(
                                PB[half][hp:hp + 64, sl:sl + 64],
                                SC_sb[hp:hp + 64,
                                      (h // 2) * 64:(h // 2) * 64 + 64],
                                BBT[4][hp:hp + 64,
                                       (h // 2) * TILE + m * CH:
                                       (h // 2) * TILE + (m + 1) * CH],
                                start=True, stop=True)
                        return last
                    p.op("tensor", cmm, reads=[b_SC, bb[4]],
                         writes=[pb[half]])
                for half in range(2):
                    p.op("vector", lambda e, half=half, m=m: e.tensor_add(
                        BBT[1][:, m * 1024 + half * 512:
                               m * 1024 + half * 512 + 512],
                        BBT[1][:, m * 1024 + half * 512:
                               m * 1024 + half * 512 + 512],
                        PB[half][:, 0:512]),
                        reads=[pb[half]], writes=[bb[1]])

            def ln2(tt):
                vd, vb = (BBT[5], bb[5]) if tt % 2 == 0 else (BBT[7], bb[7])
                p.op("scalar", lambda e: e.activation(
                    BBT[4][:, :], BBT[1][:, :], AF.Square),
                    reads=[bb[1]], writes=[bb[4]])

                def smm1(e):
                    last = None
                    v1 = BBT[1][:, :].rearrange("p (m q c) -> p m q c",
                                                q=PC, c=CH)
                    for k in range(PC):
                        last = e.matmul(PB4[0:1, 0:TILE], ones_sb[:, 0:1],
                                        v1[:, :, k, :],
                                        start=(k == 0), stop=(k == PC - 1))
                    return last
                p.op("tensor", smm1, reads=[bb[1], b_ones], writes=[pb[4]])
                p.op("vector", lambda e: e.tensor_scalar(
                    abrow[0:1, 0:TILE], PB4[0:1, 0:TILE], 1.0 / D, None,
                    OP.mult), reads=[pb[4]], writes=[b_ab])

                def smm2(e):
                    last = None
                    v6 = BBT[4][:, :].rearrange("p (m q c) -> p m q c",
                                                q=PC, c=CH)
                    for k in range(PC):
                        last = e.matmul(PB4[0:1, 256:256 + TILE],
                                        ones_sb[:, 0:1], v6[:, :, k, :],
                                        start=(k == 0), stop=(k == PC - 1))
                    return last
                p.op("tensor", smm2, reads=[bb[4], b_ones], writes=[pb[4]])
                p.op("vector", lambda e: e.tensor_scalar(
                    abrow[0:1, TILE:2 * TILE], PB4[0:1, 256:256 + TILE],
                    1.0 / D, None, OP.mult), reads=[pb[4]], writes=[b_ab])
                p.op("scalar", lambda e: e.activation(
                    abrow[0:1, 2 * TILE:3 * TILE], abrow[0:1, 0:TILE],
                    AF.Square), writes=[b_ab])
                p.op("vector", lambda e: e.tensor_sub(
                    abrow[0:1, TILE:2 * TILE], abrow[0:1, TILE:2 * TILE],
                    abrow[0:1, 2 * TILE:3 * TILE]), writes=[b_ab])
                p.op("scalar", lambda e: e.activation(
                    abrow[0:1, TILE:2 * TILE], abrow[0:1, TILE:2 * TILE],
                    AF.Sqrt, bias=cstc[0:1, 1:2]), writes=[b_ab])
                p.op("vector", lambda e: e.reciprocal(
                    abrow[0:1, TILE:2 * TILE], abrow[0:1, TILE:2 * TILE]),
                    writes=[b_ab])
                p.op("vector", lambda e: e.tensor_mul(
                    abrow[0:1, 2 * TILE:3 * TILE], abrow[0:1, 0:TILE],
                    abrow[0:1, TILE:2 * TILE]), writes=[b_ab])
                p.op("vector", lambda e: e.tensor_scalar(
                    abrow[0:1, 2 * TILE:3 * TILE],
                    abrow[0:1, 2 * TILE:3 * TILE],
                    -1.0, None, OP.mult), writes=[b_ab])
                p.op("tensor", lambda e: e.matmul(
                    PB5[:, 0:TILE], onesr_sb[0:1, :],
                    abrow[0:1, TILE:2 * TILE],
                    start=True, stop=True), reads=[b_ab, b_onesr],
                    writes=[pb[5]])
                p.op("tensor", lambda e: e.matmul(
                    PB5[:, 256:256 + TILE], onesr_sb[0:1, :],
                    abrow[0:1, 2 * TILE:3 * TILE],
                    start=True, stop=True), reads=[b_ab, b_onesr],
                    writes=[pb[5]])
                v1c = BBT[1][:, :].rearrange("p (m q c) -> p m q c",
                                             q=PC, c=CH)
                v5c = vd[:, :].rearrange("p (m q c) -> p m q c", q=PC, c=CH)
                abc = PB5[:, 0:TILE].rearrange("p (m c) -> p m c", c=CH)
                bbc = PB5[:, 256:256 + TILE].rearrange("p (m c) -> p m c",
                                                       c=CH)
                p.op("vector", lambda e, v5c=v5c, v1c=v1c, abc=abc:
                     e.tensor_mul(
                         v5c, v1c,
                         abc.unsqueeze(2).broadcast_to([128, NCH_T, PC, CH])),
                     reads=[bb[1], pb[5]], writes=[vb])
                p.op("vector", lambda e, v5c=v5c, bbc=bbc: e.tensor_add(
                    v5c, v5c,
                    bbc.unsqueeze(2).broadcast_to([128, NCH_T, PC, CH])),
                    reads=[pb[5]], writes=[vb])
                p.op("vector", lambda e, v5c=v5c: e.tensor_mul(
                    v5c, v5c,
                    cols_sb[:, CL["ln_g"] * PC:(CL["ln_g"] + 1) * PC]
                    .unsqueeze(1).unsqueeze(3)
                    .broadcast_to([128, NCH_T, PC, CH])),
                    reads=[b_cols], writes=[vb])
                p.op("vector", lambda e, v5c=v5c: e.tensor_add(
                    v5c, v5c,
                    cols_sb[:, CL["ln_b"] * PC:(CL["ln_b"] + 1) * PC]
                    .unsqueeze(1).unsqueeze(3)
                    .broadcast_to([128, NCH_T, PC, CH])),
                    reads=[b_cols], writes=[vb])

            for t in range(NT):
                tok0 = t * TILE
                if t == 0:
                    for m in range(NCH_T):
                        corr_m(0, m)
                ck("creload")
                ck("corr")
                if t >= FOLD0 and pending:
                    acc_tile(pending.pop(0))
                if t >= FOLD0:
                    p.dma("scalar", BBT[3][:, :],
                          ffn_scr[:, tok0 * PC:(tok0 + TILE) * PC],
                          reads=[d_ffn], writes=[bb[3]], sem="d_rf")
                    wgt_bc(tok0, 0)
                    mul_wgt(3, 0)
                if t == 0:
                    ln2(0)
                ck("ln2")
                vy, by = (BBT[5], bb[5]) if t % 2 == 0 else (BBT[7], bb[7])
                for do in range(PC):
                    ri = rn()
                    bank = 6 + do % 2
                    p.dma("sync", RGT[ri][:, :],
                          ow4[do].rearrange("p a b -> p (a b)"),
                          writes=[rg[ri]], sem="d_w%d" % ri)

                    def wmm(e, ri=ri, bank=bank, vy=vy):
                        last = None
                        v5 = vy[:, :].rearrange("p (m q c) -> p m q c",
                                                q=PC, c=CH)
                        for k in range(PC):
                            last = e.matmul(PB[bank][:, 0:TILE],
                                            RGT[ri][:, k * 128:(k + 1) * 128],
                                            v5[:, :, k, :],
                                            start=(k == 0), stop=(k == PC - 1))
                        return last
                    p.op("tensor", wmm, reads=[rg[ri], by], writes=[pb[bank]])
                    p.op("vector", lambda e, do=do, bank=bank: e.tensor_add(
                        BBT[6][:, do * TILE:(do + 1) * TILE], PB[bank][:, 0:TILE],
                        BBT[2][:, do * TILE:(do + 1) * TILE]),
                        reads=[pb[bank], bb[2]], writes=[bb[6]])
                    if t < FOLD0:
                        tgt = min(BIS_IT, t * BIS_PER
                                  + (BIS_PER * (do + 1) + PC - 1) // PC)
                        while bis_done[0] < tgt:
                            bis_iter()
                            bis_done[0] += 1
                    if t + 1 < NT:
                        if do == 0:
                            reload_qo(t + 1)
                        if do >= 2 and do % 2 == 0 and (do - 2) // 2 < NCH_T:
                            corr_m(t + 1, (do - 2) // 2)
                        if do == 10:
                            ln2(t + 1)
                if t == FOLD0 - 1:
                    while bis_done[0] < BIS_IT:
                        bis_iter()
                        bis_done[0] += 1
                    emit_wgt()
                if t >= FOLD0:
                    p.op("vector", lambda e: e.tensor_add(
                        BBT[6][:, :], BBT[6][:, :], BBT[3][:, :]),
                        reads=[bb[3]], writes=[bb[6]])
                p.dma("scalar", y_fm[:, tok0 * PC:(tok0 + TILE) * PC],
                      BBT[6][:, :], reads=[bb[6]], writes_acc=[d_y], sem="d_y")
                if t + 1 < NT:
                    p.dma("scalar", BBT[2][:, :],
                          x_fm[:, (tok0 + TILE) * PC:(tok0 + 2 * TILE) * PC],
                          writes=[bb[2]], sem="d_rx")

            ck("ow")
            for j in list(pending):
                pending.remove(j)
                acc_tile(j)
            ck("wgt")

        except _Done:
            pass
        if trunc is not None:
            p.dma("scalar", y_fm[:, 0:PC * TILE], BBT[0][:, :],
                  writes_acc=[d_y], sem="d_y")
        p.wait_only("sync", [d_y])

        handles = {}
        cms = []
        for sname in sorted(p.val):
            cm = nc.semaphore(sname)
            handles[sname] = cm.__enter__()
            cms.append(cm)
        for ename, deco in (("tensor", block.tensor), ("vector", block.vector),
                            ("scalar", block.scalar), ("sync", block.sync),
                            ("gpsimd", block.gpsimd)):
            items = p.items[ename]
            waited = {}
            deco(lambda eng, items=items, waited=waited:
                 _emit(eng, items, handles, waited))
        for cm in reversed(cms):
            cm.__exit__(None, None, None)

    return nc


def stage_inputs(inputs, ncores, tpc):
    f32 = np.float32
    x = np.ascontiguousarray(np.asarray(inputs["L0_attn_x"], dtype=f32)[0])
    T = x.shape[0]
    assert T == ncores * tpc

    def c4(w):
        return np.ascontiguousarray(
            w.reshape(PC, 128, PC, 128).transpose(2, 1, 0, 3))

    qw4 = c4(np.asarray(inputs["L0_attn_q_w"], dtype=f32))
    kw4 = c4(np.asarray(inputs["L0_attn_k_w"], dtype=f32))
    ow4 = c4(np.asarray(inputs["L0_attn_o_w"], dtype=f32))
    vw = np.ascontiguousarray(np.asarray(inputs["L0_attn_v_w"], dtype=f32))
    wg = np.zeros((D, DFFP), dtype=f32)
    wg[:, :DFF] = np.asarray(inputs["L0_ffn_w_gate"], dtype=f32)
    wu = np.zeros((D, DFFP), dtype=f32)
    wu[:, :DFF] = np.asarray(inputs["L0_ffn_w_up"], dtype=f32)
    wd = np.zeros((DFFP, D), dtype=f32)
    wd[:DFF, :] = np.asarray(inputs["L0_ffn_w_down"], dtype=f32)
    gw4 = np.ascontiguousarray(
        wg.reshape(PC, 128, NF, 128).transpose(2, 1, 0, 3))
    uw4 = np.ascontiguousarray(
        wu.reshape(PC, 128, NF, 128).transpose(2, 1, 0, 3))
    dw4 = np.ascontiguousarray(
        wd.reshape(NF, 128, PC, 128).transpose(2, 1, 0, 3))
    m2 = np.asarray(inputs["L0_attn_maa_w2"], dtype=f32)
    m2w4 = np.zeros((4, PC, 128, 128), dtype=f32)
    for r in range(4):
        m2w4[r, :, 32 * r:32 * r + 32, :] = \
            m2[r].reshape(32, PC, 128).transpose(1, 0, 2)
    maw1 = np.ascontiguousarray(
        np.asarray(inputs["L0_attn_maa_w1"], dtype=f32)
        .reshape(PC, 128, 128).transpose(1, 0, 2).reshape(128, PC * 128))
    tdw1 = np.ascontiguousarray(
        np.asarray(inputs["L0_attn_td_w1"], dtype=f32)
        .reshape(PC, 128, 64).transpose(1, 0, 2).reshape(128, PC * 64))
    _t2 = np.asarray(inputs["L0_attn_td_w2"], dtype=f32)
    tdw2 = np.ascontiguousarray(np.concatenate([_t2[:, 0:1024], _t2[:, 1024:2048]], axis=0))
    rwm = np.ascontiguousarray(
        np.asarray(inputs["L0_ffn_router_w"], dtype=f32)
        .reshape(PC, 128, E8).transpose(1, 0, 2).reshape(128, PC * E8))

    def vcol(name):
        return np.asarray(inputs[name], dtype=f32).reshape(PC, 128).T

    cols = np.zeros((128, 16 * PC), dtype=f32)
    for i, nm in enumerate(["L0_attn_maa_x", "L0_attn_maa_r", "L0_attn_maa_k",
                            "L0_attn_maa_v", "L0_attn_maa_w",
                            "L0_attn_time_decay", "L0_attn_q_b",
                            "L0_attn_k_b", "L0_attn_ln_g", "L0_attn_ln_b",
                            "L0_ffn_b_down"]):
        cols[:, i * PC:(i + 1) * PC] = vcol(nm)
    bg = np.zeros(DFFP, dtype=f32)
    bg[:DFF] = np.asarray(inputs["L0_ffn_b_gate"], dtype=f32)
    bu = np.zeros(DFFP, dtype=f32)
    bu[:DFF] = np.asarray(inputs["L0_ffn_b_up"], dtype=f32)
    bgu = np.concatenate([bg.reshape(NF, 128).T, bu.reshape(NF, 128).T],
                         axis=1).astype(f32)
    bgu = np.ascontiguousarray(bgu)
    vbrow = np.asarray(inputs["L0_attn_v_b"], dtype=f32).reshape(1, D)
    rbrow = np.asarray(inputs["L0_ffn_router_b"], dtype=f32).reshape(1, E8)
    ident = np.eye(128, dtype=f32)
    m1 = np.triu(np.ones((CH, CH), dtype=f32))
    mask8 = np.tile(m1, (1, 8))
    onesc = np.ones((128, 1), dtype=f32)
    onesr = np.ones((1, 128), dtype=f32)

    n = T * E8
    u32 = uniform_f32(42, n).reshape(T, E8)
    u64 = uniform_f64(42, n).reshape(T, E8)
    du = u64 - u32.astype(np.float64)
    duhi = du.astype(f32)
    dulo = (du - duhi.astype(np.float64)).astype(f32)

    SMW = (tpc // 128) * E8
    in_maps = []
    for c in range(ncores):
        sl = slice(c * tpc, (c + 1) * tpc)
        x_sl = np.ascontiguousarray(x[sl])
        x_fmc = np.ascontiguousarray(
            x_sl.reshape(tpc // TILE, TILE, PC, 128)
            .transpose(3, 0, 2, 1).reshape(128, PC * tpc))
        x_bnd = (np.zeros((1, D), dtype=f32) if c == 0
                 else np.ascontiguousarray(x[c * tpc - 1:c * tpc, :]))
        incl2 = np.zeros((128, 16), dtype=f32)
        for r in range(E8):
            incl2[:, r] = 1.0 if r < c else 0.0
            incl2[:, 8 + r] = 0.0 if r < c else 1.0

        def ublk(a):
            return np.ascontiguousarray(
                a[sl].reshape(tpc // 128, 128, E8).transpose(1, 0, 2)
                .reshape(128, SMW))

        u3 = np.concatenate([ublk(u32), ublk(duhi), ublk(dulo)], axis=1)
        in_maps.append({
            "x_tm": x_sl, "x_fm": x_fmc, "x_bnd": x_bnd,
            "qw4": qw4, "kw4": kw4, "ow4": ow4, "vw": vw,
            "gw4": gw4, "uw4": uw4, "dw4": dw4, "m2w4": m2w4,
            "maw1": maw1, "tdw1": tdw1, "tdw2": tdw2, "rw": rwm,
            "cols": cols, "bgu": bgu, "vbrow": vbrow, "rbrow": rbrow,
            "ident": ident, "mask8": mask8, "onesc": onesc, "onesr": onesr,
            "incl2": incl2, "u3": np.ascontiguousarray(u3),
        })
    return in_maps


def kernel(**inputs):
    ncores = NCORES
    T = np.asarray(inputs["L0_attn_x"]).shape[1]
    tpc = T // ncores
    in_maps = stage_inputs(inputs, ncores, tpc)
    nc = build_nc(ncores, tpc)
    res = run_bass_kernel_spmd(nc, in_maps, core_ids=list(range(ncores)))
    outs = []
    for c in range(ncores):
        yf = np.asarray(res.results[c]["y_fm"]).reshape(
            128, tpc // TILE, PC, TILE)
        outs.append(yf.transpose(1, 3, 2, 0).reshape(tpc, D))
    return np.concatenate(outs, axis=0).reshape(1, T, D).astype(np.float32)


# version 14
# speedup vs baseline: 1.0784x; 1.0005x over previous
import numpy as np

import concourse.bass as bass
import concourse.mybir as mybir
from concourse.bass_utils import run_bass_kernel_spmd

F32 = mybir.dt.float32
AF = mybir.ActivationFunctionType
OP = mybir.AluOpType
AX = mybir.AxisListType

D = 2048
NCORES = 8
TILE = 256
CH = 64
NCH_T = TILE // CH
H = 32
DH = 64
PC = D // 128
DFF = 5461
DFFP = 5504
NF = DFFP // 128
E8 = 8
LN_EPS = 1e-5
REPS = 1e-10
BIS_IT = 64

ENGS = ("tensor", "vector", "scalar", "sync", "gpsimd")


def _tf_round(v0, v1, rot):
    v0 = (v0 + v1).astype(np.uint32)
    v1 = ((v1 << np.uint32(rot)) | (v1 >> np.uint32(32 - rot))).astype(np.uint32)
    return v0, (v0 ^ v1).astype(np.uint32)


def _threefry2x32(k1, k2, x1, x2):
    rots = ([13, 15, 26, 6], [17, 29, 16, 24])
    ks = [np.uint32(k1), np.uint32(k2),
          np.uint32(np.uint32(k1) ^ np.uint32(k2) ^ np.uint32(0x1BD11BDA))]
    x0 = (x1 + ks[0]).astype(np.uint32)
    y0 = (x2 + ks[1]).astype(np.uint32)
    for i in range(5):
        for r in rots[i % 2]:
            x0, y0 = _tf_round(x0, y0, r)
        x0 = (x0 + ks[(i + 1) % 3]).astype(np.uint32)
        y0 = (y0 + ks[(i + 2) % 3] + np.uint32(i + 1)).astype(np.uint32)
    return x0, y0


def _bits(seed, n):
    k1 = np.uint32(np.uint64(seed) >> np.uint64(32))
    k2 = np.uint32(np.uint64(seed) & np.uint64(0xFFFFFFFF))
    idx = np.arange(n, dtype=np.uint64)
    return _threefry2x32(k1, k2, (idx >> np.uint64(32)).astype(np.uint32),
                         (idx & np.uint64(0xFFFFFFFF)).astype(np.uint32))


def uniform_f32(seed, n):
    b1, b2 = _bits(seed, n)
    fb = ((b1 ^ b2).astype(np.uint32) >> np.uint32(9)) | np.uint32(0x3F800000)
    return fb.view(np.float32) - np.float32(1.0)


def uniform_f64(seed, n):
    b1, b2 = _bits(seed, n)
    bits64 = (b1.astype(np.uint64) << np.uint64(32)) | b2.astype(np.uint64)
    fb = (bits64 >> np.uint64(12)) | np.uint64(0x3FF0000000000000)
    return fb.view(np.float64) - np.float64(1.0)


class Buf:
    __slots__ = ("name", "writer", "readers")

    def __init__(self, name):
        self.name = name
        self.writer = None
        self.readers = []


class Prog:
    def __init__(self):
        self.items = {e: [] for e in ENGS}
        self.val = {}
        self.dma_chain = {}

    def op(self, eng, fn, reads=(), writes=(), writes_acc=(), sem=None, amt=1):
        s = sem if sem else "P_" + eng
        if s not in self.val:
            self.val[s] = 0
        waits = {}

        def addw(tok):
            if tok is not None and waits.get(tok[0], 0) < tok[1]:
                waits[tok[0]] = tok[1]

        for b in reads:
            addw(b.writer)
        for b in writes:
            addw(b.writer)
            for t in b.readers:
                addw(t)
        self.val[s] += amt
        tok = (s, self.val[s])
        self.items[eng].append((waits, fn, (s, amt)))
        for b in reads:
            b.readers.append(tok)
        for b in writes:
            b.writer = tok
            b.readers = []
        for b in writes_acc:
            b.writer = tok
        return tok

    def dma(self, eng, out_ap, in_ap, reads=(), writes=(), writes_acc=(),
            sem="d_gen", accum=None, slow=False):
        if accum is None:
            fn = (lambda e, o=out_ap, i=in_ap, sl=slow:
                  e.dma_start(out=o, in_=i, allow_slow_non_contiguous=sl))
        else:
            fn = (lambda e, o=out_ap, i=in_ap, a=accum, sl=slow:
                  e.dma_start(out=o, in_=i, accum_op=a,
                              allow_slow_non_contiguous=sl))
        ch = self.dma_chain.get(sem)
        if ch is None:
            ch = Buf("chain_" + sem)
            self.dma_chain[sem] = ch
        tok = self.op(eng, fn, list(reads) + [ch], writes, writes_acc,
                      sem=sem, amt=16)
        ch.writer = tok
        ch.readers = []
        return tok

    def wait_only(self, eng, bufs):
        waits = {}
        for b in bufs:
            if b.writer is not None:
                sm, v = b.writer
                if waits.get(sm, 0) < v:
                    waits[sm] = v
                if sm not in self.val:
                    self.val[sm] = v
        self.items[eng].append((waits, None, None))


def _emit(eng_obj, items, handles, waited):
    for waits, fn, inc in items:
        for sm in sorted(waits):
            v = waits[sm]
            if waited.get(sm, 0) >= v:
                continue
            eng_obj.wait_ge(handles[sm], v)
            waited[sm] = v
        if fn is None:
            continue
        inst = fn(eng_obj)
        if inc is not None:
            inst.then_inc(handles[inc[0]], inc[1])


def build_nc(ncores, tpc, dbg=False, trunc=None):
    attn_lvl = 9
    if trunc is not None and trunc.startswith("a"):
        attn_lvl = int(trunc[1:])
        trunc = "t7"
    NT = tpc // TILE
    NCHUNK = NT * NCH_T
    TB = TILE // 128
    SMW = (tpc // 128) * E8
    GW = ncores * SMW
    NIDX = GW // E8
    KSEL = ncores * tpc // E8

    nc = bass.Bass("TRN2", target_bir_lowering=False)
    p = Prog()

    class _Done(Exception):
        pass

    def ck(stage):
        if trunc == stage:
            raise _Done()

    if trunc == "noacc":
        pass

    def din(name, shape):
        return nc.dram_tensor(name, list(shape), F32, kind="ExternalInput").ap()

    x_tm = din("x_tm", (tpc, D))
    x_fm = din("x_fm", (128, PC * tpc))
    x_bnd = din("x_bnd", (1, D))
    qw4 = din("qw4", (PC, 128, PC, 128))
    kw4 = din("kw4", (PC, 128, PC, 128))
    ow4 = din("ow4", (PC, 128, PC, 128))
    vw = din("vw", (D, D))
    gw4 = din("gw4", (NF, 128, PC, 128))
    uw4 = din("uw4", (NF, 128, PC, 128))
    dw4 = din("dw4", (PC, 128, NF, 128))
    m2w4 = din("m2w4", (4, PC, 128, 128))
    maw1 = din("maw1", (128, PC * 128))
    tdw1 = din("tdw1", (128, PC * 64))
    tdw2 = din("tdw2", (128, 1024))
    rw = din("rw", (128, PC * E8))
    cols_d = din("cols", (128, 16 * PC))
    bgu = din("bgu", (128, 2 * NF))
    vbrow = din("vbrow", (1, D))
    rbrow = din("rbrow", (1, E8))
    ident = din("ident", (128, 128))
    mask8d = din("mask8", (64, 512))
    onescd = din("onesc", (128, 1))
    onesrd = din("onesr", (1, 128))
    incl2 = din("incl2", (128, 16))
    u3 = din("u3", (128, 3 * SMW))

    y_fm = nc.dram_tensor("y_fm", [128, PC * tpc], F32, kind="ExternalOutput").ap()

    qt_scr = nc.dram_tensor("qt_scr", [128, PC * tpc], F32).ap()
    o_scr = nc.dram_tensor("o_scr", [128, PC * tpc], F32).ap()
    ffn_scr = nc.dram_tensor("ffn_scr", [128, PC * tpc], F32).ap()
    bnd_scr = nc.dram_tensor("bnd_scr", [1, D], F32).ap()
    if dbg:
        dbg_wlog = nc.dram_tensor("dbg_wlog", [128, PC * tpc], F32).ap()
        dbg_xw = nc.dram_tensor("dbg_xw", [128, PC * tpc], F32).ap()
        dbg_oc = nc.dram_tensor("dbg_oc", [128, PC * tpc], F32).ap()
        dbg_y = nc.dram_tensor("dbg_y", [128, PC * tpc], F32).ap()
        d_dbg = Buf("d_dbg")
    ccsm_i = nc.dram_tensor("ccsm_i", [128, SMW], F32).ap()
    ccsm_o = nc.dram_tensor("ccsm_o", [128 * ncores, SMW], F32).ap()
    ccsu_i = nc.dram_tensor("ccsu_i", [128, 1040], F32).ap()
    ccsu_o = nc.dram_tensor("ccsu_o", [128 * ncores, 1040], F32).ap()

    d_qt = Buf("d_qt"); d_o = Buf("d_o"); d_ffn = Buf("d_ffn")
    d_bnd = Buf("d_bnd"); d_y = Buf("d_y")
    d_ci = Buf("d_ci"); d_co = Buf("d_co")
    d_si = Buf("d_si"); d_so = Buf("d_so")

    from contextlib import ExitStack
    with ExitStack() as _st:
        BB0 = _st.enter_context(nc.sbuf_tensor("BB0", [128, PC * TILE], F32))
        BB1 = _st.enter_context(nc.sbuf_tensor("BB1", [128, PC * TILE], F32))
        BB2 = _st.enter_context(nc.sbuf_tensor("BB2", [128, PC * TILE], F32))
        BB3 = _st.enter_context(nc.sbuf_tensor("BB3", [128, PC * TILE], F32))
        BB4 = _st.enter_context(nc.sbuf_tensor("BB4", [128, PC * TILE], F32))
        BB5 = _st.enter_context(nc.sbuf_tensor("BB5", [128, PC * TILE], F32))
        BB6 = _st.enter_context(nc.sbuf_tensor("BB6", [128, PC * TILE], F32))
        BB7 = _st.enter_context(nc.sbuf_tensor("BB7", [128, PC * TILE], F32))
        RG0 = _st.enter_context(nc.sbuf_tensor("RG0", [128, 2048], F32))
        RG1 = _st.enter_context(nc.sbuf_tensor("RG1", [128, 2048], F32))
        RG2 = _st.enter_context(nc.sbuf_tensor("RG2", [128, 2048], F32))
        RG3 = _st.enter_context(nc.sbuf_tensor("RG3", [128, 2048], F32))
        S_sb = _st.enter_context(nc.sbuf_tensor("S_sb", [128, H * 32], F32))
        Sin_sb = _st.enter_context(nc.sbuf_tensor("Sin_sb", [128, H * 32], F32))
        SC_sb = _st.enter_context(nc.sbuf_tensor("SC_sb", [128, H * 32], F32))
        rw_sb = _st.enter_context(nc.sbuf_tensor("rw_sb", [128, PC * E8], F32))
        vbbc_sb = BB7[:, 0:D]
        rbbc_sb = _st.enter_context(nc.sbuf_tensor("rbbc_sb", [128, E8], F32))
        cols_sb = _st.enter_context(nc.sbuf_tensor("cols_sb", [128, 16 * PC], F32))
        bgu_sb = _st.enter_context(nc.sbuf_tensor("bgu_sb", [128, 2 * NF], F32))
        id_sb = _st.enter_context(nc.sbuf_tensor("id_sb", [128, 128], F32))
        mask8_sb = _st.enter_context(nc.sbuf_tensor("mask8_sb", [64, 512], F32))
        ones_sb = _st.enter_context(nc.sbuf_tensor("ones_sb", [128, 1], F32))
        onesr_sb = _st.enter_context(nc.sbuf_tensor("onesr_sb", [1, 128], F32))
        incl_sb = _st.enter_context(nc.sbuf_tensor("incl_sb", [128, 16], F32))
        gum_sb = _st.enter_context(nc.sbuf_tensor("gum_sb", [128, SMW], F32))
        smOwn = _st.enter_context(nc.sbuf_tensor("smOwn", [128, SMW], F32))
        flag_sb = _st.enter_context(nc.sbuf_tensor("flag_sb", [128, 1], F32))
        qb8_sb = _st.enter_context(nc.sbuf_tensor("qb8_sb", [128, PC], F32))
        Bbase = _st.enter_context(nc.sbuf_tensor("Bbase", [128, (NCHUNK + 1) * PC], F32))
        Bcarry = _st.enter_context(nc.sbuf_tensor("Bcarry", [128, PC], F32))
        Dcols = _st.enter_context(nc.sbuf_tensor("Dcols", [128, NCH_T * PC], F32))
        eb_sb = _st.enter_context(nc.sbuf_tensor("eb_sb", [128, PC], F32))
        xnlast = _st.enter_context(nc.sbuf_tensor("xnlast", [128, PC], F32))
        tanh_sb = _st.enter_context(nc.sbuf_tensor("tanh_sb", [128, TILE], F32))
        wtd_sb = _st.enter_context(nc.sbuf_tensor("wtd_sb", [128, TILE], F32))
        statc = _st.enter_context(nc.sbuf_tensor("statc", [128, 24], F32))
        wgt_sb = _st.enter_context(nc.sbuf_tensor("wgt_sb", [128, tpc // 128], F32))
        wgtbc_sb = _st.enter_context(nc.sbuf_tensor("wgtbc_sb", [128, 2 * TILE], F32))
        ones128 = _st.enter_context(nc.sbuf_tensor("ones128", [128, 128], F32))
        bis8 = _st.enter_context(nc.sbuf_tensor("bis8", [128, 48], F32))
        dsl_sb = _st.enter_context(nc.sbuf_tensor("dsl_sb", [128, PC], F32))
        cstc = _st.enter_context(nc.sbuf_tensor("cstc", [128, 2], F32))
        ROWA = _st.enter_context(nc.sbuf_tensor("ROWA", [1, 2048], F32))
        ROWB = _st.enter_context(nc.sbuf_tensor("ROWB", [1, 2048], F32))
        PB0 = _st.enter_context(nc.psum_tensor("PB0", [128, 512], F32))
        PB1 = _st.enter_context(nc.psum_tensor("PB1", [128, 512], F32))
        PB2 = _st.enter_context(nc.psum_tensor("PB2", [128, 512], F32))
        PB3 = _st.enter_context(nc.psum_tensor("PB3", [128, 512], F32))
        PB4 = _st.enter_context(nc.psum_tensor("PB4", [128, 512], F32))
        PB5 = _st.enter_context(nc.psum_tensor("PB5", [128, 512], F32))
        PB6 = _st.enter_context(nc.psum_tensor("PB6", [128, 512], F32))
        PB7 = _st.enter_context(nc.psum_tensor("PB7", [128, 512], F32))
        block = _st.enter_context(nc.Block())
        smAll = BB0[:, 0:GW]
        bisc = BB0[:, 1024:1024 + GW]
        midbc = BB0[:, 2048:2048 + GW]
        u3_sb = BB2[:, 0:3 * SMW]
        Ur_sb = BB6[:, 0:1040]
        wgtrow = ROWB
        vrow_sb = ROWA
        bndrow = ROWB
        abrow = ROWA
        PB = [PB0, PB1, PB2, PB3, PB4, PB5, PB6, PB7]
        pb = [Buf("PB%d" % i) for i in range(8)]
        BBT = [BB0, BB1, BB2, BB3, BB4, BB5, BB6, BB7]
        bb = [Buf("BB%d" % i) for i in range(8)]
        RGT = [RG0, RG1, RG2, RG3]
        rg = [Buf("RG%d" % i) for i in range(4)]

        b_S = Buf("S"); b_Sin = Buf("Sin"); b_SC = Buf("SC")
        b_smAll = bb[0]; b_bisc = bb[0]; b_midbc = bb[0]
        b_Ur = bb[6]
        b_rw = Buf("rw"); b_vbbc = bb[7]
        b_rbbc = Buf("rbbc"); b_cols = Buf("cols"); b_bgu = Buf("bgu")
        b_id = Buf("id"); b_mask8 = Buf("mask8"); b_ones = Buf("ones")
        b_onesr = Buf("onesr"); b_incl = Buf("incl"); b_u3 = bb[2]
        b_gum = Buf("gum"); b_smOwn = Buf("smOwn"); b_flag = Buf("flag")
        b_qb8 = Buf("qb8"); b_Bb = Buf("Bbase"); b_Bc = Buf("Bcarry")
        b_Dc = Buf("Dcols"); b_eb = Buf("eb"); b_xnl = Buf("xnlast")
        b_tanh = Buf("tanh"); b_wtd = Buf("wtd"); b_st = Buf("statc")
        b_wgt = Buf("wgt"); b_wgb = Buf("wgtbc")
        b_bis = Buf("bis8"); b_vr = Buf("rowa"); b_bnd = Buf("rowb")
        b_ab = b_vr; b_wgr = b_bnd; b_dsl = Buf("dsl")

        CL = {"maa_x": 0, "maa_r": 1, "maa_k": 2, "maa_v": 3, "maa_w": 4,
              "tdec": 5, "q_b": 6, "k_b": 7, "ln_g": 8, "ln_b": 9, "b_dn": 10}

        def col(nm, pcc):
            i = CL[nm] * PC + pcc
            return cols_sb[:, i:i + 1]

        wring = [0]
        held = set()

        def rn():
            for _ in range(8):
                i = wring[0] % 4
                wring[0] += 1
                if i not in held:
                    return i
            raise RuntimeError("ring exhausted")

        def v3(i, b=TILE):
            return BBT[i][:, :].rearrange("p (a c) -> p a c", c=b)

        Bb3 = Bbase[:, :].rearrange("p (m c) -> p m c", c=PC)

        try:
            LD = "d_c"
            p.dma("sync", cols_sb[:, :], cols_d[:, :], writes=[b_cols], sem=LD)
            p.dma("sync", bgu_sb[:, :], bgu[:, :], writes=[b_bgu], sem=LD)
            p.dma("sync", id_sb[:, :], ident[:, :], writes=[b_id], sem=LD)
            p.dma("sync", mask8_sb[:, :], mask8d[:, :], writes=[b_mask8], sem=LD)
            p.dma("sync", ones_sb[:, :], onescd[:, :], writes=[b_ones], sem=LD)
            p.dma("sync", onesr_sb[:, :], onesrd[:, :], writes=[b_onesr], sem=LD)
            p.dma("sync", incl_sb[:, :], incl2[:, :], writes=[b_incl], sem=LD)
            p.dma("sync", u3_sb[:, :], u3[:, :], writes=[b_u3], sem=LD)
            p.dma("sync", rw_sb[:, :], rw[:, :], writes=[b_rw], sem=LD)
            p.dma("sync", bndrow[:, :], x_bnd[:, :], writes=[b_bnd], sem=LD)
            p.dma("sync", vrow_sb[:, :], vbrow[:, :], writes=[b_vr], sem=LD)
            p.dma("sync", statc[0:1, 0:E8], rbrow[:, :], writes=[b_st], sem=LD)

            b_cst = Buf("cst")
            p.op("vector", lambda e: e.memset(cstc[:, 0:1], REPS), writes=[b_cst])
            p.op("vector", lambda e: e.memset(cstc[:, 1:2], LN_EPS), writes=[b_cst])
            b_o128 = Buf("o128")
            p.op("vector", lambda e: e.memset(ones128[:, :], 1.0), writes=[b_o128])

            p.op("vector", lambda e: e.memset(flag_sb[:, :], 1.0), writes=[b_flag])
            p.op("vector", lambda e: e.tensor_scalar(
                flag_sb[:, :], flag_sb[:, :], 2.0 ** -30, None, OP.add),
                writes=[b_flag])
            p.op("vector", lambda e: e.tensor_scalar(
                flag_sb[:, :], flag_sb[:, :], -1.0, 2.0 ** 30, OP.add, OP.mult),
                writes=[b_flag])
            p.op("vector", lambda e: e.tensor_scalar(
                qb8_sb[:, :], cols_sb[:, CL["q_b"] * PC:(CL["q_b"] + 1) * PC],
                0.125, None, OP.mult), reads=[b_cols], writes=[b_qb8])
            p.op("vector", lambda e: e.scalar_tensor_tensor(
                u3_sb[:, 0:SMW], u3_sb[:, SMW:2 * SMW], flag_sb[:, 0:1],
                u3_sb[:, 0:SMW], OP.mult, OP.add), reads=[b_flag], writes=[b_u3])
            p.op("vector", lambda e: e.scalar_tensor_tensor(
                u3_sb[:, 0:SMW], u3_sb[:, 2 * SMW:3 * SMW], flag_sb[:, 0:1],
                u3_sb[:, 0:SMW], OP.mult, OP.add), reads=[b_flag], writes=[b_u3])
            p.op("scalar", lambda e: e.activation(
                gum_sb[:, :], u3_sb[:, 0:SMW], AF.Ln, bias=cstc[:, 0:1]),
                reads=[b_u3], writes=[b_gum])
            p.op("scalar", lambda e: e.activation(
                gum_sb[:, :], gum_sb[:, :], AF.Ln, bias=cstc[:, 0:1], scale=-1.0),
                writes=[b_gum])
            for i in range(4):
                p.op("tensor", lambda e, i=i: e.matmul(
                    PB0[:, 0:512], onesr_sb[0:1, :], vrow_sb[0:1, i * 512:(i + 1) * 512],
                    start=True, stop=True), reads=[b_vr, b_onesr], writes=[pb[0]])
                p.op("vector", lambda e, i=i: e.tensor_copy(
                    vbbc_sb[:, i * 512:(i + 1) * 512], PB0[:, 0:512]),
                    reads=[pb[0]], writes=[b_vbbc])
            p.op("tensor", lambda e: e.matmul(
                PB1[:, 0:E8], onesr_sb[0:1, :], statc[0:1, 0:E8],
                start=True, stop=True), reads=[b_st, b_onesr], writes=[pb[1]])
            p.op("vector", lambda e: e.tensor_copy(rbbc_sb[:, :], PB1[:, 0:E8]),
                 reads=[pb[1]], writes=[b_rbbc])

            p.op("vector", lambda e: e.reduce_sum(
                statc[0:1, 16:17], bndrow[0:1, :], axis=AX.X),
                reads=[b_bnd], writes=[b_st])
            p.op("scalar", lambda e: e.activation(
                BBT[1][0:1, 0:D], bndrow[0:1, 0:D],
                AF.Square, accum_out=statc[0:1, 17:18]),
                reads=[b_bnd], writes=[bb[1], b_st])
            p.op("vector", lambda e: e.tensor_scalar(
                statc[0:1, 16:18], statc[0:1, 16:18], 1.0 / D, None, OP.mult),
                writes=[b_st])
            p.op("scalar", lambda e: e.activation(
                statc[0:1, 18:19], statc[0:1, 16:17], AF.Square), reads=[b_st],
                writes=[b_st])
            p.op("vector", lambda e: e.tensor_sub(
                statc[0:1, 19:20], statc[0:1, 17:18], statc[0:1, 18:19]),
                writes=[b_st])
            p.op("scalar", lambda e: e.activation(
                statc[0:1, 20:21], statc[0:1, 19:20], AF.Sqrt, bias=cstc[0:1, 1:2]),
                writes=[b_st])
            p.op("vector", lambda e: e.reciprocal(
                statc[0:1, 21:22], statc[0:1, 20:21]), writes=[b_st])
            p.op("vector", lambda e: e.tensor_scalar(
                bndrow[0:1, :], bndrow[0:1, :], statc[0:1, 16:17],
                statc[0:1, 21:22], OP.subtract, OP.mult),
                reads=[b_st], writes=[b_bnd])
            for pcc in range(PC):
                p.op("tensor", lambda e, pcc=pcc: e.matmul(
                    PB2[:, pcc:pcc + 1], bndrow[0:1, pcc * 128:(pcc + 1) * 128],
                    ones_sb[0:1, 0:1], start=True, stop=True),
                    reads=[b_bnd, b_ones], writes=[pb[2]])
            p.op("vector", lambda e: e.tensor_copy(
                xnlast[:, :], PB2[:, 0:PC]), reads=[pb[2]], writes=[b_xnl])

            p.op("vector", lambda e: e.memset(S_sb[:, :], 0.0), writes=[b_S])
            p.op("vector", lambda e: e.memset(Bcarry[:, :], 0.0), writes=[b_Bc])
            p.op("vector", lambda e: e.memset(Bbase[:, 0:PC], 0.0), writes=[b_Bb])

            ck("pre")
            def mix_into_bb2(mname, r, dst_i=2):
                for pcc in range(PC):
                    ri = rn()
                    bank = pcc % 2
                    p.dma("sync", RGT[ri][:, 0:128], m2w4[r, pcc],
                          writes=[rg[ri]], sem="d_w%d" % ri)
                    p.op("tensor", lambda e, ri=ri, bank=bank: e.matmul(
                        PB[bank][:, 0:TILE], RGT[ri][:, 0:128], tanh_sb[:, :],
                        start=True, stop=True),
                        reads=[rg[ri], b_tanh], writes=[pb[bank]])
                    p.op("vector", lambda e, pcc=pcc, bank=bank, mname=mname,
                         dst_i=dst_i: e.scalar_tensor_tensor(
                             BBT[dst_i][:, pcc * TILE:(pcc + 1) * TILE],
                             PB[bank][:, 0:TILE], col(mname, pcc),
                             BBT[1][:, pcc * TILE:(pcc + 1) * TILE],
                             OP.add, OP.mult),
                         reads=[pb[bank], b_cols, bb[1]], writes=[bb[dst_i]])
                    p.op("vector", lambda e, pcc=pcc, dst_i=dst_i: e.tensor_add(
                        BBT[dst_i][:, pcc * TILE:(pcc + 1) * TILE],
                        BBT[dst_i][:, pcc * TILE:(pcc + 1) * TILE],
                        BBT[0][:, pcc * TILE:(pcc + 1) * TILE]),
                        reads=[bb[0]], writes=[bb[dst_i]])

            def proj_2048(w4, dst_i, evac):
                for do in range(PC):
                    ri = rn()
                    bank = do % 2
                    p.dma("sync", RGT[ri][:, :],
                          w4[do].rearrange("p a b -> p (a b)"),
                          writes=[rg[ri]], sem="d_w%d" % ri)

                    def mmg(e, ri=ri, bank=bank):
                        last = None
                        for k in range(PC):
                            last = e.matmul(PB[bank][:, 0:TILE],
                                            RGT[ri][:, k * 128:(k + 1) * 128],
                                            BBT[2][:, k * TILE:(k + 1) * TILE],
                                            start=(k == 0), stop=(k == PC - 1))
                        return last
                    p.op("tensor", mmg, reads=[rg[ri], bb[2]], writes=[pb[bank]])
                    evac(do, bank)

            def bis_iter():
                p.op("vector", lambda e: e.tensor_tensor(
                    bisc[:, :], smAll[:, :], midbc[:, :], OP.is_ge),
                    reads=[b_smAll, b_midbc], writes=[b_bisc])
                p.op("vector", lambda e: e.tensor_reduce(
                    bis8[:, 24:32].rearrange("p (a c) -> p a c", c=1),
                    bisc[:, :].rearrange("p (i e) -> p i e", e=E8).transpose([0, 2, 1]),
                    AX.X, OP.add), reads=[b_bisc], writes=[b_bis])
                p.op("tensor", lambda e: e.matmul(
                    PB3[:, 0:E8], ones128[:, :], bis8[:, 24:32],
                    start=True, stop=True), reads=[b_bis, b_o128], writes=[pb[3]])
                p.op("vector", lambda e: e.tensor_copy(
                    bis8[:, 32:40], PB3[:, 0:E8]), reads=[pb[3]], writes=[b_bis])
                p.op("vector", lambda e: e.tensor_scalar(
                    bis8[:, 32:40], bis8[:, 32:40], float(KSEL), None, OP.is_ge),
                    writes=[b_bis])
                p.op("vector", lambda e: e.tensor_sub(
                    bis8[:, 40:48], bis8[:, 16:24], bis8[:, 0:8]), writes=[b_bis])
                p.op("vector", lambda e: e.tensor_mul(
                    bis8[:, 40:48], bis8[:, 40:48], bis8[:, 32:40]), writes=[b_bis])
                p.op("vector", lambda e: e.tensor_add(
                    bis8[:, 0:8], bis8[:, 0:8], bis8[:, 40:48]), writes=[b_bis])
                p.op("vector", lambda e: e.tensor_sub(
                    bis8[:, 40:48], bis8[:, 16:24], bis8[:, 8:16]), writes=[b_bis])
                p.op("vector", lambda e: e.tensor_scalar(
                    bis8[:, 32:40], bis8[:, 32:40], -1.0, 1.0, OP.mult, OP.add),
                    writes=[b_bis])
                p.op("vector", lambda e: e.tensor_mul(
                    bis8[:, 40:48], bis8[:, 40:48], bis8[:, 32:40]), writes=[b_bis])
                p.op("vector", lambda e: e.tensor_add(
                    bis8[:, 8:16], bis8[:, 8:16], bis8[:, 40:48]), writes=[b_bis])
                p.op("vector", lambda e: e.tensor_add(
                    bis8[:, 16:24], bis8[:, 0:8], bis8[:, 8:16]), writes=[b_bis])
                p.op("vector", lambda e: e.tensor_scalar(
                    bis8[:, 16:24], bis8[:, 16:24], 0.5, None, OP.mult),
                    writes=[b_bis])
                p.op("vector", lambda e: e.tensor_copy(
                    midbc[:, :].rearrange("p (i e) -> p i e", e=E8),
                    bis8[:, 16:24].unsqueeze(1).broadcast_to([128, GW // E8, E8])),
                    reads=[b_bis], writes=[b_midbc])

            def fold_round(r):
                off = 0 if r % 2 == 0 else 2048
                p.dma("gpsimd", BBT[3][:, off:off + 1040],
                      ccsu_o[r * 128:(r + 1) * 128, :],
                      reads=[d_so], writes=[bb[3]], sem="d_url")
                p.op("vector", lambda e, r=r, off=off: e.scalar_tensor_tensor(
                    dsl_sb[:, :], BBT[3][:, off + 1024:off + 1040],
                    incl_sb[:, r:r + 1],
                    incl_sb[:, 8 + r:8 + r + 1].broadcast_to([128, PC]),
                    OP.mult, OP.add), reads=[bb[3], b_incl], writes=[b_dsl])
                p.op("vector", lambda e: e.tensor_mul(
                    Sin_sb[:, :].rearrange("p (q c) -> p q c", c=64),
                    Sin_sb[:, :].rearrange("p (q c) -> p q c", c=64),
                    dsl_sb[:, :].unsqueeze(2).broadcast_to([128, PC, 64])),
                    reads=[b_dsl], writes=[b_Sin])
                p.op("vector", lambda e, r=r, off=off: e.scalar_tensor_tensor(
                    Sin_sb[:, :], BBT[3][:, off:off + 1024],
                    incl_sb[:, r:r + 1],
                    Sin_sb[:, :], OP.mult, OP.add),
                    reads=[bb[3], b_incl], writes=[b_Sin])

            def c0_sm():
                for r in range(ncores):
                    p.dma("gpsimd", smAll[:, r * SMW:(r + 1) * SMW],
                          ccsm_o[r * 128:(r + 1) * 128, :],
                          reads=[d_co], writes=[b_smAll], sem="d_sml")

            c0_tasks = [c0_sm,
                        lambda: p.op("vector", lambda e: e.memset(
                            Sin_sb[:, :], 0.0), writes=[b_Sin])]
            for r in range(ncores - 1):
                c0_tasks.append(lambda r=r: fold_round(r))

            def stage_T12(t):
                tok0 = t * TILE

                for tb in range(TB):
                    dst = BBT[2][:, tb * D:(tb + 1) * D]
                    p.dma("gpsimd", dst,
                          x_tm[tok0 + tb * 128: tok0 + (tb + 1) * 128, :],
                          writes=[bb[2]], sem="d_x%d" % tb)
                    sc = 8 * tb
                    p.op("vector", lambda e, dst=dst, sc=sc: e.reduce_sum(
                        statc[:, sc:sc + 1], dst, axis=AX.X),
                        reads=[bb[2]], writes=[b_st])
                    p.op("scalar", lambda e, dst=dst, sc=sc, tb=tb: e.activation(
                        BBT[1][:, tb * D:(tb + 1) * D], dst, AF.Square,
                        accum_out=statc[:, sc + 1:sc + 2]),
                        reads=[bb[2]], writes=[bb[1], b_st])
                    p.op("vector", lambda e, sc=sc: e.tensor_scalar(
                        statc[:, sc + 2:sc + 4], statc[:, sc:sc + 2], 1.0 / D,
                        None, OP.mult), writes=[b_st])
                    p.op("scalar", lambda e, sc=sc: e.activation(
                        statc[:, sc + 4:sc + 5], statc[:, sc + 2:sc + 3],
                        AF.Square), writes=[b_st])
                    p.op("vector", lambda e, sc=sc: e.tensor_sub(
                        statc[:, sc + 5:sc + 6], statc[:, sc + 3:sc + 4],
                        statc[:, sc + 4:sc + 5]), writes=[b_st])
                    p.op("scalar", lambda e, sc=sc: e.activation(
                        statc[:, sc + 6:sc + 7], statc[:, sc + 5:sc + 6], AF.Sqrt,
                        bias=cstc[:, 1:2]), writes=[b_st])
                    p.op("vector", lambda e, sc=sc: e.reciprocal(
                        statc[:, sc + 7:sc + 8], statc[:, sc + 6:sc + 7]),
                        writes=[b_st])
                    p.op("vector", lambda e, dst=dst, sc=sc: e.tensor_scalar(
                        dst, dst, statc[:, sc + 2:sc + 3], statc[:, sc + 7:sc + 8],
                        OP.subtract, OP.mult), reads=[b_st], writes=[bb[2]])


            def stage_T34(t):
                tok0 = t * TILE
                for tb in range(TB):
                    for g in range(4):
                        def tg(e, tb=tb, g=g):
                            last = None
                            for j in range(4):
                                pcx = g * 4 + j
                                last = e.transpose(
                                    PB6[:, j * 128:(j + 1) * 128],
                                    BBT[2][:, tb * D + pcx * 128: tb * D + (pcx + 1) * 128],
                                    id_sb[:, :])
                            return last
                        p.op("tensor", tg, reads=[bb[2], b_id], writes=[pb[6]])
                        p.op("vector", lambda e, tb=tb, g=g: e.tensor_copy(
                            v3(0)[:, g * 4:(g + 1) * 4, tb * 128:(tb + 1) * 128],
                            PB6[:, 0:512].rearrange("p (a c) -> p a c", c=128)),
                            reads=[pb[6]], writes=[bb[0]])

                p.op("vector", lambda e: e.tensor_copy(
                    statc[:, 16:16 + PC] if False else dsl_sb[:, :],
                    v3(0)[:, :, TILE - 1:TILE].rearrange("p a c -> p (a c)")),
                    reads=[bb[0]], writes=[b_dsl])
                p.op("vector", lambda e: e.tensor_sub(
                    v3(1)[:, :, 1:TILE], v3(0)[:, :, 0:TILE - 1],
                    v3(0)[:, :, 1:TILE]), reads=[bb[0]], writes=[bb[1]])
                p.op("vector", lambda e: e.tensor_sub(
                    v3(1)[:, :, 0:1], xnlast[:, :].unsqueeze(2),
                    v3(0)[:, :, 0:1]), reads=[bb[0], b_xnl], writes=[bb[1]])
                p.op("vector", lambda e: e.tensor_copy(xnlast[:, :], dsl_sb[:, :]),
                     reads=[b_dsl], writes=[b_xnl])

                ck("t3")
                for tb in range(TB):
                    blk = t * TB + tb

                    def rmm(e, tb=tb):
                        last = None
                        for k in range(PC):
                            last = e.matmul(
                                PB7[:, 0:E8],
                                BBT[0][:, k * TILE + tb * 128: k * TILE + (tb + 1) * 128],
                                rw_sb[:, k * E8:(k + 1) * E8],
                                start=(k == 0), stop=(k == PC - 1))
                        return last
                    p.op("tensor", rmm, reads=[bb[0], b_rw], writes=[pb[7]])
                    sm = smOwn[:, blk * E8:(blk + 1) * E8]
                    p.op("vector", lambda e, sm=sm: e.tensor_add(
                        sm, PB7[:, 0:E8], rbbc_sb[:, :]),
                        reads=[pb[7], b_rbbc], writes=[b_smOwn])
                    p.op("vector", lambda e, sm=sm, blk=blk: e.tensor_sub(
                        sm, sm, gum_sb[:, blk * E8:(blk + 1) * E8]),
                        reads=[b_gum], writes=[b_smOwn])
                    p.op("vector", lambda e, sm=sm: e.reduce_max(
                        statc[:, 16:17], sm, axis=AX.X, negate=True),
                        reads=[b_smOwn], writes=[b_st])
                    p.op("scalar", lambda e, sm=sm: e.activation(
                        sm, sm, AF.Exp, bias=statc[:, 16:17],
                        accum_out=statc[:, 17:18]),
                        reads=[b_st], writes=[b_smOwn, b_st])
                    p.op("vector", lambda e: e.reciprocal(
                        statc[:, 18:19], statc[:, 17:18]), writes=[b_st])
                    p.op("vector", lambda e, sm=sm: e.tensor_scalar(
                        sm, sm, statc[:, 18:19], None, OP.mult),
                        reads=[b_st], writes=[b_smOwn])


            stage_T12(0)
            stage_T34(0)

            for t in range(NT):
                tok0 = t * TILE
                ck("t4")
                ri_m = rn()
                p.dma("sync", RGT[ri_m][:, :], maw1[:, :], writes=[rg[ri_m]],
                      sem="d_w%d" % ri_m)
                ri_t = rn()
                held.add(ri_t)
                p.dma("sync", RGT[ri_t][:, 0:1024], tdw1[:, :], writes=[rg[ri_t]],
                      sem="d_w%d" % ri_t)
                p.dma("sync", RGT[ri_t][:, 1024:2048], tdw2[:, :],
                      writes_acc=[rg[ri_t]], sem="d_w%d" % ri_t)
                for pcc in range(PC):
                    p.op("vector", lambda e, pcc=pcc: e.scalar_tensor_tensor(
                        BBT[2][:, pcc * TILE:(pcc + 1) * TILE],
                        BBT[1][:, pcc * TILE:(pcc + 1) * TILE],
                        col("maa_x", pcc),
                        BBT[0][:, pcc * TILE:(pcc + 1) * TILE],
                        OP.mult, OP.add),
                        reads=[bb[1], bb[0], b_cols], writes=[bb[2]])

                def mm_tanh(e, ri=ri_m):
                    last = None
                    for k in range(PC):
                        last = e.matmul(PB0[:, 0:TILE],
                                        RGT[ri][:, k * 128:(k + 1) * 128],
                                        BBT[2][:, k * TILE:(k + 1) * TILE],
                                        start=(k == 0), stop=(k == PC - 1))
                    return last
                p.op("tensor", mm_tanh, reads=[bb[2], rg[ri_m]], writes=[pb[0]])
                p.op("scalar", lambda e: e.activation(
                    tanh_sb[:, :], PB0[:, 0:TILE], AF.Tanh),
                    reads=[pb[0]], writes=[b_tanh])

                mix_into_bb2("maa_w", 3)
                if dbg:
                    p.dma("scalar", dbg_xw[:, tok0 * PC:(tok0 + TILE) * PC],
                        BBT[2][:, :],
                        reads=[bb[2]], writes_acc=[d_dbg], sem="d_dbg")

                def mm_wtd(e, ri=ri_t):
                    last = None
                    for k in range(PC):
                        last = e.matmul(PB2[0:64, 0:TILE],
                                        RGT[ri][:, k * 64:(k + 1) * 64],
                                        BBT[2][:, k * TILE:(k + 1) * TILE],
                                        start=(k == 0), stop=(k == PC - 1))
                    return last
                p.op("tensor", mm_wtd, reads=[bb[2], rg[ri_t]], writes=[pb[2]])
                p.op("scalar", lambda e: e.activation(
                    wtd_sb[0:64, :], PB2[0:64, 0:TILE], AF.Tanh),
                    reads=[pb[2]], writes=[b_wtd])
                p.op("vector", lambda e: e.tensor_copy(
                    wtd_sb[64:128, :], wtd_sb[0:64, :]), writes=[b_wtd])
                for pcc in range(PC):
                    bank = pcc % 2
                    rh = (pcc // 8) * 64
                    cc0 = 1024 + (pcc % 8) * 128
                    p.op("tensor", lambda e, bank=bank, rh=rh, cc0=cc0, ri=ri_t:
                         e.matmul(
                             PB[bank][:, 0:TILE],
                             RGT[ri][rh:rh + 64, cc0:cc0 + 128],
                             wtd_sb[rh:rh + 64, :], start=True, stop=True),
                         reads=[rg[ri_t], b_wtd], writes=[pb[bank]])
                    p.op("scalar", lambda e, pcc=pcc, bank=bank: e.activation(
                        BBT[3][:, pcc * TILE:(pcc + 1) * TILE],
                        PB[bank][:, 0:TILE], AF.Exp, bias=col("tdec", pcc)),
                        reads=[pb[bank], b_cols], writes=[bb[3]])
                    p.op("vector", lambda e, pcc=pcc: e.tensor_scalar(
                        BBT[3][:, pcc * TILE:(pcc + 1) * TILE],
                        BBT[3][:, pcc * TILE:(pcc + 1) * TILE],
                        5.0, -1.0, OP.min, OP.mult), writes=[bb[3]])

                if dbg:
                    p.dma("scalar", dbg_wlog[:, tok0 * PC:(tok0 + TILE) * PC],
                        BBT[3][:, :],
                        reads=[bb[3]], writes_acc=[d_dbg], sem="d_dbg")
                held.discard(ri_t)
                mix_into_bb2("maa_v", 2, dst_i=6)
                for k in range(PC):
                    ri = rn()
                    p.dma("sync", RGT[ri][:, :], vw[k * 128:(k + 1) * 128, :],
                          writes=[rg[ri]], sem="d_w%d" % ri)

                    def vmm(e, ri=ri, k=k):
                        last = None
                        for tb in range(TB):
                            for do in range(4):
                                last = e.matmul(
                                    PB[tb * 4 + do][:, 0:512],
                                    BBT[6][:, k * TILE + tb * 128: k * TILE + (tb + 1) * 128],
                                    RGT[ri][:, do * 512:(do + 1) * 512],
                                    start=(k == 0), stop=(k == PC - 1))
                        return last
                    p.op("tensor", vmm, reads=[rg[ri], bb[6]], writes=pb[0:8])
                for tb in range(TB):
                    for do in range(4):
                        p.op("vector", lambda e, tb=tb, do=do: e.tensor_add(
                            BBT[6][:, tb * D + do * 512: tb * D + (do + 1) * 512],
                            PB[tb * 4 + do][:, 0:512],
                            vbbc_sb[:, do * 512:(do + 1) * 512]),
                            reads=[pb[tb * 4 + do], b_vbbc], writes=[bb[6]])

                mix_into_bb2("maa_r", 0)

                def ev_q(do, bank):
                    p.op("scalar", lambda e, do=do, bank=bank: e.activation(
                        BBT[4][:, do * TILE:(do + 1) * TILE], PB[bank][:, 0:TILE],
                        AF.Identity, bias=qb8_sb[:, do:do + 1], scale=0.125),
                        reads=[pb[bank], b_qb8], writes=[bb[4]])
                proj_2048(qw4, 4, ev_q)

                mix_into_bb2("maa_k", 1)

                CH0 = t * NCH_T
                for pcc in range(PC):
                    p.op("vector", lambda e, pcc=pcc: e.tensor_tensor_scan(
                        BBT[1][:, pcc * TILE:(pcc + 1) * TILE],
                        BBT[3][:, pcc * TILE:(pcc + 1) * TILE],
                        BBT[3][:, pcc * TILE:(pcc + 1) * TILE],
                        Bcarry[:, pcc:pcc + 1], OP.add, OP.bypass),
                        reads=[bb[3], b_Bc], writes=[bb[1]])
                p.op("vector", lambda e, CH0=CH0: e.tensor_copy(
                    Bbase[:, CH0 * PC:(CH0 + 1) * PC], Bcarry[:, :]),
                    reads=[b_Bc], writes=[b_Bb])
                for m in range(1, NCH_T):
                    p.op("vector", lambda e, m=m, CH0=CH0: e.tensor_copy(
                        Bbase[:, (CH0 + m) * PC:(CH0 + m + 1) * PC],
                        v3(1)[:, :, m * CH - 1:m * CH].rearrange(
                            "p a c -> p (a c)")),
                        reads=[bb[1]], writes=[b_Bb])
                p.op("vector", lambda e, CH0=CH0: e.tensor_copy(
                    Bbase[:, (CH0 + NCH_T) * PC:(CH0 + NCH_T + 1) * PC],
                    v3(1)[:, :, TILE - 1:TILE].rearrange("p a c -> p (a c)")),
                    reads=[bb[1]], writes=[b_Bb])
                p.op("vector", lambda e, CH0=CH0: e.tensor_copy(
                    Bcarry[:, :],
                    Bbase[:, (CH0 + NCH_T) * PC:(CH0 + NCH_T + 1) * PC]),
                    reads=[b_Bb], writes=[b_Bc])
                p.op("vector", lambda e, CH0=CH0: e.tensor_sub(
                    Dcols[:, :],
                    Bbase[:, (CH0 + 1) * PC:(CH0 + 1 + NCH_T) * PC],
                    Bbase[:, CH0 * PC:(CH0 + NCH_T) * PC]),
                    reads=[b_Bb], writes=[b_Dc])
                p.op("scalar", lambda e: e.activation(
                    Dcols[:, :], Dcols[:, :], AF.Exp), writes=[b_Dc])
                X4 = lambda i: BBT[i][:, :].rearrange(
                    "p (q m c) -> p q m c", m=NCH_T, c=CH)
                for m in range(NCH_T):
                    p.op("scalar", lambda e, m=m, CH0=CH0: e.copy(
                        v3(3)[:, :, m * CH:(m + 1) * CH],
                        Bbase[:, (CH0 + m) * PC:(CH0 + m + 1) * PC]
                        .unsqueeze(2).broadcast_to([128, PC, CH])),
                        reads=[b_Bb], writes=[bb[3]])
                p.op("vector", lambda e: e.tensor_sub(
                    BBT[3][:, :], BBT[1][:, :], BBT[3][:, :]),
                    reads=[bb[1]], writes=[bb[3]])
                p.op("scalar", lambda e: e.activation(
                    BBT[1][:, :], BBT[3][:, :], AF.Exp, scale=-1.0),
                    reads=[bb[3]], writes=[bb[1]])
                p.op("scalar", lambda e: e.activation(
                    BBT[3][:, :], BBT[3][:, :], AF.Exp), writes=[bb[3]])
                p.op("vector", lambda e: e.tensor_mul(
                    BBT[4][:, :], BBT[4][:, :], BBT[3][:, :]),
                    reads=[bb[3]], writes=[bb[4]])
                p.op("scalar", lambda e: e.copy(
                    X4(3)[:, :, :, 1:CH], X4(1)[:, :, :, 0:CH - 1]),
                    reads=[bb[1]], writes=[bb[3]])
                p.op("vector", lambda e: e.memset(
                    X4(3)[:, :, :, 0:1], 1.0), writes=[bb[3]])
                p.op("vector", lambda e: e.tensor_sub(
                    BBT[3][:, :], BBT[1][:, :], BBT[3][:, :]),
                    reads=[bb[1]], writes=[bb[3]])
                p.dma("scalar", qt_scr[:, tok0 * PC:(tok0 + TILE) * PC],
                      BBT[4][:, :], reads=[bb[4]], writes_acc=[d_qt], sem="d_qt")

                def ev_k(do, bank):
                    p.op("scalar", lambda e, do=do, bank=bank: e.activation(
                        BBT[5][:, do * TILE:(do + 1) * TILE], PB[bank][:, 0:TILE],
                        AF.Identity, bias=col("k_b", do)),
                        reads=[pb[bank], b_cols], writes=[bb[5]])
                    p.op("vector", lambda e, do=do: e.tensor_mul(
                        BBT[5][:, do * TILE:(do + 1) * TILE],
                        BBT[5][:, do * TILE:(do + 1) * TILE],
                        BBT[3][:, do * TILE:(do + 1) * TILE]),
                        reads=[bb[3]], writes=[bb[5]])
                proj_2048(kw4, 5, ev_k)

                ck("t5")
                ck("t6")
                b_kh = bb[1]
                b_Pt = bb[3]
                b_ost = bb[1]
                b_vsw = bb[3]
                for m in range(NCH_T):
                    mg = CH0 + m
                    par = (m % 2) * 64
                    opar = 64 - par
                    tb = m // 2
                    if m % 2 == 0:
                        p.op("vector", lambda e, tb=tb: e.tensor_copy(
                            BBT[3][0:64, 2048:4096],
                            BBT[6][64:128, tb * D:tb * D + 2048]),
                            reads=[bb[6]], writes=[b_vsw])
                        p.op("vector", lambda e, tb=tb: e.tensor_copy(
                            BBT[3][64:128, 2048:4096],
                            BBT[6][0:64, tb * D:tb * D + 2048]),
                            reads=[bb[6]], writes=[b_vsw])

                    def vsl(h, par=par, tb=tb):
                        hp = (h % 2) * 64
                        if hp == par:
                            return BBT[6][par:par + 64,
                                          tb * D + h * DH: tb * D + (h + 1) * DH]
                        return BBT[3][hp:hp + 64, 2048 + h * DH:2048 + (h + 1) * DH]

                    def vrd(h, par=par):
                        return bb[6] if (h % 2) * 64 == par else b_vsw

                    for g in range(4):
                        def ttg(e, m=m, g=g):
                            last = None
                            for j in range(4):
                                pcx = g * 4 + j
                                last = e.transpose(
                                    PB6[0:64, j * 128:(j + 1) * 128],
                                    BBT[5][:, pcx * TILE + m * CH: pcx * TILE + (m + 1) * CH],
                                    id_sb[:, :])
                            return last
                        p.op("tensor", ttg, reads=[bb[5], b_id], writes=[pb[6]])
                        p.op("vector", lambda e, g=g: e.tensor_copy(
                            BBT[1][0:64, g * 512:(g + 1) * 512],
                            PB6[0:64, 0:512]), reads=[pb[6]], writes=[b_kh])
                        p.op("scalar", lambda e, g=g: e.activation(
                            BBT[1][64:128, g * 512:(g + 1) * 512],
                            PB6[0:64, 0:512], AF.Identity),
                            reads=[pb[6]], writes=[b_kh])

                    def qsl(h, m=m):
                        return BBT[4][(h % 2) * 64:(h % 2) * 64 + 64,
                                      (h // 2) * TILE + m * CH:
                                      (h // 2) * TILE + (m + 1) * CH]

                    def ksl(h, m=m):
                        return BBT[5][(h % 2) * 64:(h % 2) * 64 + 64,
                                      (h // 2) * TILE + m * CH:
                                      (h // 2) * TILE + (m + 1) * CH]

                    for half in range(2 if attn_lvl >= 2 else 0):
                        pbank = 4 + half

                        def pmm(e, half=half, pbank=pbank, qsl=qsl, ksl=ksl):
                            last = None
                            for i in range(16):
                                h = half * 16 + i
                                hp = (h % 2) * 64
                                sl = ((h // 2) % 8) * 64
                                last = e.matmul(
                                    PB[pbank][hp:hp + 64, sl:sl + 64],
                                    ksl(h), qsl(h), start=True, stop=True)
                            return last
                        p.op("tensor", pmm, reads=[bb[4], bb[5]],
                             writes=[pb[pbank]])
                        for hp in (0, 64):
                            p.op("vector", lambda e, half=half, pbank=pbank, hp=hp:
                                 e.tensor_mul(
                                     BBT[3][hp:hp + 64, :]
                                     .rearrange("p (a c) -> p a c", c=CH)
                                     [:, half * 16 + (hp // 64):half * 16 + 16:2, :],
                                     PB[pbank][hp:hp + 64, 0:512]
                                     .rearrange("p (a c) -> p a c", c=CH),
                                     mask8_sb[:, :]
                                     .rearrange("p (a c) -> p a c", c=CH)),
                                 reads=[pb[pbank], b_mask8], writes=[b_Pt])

                    for half in range(2 if attn_lvl >= 3 else 0):
                        obank = half

                        def omm(e, half=half, obank=obank, vsl=vsl, qsl=qsl):
                            last = None
                            for i in range(16):
                                h = half * 16 + i
                                hp = (h % 2) * 64
                                sl = ((h // 2) % 8) * 64
                                last = e.matmul(
                                    PB[obank][hp:hp + 64, sl:sl + 64],
                                    vsl(h),
                                    BBT[3][hp:hp + 64, h * 64:(h + 1) * 64],
                                    start=True, stop=False)
                                last = e.matmul(
                                    PB[obank][hp:hp + 64, sl:sl + 64],
                                    S_sb[hp:hp + 64,
                                         (h // 2) * 64:(h // 2) * 64 + 64],
                                    qsl(h), start=False, stop=True)
                            return last
                        p.op("tensor", omm,
                             reads=[bb[6], b_vsw, b_Pt, b_S, bb[4]],
                             writes=[pb[obank]])

                    for half in range(2 if attn_lvl >= 4 else 0):
                        sbank = 2 + half

                        def smm(e, half=half, sbank=sbank, par=par, vsl=vsl):
                            last = None
                            for i in range(16):
                                h = half * 16 + i
                                hp = (h % 2) * 64
                                pcc = h // 2
                                sl = (pcc % 8) * 64
                                last = e.matmul(
                                    PB[sbank][hp:hp + 64, sl:sl + 64],
                                    BBT[1][hp:hp + 64,
                                           pcc * 128 + hp: pcc * 128 + hp + 64],
                                    vsl(h), start=True, stop=True)
                            return last
                        p.op("tensor", smm, reads=[b_kh, bb[6], b_vsw],
                             writes=[pb[sbank]])

                    if attn_lvl >= 5:
                        for half in range(2):
                            p.op("scalar", lambda e, half=half: e.activation(
                                BBT[1][:, 2048 + half * 512:2048 + half * 512 + 512],
                                PB[half][:, 0:512], AF.Identity),
                                reads=[pb[half]], writes=[b_ost])
                        p.dma("scalar",
                              o_scr[:, (tok0 + m * CH) * PC:(tok0 + (m + 1) * CH) * PC],
                              BBT[1][:, 2048:3072], reads=[b_ost],
                              writes_acc=[d_o], sem="d_o")

                    if attn_lvl >= 4:
                        for half in range(2):
                            p.op("vector", lambda e, half=half: e.tensor_add(
                                S_sb[:, half * 512:(half + 1) * 512],
                                S_sb[:, half * 512:(half + 1) * 512],
                                PB[2 + half][:, 0:512]),
                                reads=[pb[2 + half]], writes=[b_S])
                        p.op("vector", lambda e, m=m: e.tensor_mul(
                            S_sb[:, :].rearrange("p (q c) -> p q c", c=64),
                            S_sb[:, :].rearrange("p (q c) -> p q c", c=64),
                            Dcols[:, m * PC:(m + 1) * PC]
                            .unsqueeze(2).broadcast_to([128, PC, 64])),
                            reads=[b_Dc], writes=[b_S])

                if t == NT - 1:
                    p.dma("gpsimd", ccsm_i[:, :], smOwn[:, :], reads=[b_smOwn],
                          writes=[d_ci], sem="d_cci")
                    if ncores > 1:
                        p.op("gpsimd", lambda e: e.collective_compute(
                            "AllGather", OP.bypass,
                            replica_groups=[list(range(ncores))],
                            ins=[ccsm_i.opt()], outs=[ccsm_o.opt()]),
                            reads=[d_ci], writes=[d_co], sem="cc1")
                    else:
                        p.dma("gpsimd", ccsm_o[0:128, :], ccsm_i[:, :],
                              reads=[d_ci], writes=[d_co], sem="cc1")
                    p.op("scalar", lambda e: e.activation(
                        dsl_sb[:, :], Bcarry[:, :], AF.Exp),
                        reads=[b_Bc], writes=[b_dsl])
                    p.dma("gpsimd", ccsu_i[:, 0:1024], S_sb[:, :], reads=[b_S],
                          writes=[d_si], sem="d_ccs")
                    p.dma("gpsimd", ccsu_i[:, 1024:1040], dsl_sb[:, :],
                          reads=[b_dsl], writes_acc=[d_si], sem="d_ccs")
                    if ncores > 1:
                        p.op("gpsimd", lambda e: e.collective_compute(
                            "AllGather", OP.bypass,
                            replica_groups=[list(range(ncores))],
                            ins=[ccsu_i.opt()], outs=[ccsu_o.opt()]),
                            reads=[d_si], writes=[d_so], sem="cc2")
                    else:
                        p.dma("gpsimd", ccsu_o[0:128, :], ccsu_i[:, :],
                              reads=[d_si], writes=[d_so], sem="cc2")

                if t + 1 < NT:
                    stage_T12(t + 1)
                ck("t7b")
                hhome = [1, 4, 5]

                def hT(c):
                    return BBT[hhome[c // 16]], bb[hhome[c // 16]], (c % 16) * TILE

                for c in range(NF):
                    rg_g = rn()
                    rg_u = rn()
                    p.dma("sync", RGT[rg_g][:, :],
                          gw4[c].rearrange("p a b -> p (a b)"),
                          writes=[rg[rg_g]], sem="d_w%d" % rg_g)
                    p.dma("sync", RGT[rg_u][:, :],
                          uw4[c].rearrange("p a b -> p (a b)"),
                          writes=[rg[rg_u]], sem="d_w%d" % rg_u)

                    def gmm(e, ri=rg_g):
                        last = None
                        for k in range(PC):
                            last = e.matmul(PB2[:, 0:TILE],
                                            RGT[ri][:, k * 128:(k + 1) * 128],
                                            BBT[0][:, k * TILE:(k + 1) * TILE],
                                            start=(k == 0), stop=(k == PC - 1))
                        return last
                    p.op("tensor", gmm, reads=[rg[rg_g], bb[0]], writes=[pb[2]])

                    def umm(e, ri=rg_u):
                        last = None
                        for k in range(PC):
                            last = e.matmul(PB3[:, 0:TILE],
                                            RGT[ri][:, k * 128:(k + 1) * 128],
                                            BBT[0][:, k * TILE:(k + 1) * TILE],
                                            start=(k == 0), stop=(k == PC - 1))
                        return last
                    p.op("tensor", umm, reads=[rg[rg_u], bb[0]], writes=[pb[3]])
                    hten, hbuf, hoff = hT(c)
                    p.op("scalar", lambda e, c=c: e.activation(
                        tanh_sb[:, :], PB2[:, 0:TILE], AF.Silu,
                        bias=bgu_sb[:, c:c + 1]),
                        reads=[pb[2], b_bgu], writes=[b_tanh])
                    p.op("vector", lambda e, c=c, hten=hten, hoff=hoff:
                         e.scalar_tensor_tensor(
                             hten[:, hoff:hoff + TILE], PB3[:, 0:TILE],
                             bgu_sb[:, NF + c:NF + c + 1], tanh_sb[:, :],
                             OP.add, OP.mult),
                         reads=[pb[3], b_bgu, b_tanh], writes=[hbuf])

                parts = [(0, 15), (15, 30), (30, NF)]
                for do in range(PC):
                    bank = do % 2
                    for (c0, c1) in parts:
                        ri = rn()
                        p.dma("sync", RGT[ri][:, 0:(c1 - c0) * 128],
                              dw4[do][:, c0:c1, :].rearrange("p a b -> p (a b)"),
                              writes=[rg[ri]], sem="d_w%d" % ri)

                        def dmm(e, ri=ri, c0=c0, c1=c1, bank=bank):
                            last = None
                            for c in range(c0, c1):
                                hten, _, hoff = hT(c)
                                last = e.matmul(
                                    PB[bank][:, 0:TILE],
                                    RGT[ri][:, (c - c0) * 128:(c - c0 + 1) * 128],
                                    hten[:, hoff:hoff + TILE],
                                    start=(c == 0), stop=(c == NF - 1))
                            return last
                        p.op("tensor", dmm,
                             reads=[rg[ri]] + [bb[i] for i in hhome],
                             writes=[pb[bank]])
                    p.op("vector", lambda e, do=do, bank=bank: e.tensor_scalar(
                        BBT[6][:, do * TILE:(do + 1) * TILE], PB[bank][:, 0:TILE],
                        col("b_dn", do), None, OP.add),
                        reads=[pb[bank], b_cols], writes=[bb[6]])
                    if t == NT - 1 and do >= 8:
                        for _ in range(2):
                            if c0_tasks:
                                c0_tasks.pop(0)()
                p.dma("scalar", ffn_scr[:, tok0 * PC:(tok0 + TILE) * PC],
                      BBT[6][:, :], reads=[bb[6]], writes_acc=[d_ffn], sem="d_ff")

                if t + 1 < NT:
                    stage_T34(t + 1)
            ck("t8")
            p.dma("scalar", BBT[4][:, :], qt_scr[:, 0:TILE * PC],
                  reads=[d_qt], writes=[bb[4]], sem="d_rq")
            p.dma("scalar", BBT[1][:, :], o_scr[:, 0:TILE * PC],
                  reads=[d_o], writes=[bb[1]], sem="d_ro")
            p.dma("scalar", BBT[2][:, :], x_fm[:, 0:TILE * PC],
                  writes=[bb[2]], sem="d_rx")
            while c0_tasks:
                c0_tasks.pop(0)()
            ck("c0")
            p.op("vector", lambda e: e.memset(bis8[:, 0:8], 0.0), writes=[b_bis])
            p.op("vector", lambda e: e.memset(bis8[:, 8:16], 1.0), writes=[b_bis])
            p.op("vector", lambda e: e.memset(bis8[:, 16:24], 0.5), writes=[b_bis])
            p.op("vector", lambda e: e.memset(midbc[:, :], 0.5), writes=[b_midbc])
            bis_done = [0]

            ck("bisinit")

            FOLD0 = max(1, (NT + 1) // 2)
            BIS_PER = -(-BIS_IT // FOLD0)
            pending = list(range(FOLD0))

            def wgt_bc(tok, half):
                for tb in range(TB):
                    p.op("tensor", lambda e, tok=tok, tb=tb, half=half: e.matmul(
                        PB2[:, half * TILE + tb * 128:
                            half * TILE + (tb + 1) * 128],
                        onesr_sb[0:1, :],
                        wgtrow[0:1, tok + tb * 128:tok + (tb + 1) * 128],
                        start=True, stop=True),
                        reads=[b_wgr, b_onesr], writes=[pb[2]])
                p.op("vector", lambda e, half=half: e.tensor_copy(
                    wgtbc_sb[:, half * TILE:(half + 1) * TILE],
                    PB2[:, half * TILE:(half + 1) * TILE]),
                    reads=[pb[2]], writes=[b_wgb])

            def mul_wgt(i, half):
                p.op("vector", lambda e, i=i, half=half: e.tensor_mul(
                    BBT[i][:, :].rearrange("p (a c) -> p a c", c=TILE),
                    BBT[i][:, :].rearrange("p (a c) -> p a c", c=TILE),
                    wgtbc_sb[:, half * TILE:(half + 1) * TILE]
                    .unsqueeze(1).broadcast_to([128, PC, TILE])),
                    reads=[b_wgb], writes=[bb[i]])

            def emit_wgt():
                for blk in range(tpc // 128):
                    p.op("vector", lambda e, blk=blk: e.tensor_tensor(
                        bisc[:, 0:E8], smOwn[:, blk * E8:(blk + 1) * E8],
                        bis8[:, 0:8], OP.is_ge),
                        reads=[b_smOwn, b_bis], writes=[b_bisc])
                    p.op("vector", lambda e, blk=blk: e.tensor_mul(
                        bisc[:, 0:E8], bisc[:, 0:E8],
                        smOwn[:, blk * E8:(blk + 1) * E8]),
                        reads=[b_smOwn], writes=[b_bisc])
                    p.op("vector", lambda e, blk=blk: e.reduce_sum(
                        wgt_sb[:, blk:blk + 1], bisc[:, 0:E8], axis=AX.X),
                        reads=[b_bisc], writes=[b_wgt])
                NBLK = tpc // 128
                p.op("tensor", lambda e: e.transpose(
                    PB7[0:NBLK, 0:128], wgt_sb[:, :], id_sb[:, :]),
                    reads=[b_wgt, b_id], writes=[pb[7]])
                p.op("vector", lambda e: e.tensor_copy(
                    tanh_sb[0:NBLK, 0:128], PB7[0:NBLK, 0:128]),
                    reads=[pb[7]], writes=[b_tanh])
                for blk in range(NBLK):
                    p.op("tensor", lambda e, blk=blk: e.matmul(
                        PB7[0:1, 128:256], id_sb[0:NBLK, blk:blk + 1],
                        tanh_sb[0:NBLK, 0:128], start=True, stop=True),
                        reads=[b_tanh, b_id], writes=[pb[7]])
                    p.op("vector", lambda e, blk=blk: e.tensor_copy(
                        wgtrow[0:1, blk * 128:(blk + 1) * 128],
                        PB7[0:1, 128:256]), reads=[pb[7]], writes=[b_wgr])

            def acc_tile(j):
                tj = j * TILE
                p.dma("scalar", BBT[4][:, :],
                      ffn_scr[:, tj * PC:(tj + TILE) * PC],
                      reads=[d_ffn], writes=[bb[4]], sem="d_rf")
                wgt_bc(tj, 1)
                mul_wgt(4, 1)
                if trunc == "noacc":
                    p.dma("gpsimd", ffn_scr[:, tj * PC:(tj + TILE) * PC],
                          BBT[4][:, :], reads=[bb[4], d_ffn],
                          writes_acc=[d_y], sem="d_ya")
                else:
                    for q in range(4):
                        p.dma("gpsimd",
                              y_fm[:, tj * PC + q * 1024:
                                   tj * PC + (q + 1) * 1024],
                              BBT[4][:, q * 1024:(q + 1) * 1024],
                              reads=[bb[4], d_y], writes_acc=[d_y],
                              sem="d_ya", accum=OP.add)

            def reload_qo(tt):
                t0 = tt * TILE
                p.dma("scalar", BBT[4][:, :],
                      qt_scr[:, t0 * PC:(t0 + TILE) * PC],
                      reads=[d_qt], writes=[bb[4]], sem="d_rq")
                p.dma("scalar", BBT[1][:, :],
                      o_scr[:, t0 * PC:(t0 + TILE) * PC],
                      reads=[d_o], writes=[bb[1]], sem="d_ro")

            def corr_m(tt, m):
                mg = tt * NCH_T + m
                p.op("scalar", lambda e, mg=mg: e.activation(
                    eb_sb[:, :], Bbase[:, mg * PC:(mg + 1) * PC], AF.Exp),
                    reads=[b_Bb], writes=[b_eb])
                p.op("vector", lambda e: e.tensor_mul(
                    SC_sb[:, :].rearrange("p (q c) -> p q c", c=64),
                    Sin_sb[:, :].rearrange("p (q c) -> p q c", c=64),
                    eb_sb[:, :].unsqueeze(2).broadcast_to([128, PC, 64])),
                    reads=[b_Sin, b_eb], writes=[b_SC])
                for half in range(2):
                    def cmm(e, half=half, m=m):
                        last = None
                        for i in range(16):
                            h = half * 16 + i
                            hp = (h % 2) * 64
                            sl = ((h // 2) % 8) * 64
                            last = e.matmul(
                                PB[half][hp:hp + 64, sl:sl + 64],
                                SC_sb[hp:hp + 64,
                                      (h // 2) * 64:(h // 2) * 64 + 64],
                                BBT[4][hp:hp + 64,
                                       (h // 2) * TILE + m * CH:
                                       (h // 2) * TILE + (m + 1) * CH],
                                start=True, stop=True)
                        return last
                    p.op("tensor", cmm, reads=[b_SC, bb[4]],
                         writes=[pb[half]])
                for half in range(2):
                    p.op("vector", lambda e, half=half, m=m: e.tensor_add(
                        BBT[1][:, m * 1024 + half * 512:
                               m * 1024 + half * 512 + 512],
                        BBT[1][:, m * 1024 + half * 512:
                               m * 1024 + half * 512 + 512],
                        PB[half][:, 0:512]),
                        reads=[pb[half]], writes=[bb[1]])

            def ln2(tt):
                vd, vb = (BBT[5], bb[5]) if tt % 2 == 0 else (BBT[7], bb[7])
                p.op("scalar", lambda e: e.activation(
                    BBT[4][:, :], BBT[1][:, :], AF.Square),
                    reads=[bb[1]], writes=[bb[4]])

                def smm1(e):
                    last = None
                    v1 = BBT[1][:, :].rearrange("p (m q c) -> p m q c",
                                                q=PC, c=CH)
                    for k in range(PC):
                        last = e.matmul(PB4[0:1, 0:TILE], ones_sb[:, 0:1],
                                        v1[:, :, k, :],
                                        start=(k == 0), stop=(k == PC - 1))
                    return last
                p.op("tensor", smm1, reads=[bb[1], b_ones], writes=[pb[4]])
                p.op("vector", lambda e: e.tensor_scalar(
                    abrow[0:1, 0:TILE], PB4[0:1, 0:TILE], 1.0 / D, None,
                    OP.mult), reads=[pb[4]], writes=[b_ab])

                def smm2(e):
                    last = None
                    v6 = BBT[4][:, :].rearrange("p (m q c) -> p m q c",
                                                q=PC, c=CH)
                    for k in range(PC):
                        last = e.matmul(PB4[0:1, 256:256 + TILE],
                                        ones_sb[:, 0:1], v6[:, :, k, :],
                                        start=(k == 0), stop=(k == PC - 1))
                    return last
                p.op("tensor", smm2, reads=[bb[4], b_ones], writes=[pb[4]])
                p.op("vector", lambda e: e.tensor_scalar(
                    abrow[0:1, TILE:2 * TILE], PB4[0:1, 256:256 + TILE],
                    1.0 / D, None, OP.mult), reads=[pb[4]], writes=[b_ab])
                p.op("scalar", lambda e: e.activation(
                    abrow[0:1, 2 * TILE:3 * TILE], abrow[0:1, 0:TILE],
                    AF.Square), writes=[b_ab])
                p.op("vector", lambda e: e.tensor_sub(
                    abrow[0:1, TILE:2 * TILE], abrow[0:1, TILE:2 * TILE],
                    abrow[0:1, 2 * TILE:3 * TILE]), writes=[b_ab])
                p.op("scalar", lambda e: e.activation(
                    abrow[0:1, TILE:2 * TILE], abrow[0:1, TILE:2 * TILE],
                    AF.Sqrt, bias=cstc[0:1, 1:2]), writes=[b_ab])
                p.op("vector", lambda e: e.reciprocal(
                    abrow[0:1, TILE:2 * TILE], abrow[0:1, TILE:2 * TILE]),
                    writes=[b_ab])
                p.op("vector", lambda e: e.tensor_mul(
                    abrow[0:1, 2 * TILE:3 * TILE], abrow[0:1, 0:TILE],
                    abrow[0:1, TILE:2 * TILE]), writes=[b_ab])
                p.op("vector", lambda e: e.tensor_scalar(
                    abrow[0:1, 2 * TILE:3 * TILE],
                    abrow[0:1, 2 * TILE:3 * TILE],
                    -1.0, None, OP.mult), writes=[b_ab])
                p.op("tensor", lambda e: e.matmul(
                    PB5[:, 0:TILE], onesr_sb[0:1, :],
                    abrow[0:1, TILE:2 * TILE],
                    start=True, stop=True), reads=[b_ab, b_onesr],
                    writes=[pb[5]])
                p.op("tensor", lambda e: e.matmul(
                    PB5[:, 256:256 + TILE], onesr_sb[0:1, :],
                    abrow[0:1, 2 * TILE:3 * TILE],
                    start=True, stop=True), reads=[b_ab, b_onesr],
                    writes=[pb[5]])
                v1c = BBT[1][:, :].rearrange("p (m q c) -> p m q c",
                                             q=PC, c=CH)
                v5c = vd[:, :].rearrange("p (m q c) -> p m q c", q=PC, c=CH)
                abc = PB5[:, 0:TILE].rearrange("p (m c) -> p m c", c=CH)
                bbc = PB5[:, 256:256 + TILE].rearrange("p (m c) -> p m c",
                                                       c=CH)
                p.op("vector", lambda e, v5c=v5c, v1c=v1c, abc=abc:
                     e.tensor_mul(
                         v5c, v1c,
                         abc.unsqueeze(2).broadcast_to([128, NCH_T, PC, CH])),
                     reads=[bb[1], pb[5]], writes=[vb])
                p.op("vector", lambda e, v5c=v5c, bbc=bbc: e.tensor_add(
                    v5c, v5c,
                    bbc.unsqueeze(2).broadcast_to([128, NCH_T, PC, CH])),
                    reads=[pb[5]], writes=[vb])
                p.op("vector", lambda e, v5c=v5c: e.tensor_mul(
                    v5c, v5c,
                    cols_sb[:, CL["ln_g"] * PC:(CL["ln_g"] + 1) * PC]
                    .unsqueeze(1).unsqueeze(3)
                    .broadcast_to([128, NCH_T, PC, CH])),
                    reads=[b_cols], writes=[vb])
                p.op("vector", lambda e, v5c=v5c: e.tensor_add(
                    v5c, v5c,
                    cols_sb[:, CL["ln_b"] * PC:(CL["ln_b"] + 1) * PC]
                    .unsqueeze(1).unsqueeze(3)
                    .broadcast_to([128, NCH_T, PC, CH])),
                    reads=[b_cols], writes=[vb])

            for t in range(NT):
                tok0 = t * TILE
                if t == 0:
                    for m in range(NCH_T):
                        corr_m(0, m)
                ck("creload")
                ck("corr")
                if t >= FOLD0 and pending:
                    acc_tile(pending.pop(0))
                if t >= FOLD0:
                    p.dma("scalar", BBT[3][:, :],
                          ffn_scr[:, tok0 * PC:(tok0 + TILE) * PC],
                          reads=[d_ffn], writes=[bb[3]], sem="d_rf")
                    wgt_bc(tok0, 0)
                    mul_wgt(3, 0)
                if t == 0:
                    ln2(0)
                ck("ln2")
                vy, by = (BBT[5], bb[5]) if t % 2 == 0 else (BBT[7], bb[7])
                for do in range(PC):
                    ri = rn()
                    bank = 6 + do % 2
                    p.dma("sync", RGT[ri][:, :],
                          ow4[do].rearrange("p a b -> p (a b)"),
                          writes=[rg[ri]], sem="d_w%d" % ri)

                    def wmm(e, ri=ri, bank=bank, vy=vy):
                        last = None
                        v5 = vy[:, :].rearrange("p (m q c) -> p m q c",
                                                q=PC, c=CH)
                        for k in range(PC):
                            last = e.matmul(PB[bank][:, 0:TILE],
                                            RGT[ri][:, k * 128:(k + 1) * 128],
                                            v5[:, :, k, :],
                                            start=(k == 0), stop=(k == PC - 1))
                        return last
                    p.op("tensor", wmm, reads=[rg[ri], by], writes=[pb[bank]])
                    p.op("vector", lambda e, do=do, bank=bank: e.tensor_add(
                        BBT[6][:, do * TILE:(do + 1) * TILE], PB[bank][:, 0:TILE],
                        BBT[2][:, do * TILE:(do + 1) * TILE]),
                        reads=[pb[bank], bb[2]], writes=[bb[6]])
                    if t < FOLD0:
                        tgt = min(BIS_IT, t * BIS_PER
                                  + (BIS_PER * (do + 1) + PC - 1) // PC)
                        while bis_done[0] < tgt:
                            bis_iter()
                            bis_done[0] += 1
                    if t + 1 < NT:
                        if do == 0:
                            reload_qo(t + 1)
                        if do >= 2 and do % 2 == 0 and (do - 2) // 2 < NCH_T:
                            corr_m(t + 1, (do - 2) // 2)
                        if do == 10:
                            ln2(t + 1)
                if t == FOLD0 - 1:
                    while bis_done[0] < BIS_IT:
                        bis_iter()
                        bis_done[0] += 1
                    emit_wgt()
                if t >= FOLD0:
                    p.op("vector", lambda e: e.tensor_add(
                        BBT[6][:, :], BBT[6][:, :], BBT[3][:, :]),
                        reads=[bb[3]], writes=[bb[6]])
                p.dma("scalar", y_fm[:, tok0 * PC:(tok0 + TILE) * PC],
                      BBT[6][:, :], reads=[bb[6]], writes_acc=[d_y], sem="d_y")
                if t + 1 < NT:
                    p.dma("scalar", BBT[2][:, :],
                          x_fm[:, (tok0 + TILE) * PC:(tok0 + 2 * TILE) * PC],
                          writes=[bb[2]], sem="d_rx")

            ck("ow")
            for j in list(pending):
                pending.remove(j)
                acc_tile(j)
            ck("wgt")

        except _Done:
            pass
        if trunc is not None:
            p.dma("scalar", y_fm[:, 0:PC * TILE], BBT[0][:, :],
                  writes_acc=[d_y], sem="d_y")
        p.wait_only("sync", [d_y])

        handles = {}
        cms = []
        for sname in sorted(p.val):
            cm = nc.semaphore(sname)
            handles[sname] = cm.__enter__()
            cms.append(cm)
        for ename, deco in (("tensor", block.tensor), ("vector", block.vector),
                            ("scalar", block.scalar), ("sync", block.sync),
                            ("gpsimd", block.gpsimd)):
            items = p.items[ename]
            waited = {}
            deco(lambda eng, items=items, waited=waited:
                 _emit(eng, items, handles, waited))
        for cm in reversed(cms):
            cm.__exit__(None, None, None)

    return nc


def stage_inputs(inputs, ncores, tpc):
    f32 = np.float32
    x = np.ascontiguousarray(np.asarray(inputs["L0_attn_x"], dtype=f32)[0])
    T = x.shape[0]
    assert T == ncores * tpc

    def c4(w):
        return np.ascontiguousarray(
            w.reshape(PC, 128, PC, 128).transpose(2, 1, 0, 3))

    qw4 = c4(np.asarray(inputs["L0_attn_q_w"], dtype=f32))
    kw4 = c4(np.asarray(inputs["L0_attn_k_w"], dtype=f32))
    ow4 = c4(np.asarray(inputs["L0_attn_o_w"], dtype=f32))
    vw = np.ascontiguousarray(np.asarray(inputs["L0_attn_v_w"], dtype=f32))
    wg = np.zeros((D, DFFP), dtype=f32)
    wg[:, :DFF] = np.asarray(inputs["L0_ffn_w_gate"], dtype=f32)
    wu = np.zeros((D, DFFP), dtype=f32)
    wu[:, :DFF] = np.asarray(inputs["L0_ffn_w_up"], dtype=f32)
    wd = np.zeros((DFFP, D), dtype=f32)
    wd[:DFF, :] = np.asarray(inputs["L0_ffn_w_down"], dtype=f32)
    gw4 = np.ascontiguousarray(
        wg.reshape(PC, 128, NF, 128).transpose(2, 1, 0, 3))
    uw4 = np.ascontiguousarray(
        wu.reshape(PC, 128, NF, 128).transpose(2, 1, 0, 3))
    dw4 = np.ascontiguousarray(
        wd.reshape(NF, 128, PC, 128).transpose(2, 1, 0, 3))
    m2 = np.asarray(inputs["L0_attn_maa_w2"], dtype=f32)
    m2w4 = np.zeros((4, PC, 128, 128), dtype=f32)
    for r in range(4):
        m2w4[r, :, 32 * r:32 * r + 32, :] = \
            m2[r].reshape(32, PC, 128).transpose(1, 0, 2)
    maw1 = np.ascontiguousarray(
        np.asarray(inputs["L0_attn_maa_w1"], dtype=f32)
        .reshape(PC, 128, 128).transpose(1, 0, 2).reshape(128, PC * 128))
    tdw1 = np.ascontiguousarray(
        np.asarray(inputs["L0_attn_td_w1"], dtype=f32)
        .reshape(PC, 128, 64).transpose(1, 0, 2).reshape(128, PC * 64))
    _t2 = np.asarray(inputs["L0_attn_td_w2"], dtype=f32)
    tdw2 = np.ascontiguousarray(np.concatenate([_t2[:, 0:1024], _t2[:, 1024:2048]], axis=0))
    rwm = np.ascontiguousarray(
        np.asarray(inputs["L0_ffn_router_w"], dtype=f32)
        .reshape(PC, 128, E8).transpose(1, 0, 2).reshape(128, PC * E8))

    def vcol(name):
        return np.asarray(inputs[name], dtype=f32).reshape(PC, 128).T

    cols = np.zeros((128, 16 * PC), dtype=f32)
    for i, nm in enumerate(["L0_attn_maa_x", "L0_attn_maa_r", "L0_attn_maa_k",
                            "L0_attn_maa_v", "L0_attn_maa_w",
                            "L0_attn_time_decay", "L0_attn_q_b",
                            "L0_attn_k_b", "L0_attn_ln_g", "L0_attn_ln_b",
                            "L0_ffn_b_down"]):
        cols[:, i * PC:(i + 1) * PC] = vcol(nm)
    bg = np.zeros(DFFP, dtype=f32)
    bg[:DFF] = np.asarray(inputs["L0_ffn_b_gate"], dtype=f32)
    bu = np.zeros(DFFP, dtype=f32)
    bu[:DFF] = np.asarray(inputs["L0_ffn_b_up"], dtype=f32)
    bgu = np.concatenate([bg.reshape(NF, 128).T, bu.reshape(NF, 128).T],
                         axis=1).astype(f32)
    bgu = np.ascontiguousarray(bgu)
    vbrow = np.asarray(inputs["L0_attn_v_b"], dtype=f32).reshape(1, D)
    rbrow = np.asarray(inputs["L0_ffn_router_b"], dtype=f32).reshape(1, E8)
    ident = np.eye(128, dtype=f32)
    m1 = np.triu(np.ones((CH, CH), dtype=f32))
    mask8 = np.tile(m1, (1, 8))
    onesc = np.ones((128, 1), dtype=f32)
    onesr = np.ones((1, 128), dtype=f32)

    n = T * E8
    u32 = uniform_f32(42, n).reshape(T, E8)
    u64 = uniform_f64(42, n).reshape(T, E8)
    du = u64 - u32.astype(np.float64)
    duhi = du.astype(f32)
    dulo = (du - duhi.astype(np.float64)).astype(f32)

    SMW = (tpc // 128) * E8
    in_maps = []
    for c in range(ncores):
        sl = slice(c * tpc, (c + 1) * tpc)
        x_sl = np.ascontiguousarray(x[sl])
        x_fmc = np.ascontiguousarray(
            x_sl.reshape(tpc // TILE, TILE, PC, 128)
            .transpose(3, 0, 2, 1).reshape(128, PC * tpc))
        x_bnd = (np.zeros((1, D), dtype=f32) if c == 0
                 else np.ascontiguousarray(x[c * tpc - 1:c * tpc, :]))
        incl2 = np.zeros((128, 16), dtype=f32)
        for r in range(E8):
            incl2[:, r] = 1.0 if r < c else 0.0
            incl2[:, 8 + r] = 0.0 if r < c else 1.0

        def ublk(a):
            return np.ascontiguousarray(
                a[sl].reshape(tpc // 128, 128, E8).transpose(1, 0, 2)
                .reshape(128, SMW))

        u3 = np.concatenate([ublk(u32), ublk(duhi), ublk(dulo)], axis=1)
        in_maps.append({
            "x_tm": x_sl, "x_fm": x_fmc, "x_bnd": x_bnd,
            "qw4": qw4, "kw4": kw4, "ow4": ow4, "vw": vw,
            "gw4": gw4, "uw4": uw4, "dw4": dw4, "m2w4": m2w4,
            "maw1": maw1, "tdw1": tdw1, "tdw2": tdw2, "rw": rwm,
            "cols": cols, "bgu": bgu, "vbrow": vbrow, "rbrow": rbrow,
            "ident": ident, "mask8": mask8, "onesc": onesc, "onesr": onesr,
            "incl2": incl2, "u3": np.ascontiguousarray(u3),
        })
    return in_maps


def kernel(**inputs):
    ncores = NCORES
    T = np.asarray(inputs["L0_attn_x"]).shape[1]
    tpc = T // ncores
    in_maps = stage_inputs(inputs, ncores, tpc)
    nc = build_nc(ncores, tpc)
    res = run_bass_kernel_spmd(nc, in_maps, core_ids=list(range(ncores)))
    outs = []
    for c in range(ncores):
        yf = np.asarray(res.results[c]["y_fm"]).reshape(
            128, tpc // TILE, PC, TILE)
        outs.append(yf.transpose(1, 3, 2, 0).reshape(tpc, D))
    return np.concatenate(outs, axis=0).reshape(1, T, D).astype(np.float32)
